# version 02
# baseline (speedup 1.0000x reference)
import numpy as np
import ml_dtypes
from contextlib import ExitStack

import concourse.bass as bass
import concourse.mybir as mybir
from concourse.bass_utils import run_bass_kernel_spmd

F32 = mybir.dt.float32
BF16 = mybir.dt.bfloat16
I32 = mybir.dt.int32
AF = mybir.ActivationFunctionType
ALU = mybir.AluOpType
AX = mybir.AxisListType

NCORES = 8
DM = 1024
DC = 8
NT = 2048
LR = 0.001
CGRAD = 2.0 / (8192.0 * 1024.0)
EPS = 1e-5
NEG = -(2.0 ** 100)
HEADS = 16
DH = 64
WIN = 256
DIN = 2730
QKSC = 8.0
HALO = 513
TC = 512
BF = ml_dtypes.bfloat16

_ENGS = ("tensor", "vector", "scalar", "gpsimd", "sync")


def f_chunks(total, step=128):
    out = []
    o = 0
    while o < total:
        out.append((o, min(step, total - o)))
        o += step
    return out


class S:
    def __init__(self, nc, ctx, name):
        self.sem = ctx.enter_context(nc.semaphore(name))
        self.n = 0

    def inc(self, instr, k=1):
        instr.then_inc(self.sem, k)
        self.n += k
        return self.n

    def wait(self, eng, n=None):
        eng.wait_ge(self.sem, self.n if n is None else n)


class P:
    def __init__(self):
        self.nc = bass.Bass("TRN2", target_bir_lowering=False, debug=False)
        self.ctx = ExitStack()
        self.blk = None
        for nm in ("dg", "dh", "da", "mm", "vv", "ss", "pg", "st", "sg"):
            setattr(self, nm, S(self.nc, self.ctx, nm))

    def sb(self, name, shape, dtype):
        return self.ctx.enter_context(self.nc.sbuf_tensor(name, shape, dtype))

    def pt(self, name, shape, dtype=F32):
        return self.ctx.enter_context(self.nc.psum_tensor(name, shape, dtype))

    def inp(self, name, shape, dtype=F32):
        return self.nc.dram_tensor(name, shape, dtype, kind="ExternalInput").ap()

    def outp(self, name, shape, dtype=F32):
        return self.nc.dram_tensor(name, shape, dtype, kind="ExternalOutput").ap()

    def begin(self, pool="full"):
        self.pool = pool
        self.blk = self.ctx.enter_context(self.nc.Block())
        self.c_onesf = self.sb("c_onesf", [128, 1], F32)
        self.c_onesb = self.sb("c_onesb", [128, 1], BF16)
        self.c_onesr_f = self.sb("c_onesr_f", [1, 128], F32)
        self.c_onesr_b = self.sb("c_onesr_b", [1, 128], BF16)

        if self.pool == "full":
            self.psA_b = self.pt("pool_psA_b", [128, 128], BF16)
            self.psB_b = self.pt("pool_psB_b", [128, 128], BF16)
        if self.pool != "tiny":
            self.psA_f = self.pt("pool_psA_f", [128, 128], F32)
            self.psB_f = self.pt("pool_psB_f", [128, 128], F32)
        self.ps_main = self.pt("pool_ps_main", [128, 512], F32)

        @self.blk.gpsimd
        def _(g):
            self.pg.inc(g.memset(self.c_onesf[:], 1.0))
            self.pg.inc(g.memset(self.c_onesb[:], 1.0))
            self.pg.inc(g.memset(self.c_onesr_f[:], 1.0))
            self.pg.inc(g.memset(self.c_onesr_b[:], 1.0))
        return self.blk

    def bar(self):
        marks = [(s, s.n) for s in
                 (self.dg, self.dh, self.da, self.mm, self.vv, self.ss,
                  self.pg, self.sg)]

        def mkbody(eng_name):
            def body(e):
                for sem, val in marks:
                    if val:
                        e.wait_ge(sem.sem, val)
            return body
        self.blk.tensor(mkbody("tensor"))
        self.blk.vector(mkbody("vector"))
        self.blk.scalar(mkbody("scalar"))
        self.blk.gpsimd(mkbody("gpsimd"))
        self.blk.sync(mkbody("sync"))

    def done(self):
        self.ctx.close()
        return self.nc



def transpose_tiles(p, pairs, ident, psA, psB):
    mb = p.mm.n
    vb = p.vv.n
    ps = (psA, psB)

    @p.blk.tensor
    def _(t):
        t.wait_ge(p.vv.sem, vb)
        for i, (s, d) in enumerate(pairs):
            if i >= 2:
                t.wait_ge(p.vv.sem, vb + i - 1)
            p.mm.inc(t.transpose(ps[i % 2][:, :], s, ident[:]))

    @p.blk.vector
    def _(v):
        for i, (s, d) in enumerate(pairs):
            v.wait_ge(p.mm.sem, mb + i + 1)
            p.vv.inc(v.tensor_copy(d, ps[i % 2][:, 0:d.shape[-1]]))


def row_to_cols(p, row_ap, ncol, out_sb, out_off, psum):
    mb = p.mm.n
    vb = p.vv.n

    @p.blk.tensor
    def _(t):
        t.wait_ge(p.vv.sem, vb)
        for c in range(ncol):
            m = t.matmul(psum[:, c:c + 1],
                         row_ap[0:1, c * 128:(c + 1) * 128],
                         p.c_onesf[0:1, 0:1], start=True, stop=True)
        p.mm.inc(m)

    @p.blk.vector
    def _(v):
        v.wait_ge(p.mm.sem, p.mm.n)
        p.vv.inc(v.tensor_copy(out_sb[:, out_off:out_off + ncol], psum[:, 0:ncol]))


def ln_tm_tile(p, x_ap, xn_ap, sq_ap, scr, d=DM, eps=EPS):
    @p.blk.vector
    def _(v):
        p.vv.inc(v.tensor_reduce(scr["s"][:], x_ap, AX.X, ALU.add))

    @p.blk.scalar
    def _(s):
        p.ss.inc(s.activation(sq_ap, x_ap, AF.Square, accum_out=scr["q"][:]))

    vb, sb_ = p.vv.n, p.ss.n

    @p.blk.vector
    def _(v):
        v.wait_ge(p.vv.sem, vb)
        v.wait_ge(p.ss.sem, sb_)
        p.vv.inc(v.tensor_scalar_mul(scr["mu"][:], scr["s"][:], 1.0 / d))
        p.vv.inc(v.tensor_scalar_mul(scr["rstd"][:], scr["q"][:], 1.0 / d))
        v.wait_ge(p.vv.sem, p.vv.n)
        p.vv.inc(v.tensor_mul(scr["s"][:], scr["mu"][:], scr["mu"][:]))
        v.wait_ge(p.vv.sem, p.vv.n)
        p.vv.inc(v.tensor_tensor(scr["rstd"][:], scr["rstd"][:], scr["s"][:], ALU.subtract))
        v.wait_ge(p.vv.sem, p.vv.n)
        p.vv.inc(v.tensor_scalar_add(scr["rstd"][:], scr["rstd"][:], eps))

    sb2 = p.ss.n

    @p.blk.scalar
    def _(s):
        s.wait_ge(p.vv.sem, p.vv.n)
        p.ss.inc(s.activation(scr["rstd"][:], scr["rstd"][:], AF.Sqrt))

    @p.blk.vector
    def _(v):
        v.wait_ge(p.ss.sem, p.ss.n)
        p.vv.inc(v.reciprocal(scr["rstd"][:], scr["rstd"][:]))
        v.wait_ge(p.vv.sem, p.vv.n)
        p.vv.inc(v.tensor_scalar(xn_ap, x_ap, scr["mu"][:], scr["rstd"][:],
                                 ALU.subtract, ALU.mult))


def ln_fm(p, src_off_fn, nfree, dst_off_fn, x_add_off_fn, bufs, g_col=None,
          b_col=None, src_sb=None, dst_sb=None, x_sb=None):
    assert nfree <= 512
    sq = bufs["sq"]; tmp = bufs["tmp"]
    mu = bufs["mu"]; rstd = bufs["rstd"]
    mub = bufs["mub"]; rsb = bufs["rsb"]
    ps = p.ps_main

    mm_b0 = p.mm.n
    ss_b0 = p.ss.n

    @p.blk.scalar
    def _(s):
        for c in range(DC):
            if c >= 2:
                s.wait_ge(p.mm.sem, mm_b0 + 1 + c - 1)
            p.ss.inc(s.activation(sq[:, (c % 2) * nfree:(c % 2 + 1) * nfree],
                                  src_sb[:, src_off_fn(c):src_off_fn(c) + nfree],
                                  AF.Square))

    @p.blk.tensor
    def _(t):
        t.wait_ge(p.vv.sem, p.vv.n)
        for c in range(DC):
            m = t.matmul(ps[0:1, 0:nfree], p.c_onesf[:, 0:1],
                         src_sb[:, src_off_fn(c):src_off_fn(c) + nfree],
                         start=(c == 0), stop=(c == DC - 1))
        p.mm.inc(m)
    mm1 = p.mm.n

    @p.blk.vector
    def _(v):
        v.wait_ge(p.mm.sem, mm1)
        p.vv.inc(v.tensor_scalar_mul(mu[:], ps[0:1, 0:nfree], 1.0 / DM))
    vv1 = p.vv.n

    @p.blk.tensor
    def _(t):
        t.wait_ge(p.vv.sem, vv1)
        for c in range(DC):
            t.wait_ge(p.ss.sem, ss_b0 + c + 1)
            p.mm.inc(t.matmul(ps[0:1, 0:nfree], p.c_onesf[:, 0:1],
                              sq[:, (c % 2) * nfree:(c % 2 + 1) * nfree],
                              start=(c == 0), stop=(c == DC - 1)))
    mm2 = p.mm.n

    @p.blk.vector
    def _(v):
        v.wait_ge(p.mm.sem, mm2)
        p.vv.inc(v.tensor_scalar_mul(rstd[:], ps[0:1, 0:nfree], 1.0 / DM))
        v.wait_ge(p.vv.sem, p.vv.n)
        p.vv.inc(v.tensor_mul(tmp[0:1, 0:nfree], mu[:], mu[:]))
        v.wait_ge(p.vv.sem, p.vv.n)
        p.vv.inc(v.tensor_tensor(rstd[:], rstd[:], tmp[0:1, 0:nfree], ALU.subtract))
        v.wait_ge(p.vv.sem, p.vv.n)
        p.vv.inc(v.tensor_scalar_add(rstd[:], rstd[:], EPS))

    @p.blk.scalar
    def _(s):
        s.wait_ge(p.vv.sem, p.vv.n)
        p.ss.inc(s.activation(rstd[:], rstd[:], AF.Sqrt))

    @p.blk.vector
    def _(v):
        v.wait_ge(p.ss.sem, p.ss.n)
        p.vv.inc(v.reciprocal(rstd[:], rstd[:]))
    vrec = p.vv.n

    @p.blk.tensor
    def _(t):
        t.wait_ge(p.vv.sem, vrec)
        p.mm.inc(t.matmul(ps[:, 0:nfree], p.c_onesr_f[0:1, :], mu[0:1, 0:nfree],
                          start=True, stop=True))
    mmb1 = p.mm.n

    @p.blk.vector
    def _(v):
        v.wait_ge(p.mm.sem, mmb1)
        p.vv.inc(v.tensor_copy(mub[:, 0:nfree], ps[:, 0:nfree]))
    vvb1 = p.vv.n

    @p.blk.tensor
    def _(t):
        t.wait_ge(p.vv.sem, vvb1)
        p.mm.inc(t.matmul(ps[:, 0:nfree], p.c_onesr_f[0:1, :], rstd[0:1, 0:nfree],
                          start=True, stop=True))
    mmb2 = p.mm.n

    @p.blk.vector
    def _(v):
        v.wait_ge(p.mm.sem, mmb2)
        p.vv.inc(v.tensor_copy(rsb[:, 0:nfree], ps[:, 0:nfree]))
        v.wait_ge(p.vv.sem, p.vv.n)
        for c in range(DC):
            src = src_sb[:, src_off_fn(c):src_off_fn(c) + nfree]
            p.vv.inc(v.tensor_tensor(tmp[:, 0:nfree], src, mub[:, 0:nfree],
                                     ALU.subtract))
            v.wait_ge(p.vv.sem, p.vv.n)
            p.vv.inc(v.tensor_tensor(tmp[:, 0:nfree], tmp[:, 0:nfree],
                                     rsb[:, 0:nfree], ALU.mult))
            v.wait_ge(p.vv.sem, p.vv.n)
            if g_col is not None:
                p.vv.inc(v.tensor_scalar(tmp[:, 0:nfree], tmp[:, 0:nfree],
                                         g_col(c), b_col(c), ALU.mult, ALU.add))
                v.wait_ge(p.vv.sem, p.vv.n)
            dst = dst_sb[:, dst_off_fn(c):dst_off_fn(c) + nfree]
            if x_add_off_fn is not None:
                xa = x_sb[:, x_add_off_fn(c):x_add_off_fn(c) + nfree]
                p.vv.inc(v.tensor_tensor(dst, tmp[:, 0:nfree], xa, ALU.add))
            else:
                p.vv.inc(v.tensor_copy(dst, tmp[:, 0:nfree]))
            v.wait_ge(p.vv.sem, p.vv.n)



def ttt_prep_params(p, w1_in, w2_in, b1_in, b2_in, identf, masters=False):
    r = {}
    r["w1t_b"] = p.sb("w1t_b", [128, DC * 256], BF16)
    r["w2_b"] = p.sb("w2_b", [128, DC * 256], BF16)
    r["w2t_b"] = p.sb("w2t_b", [128, 2 * 1024], BF16)
    r["b1c"] = p.sb("b1c", [128, 2], F32)
    r["b2c"] = p.sb("b2c", [128, DC], F32)
    r["b1row"] = p.sb("b1row", [1, 256], F32)
    r["b2row"] = p.sb("b2row", [1, 1024], F32)
    if masters:
        r["w1t_f"] = p.sb("w1t_f", [128, DC * 256], F32)
        r["w2_f"] = p.sb("w2_f", [128, DC * 256], F32)
    _w1w2 = ExitStack()
    w1r = _w1w2.enter_context(p.nc.sbuf_tensor("w1r_tmp", [128, 2 * 1024], F32))
    w2r = _w1w2.enter_context(p.nc.sbuf_tensor("w2r_tmp", [128, DC * 256], F32))
    psA = p.psA_f
    psB = p.psB_f
    pcol = p.psA_f

    @p.blk.sync
    def _(sy):
        for c in range(2):
            p.dh.inc(sy.dma_start(out=w1r[:, c * 1024:(c + 1) * 1024],
                                  in_=w1_in[c * 128:(c + 1) * 128, :]), 16)
        for c in range(DC):
            p.dh.inc(sy.dma_start(out=w2r[:, c * 256:(c + 1) * 256],
                                  in_=w2_in[c * 128:(c + 1) * 128, :]), 16)
        p.dh.inc(sy.dma_start(out=r["b1row"][:], in_=b1_in), 16)
        p.dh.inc(sy.dma_start(out=r["b2row"][:], in_=b2_in), 16)
    p.bar()

    dst_w1t = r["w1t_f"] if masters else r["w1t_b"]
    pairs = []
    for hc in range(DC):
        for mc in range(2):
            pairs.append((w1r[:, mc * 1024 + hc * 128: mc * 1024 + hc * 128 + 128],
                          dst_w1t[:, hc * 256 + mc * 128: hc * 256 + mc * 128 + 128]))
    for mc in range(2):
        for hc in range(DC):
            pairs.append((w2r[:, hc * 256 + mc * 128: hc * 256 + mc * 128 + 128],
                          r["w2t_b"][:, mc * 1024 + hc * 128: mc * 1024 + hc * 128 + 128]))
    transpose_tiles(p, pairs, identf, psA, psB)

    @p.blk.vector
    def _(v):
        v.wait_ge(p.dh.sem, p.dh.n)
        if masters:
            for c in range(DC):
                p.vv.inc(v.tensor_copy(r["w2_f"][:, c * 256:(c + 1) * 256],
                                       w2r[:, c * 256:(c + 1) * 256]))
        for c in range(DC):
            p.vv.inc(v.tensor_copy(r["w2_b"][:, c * 256:(c + 1) * 256],
                                   w2r[:, c * 256:(c + 1) * 256]))
    row_to_cols(p, r["b1row"], 2, r["b1c"], 0, pcol)
    row_to_cols(p, r["b2row"], DC, r["b2c"], 0, pcol)
    p.bar()
    _w1w2.close()
    if masters:
        @p.blk.vector
        def _(v):
            for c in range(DC):
                p.vv.inc(v.tensor_copy(r["w1t_b"][:, c * 256:(c + 1) * 256],
                                       r["w1t_f"][:, c * 256:(c + 1) * 256]))
        p.bar()
    return r


def ttt_update_params(p, pr, g, lr=LR):
    @p.blk.vector
    def _(v):
        for c in range(DC):
            sl = slice(c * 256, (c + 1) * 256)
            p.vv.inc(v.scalar_tensor_tensor(pr["w1t_f"][:, sl], g["gw1t"][:, sl],
                                            -lr, pr["w1t_f"][:, sl], ALU.mult, ALU.add))
            p.vv.inc(v.scalar_tensor_tensor(pr["w2_f"][:, sl], g["gw2"][:, sl],
                                            -lr, pr["w2_f"][:, sl], ALU.mult, ALU.add))
        p.vv.inc(v.scalar_tensor_tensor(pr["b1row"][:], g["gb1"][:], -lr,
                                        pr["b1row"][:], ALU.mult, ALU.add))
        p.vv.inc(v.scalar_tensor_tensor(pr["b2row"][:], g["gb2"][:], -lr,
                                        pr["b2row"][:], ALU.mult, ALU.add))
    p.bar()

    @p.blk.vector
    def _(v):
        for c in range(DC):
            sl = slice(c * 256, (c + 1) * 256)
            p.vv.inc(v.tensor_copy(pr["w1t_b"][:, sl], pr["w1t_f"][:, sl]))
            p.vv.inc(v.tensor_copy(pr["w2_b"][:, sl], pr["w2_f"][:, sl]))
    pairs = []
    for mc in range(2):
        for hc in range(DC):
            pairs.append((pr["w2_f"][:, hc * 256 + mc * 128: hc * 256 + mc * 128 + 128],
                          pr["w2t_b"][:, mc * 1024 + hc * 128: mc * 1024 + hc * 128 + 128]))
    transpose_tiles(p, pairs, pr["identf"], pr["psA"], pr["psB"])
    row_to_cols(p, pr["b1row"], 2, pr["b1c"], 0, pr["pcol"])
    row_to_cols(p, pr["b2row"], DC, pr["b2c"], 0, pr["pcol"])
    p.bar()


def ttt_recon_fwd(p, pr, zin_bf_sb, zoff_fn, n, apb, h_b, psum):
    for mc in range(2):
        mmn0 = p.mm.n

        @p.blk.tensor
        def _(t):
            for hc in range(DC):
                m = t.matmul(psum[:, 0:n],
                             pr["w1t_b"][:, hc * 256 + mc * 128: hc * 256 + mc * 128 + 128],
                             zin_bf_sb[:, zoff_fn(hc):zoff_fn(hc) + n],
                             start=(hc == 0), stop=(hc == DC - 1))
            p.mm.inc(m)

        @p.blk.scalar
        def _(s):
            s.wait_ge(p.mm.sem, p.mm.n)
            p.ss.inc(s.activation(apb[:, mc * n:(mc + 1) * n], psum[:, 0:n],
                                  AF.Identity, bias=pr["b1c"][:, mc:mc + 1]))
            s.wait_ge(p.ss.sem, p.ss.n)
            p.ss.inc(s.activation(h_b[:, mc * n:(mc + 1) * n],
                                  apb[:, mc * n:(mc + 1) * n], AF.Gelu))

        @p.blk.tensor
        def _(t):
            t.wait_ge(p.ss.sem, p.ss.n)


def ttt_recon_out(p, pr, h_b, n, zadd_sb, zadd_off_fn, out_sb, out_off_fn, psum,
                  out_dtype_note="f32"):
    for hc in range(DC):
        @p.blk.tensor
        def _(t):
            for mc in range(2):
                m = t.matmul(psum[:, 0:n],
                             pr["w2t_b"][:, mc * 1024 + hc * 128: mc * 1024 + hc * 128 + 128],
                             h_b[:, mc * n:(mc + 1) * n],
                             start=(mc == 0), stop=(mc == 1))
            p.mm.inc(m)

        @p.blk.scalar
        def _(s):
            s.wait_ge(p.mm.sem, p.mm.n)
            p.ss.inc(s.activation(psum[:, 0:n], psum[:, 0:n], AF.Identity,
                                  bias=pr["b2c"][:, hc:hc + 1]))

        @p.blk.vector
        def _(v):
            v.wait_ge(p.ss.sem, p.ss.n)
            p.vv.inc(v.tensor_tensor(out_sb[:, out_off_fn(hc):out_off_fn(hc) + n],
                                     psum[:, 0:n],
                                     zadd_sb[:, zadd_off_fn(hc):zadd_off_fn(hc) + n],
                                     ALU.add))

        @p.blk.tensor
        def _(t):
            t.wait_ge(p.vv.sem, p.vv.n)


def ttt_grad_chunk(p, pr, bufs, n=TC):
    nt_t = n // 128
    z_bf = bufs["z_bf"]; z_tm = bufs["z_tm"]; zmy = bufs["zmy"]
    apb = bufs["apb"]; h_b = bufs["h_b"]; gp = bufs["gp"]
    rb = bufs["rb"]; r_tm = bufs["r_tm"]; h_tm = bufs["h_tm"]
    da_b = bufs["da_b"]; da_tm = bufs["da_tm"]
    psum = bufs["psum"]; identb = bufs["identb"]
    psA = bufs["psA"]; psB = bufs["psB"]

    ttt_recon_fwd(p, pr, z_bf, lambda hc: hc * n, n, apb, h_b, psum)
    p.bar()

    @p.blk.scalar
    def _(s):
        for mc in range(2):
            sl = slice(mc * n, (mc + 1) * n)
            p.ss.inc(s.activation(gp[:, sl], apb[:, sl], AF.Erf,
                                  scale=0.7071067811865476))
            p.ss.inc(s.activation(bufs["gp2"][:, sl], apb[:, sl], AF.Square))
        s.wait_ge(p.ss.sem, p.ss.n)
        for mc in range(2):
            sl = slice(mc * n, (mc + 1) * n)
            p.ss.inc(s.activation(bufs["gp2"][:, sl], bufs["gp2"][:, sl],
                                  AF.Exp, scale=-0.5))

    @p.blk.vector
    def _(v):
        v.wait_ge(p.ss.sem, p.ss.n)
        for mc in range(2):
            sl = slice(mc * n, (mc + 1) * n)
            p.vv.inc(v.tensor_scalar(gp[:, sl], gp[:, sl], 0.5, 0.5,
                                     ALU.mult, ALU.add))
            p.vv.inc(v.tensor_mul(bufs["gp2"][:, sl], bufs["gp2"][:, sl],
                                  apb[:, sl]))
        v.wait_ge(p.vv.sem, p.vv.n)
        for mc in range(2):
            sl = slice(mc * n, (mc + 1) * n)
            p.vv.inc(v.scalar_tensor_tensor(gp[:, sl], bufs["gp2"][:, sl],
                                            0.3989422804014327, gp[:, sl],
                                            ALU.mult, ALU.add))
    p.bar()

    pairs = []
    for mc in range(2):
        for t_ in range(nt_t):
            pairs.append((h_b[:, mc * n + t_ * 128: mc * n + t_ * 128 + 128],
                          h_tm[:, t_ * 256 + mc * 128: t_ * 256 + mc * 128 + 128]))
    transpose_tiles(p, pairs, identb, psA, psB)
    p.bar()

    for hc in range(DC):
        @p.blk.tensor
        def _(t):
            for mc in range(2):
                m = t.matmul(psum[:, 0:n],
                             pr["w2t_b"][:, mc * 1024 + hc * 128: mc * 1024 + hc * 128 + 128],
                             h_b[:, mc * n:(mc + 1) * n],
                             start=(mc == 0), stop=(mc == 1))
            p.mm.inc(m)

        @p.blk.scalar
        def _(s):
            s.wait_ge(p.mm.sem, p.mm.n)
            p.ss.inc(s.activation(psum[:, 0:n], psum[:, 0:n], AF.Identity,
                                  bias=pr["b2c"][:, hc:hc + 1]))

        @p.blk.vector
        def _(v):
            v.wait_ge(p.ss.sem, p.ss.n)
            p.vv.inc(v.tensor_tensor(rb[:, hc * n:(hc + 1) * n], psum[:, 0:n],
                                     zmy[:, hc * n:(hc + 1) * n], ALU.add))

        @p.blk.tensor
        def _(t):
            t.wait_ge(p.vv.sem, p.vv.n)
    p.bar()

    pairs = []
    for hc in range(DC):
        for t_ in range(nt_t):
            pairs.append((rb[:, hc * n + t_ * 128: hc * n + t_ * 128 + 128],
                          r_tm[:, t_ * 1024 + hc * 128: t_ * 1024 + hc * 128 + 128]))
    transpose_tiles(p, pairs, identb, psA, psB)
    p.bar()

    for mc in range(2):
        @p.blk.tensor
        def _(t):
            for hc in range(DC):
                m = t.matmul(psum[:, 0:n],
                             pr["w2_b"][:, hc * 256 + mc * 128: hc * 256 + mc * 128 + 128],
                             rb[:, hc * n:(hc + 1) * n],
                             start=(hc == 0), stop=(hc == DC - 1))
            p.mm.inc(m)

        @p.blk.vector
        def _(v):
            v.wait_ge(p.mm.sem, p.mm.n)
            p.vv.inc(v.tensor_tensor(da_b[:, mc * n:(mc + 1) * n], psum[:, 0:n],
                                     gp[:, mc * n:(mc + 1) * n], ALU.mult))

        @p.blk.tensor
        def _(t):
            t.wait_ge(p.vv.sem, p.vv.n)
    p.bar()

    pairs = []
    for mc in range(2):
        for t_ in range(nt_t):
            pairs.append((da_b[:, mc * n + t_ * 128: mc * n + t_ * 128 + 128],
                          da_tm[:, t_ * 256 + mc * 128: t_ * 256 + mc * 128 + 128]))
    transpose_tiles(p, pairs, identb, psA, psB)
    p.bar()

    for hc in range(DC):
        @p.blk.tensor
        def _(t):
            for t_ in range(nt_t):
                m = t.matmul(psum[:, 0:256],
                             z_tm[:, t_ * 1024 + hc * 128: t_ * 1024 + hc * 128 + 128],
                             da_tm[:, t_ * 256:(t_ + 1) * 256],
                             start=(t_ == 0), stop=(t_ == nt_t - 1))
            p.mm.inc(m)

        @p.blk.vector
        def _(v):
            v.wait_ge(p.mm.sem, p.mm.n)
            sl = slice(hc * 256, (hc + 1) * 256)
            p.vv.inc(v.tensor_tensor(bufs["acc_w1t"][:, sl], bufs["acc_w1t"][:, sl],
                                     psum[:, 0:256], ALU.add))

        @p.blk.tensor
        def _(t):
            t.wait_ge(p.vv.sem, p.vv.n)
            for t_ in range(nt_t):
                m = t.matmul(psum[:, 0:256],
                             r_tm[:, t_ * 1024 + hc * 128: t_ * 1024 + hc * 128 + 128],
                             h_tm[:, t_ * 256:(t_ + 1) * 256],
                             start=(t_ == 0), stop=(t_ == nt_t - 1))
            p.mm.inc(m)

        @p.blk.vector
        def _(v):
            v.wait_ge(p.mm.sem, p.mm.n)
            sl = slice(hc * 256, (hc + 1) * 256)
            p.vv.inc(v.tensor_tensor(bufs["acc_w2"][:, sl], bufs["acc_w2"][:, sl],
                                     psum[:, 0:256], ALU.add))

        @p.blk.tensor
        def _(t):
            t.wait_ge(p.vv.sem, p.vv.n)

    first = bufs["db_first"]

    @p.blk.tensor
    def _(t):
        for t_ in range(nt_t):
            st0 = first and t_ == 0
            sp = bool(bufs.get("db_stop")) and t_ == nt_t - 1
            t.matmul(bufs["psum_b1"][0:1, 0:256], p.c_onesb[:, 0:1],
                     da_tm[:, t_ * 256:(t_ + 1) * 256],
                     start=st0, stop=sp)
            t.matmul(bufs["psum_b2"][0:1, 0:512], p.c_onesb[:, 0:1],
                     r_tm[:, t_ * 1024:t_ * 1024 + 512],
                     start=st0, stop=sp)
            m = t.matmul(bufs["psum_b2"][0:1, 512:1024], p.c_onesb[:, 0:1],
                         r_tm[:, t_ * 1024 + 512:t_ * 1024 + 1024],
                         start=st0, stop=sp)
        p.mm.inc(m)
    bufs["db_first"] = False
    p.bar()



def build_launch_A(nt=NT):
    p = P()
    nch = nt // TC
    x_tm = p.inp("x_tm", [nt, DM])
    tk_in = p.inp("tk", [DM, DM]); tv_in = p.inp("tv", [DM, DM])
    tq_in = p.inp("tq", [DM, DM])
    w1_in = p.inp("w1", [256, DM]); w2_in = p.inp("w2", [DM, 256])
    b1_in = p.inp("b1", [1, 256]); b2_in = p.inp("b2", [1, DM])
    identf_in = p.inp("identf", [128, 128])
    identb_in = p.inp("identb", [128, 128], BF16)
    xfm_o = p.outp("xfm", [DM, nt])
    qfm_o = p.outp("qfm", [DM, nt])
    zbf_o = p.outp("zbf", [DM, nt], BF16)
    ztm_o = p.outp("ztm", [nt, DM], BF16)
    zmy_o = p.outp("zmyo", [DM, nt])
    gw1t_o = p.outp("gw1t", [128, DC * 256])
    gw2_o = p.outp("gw2", [128, DC * 256])
    gb1_o = p.outp("gb1", [1, 256])
    gb2_o = p.outp("gb2", [1, DM])

    p.begin()
    identf = p.sb("sb_identf", [128, 128], F32)
    identb = p.sb("sb_identb", [128, 128], BF16)

    @p.blk.sync
    def _(sy):
        p.dh.inc(sy.dma_start(out=identf[:], in_=identf_in), 16)
        p.dh.inc(sy.dma_start(out=identb[:], in_=identb_in), 16)

    pr = ttt_prep_params(p, w1_in, w2_in, b1_in, b2_in, identf, masters=False)

    tkT = p.sb("tkT", [128, DC * DM], BF16)
    tvT = p.sb("tvT", [128, DC * DM], BF16)
    tqT = p.sb("tqT", [128, DC * DM], BF16)
    psA = p.psA_b
    psB = p.psB_b
    psAf = p.psA_f
    psBf = p.psB_f
    with p.nc.sbuf_tensor("traw", [128, DC * DM], BF16) as traw:
        for t_in, tT in ((tk_in, tkT), (tv_in, tvT), (tq_in, tqT)):
            @p.blk.gpsimd
            def _(g):
                for c in range(DC):
                    p.dg.inc(g.dma_start(out=traw[:, c * DM:(c + 1) * DM],
                                         in_=t_in[c * 128:(c + 1) * 128, :]), 16)
            p.bar()
            pairs = []
            for hc in range(DC):
                for dc in range(DC):
                    pairs.append(
                        (traw[:, hc * DM + dc * 128: hc * DM + dc * 128 + 128],
                         tT[:, dc * DM + hc * 128: dc * DM + hc * 128 + 128]))
            transpose_tiles(p, pairs, identb, psA, psB)
            p.bar()

    acc_w1t = p.sb("acc_w1t", [128, DC * 256], F32)
    acc_w2 = p.sb("acc_w2", [128, DC * 256], F32)
    psum_b1 = p.pt("psum_b1", [1, 256])
    psum_b2 = p.pt("psum_b2", [1, 1024])

    @p.blk.vector
    def _(v):
        p.vv.inc(v.memset(acc_w1t[:], 0.0))
        p.vv.inc(v.memset(acc_w2[:], 0.0))
    p.bar()

    n = TC
    nt_t = n // 128
    xtmc = p.sb("xtmc", [128, nt_t * DM], F32)
    sqs = p.sb("sqs", [128, DM], F32)
    scr = {k: p.sb("lns_" + k, [128, 1], F32) for k in ("s", "q", "mu", "rstd")}
    xntm = p.sb("xntm", [128, nt_t * DM], BF16)
    xnfm = p.sb("xnfm", [128, DC * n], BF16)
    z_bf = p.sb("z_bf", [128, DC * n], BF16)
    z_tm = p.sb("z_tm", [128, nt_t * DM], BF16)
    zmy = p.sb("zmy", [128, DC * n], F32)
    zfr = p.sb("zfr", [128, n], F32)
    qfr = p.sb("qfr", [128, n], F32)
    apb = p.sb("apb", [128, 2 * n], F32)
    h_b = p.sb("h_b", [128, 2 * n], BF16)
    gp = p.sb("gp", [128, 2 * n], F32)
    gp2 = p.sb("gp2", [128, 2 * n], F32)
    rb = p.sb("rb", [128, DC * n], BF16)
    r_tm = p.sb("r_tm", [128, nt_t * DM], BF16)
    h_tm = p.sb("h_tm", [128, nt_t * 256], BF16)
    da_b = p.sb("da_b", [128, 2 * n], BF16)
    da_tm = p.sb("da_tm", [128, nt_t * 256], BF16)
    psum = p.ps_main
    psum2 = p.ps_main

    bufs = dict(z_bf=z_bf, z_tm=z_tm, zmy=zmy, apb=apb, h_b=h_b, gp=gp, gp2=gp2,
                rb=rb, r_tm=r_tm, h_tm=h_tm, da_b=da_b, da_tm=da_tm,
                psum=psum, psA=psA, psB=psB, identb=identb,
                acc_w1t=acc_w1t, acc_w2=acc_w2, psum_b1=psum_b1,
                psum_b2=psum_b2, db_first=True, db_stop=False)

    for ch in range(nch):
        if ch == nch - 1:
            bufs["db_stop"] = True
        t0 = ch * n

        @p.blk.gpsimd
        def _(g):
            for t_ in range(nt_t):
                p.dg.inc(g.dma_start(out=xtmc[:, t_ * DM:(t_ + 1) * DM],
                                     in_=x_tm[t0 + t_ * 128: t0 + (t_ + 1) * 128, :]), 16)
        p.bar()
        for t_ in range(nt_t):
            ln_tm_tile(p, xtmc[:, t_ * DM:(t_ + 1) * DM],
                       xntm[:, t_ * DM:(t_ + 1) * DM], sqs[:, :], scr)
            p.bar()
        pairs = []
        for t_ in range(nt_t):
            for dc in range(DC):
                pairs.append((xntm[:, t_ * DM + dc * 128: t_ * DM + dc * 128 + 128],
                              xnfm[:, dc * n + t_ * 128: dc * n + t_ * 128 + 128]))
        transpose_tiles(p, pairs, identb, psA, psB)
        xf_stage = zmy
        pairs = []
        for t_ in range(nt_t):
            for dc in range(DC):
                pairs.append((xtmc[:, t_ * DM + dc * 128: t_ * DM + dc * 128 + 128],
                              xf_stage[:, dc * n + t_ * 128: dc * n + t_ * 128 + 128]))
        transpose_tiles(p, pairs, identf, psAf, psBf)
        p.bar()

        @p.blk.sync
        def _(sy):
            for dc in range(DC):
                p.dh.inc(sy.dma_start(out=xfm_o[dc * 128:(dc + 1) * 128, t0:t0 + n],
                                      in_=xf_stage[:, dc * n:(dc + 1) * n]), 16)
        p.bar()

        for (tT, mode) in ((tkT, "z"), (tvT, "y"), (tqT, "q")):
            for hc in range(DC):
                @p.blk.tensor
                def _(t):
                    for dc in range(DC):
                        m = t.matmul(psum[:, 0:n],
                                     tT[:, dc * DM + hc * 128: dc * DM + hc * 128 + 128],
                                     xnfm[:, dc * n:(dc + 1) * n],
                                     start=(dc == 0), stop=(dc == DC - 1))
                    p.mm.inc(m)

                if mode == "z":
                    @p.blk.vector
                    def _(v):
                        v.wait_ge(p.mm.sem, p.mm.n)
                        p.vv.inc(v.tensor_copy(z_bf[:, hc * n:(hc + 1) * n],
                                               psum[:, 0:n]))

                    @p.blk.scalar
                    def _(s):
                        s.wait_ge(p.mm.sem, p.mm.n)
                        p.ss.inc(s.activation(zfr[:, 0:n], psum[:, 0:n], AF.Identity))

                    @p.blk.tensor
                    def _(t):
                        t.wait_ge(p.vv.sem, p.vv.n)
                        t.wait_ge(p.ss.sem, p.ss.n)
                        for dc in range(DC):
                            m = t.matmul(psum2[:, 0:n],
                                         tvT[:, dc * DM + hc * 128: dc * DM + hc * 128 + 128],
                                         xnfm[:, dc * n:(dc + 1) * n],
                                         start=(dc == 0), stop=(dc == DC - 1))
                        p.mm.inc(m)

                    @p.blk.vector
                    def _(v):
                        v.wait_ge(p.mm.sem, p.mm.n)
                        p.vv.inc(v.tensor_tensor(zmy[:, hc * n:(hc + 1) * n],
                                                 zfr[:, 0:n], psum2[:, 0:n],
                                                 ALU.subtract))

                    @p.blk.tensor
                    def _(t):
                        t.wait_ge(p.vv.sem, p.vv.n)
                elif mode == "q":
                    @p.blk.vector
                    def _(v):
                        v.wait_ge(p.mm.sem, p.mm.n)
                        p.vv.inc(v.tensor_copy(qfr[:, 0:n], psum[:, 0:n]))

                    @p.blk.sync
                    def _(sy):
                        sy.wait_ge(p.vv.sem, p.vv.n)
                        p.dh.inc(sy.dma_start(
                            out=qfm_o[hc * 128:(hc + 1) * 128, t0:t0 + n],
                            in_=qfr[:, 0:n]), 16)

                    @p.blk.tensor
                    def _(t):
                        t.wait_ge(p.dh.sem, p.dh.n)
        p.bar()

        for t_ in range(nt_t):
            for half in range(2):
                @p.blk.tensor
                def _(t):
                    for dc in range(DC):
                        m = t.matmul(psum[:, 0:512],
                                     xnfm[:, dc * n + t_ * 128: dc * n + t_ * 128 + 128],
                                     tkT[:, dc * DM + half * 512: dc * DM + (half + 1) * 512],
                                     start=(dc == 0), stop=(dc == DC - 1))
                    p.mm.inc(m)

                @p.blk.vector
                def _(v):
                    v.wait_ge(p.mm.sem, p.mm.n)
                    p.vv.inc(v.tensor_copy(
                        z_tm[:, t_ * DM + half * 512: t_ * DM + (half + 1) * 512],
                        psum[:, 0:512]))

                @p.blk.tensor
                def _(t):
                    t.wait_ge(p.vv.sem, p.vv.n)
        p.bar()

        @p.blk.gpsimd
        def _(g):
            for hc in range(DC):
                p.dg.inc(g.dma_start(out=zbf_o[hc * 128:(hc + 1) * 128, t0:t0 + n],
                                     in_=z_bf[:, hc * n:(hc + 1) * n]), 16)
                p.dg.inc(g.dma_start(out=zmy_o[hc * 128:(hc + 1) * 128, t0:t0 + n],
                                     in_=zmy[:, hc * n:(hc + 1) * n]), 16)
            for t_ in range(nt_t):
                p.dg.inc(g.dma_start(out=ztm_o[t0 + t_ * 128: t0 + (t_ + 1) * 128, :],
                                     in_=z_tm[:, t_ * DM:(t_ + 1) * DM]), 16)

        ttt_grad_chunk(p, pr, bufs, n)

    grow = p.sb("grow", [1, 1280], F32)

    @p.blk.vector
    def _(v):
        p.vv.inc(v.tensor_scalar_mul(acc_w1t[:], acc_w1t[:], CGRAD))
        p.vv.inc(v.tensor_scalar_mul(acc_w2[:], acc_w2[:], CGRAD))
        p.vv.inc(v.tensor_scalar_mul(grow[0:1, 1024:1280], psum_b1[0:1, :], CGRAD))
        p.vv.inc(v.tensor_scalar_mul(grow[0:1, 0:1024], psum_b2[0:1, :], CGRAD))
    p.bar()

    @p.blk.sync
    def _(sy):
        p.st.inc(sy.dma_start(out=gw1t_o, in_=acc_w1t[:]), 16)
        p.st.inc(sy.dma_start(out=gw2_o, in_=acc_w2[:]), 16)
        p.st.inc(sy.dma_start(out=gb1_o, in_=grow[0:1, 1024:1280]), 16)
        p.st.inc(sy.dma_start(out=gb2_o, in_=grow[0:1, 0:1024]), 16)
        p.st.wait(sy)

    @p.blk.gpsimd
    def _(g):
        p.dg.wait(g)

    @p.blk.sync
    def _(sy):
        p.dh.wait(sy)
    return p.done()



def build_launch_BC(which, nt=NT):
    p = P()
    nch = nt // TC
    xfm_i = p.inp("xfm", [DM, nt])
    qfm_i = p.inp("qfm", [DM, nt])
    zbf_i = p.inp("zbf", [DM, nt], BF16)
    ztm_i = p.inp("ztm", [nt, DM], BF16)
    zmy_i = p.inp("zmyo", [DM, nt])
    w1_in = p.inp("w1", [256, DM]); w2_in = p.inp("w2", [DM, 256])
    b1_in = p.inp("b1", [1, 256]); b2_in = p.inp("b2", [1, DM])
    identf_in = p.inp("identf", [128, 128])
    identb_in = p.inp("identb", [128, 128], BF16)
    gsets = []
    rounds = 1 if which == "B" else 2
    for r_ in range(rounds):
        for k in range(4):
            gsets.append({
                "w1t": p.inp(f"g{r_}w1t_{k}", [128, DC * 256]),
                "w2": p.inp(f"g{r_}w2_{k}", [128, DC * 256]),
                "b1": p.inp(f"g{r_}b1_{k}", [1, 256]),
                "b2": p.inp(f"g{r_}b2_{k}", [1, DM]),
            })
    xnew_o = p.outp("xnew", [DM, nt])
    if which == "B":
        gw1t_o = p.outp("gw1t", [128, DC * 256])
        gw2_o = p.outp("gw2", [128, DC * 256])
        gb1_o = p.outp("gb1", [1, 256])
        gb2_o = p.outp("gb2", [1, DM])

    p.begin()
    identf = p.sb("sb_identf", [128, 128], F32)
    identb = p.sb("sb_identb", [128, 128], BF16)

    @p.blk.sync
    def _(sy):
        p.dh.inc(sy.dma_start(out=identf[:], in_=identf_in), 16)
        p.dh.inc(sy.dma_start(out=identb[:], in_=identb_in), 16)

    pr = ttt_prep_params(p, w1_in, w2_in, b1_in, b2_in, identf, masters=True)
    pr["identf"] = identf
    pr["psA"] = p.psA_f
    pr["psB"] = p.psB_f
    pr["pcol"] = p.psA_f

    _gldsc = ExitStack()
    gsum = {
        "gw1t": _gldsc.enter_context(p.nc.sbuf_tensor("gsum_w1t", [128, DC * 256], F32)),
        "gw2": _gldsc.enter_context(p.nc.sbuf_tensor("gsum_w2", [128, DC * 256], F32)),
        "gb1": _gldsc.enter_context(p.nc.sbuf_tensor("gsum_b1", [1, 256], F32)),
        "gb2": _gldsc.enter_context(p.nc.sbuf_tensor("gsum_b2", [1, DM], F32)),
    }
    gld = {
        "w1t": _gldsc.enter_context(p.nc.sbuf_tensor("gld_w1t", [128, DC * 256], F32)),
        "w2": _gldsc.enter_context(p.nc.sbuf_tensor("gld_w2", [128, DC * 256], F32)),
        "b1": _gldsc.enter_context(p.nc.sbuf_tensor("gld_b1", [1, 256], F32)),
        "b2": _gldsc.enter_context(p.nc.sbuf_tensor("gld_b2", [1, DM], F32)),
    }
    for r_ in range(rounds):
        for k in range(4):
            gs = gsets[r_ * 4 + k]

            @p.blk.sync
            def _(sy):
                p.dh.inc(sy.dma_start(out=gld["w1t"][:], in_=gs["w1t"]), 16)
                p.dh.inc(sy.dma_start(out=gld["w2"][:], in_=gs["w2"]), 16)
                p.dh.inc(sy.dma_start(out=gld["b1"][:], in_=gs["b1"]), 16)
                p.dh.inc(sy.dma_start(out=gld["b2"][:], in_=gs["b2"]), 16)
            p.bar()

            @p.blk.vector
            def _(v):
                for key, big in (("gw1t", "w1t"), ("gw2", "w2"),
                                 ("gb1", "b1"), ("gb2", "b2")):
                    rows = gsum[key].shape[0]
                    w = gsum[key].shape[1]
                    if k == 0:
                        p.vv.inc(v.tensor_copy(gsum[key][:], gld[big][0:rows, 0:w]))
                    else:
                        p.vv.inc(v.tensor_tensor(gsum[key][:], gsum[key][:],
                                                 gld[big][0:rows, 0:w], ALU.add))
                    v.wait_ge(p.vv.sem, p.vv.n)
            p.bar()
        ttt_update_params(p, pr, gsum)
    _gldsc.close()

    n = TC
    nt_t = n // 128
    xfmc = p.sb("xfmc", [128, DC * n], F32)
    qfc = p.sb("qfc", [128, DC * n], F32)
    qbc = p.sb("qbc", [128, DC * n], BF16)
    outr = p.sb("outr", [128, DC * n], F32)
    apb = p.sb("apb", [128, 2 * n], F32)
    h_b = p.sb("h_b", [128, 2 * n], BF16)
    psum = p.ps_main
    lnb_bufs = {
        "sq": p.sb("lnf_sq", [128, 2 * n], F32),
        "tmp": p.sb("lnf_tmp", [128, n], F32),
        "mu": p.sb("lnf_mu", [1, n], F32),
        "rstd": p.sb("lnf_rstd", [1, n], F32),
        "mub": p.sb("lnf_mub", [128, n], F32),
        "rsb": p.sb("lnf_rsb", [128, n], F32),
    }
    if which == "B":
        z_bf = p.sb("z_bf", [128, DC * n], BF16)
        z_tm = p.sb("z_tm", [128, nt_t * DM], BF16)
        zmy = p.sb("zmy", [128, DC * n], F32)
        gp = p.sb("gp", [128, 2 * n], F32)
        gp2 = p.sb("gp2", [128, 2 * n], F32)
        rb = p.sb("rb", [128, DC * n], BF16)
        r_tm = p.sb("r_tm", [128, nt_t * DM], BF16)
        h_tm = p.sb("h_tm", [128, nt_t * 256], BF16)
        da_b = p.sb("da_b", [128, 2 * n], BF16)
        da_tm = p.sb("da_tm", [128, nt_t * 256], BF16)
        acc_w1t = p.sb("acc_w1t", [128, DC * 256], F32)
        acc_w2 = p.sb("acc_w2", [128, DC * 256], F32)
        psum_b1 = p.pt("psum_b1", [1, 256])
        psum_b2 = p.pt("psum_b2", [1, 1024])

        @p.blk.vector
        def _(v):
            p.vv.inc(v.memset(acc_w1t[:], 0.0))
            p.vv.inc(v.memset(acc_w2[:], 0.0))
        p.bar()
        bufs = dict(z_bf=z_bf, z_tm=z_tm, zmy=zmy, apb=apb, h_b=h_b, gp=gp,
                    gp2=gp2, rb=rb, r_tm=r_tm, h_tm=h_tm, da_b=da_b,
                    da_tm=da_tm, psum=psum, psA=p.psA_b, psB=p.psB_b,
                    identb=identb, acc_w1t=acc_w1t, acc_w2=acc_w2,
                    psum_b1=psum_b1, psum_b2=psum_b2,
                    db_first=True, db_stop=False)

    for ch in range(nch):
        t0 = ch * n

        @p.blk.gpsimd
        def _(g):
            for hc in range(DC):
                p.dg.inc(g.dma_start(out=xfmc[:, hc * n:(hc + 1) * n],
                                     in_=xfm_i[hc * 128:(hc + 1) * 128, t0:t0 + n]), 16)
                p.dg.inc(g.dma_start(out=qfc[:, hc * n:(hc + 1) * n],
                                     in_=qfm_i[hc * 128:(hc + 1) * 128, t0:t0 + n]), 16)
        p.bar()

        @p.blk.vector
        def _(v):
            for hc in range(DC):
                p.vv.inc(v.tensor_copy(qbc[:, hc * n:(hc + 1) * n],
                                       qfc[:, hc * n:(hc + 1) * n]))
        p.bar()

        ttt_recon_fwd(p, pr, qbc, lambda hc: hc * n, n, apb, h_b, psum)
        p.bar()
        ttt_recon_out(p, pr, h_b, n, qfc, lambda hc: hc * n,
                      outr, lambda hc: hc * n, psum)
        p.bar()

        ln_fm(p, lambda c: c * n, n, lambda c: c * n, lambda c: c * n,
              lnb_bufs, src_sb=outr, dst_sb=outr, x_sb=xfmc)
        p.bar()

        @p.blk.sync
        def _(sy):
            for hc in range(DC):
                p.dh.inc(sy.dma_start(out=xnew_o[hc * 128:(hc + 1) * 128, t0:t0 + n],
                                      in_=outr[:, hc * n:(hc + 1) * n]), 16)

        if which == "B":
            if ch == nch - 1:
                bufs["db_stop"] = True

            @p.blk.gpsimd
            def _(g):
                for hc in range(DC):
                    p.dg.inc(g.dma_start(out=z_bf[:, hc * n:(hc + 1) * n],
                                         in_=zbf_i[hc * 128:(hc + 1) * 128, t0:t0 + n]), 16)
                    p.dg.inc(g.dma_start(out=zmy[:, hc * n:(hc + 1) * n],
                                         in_=zmy_i[hc * 128:(hc + 1) * 128, t0:t0 + n]), 16)
                for t_ in range(nt_t):
                    p.dg.inc(g.dma_start(out=z_tm[:, t_ * DM:(t_ + 1) * DM],
                                         in_=ztm_i[t0 + t_ * 128:t0 + (t_ + 1) * 128, :]), 16)
            p.bar()
            ttt_grad_chunk(p, pr, bufs, n)
        else:
            p.bar()

    if which == "B":
        grow = p.sb("grow", [1, 1280], F32)

        @p.blk.vector
        def _(v):
            p.vv.inc(v.tensor_scalar_mul(acc_w1t[:], acc_w1t[:], CGRAD))
            p.vv.inc(v.tensor_scalar_mul(acc_w2[:], acc_w2[:], CGRAD))
            p.vv.inc(v.tensor_scalar_mul(grow[0:1, 1024:1280], psum_b1[0:1, :], CGRAD))
            p.vv.inc(v.tensor_scalar_mul(grow[0:1, 0:1024], psum_b2[0:1, :], CGRAD))
        p.bar()

        @p.blk.sync
        def _(sy):
            p.st.inc(sy.dma_start(out=gw1t_o, in_=acc_w1t[:]), 16)
            p.st.inc(sy.dma_start(out=gw2_o, in_=acc_w2[:]), 16)
            p.st.inc(sy.dma_start(out=gb1_o, in_=grow[0:1, 1024:1280]), 16)
            p.st.inc(sy.dma_start(out=gb2_o, in_=grow[0:1, 0:1024]), 16)
            p.st.wait(sy)

    @p.blk.gpsimd
    def _(g):
        p.dg.wait(g)

    @p.blk.sync
    def _(sy):
        p.dh.wait(sy)
    return p.done()



def build_launch_D(nt=NT):
    KR = 512 + nt
    QR = 256 + nt
    NQW = QR // WIN
    XN = HALO + nt
    MR = 1 + nt
    ach = f_chunks(DIN, 128)
    NA = len(ach)
    nkt = KR // 128

    p = P()
    xin = p.inp("xin", [DM, XN])
    wqkv_i = p.inp("wqkv", [DM, 3 * 1024])
    wo_i = p.inp("wo", [1024, DM])
    w1_i = p.inp("w1", [DM, 2 * DIN])
    w2_i = p.inp("w2", [DIN, DM])
    b1_i = p.inp("b1", [1, 2 * DIN])
    b2_i = p.inp("b2", [1, DM])
    lng_i = p.inp("lng", [1, DM])
    lnb_i = p.inp("lnb", [1, DM])
    qsc_i = p.inp("qscol", [128, 1])
    ksc_i = p.inp("kscol", [128, 1])
    rq_c = p.inp("rotqc", [128, KR]); rq_s = p.inp("rotqs", [128, KR])
    rk_c = p.inp("rotkc", [128, KR]); rk_s = p.inp("rotks", [128, KR])
    masks_i = p.inp("masks", [NQW, 4, 128, WIN])
    hval_i = p.inp("hvalid", [128, 1])
    sel2_i = p.inp("sel2c", [2, 128])
    identb_in = p.inp("identb", [128, 128], BF16)
    xout = p.outp("xout", [DM, nt])
    xmid_hbm = p.nc.dram_tensor("xmid_hbm", [DM, MR], F32).ap()

    p.begin(pool="min")
    pnorm = p.pt("pnorm", [2, 512])
    psim = p.pt("psim", [128, 1024])
    pout = p.pt("pout", [64, 256])
    posum = p.pt("posum", [1, 256])

    identb = p.sb("sb_identb", [128, 128], BF16)
    qsc = p.sb("qsc", [128, 1], F32)
    ksc = p.sb("ksc", [128, 1], F32)
    hval = p.sb("hval", [128, 1], F32)
    lngc = p.sb("lngc", [128, DC], F32)
    lnbc = p.sb("lnbc", [128, DC], F32)
    b2c = p.sb("b2c", [128, DC], F32)
    b1ca = p.sb("b1ca", [128, NA], F32)
    b1cg = p.sb("b1cg", [128, NA], F32)
    hselb = p.sb("hselb", [128, 2], BF16)
    sel2 = p.sb("sel2", [2, 128], F32)
    onesq = p.sb("onesq", [1, 512], F32)
    ones256b = p.sb("ones256b", [1, 256], BF16)
    p0s = ExitStack()
    b1r = p0s.enter_context(p.nc.sbuf_tensor("b1r", [1, 2 * DIN], F32))

    @p.blk.gpsimd
    def _(g):
        p.dg.inc(g.dma_start(out=qsc[:], in_=qsc_i), 16)
        p.dg.inc(g.dma_start(out=ksc[:], in_=ksc_i), 16)

        p.dg.inc(g.dma_start(out=hval[:], in_=hval_i), 16)
        p.dg.inc(g.dma_start(out=b1r[:], in_=b1_i), 16)
        p.dg.inc(g.dma_start(out=identb[:], in_=identb_in), 16)
        p.dg.inc(g.dma_start(out=sel2[:], in_=sel2_i), 16)
        p.pg.inc(g.memset(hselb[:], 0.0))
        p.pg.inc(g.memset(onesq[:], 1.0))
        p.pg.inc(g.memset(ones256b[:], 1.0))
        g.wait_ge(p.pg.sem, p.pg.n)
        p.pg.inc(g.memset(hselb[0:64, 0:1], 1.0))
        p.pg.inc(g.memset(hselb[64:128, 1:2], 1.0))
    p.bar()

    @p.blk.vector
    def _(v):
        p.vv.inc(v.tensor_scalar_mul(qsc[:], qsc[:], QKSC))
    p.bar()

    rowbuf = p0s.enter_context(p.nc.sbuf_tensor("rowbuf", [1, 3 * DM], F32))

    @p.blk.sync
    def _(sy):
        p.dh.inc(sy.dma_start(out=rowbuf[0:1, 0:DM], in_=lng_i), 16)
        p.dh.inc(sy.dma_start(out=rowbuf[0:1, DM:2 * DM], in_=lnb_i), 16)
        p.dh.inc(sy.dma_start(out=rowbuf[0:1, 2 * DM:3 * DM], in_=b2_i), 16)
    p.bar()
    row_to_cols(p, rowbuf[0:1, 0:DM], DC, lngc, 0, p.psA_f)
    row_to_cols(p, rowbuf[0:1, DM:2 * DM], DC, lnbc, 0, p.psA_f)
    row_to_cols(p, rowbuf[0:1, 2 * DM:3 * DM], DC, b2c, 0, p.psA_f)
    p.bar()
    for j, (off, size) in enumerate(ach):
        for (roff, dst) in ((off, b1ca), (DIN + off, b1cg)):
            @p.blk.tensor
            def _(t, roff=roff, size=size):
                t.wait_ge(p.vv.sem, p.vv.n)
                p.mm.inc(t.matmul(p.psA_f[0:size, 0:1], b1r[0:1, roff:roff + size],
                                  p.c_onesf[0:1, 0:1], start=True, stop=True))

            @p.blk.vector
            def _(v, dst=dst, size=size, j=j):
                v.wait_ge(p.mm.sem, p.mm.n)
                p.vv.inc(v.tensor_copy(dst[0:size, j:j + 1], p.psA_f[0:size, 0:1]))
    p.bar()
    p0s.close()

    pqk = ExitStack()

    def qsb(name, shape, dtype):
        return pqk.enter_context(p.nc.sbuf_tensor(name, shape, dtype))

    q2 = qsb("q2", [128, DC * QR], BF16)
    k2 = qsb("k2", [128, DC * KR], BF16)
    v_km = qsb("v_km", [128, nkt * 1024], BF16)
    qnr = qsb("qnr", [2, DC * NQW], F32)
    knr = qsb("knr", [2, DC * (NQW + 1)], F32)
    mnegb = qsb("mnegb", [1, HEADS * WIN], BF16)
    mvals1 = qsb("mvals1", [1, HEADS * NQW], F32)
    mrow = qsb("mrow", [2, NQW], F32)

    p1s = ExitStack()
    xb = p1s.enter_context(p.nc.sbuf_tensor("xb", [128, DC * XN], BF16))

    @p.blk.gpsimd
    def _(g):
        for c in range(DC):
            p.dg.inc(g.dma_start(out=xb[:, c * XN:(c + 1) * XN],
                                 in_=xin[c * 128:(c + 1) * 128, :]), 16)
    p.bar()

    def xs_ap(c, i0, ncols):
        off = (1 if c < 4 else 0) + i0
        return xb[:, c * XN + off: c * XN + off + ncols]

    p1b = ExitStack()

    def usb(name, shape, dtype):
        return p1b.enter_context(p.nc.sbuf_tensor(name, shape, dtype))

    rt_c = usb("rt_c", [128, KR], F32)
    rt_s = usb("rt_s", [128, KR], F32)
    permb = usb("permb", [128, 128], BF16)
    qraw = usb("qraw", [128, 512], F32)
    sqs = usb("sqs", [128, 512], BF16)
    u_f = usb("u_f", [128, 512], F32)
    swp = usb("swp", [128, 512], BF16)
    nrm = usb("nrm", [2, 512], F32)
    wq1 = usb("wq1", [128, DC * 128], BF16)

    @p.blk.vector
    def _(v):
        p.pg.wait(v)
        p.dg.wait(v)
        p.vv.inc(v.tensor_copy(permb[0:32, :], identb[32:64, :]))
        p.vv.inc(v.tensor_copy(permb[32:64, :], identb[0:32, :]))
        p.vv.inc(v.tensor_copy(permb[64:96, :], identb[96:128, :]))
        p.vv.inc(v.tensor_copy(permb[96:128, :], identb[64:96, :]))
    p.bar()

    for side in ("q", "k"):
        NS = QR if side == "q" else KR
        qoff = 256 if side == "q" else 0
        wcol0 = 0 if side == "q" else 1024
        dst = q2 if side == "q" else k2
        nr_dst = qnr if side == "q" else knr
        nwin = NQW if side == "q" else NQW + 1
        scol = qsc if side == "q" else ksc
        rc_i = rq_c if side == "q" else rk_c
        rs_i = rq_s if side == "q" else rk_s

        @p.blk.sync
        def _(sy, rc_i=rc_i, rs_i=rs_i):
            sy.wait_ge(p.vv.sem, p.vv.n)
            sy.wait_ge(p.mm.sem, p.mm.n)
            p.dh.inc(sy.dma_start(out=rt_c[:], in_=rc_i), 16)
            p.dh.inc(sy.dma_start(out=rt_s[:], in_=rs_i), 16)
        p.bar()
        for (o, nn) in f_chunks(KR, 512):
            @p.blk.vector
            def _(v, o=o, nn=nn):
                v.wait_ge(p.vv.sem, p.vv.n)
                v.wait_ge(p.mm.sem, p.mm.n)
                p.vv.inc(v.tensor_copy(p.ps_main[0:128, 0:nn], rt_s[:, o:o + nn]))
                v.wait_ge(p.vv.sem, p.vv.n)
                for base in (0, 64):
                    p.vv.inc(v.tensor_scalar_mul(
                        rt_s[base:base + 32, o:o + nn],
                        p.ps_main[base + 32:base + 64, 0:nn], 1.0))
                    p.vv.inc(v.tensor_scalar_mul(
                        rt_s[base + 32:base + 64, o:o + nn],
                        p.ps_main[base:base + 32, 0:nn], -1.0))
                v.wait_ge(p.vv.sem, p.vv.n)
        p.bar()

        for fc in range(DC):
            @p.blk.gpsimd
            def _(g, fc=fc):
                g.wait_ge(p.mm.sem, p.mm.n)
                p.dg.inc(g.dma_start(
                    out=wq1[:].rearrange("p (c f) -> p c f", c=DC),
                    in_=wqkv_i[:, wcol0 + fc * 128: wcol0 + (fc + 1) * 128]
                    .rearrange("(c p) f -> p c f", p=128)), 16)
            dgw = p.dg.n

            for ci, (o, nn) in enumerate(f_chunks(NS, 512)):
                @p.blk.tensor
                def _(t, o=o, nn=nn):
                    t.wait_ge(p.dg.sem, dgw)
                    t.wait_ge(p.vv.sem, p.vv.n)
                    t.wait_ge(p.ss.sem, p.ss.n)
                    for c in range(DC):
                        m = t.matmul(p.ps_main[:, 0:nn],
                                     wq1[:, c * 128:(c + 1) * 128],
                                     xs_ap(c, qoff + o, nn),
                                     start=(c == 0), stop=(c == DC - 1))
                    p.mm.inc(m)

                @p.blk.vector
                def _(v, nn=nn):
                    v.wait_ge(p.mm.sem, p.mm.n)
                    p.vv.inc(v.tensor_copy(qraw[:, 0:nn], p.ps_main[:, 0:nn]))

                @p.blk.scalar
                def _(s, nn=nn):
                    s.wait_ge(p.vv.sem, p.vv.n)
                    p.ss.inc(s.activation(sqs[:, 0:nn], qraw[:, 0:nn], AF.Square))

                @p.blk.tensor
                def _(t, nn=nn):
                    t.wait_ge(p.ss.sem, p.ss.n)
                    p.mm.inc(t.matmul(pnorm[0:2, 0:nn], hselb[:, 0:2], sqs[:, 0:nn],
                                      start=True, stop=True))

                @p.blk.vector
                def _(v, nn=nn):
                    v.wait_ge(p.mm.sem, p.mm.n)
                    p.vv.inc(v.tensor_scalar_max(nrm[:, 0:nn], pnorm[0:2, 0:nn],
                                                 1e-24))

                @p.blk.scalar
                def _(s, nn=nn):
                    s.wait_ge(p.vv.sem, p.vv.n)
                    p.ss.inc(s.activation(nrm[:, 0:nn], nrm[:, 0:nn], AF.Sqrt))

                @p.blk.vector
                def _(v, nn=nn):
                    v.wait_ge(p.ss.sem, p.ss.n)
                    p.vv.inc(v.tensor_scalar_max(nrm[:, 0:nn], nrm[:, 0:nn], 1e-12))
                    v.wait_ge(p.vv.sem, p.vv.n)
                    p.vv.inc(v.reciprocal(nrm[:, 0:nn], nrm[:, 0:nn]))
                vrec = p.vv.n

                @p.blk.tensor
                def _(t, nn=nn):
                    t.wait_ge(p.vv.sem, vrec)
                    p.mm.inc(t.matmul(p.ps_main[:, 0:nn], sel2[0:2, :],
                                      nrm[0:2, 0:nn], start=True, stop=True))
                mmrb = p.mm.n

                @p.blk.vector
                def _(v, nn=nn):
                    v.wait_ge(p.mm.sem, mmrb)
                    p.vv.inc(v.scalar_tensor_tensor(u_f[:, 0:nn], qraw[:, 0:nn],
                                                    scol[:, 0:1], p.ps_main[:, 0:nn],
                                                    ALU.mult, ALU.mult))
                @p.blk.scalar
                def _(s, nn=nn):
                    s.wait_ge(p.vv.sem, p.vv.n)
                    p.ss.inc(s.activation(sqs[:, 0:nn], u_f[:, 0:nn], AF.Square))

                @p.blk.tensor
                def _(t, nn=nn):
                    t.wait_ge(p.ss.sem, p.ss.n)
                    p.mm.inc(t.matmul(pnorm[0:2, 0:nn], hselb[:, 0:2], sqs[:, 0:nn],
                                      start=True, stop=True))
                mmn2 = p.mm.n

                @p.blk.vector
                def _(v, o=o, nn=nn):
                    v.wait_ge(p.mm.sem, mmn2)
                    w0 = o // WIN
                    for wl in range(nn // WIN):
                        p.vv.inc(v.tensor_reduce(
                            nr_dst[:, (fc * nwin + w0 + wl):(fc * nwin + w0 + wl) + 1],
                            pnorm[0:2, wl * WIN:(wl + 1) * WIN], AX.X, ALU.max))

                @p.blk.vector
                def _(v, o=o, nn=nn):
                    v.wait_ge(p.vv.sem, p.vv.n)
                    p.vv.inc(v.tensor_mul(swp[:, 0:nn], u_f[:, 0:nn],
                                          rt_s[:, qoff + o:qoff + o + nn]))
                    p.vv.inc(v.tensor_mul(qraw[:, 0:nn], u_f[:, 0:nn],
                                          rt_c[:, qoff + o:qoff + o + nn]))
                vrot = p.vv.n

                @p.blk.tensor
                def _(t, nn=nn):
                    t.wait_ge(p.vv.sem, vrot)
                    p.mm.inc(t.matmul(p.ps_main[:, 0:nn], permb[:, :],
                                      swp[:, 0:nn], start=True, stop=True))
                mmsw = p.mm.n

                @p.blk.vector
                def _(v, o=o, nn=nn, dst=dst):
                    v.wait_ge(p.mm.sem, mmsw)
                    p.vv.inc(v.tensor_tensor(dst[:, fc * NS + o: fc * NS + o + nn],
                                             p.ps_main[:, 0:nn], qraw[:, 0:nn],
                                             ALU.add))
                    v.wait_ge(p.vv.sem, p.vv.n)
        p.bar()

    for vc in range(8):
        @p.blk.gpsimd
        def _(g, vc=vc):
            g.wait_ge(p.mm.sem, p.mm.n)
            p.dg.inc(g.dma_start(
                out=wq1[:].rearrange("p (c f) -> p c f", c=DC),
                in_=wqkv_i[:, 2048 + vc * 128: 2048 + (vc + 1) * 128]
                .rearrange("(c p) f -> p c f", p=128)), 16)
        dgv = p.dg.n

        for kt in range(nkt):
            @p.blk.tensor
            def _(t, vc=vc, kt=kt):
                t.wait_ge(p.dg.sem, dgv)
                t.wait_ge(p.vv.sem, p.vv.n)
                for c in range(DC):
                    m = t.matmul(p.ps_main[:, 0:128],
                                 xs_ap(c, kt * 128, 128),
                                 wq1[:, c * 128:(c + 1) * 128],
                                 start=(c == 0), stop=(c == DC - 1))
                p.mm.inc(m)

            @p.blk.vector
            def _(v, vc=vc, kt=kt):
                v.wait_ge(p.mm.sem, p.mm.n)
                p.vv.inc(v.tensor_copy(
                    v_km[:, kt * 1024 + vc * 128: kt * 1024 + (vc + 1) * 128],
                    p.ps_main[:, 0:128]))
    p.bar()
    p1b.close()
    p1s.close()
    att = pqk.enter_context(p.nc.sbuf_tensor("att", [128, DC * QR], BF16))

    p2s = ExitStack()
    masks = p2s.enter_context(p.nc.sbuf_tensor("masks_sb", [128, NQW * 4 * WIN], F32))
    p_sb = p2s.enter_context(p.nc.sbuf_tensor("p_sb", [128, 1024], BF16))
    rcro = p2s.enter_context(p.nc.sbuf_tensor("rcro", [1, 256], F32))
    rbro = p2s.enter_context(p.nc.sbuf_tensor("rbro", [64, 256], F32))

    @p.blk.sync
    def _(sy):
        p.dh.inc(sy.dma_start(
            out=masks[:].rearrange("p (w k f) -> p w k f", w=NQW, k=4),
            in_=masks_i.rearrange("w k p f -> p w k f")), 16)

    @p.blk.scalar
    def _(s):
        s.wait_ge(p.vv.sem, p.vv.n)
        s.wait_ge(p.mm.sem, p.mm.n)
        p.ss.inc(s.activation(qnr[:], qnr[:], AF.Sqrt))
        p.ss.inc(s.activation(knr[:], knr[:], AF.Sqrt))
    p.bar()
    for fc in range(DC):
        @p.blk.vector
        def _(v, fc=fc):
            v.wait_ge(p.vv.sem, p.vv.n)
            p.vv.inc(v.tensor_tensor(mrow[:, 0:NQW],
                                     knr[:, fc * (NQW + 1): fc * (NQW + 1) + NQW],
                                     knr[:, fc * (NQW + 1) + 1: fc * (NQW + 1) + NQW + 1],
                                     ALU.max))
            v.wait_ge(p.vv.sem, p.vv.n)
            p.vv.inc(v.tensor_tensor(mrow[:, 0:NQW], mrow[:, 0:NQW],
                                     qnr[:, fc * NQW:(fc + 1) * NQW], ALU.mult))
            v.wait_ge(p.vv.sem, p.vv.n)
            p.vv.inc(v.tensor_scalar(mrow[:, 0:NQW], mrow[:, 0:NQW],
                                     -1.0, 30.0, ALU.mult, ALU.add))
        vmr = p.vv.n

        for hh in range(2):
            h = fc * 2 + hh
            @p.blk.tensor
            def _(t, hh=hh):
                t.wait_ge(p.vv.sem, p.vv.n)
                p.mm.inc(t.matmul(pnorm[0:1, 0:NQW], sel2[0:2, hh * 64:hh * 64 + 1],
                                  mrow[0:2, 0:NQW], start=True, stop=True))
            mmx = p.mm.n

            @p.blk.vector
            def _(v, h=h):
                v.wait_ge(p.mm.sem, mmx)
                p.vv.inc(v.tensor_copy(mvals1[0:1, h * NQW:(h + 1) * NQW],
                                       pnorm[0:1, 0:NQW]))
    p.bar()

    for w in range(NQW):
        @p.blk.vector
        def _(v, w=w):
            v.wait_ge(p.mm.sem, p.mm.n)
            v.wait_ge(p.vv.sem, p.vv.n)
            for h in range(HEADS):
                p.vv.inc(v.tensor_scalar_mul(
                    mnegb[0:1, h * WIN:(h + 1) * WIN],
                    onesq[0:1, 0:WIN],
                    mvals1[0:1, h * NQW + w: h * NQW + w + 1]))
        vbc = p.vv.n
        for h in range(HEADS):
            fc = h // 2
            pb = (h % 2) * 64
            @p.blk.tensor
            def _(t, w=w, h=h, fc=fc, pb=pb, vbc=vbc):
                t.wait_ge(p.vv.sem, vbc)
                t.wait_ge(p.ss.sem, p.ss.n)
                for kc in range(4):
                    t.matmul(psim[:, kc * 256:(kc + 1) * 256],
                             k2[pb:pb + 64,
                                fc * KR + w * WIN + kc * 128: fc * KR + w * WIN + kc * 128 + 128],
                             q2[pb:pb + 64, fc * QR + w * WIN: fc * QR + (w + 1) * WIN],
                             start=True, stop=False)
                    m = t.matmul(psim[:, kc * 256:(kc + 1) * 256],
                                 p.c_onesr_b[0:1, :],
                                 mnegb[0:1, h * WIN:(h + 1) * WIN],
                                 start=False, stop=True)
                p.mm.inc(m)
            mmsim = p.mm.n

            @p.blk.vector
            def _(v, w=w):
                v.wait_ge(p.mm.sem, mmsim)
                for kc in range(4):
                    p.vv.inc(v.tensor_tensor(
                        psim[:, kc * 256:(kc + 1) * 256],
                        psim[:, kc * 256:(kc + 1) * 256],
                        masks[:, (w * 4 + kc) * WIN:(w * 4 + kc + 1) * WIN],
                        ALU.add))
            vmask = p.vv.n

            @p.blk.scalar
            def _(s):
                s.wait_ge(p.vv.sem, vmask)
                p.ss.inc(s.activation(p_sb[:], psim[:, :], AF.Exp))
            ssexp = p.ss.n

            @p.blk.tensor
            def _(t, w=w, h=h):
                t.wait_ge(p.ss.sem, ssexp)
                t.wait_ge(p.vv.sem, p.vv.n)
                for kc in range(4):
                    kt = 2 * w + kc
                    t.matmul(pout[0:64, :],
                             v_km[:, kt * 1024 + h * 64: kt * 1024 + h * 64 + 64],
                             p_sb[:, kc * 256:(kc + 1) * 256],
                             start=(kc == 0), stop=(kc == 3))
                    m = t.matmul(posum[0:1, :], p.c_onesb[:, 0:1],
                                 p_sb[:, kc * 256:(kc + 1) * 256],
                                 start=(kc == 0), stop=(kc == 3))
                p.mm.inc(m)
            mmpv = p.mm.n

            @p.blk.vector
            def _(v):
                v.wait_ge(p.mm.sem, mmpv)
                p.vv.inc(v.reciprocal(rcro[0:1, :], posum[0:1, :]))
            vrc = p.vv.n

            @p.blk.tensor
            def _(t):
                t.wait_ge(p.vv.sem, vrc)
                p.mm.inc(t.matmul(p.ps_main[0:64, 0:256], onesq[0:1, 0:64],
                                  rcro[0:1, :], start=True, stop=True))
            mmbc = p.mm.n

            @p.blk.vector
            def _(v, w=w, h=h, fc=fc, pb=pb):
                v.wait_ge(p.mm.sem, mmbc)
                p.vv.inc(v.tensor_copy(rbro[:, :], p.ps_main[0:64, 0:256]))
                v.wait_ge(p.vv.sem, p.vv.n)
                p.vv.inc(v.tensor_tensor(
                    att[pb:pb + 64, fc * QR + w * WIN: fc * QR + (w + 1) * WIN],
                    pout[0:64, :], rbro[:, :], ALU.mult))
                v.wait_ge(p.vv.sem, p.vv.n)
    p.bar()
    p2s.close()

    p3s = ExitStack()
    wo1 = p3s.enter_context(p.nc.sbuf_tensor("wo1", [128, DC * 128], BF16))
    xr3 = p3s.enter_context(p.nc.sbuf_tensor("xr3", [128, 512], F32))
    xm3 = p3s.enter_context(p.nc.sbuf_tensor("xm3", [128, 512], F32))
    mchunks = f_chunks(MR, 512)
    for fc in range(DC):
        @p.blk.gpsimd
        def _(g, fc=fc):
            g.wait_ge(p.mm.sem, p.mm.n)
            p.dg.inc(g.dma_start(
                out=wo1[:].rearrange("p (c f) -> p c f", c=DC),
                in_=wo_i[:, fc * 128:(fc + 1) * 128]
                .rearrange("(c p) f -> p c f", p=128)), 16)
        dgw = p.dg.n

        for (o, nn) in mchunks:
            @p.blk.gpsimd
            def _(g, fc=fc, o=o, nn=nn):
                g.wait_ge(p.vv.sem, p.vv.n)
                p.da.inc(g.dma_start(
                    out=xr3[:, 0:nn],
                    in_=xin[fc * 128:(fc + 1) * 128, XN - MR + o: XN - MR + o + nn],
                    allow_slow_non_contiguous=(nn == 1)), 16)
            dgx = p.da.n

            @p.blk.tensor
            def _(t, fc=fc, o=o, nn=nn):
                t.wait_ge(p.dg.sem, dgw)
                t.wait_ge(p.vv.sem, p.vv.n)
                for ic in range(DC):
                    m = t.matmul(p.ps_main[:, 0:nn],
                                 wo1[:, ic * 128:(ic + 1) * 128],
                                 att[:, ic * QR + (QR - MR) + o: ic * QR + (QR - MR) + o + nn],
                                 start=(ic == 0), stop=(ic == DC - 1))
                p.mm.inc(m)
            mmwo = p.mm.n

            @p.blk.vector
            def _(v, fc=fc, o=o, nn=nn):
                v.wait_ge(p.mm.sem, mmwo)
                v.wait_ge(p.da.sem, dgx)
                p.vv.inc(v.tensor_tensor(xm3[:, 0:nn], p.ps_main[:, 0:nn],
                                         xr3[:, 0:nn], ALU.add))
                v.wait_ge(p.vv.sem, p.vv.n)
                if o == 0:
                    p.vv.inc(v.tensor_scalar_mul(xm3[:, 0:1], xm3[:, 0:1],
                                                 hval[:, 0:1]))
                    v.wait_ge(p.vv.sem, p.vv.n)

            @p.blk.sync
            def _(sy, fc=fc, o=o, nn=nn):
                sy.wait_ge(p.vv.sem, p.vv.n)
                p.dh.inc(sy.dma_start(
                    out=xmid_hbm[fc * 128:(fc + 1) * 128, o:o + nn],
                    in_=xm3[:, 0:nn],
                    allow_slow_non_contiguous=(nn == 1)), 16)

            @p.blk.vector
            def _(v):
                v.wait_ge(p.dh.sem, p.dh.n)
    p.bar()
    p3s.close()
    pqk.close()

    TF = 512
    p4s = ExitStack()
    w2r = p4s.enter_context(p.nc.sbuf_tensor("w2r", [128, NA * 1024], BF16))
    xmc = p4s.enter_context(p.nc.sbuf_tensor("xmc", [128, DC * (TF + 1)], F32))
    xn2b = p4s.enter_context(p.nc.sbuf_tensor("xn2b", [128, DC * TF], BF16))
    gbuf = p4s.enter_context(p.nc.sbuf_tensor("gbuf", [128, NA * TF], BF16))
    w1t1 = p4s.enter_context(p.nc.sbuf_tensor("w1t1", [128, DC * 128], BF16))
    w1t2 = p4s.enter_context(p.nc.sbuf_tensor("w1t2", [128, DC * 128], BF16))
    ggs = p4s.enter_context(p.nc.sbuf_tensor("ggs", [128, TF], F32))
    avs = p4s.enter_context(p.nc.sbuf_tensor("avs", [128, TF], F32))
    xob = p4s.enter_context(p.nc.sbuf_tensor("xob", [128, TF], F32))
    lnb_bufs = {
        "sq": p4s.enter_context(p.nc.sbuf_tensor("lnf_sq", [128, 2 * TF], F32)),
        "tmp": p4s.enter_context(p.nc.sbuf_tensor("lnf_tmp", [128, TF], F32)),
        "mu": p4s.enter_context(p.nc.sbuf_tensor("lnf_mu", [1, TF], F32)),
        "rstd": p4s.enter_context(p.nc.sbuf_tensor("lnf_rstd", [1, TF], F32)),
        "mub": p4s.enter_context(p.nc.sbuf_tensor("lnf_mub", [128, TF], F32)),
        "rsb": p4s.enter_context(p.nc.sbuf_tensor("lnf_rsb", [128, TF], F32)),
    }

    @p.blk.gpsimd
    def _(g):
        for j, (off, size) in enumerate(ach):
            p.dg.inc(g.dma_start(out=w2r[0:size, j * 1024:(j + 1) * 1024],
                                 in_=w2_i[off:off + size, :]), 16)
    p.bar()

    for tch in range(nt // TF):
        t0 = tch * TF

        @p.blk.gpsimd
        def _(g, t0=t0):
            g.wait_ge(p.vv.sem, p.vv.n)
            g.wait_ge(p.mm.sem, p.mm.n)
            for c in range(DC):
                p.dg.inc(g.dma_start(
                    out=xmc[:, c * (TF + 1):(c + 1) * (TF + 1)],
                    in_=xmid_hbm[c * 128:(c + 1) * 128, t0:t0 + TF + 1]), 16)
        p.bar()

        ln_fm(p, lambda c: c * (TF + 1) + (1 if c < 4 else 0), TF,
              lambda c: c * TF, None, lnb_bufs,
              g_col=lambda c: lngc[:, c:c + 1], b_col=lambda c: lnbc[:, c:c + 1],
              src_sb=xmc, dst_sb=xn2b)
        p.bar()

        for j, (off, size) in enumerate(ach):
            @p.blk.gpsimd
            def _(g, off=off, j=j, size=size):
                g.wait_ge(p.mm.sem, p.mm.n)
                p.dg.inc(g.dma_start(
                    out=w1t1[:].rearrange("p (c f) -> p c f", c=DC)[:, :, 0:size],
                    in_=w1_i[:, off:off + size].rearrange("(c p) f -> p c f", p=128)), 16)
                p.dg.inc(g.dma_start(
                    out=w1t2[:].rearrange("p (c f) -> p c f", c=DC)[:, :, 0:size],
                    in_=w1_i[:, DIN + off:DIN + off + size]
                    .rearrange("(c p) f -> p c f", p=128)), 16)
            dgw = p.dg.n

            @p.blk.tensor
            def _(t, j=j, size=size):
                t.wait_ge(p.dg.sem, dgw)
                t.wait_ge(p.ss.sem, p.ss.n)
                for c in range(DC):
                    t.matmul(psim[0:size, 0:TF],
                             w1t1[:, c * 128:c * 128 + size],
                             xn2b[:, c * TF:(c + 1) * TF],
                             start=(c == 0), stop=(c == DC - 1))
                for c in range(DC):
                    m = t.matmul(psim[0:size, 512:512 + TF],
                                 w1t2[:, c * 128:c * 128 + size],
                                 xn2b[:, c * TF:(c + 1) * TF],
                                 start=(c == 0), stop=(c == DC - 1))
                p.mm.inc(m)
            mmw1 = p.mm.n

            @p.blk.scalar
            def _(s, j=j, size=size):
                s.wait_ge(p.mm.sem, mmw1)
                s.wait_ge(p.vv.sem, p.vv.n)
                p.ss.inc(s.activation(avs[0:size, :], psim[0:size, 0:TF],
                                      AF.Identity, bias=b1ca[0:size, j:j + 1]))
                p.ss.inc(s.activation(ggs[0:size, :], psim[0:size, 512:512 + TF],
                                      AF.Gelu, bias=b1cg[0:size, j:j + 1]))
            ssgg = p.ss.n

            @p.blk.vector
            def _(v, j=j, size=size):
                v.wait_ge(p.ss.sem, ssgg)
                p.vv.inc(v.tensor_tensor(gbuf[0:size, j * TF:(j + 1) * TF],
                                         avs[0:size, :], ggs[0:size, :], ALU.mult))
        p.bar()

        for oc in range(DC):
            @p.blk.tensor
            def _(t, oc=oc):
                t.wait_ge(p.ss.sem, p.ss.n)
                t.wait_ge(p.vv.sem, p.vv.n)
                for j, (off, size) in enumerate(ach):
                    m = t.matmul(p.ps_main[:, 0:TF],
                                 w2r[0:size, j * 1024 + oc * 128: j * 1024 + oc * 128 + 128],
                                 gbuf[0:size, j * TF:(j + 1) * TF],
                                 start=(j == 0), stop=(j == NA - 1))
                p.mm.inc(m)
            mmw2 = p.mm.n

            @p.blk.scalar
            def _(s, oc=oc):
                s.wait_ge(p.mm.sem, mmw2)
                s.wait_ge(p.dh.sem, p.dh.n)
                p.ss.inc(s.activation(xob[:, :], p.ps_main[:, 0:TF], AF.Identity,
                                      bias=b2c[:, oc:oc + 1]))
            ssb2 = p.ss.n

            @p.blk.vector
            def _(v, oc=oc, t0=t0):
                v.wait_ge(p.ss.sem, ssb2)
                p.vv.inc(v.tensor_tensor(xob[:, :], xob[:, :],
                                         xmc[:, oc * (TF + 1) + 1: oc * (TF + 1) + 1 + TF],
                                         ALU.add))
            vvxo = p.vv.n

            @p.blk.sync
            def _(sy, oc=oc, t0=t0):
                sy.wait_ge(p.vv.sem, vvxo)
                p.dh.inc(sy.dma_start(out=xout[oc * 128:(oc + 1) * 128, t0:t0 + TF],
                                      in_=xob[:, :]), 16)

            @p.blk.vector
            def _(v):
                v.wait_ge(p.dh.sem, p.dh.n)
    p.bar()
    p4s.close()

    @p.blk.gpsimd
    def _(g):
        p.dg.wait(g)

    @p.blk.sync
    def _(sy):
        p.dh.wait(sy)
        p.st.wait(sy)
    return p.done()



def host_rotary_tables(n=8192):
    half = np.arange(0, DH, 2, dtype=np.float64)
    inv_freq = 1.0 / 10000.0 ** (half / DH)
    t = np.arange(n, dtype=np.float64)
    freqs = t[:, None] * inv_freq[None, :]
    freqs = np.concatenate([freqs, freqs], -1)
    base_scale = (half + 0.4 * DH) / (1.4 * DH)
    power = (t - n // 2) / (WIN // 2)
    scale = base_scale[None, :] ** power[:, None]
    scale = np.concatenate([scale, scale], -1)
    return freqs.astype(np.float32), scale.astype(np.float32)


def host_rot_tables_for(pos):
    pos = np.asarray(pos, np.float64)
    half = np.arange(0, DH, 2, dtype=np.float64)
    inv_freq = 1.0 / 10000.0 ** (half / DH)
    f0 = pos[:, None] * inv_freq[None, :]
    f = np.concatenate([f0, f0], -1)
    base_scale = (half + 0.4 * DH) / (1.4 * DH)
    power = (pos - 8192 // 2) / (WIN // 2)
    s0 = base_scale[None, :] ** power[:, None]
    s_ = np.concatenate([s0, s0], -1)
    cos, sin = np.cos(f), np.sin(f)

    def dup(a):
        return np.ascontiguousarray(np.concatenate([a, a], axis=0)).astype(np.float32)

    return (dup((cos * s_).T), dup((sin * s_).T),
            dup((cos / s_).T), dup((sin / s_).T))


def host_band_masks():
    q = np.arange(WIN)[None, :]
    out = np.zeros((2, 4, 128, WIN), np.float32)
    for kc in range(4):
        k = (kc * 128 + np.arange(128))[:, None]
        valid0 = (k >= q) & (k <= q + WIN)
        valid1 = valid0 & (k >= WIN)
        out[0, kc] = np.where(valid0, 0.0, NEG)
        out[1, kc] = np.where(valid1, 0.0, NEG)
    return out



def build_launch_E(nt=NT):
    p = P()
    xfm_i = p.inp("xfm", [DM, nt])
    wq_i = p.inp("wq", [DM, DM])
    wkv_i = p.inp("wkv", [DM, 128])
    g_i = p.inp("ln1g", [1, DM])
    b_i = p.inp("ln1b", [1, DM])
    qfm_o = p.outp("qfm", [DM, nt], BF16)
    kfm_o = p.outp("kfm", [64, nt], BF16)
    vaug_o = p.outp("vaug", [128, (nt // 128) * 65], BF16)

    p.begin(pool="min")
    gc = p.sb("gc", [128, DC], F32)
    bc = p.sb("bc", [128, DC], F32)
    rowb = p.sb("rowb", [1, 2 * DM], F32)
    xfc = p.sb("xfc", [128, DC * TC], F32)
    xnb = p.sb("xnb", [128, DC * TC], BF16)
    wq1 = p.sb("wq1", [128, DC * 128], BF16)
    wkv1 = p.sb("wkv1", [128, DC * 128], BF16)
    ob = p.sb("ob", [128, TC], BF16)
    lnb_bufs = {
        "sq": p.sb("lnf_sq", [128, 2 * TC], F32),
        "tmp": p.sb("lnf_tmp", [128, TC], F32),
        "mu": p.sb("lnf_mu", [1, TC], F32),
        "rstd": p.sb("lnf_rstd", [1, TC], F32),
        "mub": p.sb("lnf_mub", [128, TC], F32),
        "rsb": p.sb("lnf_rsb", [128, TC], F32),
    }

    @p.blk.sync
    def _(sy):
        p.dh.inc(sy.dma_start(out=rowb[0:1, 0:DM], in_=g_i), 16)
        p.dh.inc(sy.dma_start(out=rowb[0:1, DM:2 * DM], in_=b_i), 16)
    p.bar()
    row_to_cols(p, rowb[0:1, 0:DM], DC, gc, 0, p.psA_f)
    row_to_cols(p, rowb[0:1, DM:2 * DM], DC, bc, 0, p.psA_f)

    @p.blk.gpsimd
    def _(g):
        p.dg.inc(g.dma_start(out=wkv1[:].rearrange("p (c f) -> p c f", c=DC),
                             in_=wkv_i.rearrange("(c p) f -> p c f", p=128)), 16)
    p.bar()

    for ch in range(nt // TC):
        t0 = ch * TC

        @p.blk.gpsimd
        def _(g, t0=t0):
            g.wait_ge(p.vv.sem, p.vv.n)
            g.wait_ge(p.mm.sem, p.mm.n)
            for c in range(DC):
                p.dg.inc(g.dma_start(out=xfc[:, c * TC:(c + 1) * TC],
                                     in_=xfm_i[c * 128:(c + 1) * 128, t0:t0 + TC]), 16)
        p.bar()
        ln_fm(p, lambda c: c * TC, TC, lambda c: c * TC, None, lnb_bufs,
              src_sb=xfc, dst_sb=xfc)
        p.bar()
        ln_fm(p, lambda c: c * TC, TC, lambda c: c * TC, None, lnb_bufs,
              g_col=lambda c: gc[:, c:c + 1], b_col=lambda c: bc[:, c:c + 1],
              src_sb=xfc, dst_sb=xnb)
        p.bar()

        for fc in range(DC):
            @p.blk.gpsimd
            def _(g, fc=fc):
                g.wait_ge(p.mm.sem, p.mm.n)
                p.dg.inc(g.dma_start(
                    out=wq1[:].rearrange("p (c f) -> p c f", c=DC),
                    in_=wq_i[:, fc * 128:(fc + 1) * 128]
                    .rearrange("(c p) f -> p c f", p=128)), 16)
            dgw = p.dg.n

            @p.blk.tensor
            def _(t, fc=fc):
                t.wait_ge(p.dg.sem, dgw)
                t.wait_ge(p.dh.sem, p.dh.n)
                for c in range(DC):
                    m = t.matmul(p.ps_main[:, 0:TC],
                                 wq1[:, c * 128:(c + 1) * 128],
                                 xnb[:, c * TC:(c + 1) * TC],
                                 start=(c == 0), stop=(c == DC - 1))
                p.mm.inc(m)
            mmq = p.mm.n

            @p.blk.vector
            def _(v):
                v.wait_ge(p.mm.sem, mmq)
                p.vv.inc(v.tensor_scalar_mul(ob[:, :], p.ps_main[:, 0:TC], 0.125))
            vvq = p.vv.n

            @p.blk.sync
            def _(sy, fc=fc, t0=t0):
                sy.wait_ge(p.vv.sem, vvq)
                p.dh.inc(sy.dma_start(out=qfm_o[fc * 128:(fc + 1) * 128, t0:t0 + TC],
                                      in_=ob[:, :]), 16)

            @p.blk.tensor
            def _(t):
                t.wait_ge(p.dh.sem, p.dh.n)

        @p.blk.tensor
        def _(t):
            t.wait_ge(p.dh.sem, p.dh.n)
            t.wait_ge(p.vv.sem, p.vv.n)
            for c in range(DC):
                m = t.matmul(p.ps_main[:, 0:TC],
                             wkv1[:, c * 128:(c + 1) * 128],
                             xnb[:, c * TC:(c + 1) * TC],
                             start=(c == 0), stop=(c == DC - 1))
            p.mm.inc(m)
        mmkv = p.mm.n

        @p.blk.vector
        def _(v):
            v.wait_ge(p.mm.sem, mmkv)
            p.vv.inc(v.tensor_copy(ob[0:64, :], p.ps_main[0:64, 0:TC]))
        vvk = p.vv.n

        @p.blk.sync
        def _(sy, t0=t0):
            sy.wait_ge(p.vv.sem, vvk)
            p.dh.inc(sy.dma_start(out=kfm_o[:, t0:t0 + TC], in_=ob[0:64, :]), 16)

        for t_ in range(TC // 128):
            @p.blk.tensor
            def _(t, t_=t_):
                t.wait_ge(p.vv.sem, p.vv.n)
                t.wait_ge(p.dh.sem, p.dh.n)
                for c in range(DC):
                    m = t.matmul(p.ps_main[:, 0:64],
                                 xnb[:, c * TC + t_ * 128: c * TC + t_ * 128 + 128],
                                 wkv1[:, c * 128 + 64:(c + 1) * 128],
                                 start=(c == 0), stop=(c == DC - 1))
                p.mm.inc(m)
            mmv = p.mm.n

            @p.blk.vector
            def _(v, t_=t_, ch=ch):
                v.wait_ge(p.mm.sem, mmv)
                kt = ch * (TC // 128) + t_
                p.vv.inc(v.tensor_copy(ob[:, 0:64], p.ps_main[:, 0:64]))
                v.wait_ge(p.vv.sem, p.vv.n)
                p.vv.inc(v.memset(ob[:, 64:65], 1.0))
            vvv = p.vv.n

            @p.blk.sync
            def _(sy, t_=t_, ch=ch):
                sy.wait_ge(p.vv.sem, vvv)
                kt = ch * (TC // 128) + t_
                p.dh.inc(sy.dma_start(out=vaug_o[:, kt * 65:(kt + 1) * 65],
                                      in_=ob[:, 0:65]), 16)

            @p.blk.tensor
            def _(t):
                t.wait_ge(p.dh.sem, p.dh.n)
    p.bar()

    @p.blk.sync
    def _(sy):
        p.dh.wait(sy)

    @p.blk.gpsimd
    def _(g):
        p.dg.wait(g)
    return p.done()



def build_launch_G(with_x, nt=NT):
    p = P()
    yfm_i = p.inp("yfm", [DM, nt])
    xfm_i = p.inp("xfm", [DM, nt]) if with_x else None
    ofm_o = p.outp("ofm", [DM, nt])

    p.begin(pool="min")
    yc = p.sb("yc", [128, DC * TC], F32)
    oc = p.sb("oc", [128, DC * TC], F32)
    xc = p.sb("xc", [128, DC * TC], F32) if with_x else None
    lnb_bufs = {
        "sq": p.sb("lnf_sq", [128, 2 * TC], F32),
        "tmp": p.sb("lnf_tmp", [128, TC], F32),
        "mu": p.sb("lnf_mu", [1, TC], F32),
        "rstd": p.sb("lnf_rstd", [1, TC], F32),
        "mub": p.sb("lnf_mub", [128, TC], F32),
        "rsb": p.sb("lnf_rsb", [128, TC], F32),
    }

    for ch in range(nt // TC):
        t0 = ch * TC

        @p.blk.gpsimd
        def _(g, t0=t0):
            g.wait_ge(p.vv.sem, p.vv.n)
            g.wait_ge(p.ss.sem, p.ss.n)
            g.wait_ge(p.dh.sem, p.dh.n)
            for c in range(DC):
                p.dg.inc(g.dma_start(out=yc[:, c * TC:(c + 1) * TC],
                                     in_=yfm_i[c * 128:(c + 1) * 128, t0:t0 + TC]), 16)
                if with_x:
                    p.dg.inc(g.dma_start(out=xc[:, c * TC:(c + 1) * TC],
                                         in_=xfm_i[c * 128:(c + 1) * 128, t0:t0 + TC]), 16)
        p.bar()
        ln_fm(p, lambda c: c * TC, TC, lambda c: c * TC,
              (lambda c: c * TC) if with_x else None, lnb_bufs,
              src_sb=yc, dst_sb=oc, x_sb=xc)
        p.bar()

        @p.blk.sync
        def _(sy, t0=t0):
            p.vv.wait(sy)
            p.ss.wait(sy)
            for c in range(DC):
                p.dh.inc(sy.dma_start(out=ofm_o[c * 128:(c + 1) * 128, t0:t0 + TC],
                                      in_=oc[:, c * TC:(c + 1) * TC]), 16)
        p.bar()

    @p.blk.sync
    def _(sy):
        p.dh.wait(sy)

    @p.blk.gpsimd
    def _(g):
        p.dg.wait(g)
    return p.done()



def build_launch_F(nt=NT, nk=8192, qh_ov=None, dbg=False, upto=None):
    nkt = nk // 128
    QH = qh_ov if qh_ov else min(1024, nt)
    p = P()
    xfm_i = p.inp("xfm", [DM, nt])
    qfm_i = p.inp("qfm", [DM, nt], BF16)
    kfm_i = p.inp("kfull", [64, nk], BF16)
    vaug_i = p.inp("vfull", [128, nkt * 65], BF16)
    wo_i = p.inp("wo", [DM, DM])
    g2_i = p.inp("ln2g", [1, DM]); b2_i = p.inp("ln2b", [1, DM])
    wmix_i = p.inp("wmix", [8, DM, DM])
    bmix_i = p.inp("bmix", [8, DM])
    logits_i = p.inp("logits", [1, 8])
    uu_i = p.inp("uu", [1, 8])
    sel2_i = p.inp("sel2c", [2, 128])
    identf_in = p.inp("identf", [128, 128])
    if upto == "att":
        xout = p.outp("xout", [128, DC * nt], BF16)
    elif upto in ("x3", "mix", "fin"):
        xout = p.outp("xout", [DM, nt])
    elif upto == "idx":
        xout = p.outp("xout", [1, 256])
    else:
        xout = p.outp("xout", [nt, DM])

    def _finish():
        @p.blk.sync
        def _(sy):
            p.dh.wait(sy)
            p.st.wait(sy)

        @p.blk.gpsimd
        def _(g):
            p.dg.wait(g)
            p.sg.wait(g)
        return p.done()

    p.begin(pool="tiny")
    psim = p.pt("psim", [128, 2 * QH])
    pout = p.pt("pout", [65, QH])
    pnorm = p.pt("pnorm", [2, 512])

    nqh = nt // QH
    identf = p.sb("sb_identf", [128, 128], F32)
    g2c = p.sb("g2c", [128, DC], F32)
    b2c = p.sb("b2c", [128, DC], F32)
    sel2 = p.sb("sel2", [2, 128], F32)
    hselb = p.sb("hselb", [128, 2], BF16)
    ones64f = p.sb("ones64f", [1, 64], F32)
    onesr2 = p.sb("onesr2", [1, 128], BF16)
    x_f = p.sb("x_f", [128, DC * nt], F32)
    kmaxc = p.sb("kmaxc", [2, 1], F32)
    mvF = p.sb("mvF", [2, DC * nqh], F32)
    mvs = p.sb("mvs", [1, 1], F32)
    lnb_bufs = {
        "sq": p.sb("lnf_sq", [128, 2 * TC], F32),
        "tmp": p.sb("lnf_tmp", [128, TC], F32),
        "mu": p.sb("lnf_mu", [1, TC], F32),
        "rstd": p.sb("lnf_rstd", [1, TC], F32),
        "mub": p.sb("lnf_mub", [128, TC], F32),
        "rsb": p.sb("lnf_rsb", [128, TC], F32),
    }
    attsc = ExitStack()
    att = attsc.enter_context(p.nc.sbuf_tensor("att", [128, DC * nt], BF16))
    s1 = ExitStack()

    def s1b(name, shape, dtype):
        return s1.enter_context(p.nc.sbuf_tensor(name, shape, dtype))

    rowb = s1b("rowb", [1, 2 * DM], F32)
    q_b = s1b("q_b", [128, DC * nt], BF16)
    k_b = s1b("k_b", [128, nk], BF16)
    v_b = s1b("v_b", [128, nkt * 65], BF16)
    p_sb = s1b("p_sb", [128, 2 * QH], BF16)
    qn2 = s1b("qn2", [2, nt], F32)
    mbuf = s1b("mbuf", [1, QH], BF16)
    rcro = s1b("rcro", [1, QH], F32)
    rbro = s1b("rbro", [64, QH], F32)
    krow = s1b("krow", [1, QH], F32)
    nkc = len(f_chunks(nk, QH))
    km8 = s1b("km8", [1, nkc], F32)
    onesqf = s1b("onesqf", [1, 512], F32)

    @p.blk.sync
    def _(sy):
        p.dh.inc(sy.dma_start(out=identf[:], in_=identf_in), 16)
        p.dh.inc(sy.dma_start(out=rowb[0:1, 0:DM], in_=g2_i), 16)
        p.dh.inc(sy.dma_start(out=rowb[0:1, DM:2 * DM], in_=b2_i), 16)
        p.dh.inc(sy.dma_start(out=sel2[:], in_=sel2_i), 16)
        p.dh.inc(sy.dma_start(out=k_b[0:64, :], in_=kfm_i), 16)
        p.dh.inc(sy.dma_start(out=v_b[:], in_=vaug_i), 16)

    @p.blk.gpsimd
    def _(g):
        for c in range(DC):
            p.dg.inc(g.dma_start(out=x_f[:, c * nt:(c + 1) * nt],
                                 in_=xfm_i[c * 128:(c + 1) * 128, :]), 16)
            p.dg.inc(g.dma_start(out=q_b[:, c * nt:(c + 1) * nt],
                                 in_=qfm_i[c * 128:(c + 1) * 128, :]), 16)
        p.pg.inc(g.memset(hselb[:], 0.0))
        p.pg.inc(g.memset(ones64f[:], 1.0))
        p.pg.inc(g.memset(onesr2[:], 1.0))
        p.pg.inc(g.memset(onesqf[:], 1.0))
        g.wait_ge(p.pg.sem, p.pg.n)
        p.pg.inc(g.memset(hselb[0:64, 0:1], 1.0))
        p.pg.inc(g.memset(hselb[64:128, 1:2], 1.0))
    p.bar()

    @p.blk.vector
    def _(v):
        p.vv.inc(v.tensor_copy(k_b[64:128, :], k_b[0:64, :]))
    p.bar()
    row_to_cols(p, rowb[0:1, 0:DM], DC, g2c, 0, p.ps_main)
    row_to_cols(p, rowb[0:1, DM:2 * DM], DC, b2c, 0, p.ps_main)
    p.bar()

    for ci, (o, nn) in enumerate(f_chunks(nk, QH)):
        @p.blk.scalar
        def _(s, o=o, nn=nn):
            s.wait_ge(p.mm.sem, p.mm.n)
            s.wait_ge(p.vv.sem, p.vv.n)
            p.ss.inc(s.activation(p_sb[0:64, 0:nn], k_b[0:64, o:o + nn], AF.Square))
        sk = p.ss.n

        @p.blk.tensor
        def _(t, o=o, nn=nn):
            t.wait_ge(p.ss.sem, sk)
            t.wait_ge(p.vv.sem, p.vv.n)
            for (o2, n2) in f_chunks(nn, 512):
                p.mm.inc(t.matmul(p.ps_main[0:1, 0:n2], p.c_onesb[0:64, 0:1],
                                  p_sb[0:64, o2:o2 + n2], start=True, stop=True))
                mk = p.mm.n

                @p.blk.vector
                def _(v, o2=o2, n2=n2, mk=mk):
                    v.wait_ge(p.mm.sem, mk)
                    v.wait_ge(p.vv.sem, p.vv.n)
                    p.vv.inc(v.tensor_copy(krow[0:1, o2:o2 + n2],
                                           p.ps_main[0:1, 0:n2]))
                t.wait_ge(p.vv.sem, p.vv.n)

        @p.blk.vector
        def _(v, ci=ci, nn=nn):
            v.wait_ge(p.vv.sem, p.vv.n)
            p.vv.inc(v.tensor_reduce(km8[0:1, ci:ci + 1], krow[0:1, 0:nn],
                                     AX.X, ALU.max))
    p.bar()

    @p.blk.vector
    def _(v):
        p.vv.inc(v.tensor_reduce(kmaxc[0:1, 0:1], km8[0:1, :], AX.X, ALU.max))

    @p.blk.scalar
    def _(s):
        s.wait_ge(p.vv.sem, p.vv.n)
        p.ss.inc(s.activation(kmaxc[0:1, 0:1], kmaxc[0:1, 0:1], AF.Sqrt))

    @p.blk.tensor
    def _(t):
        t.wait_ge(p.ss.sem, p.ss.n)
        p.mm.inc(t.matmul(pnorm[0:2, 0:1], ones64f[0:1, 0:2], kmaxc[0:1, 0:1],
                          start=True, stop=True))

    @p.blk.vector
    def _(v):
        v.wait_ge(p.mm.sem, p.mm.n)
        p.vv.inc(v.tensor_copy(kmaxc[0:2, 0:1], pnorm[0:2, 0:1]))
    p.bar()

    for fc in range(DC):
        @p.blk.scalar
        def _(s, fc=fc):
            s.wait_ge(p.mm.sem, p.mm.n)
            s.wait_ge(p.vv.sem, p.vv.n)
            p.ss.inc(s.activation(p_sb[:, 0:nt], q_b[:, fc * nt:(fc + 1) * nt],
                                  AF.Square))
        sk = p.ss.n

        for (o2, n2) in f_chunks(nt, 512):
            @p.blk.tensor
            def _(t, o2=o2, n2=n2):
                t.wait_ge(p.ss.sem, sk)
                t.wait_ge(p.vv.sem, p.vv.n)
                p.mm.inc(t.matmul(pnorm[0:2, 0:n2], hselb[:, 0:2],
                                  p_sb[:, o2:o2 + n2], start=True, stop=True))
            mk = p.mm.n

            @p.blk.vector
            def _(v, o2=o2, n2=n2):
                v.wait_ge(p.mm.sem, mk)
                p.vv.inc(v.tensor_copy(qn2[:, o2:o2 + n2], pnorm[0:2, 0:n2]))
        p.bar()

        @p.blk.scalar
        def _(s):
            p.ss.inc(s.activation(qn2[:], qn2[:], AF.Sqrt))
        sk2 = p.ss.n

        @p.blk.vector
        def _(v):
            v.wait_ge(p.ss.sem, sk2)
            p.vv.inc(v.tensor_scalar(qn2[:], qn2[:], kmaxc[0:2, 0:1], -1.0,
                                     ALU.mult, ALU.mult))
            v.wait_ge(p.vv.sem, p.vv.n)
            p.vv.inc(v.tensor_scalar_add(qn2[:], qn2[:], 30.0))
        p.bar()

        for qhb in range(nqh):
            @p.blk.vector
            def _(v, fc=fc, qhb=qhb):
                v.wait_ge(p.vv.sem, p.vv.n)
                p.vv.inc(v.tensor_reduce(
                    mvF[0:2, fc * nqh + qhb: fc * nqh + qhb + 1],
                    qn2[:, qhb * QH:(qhb + 1) * QH], AX.X, ALU.min))
        p.bar()

    for h in range(HEADS):
        fc, pb = h // 2, (h % 2) * 64
        for qh in range(nqh):
            q0 = qh * QH
            @p.blk.tensor
            def _(t, fc=fc, hh=h % 2, qh=qh):
                t.wait_ge(p.vv.sem, p.vv.n)
                p.mm.inc(t.matmul(pnorm[0:1, 0:1], sel2[0:2, hh * 64:hh * 64 + 1],
                                  mvF[0:2, fc * nqh + qh: fc * nqh + qh + 1],
                                  start=True, stop=True))
            mex = p.mm.n

            @p.blk.vector
            def _(v, mex=mex):
                v.wait_ge(p.mm.sem, mex)
                p.vv.inc(v.tensor_copy(mvs[0:1, 0:1], pnorm[0:1, 0:1]))
                v.wait_ge(p.vv.sem, p.vv.n)
                for (ob, nb) in f_chunks(QH, 512):
                    p.vv.inc(v.tensor_scalar_mul(mbuf[0:1, ob:ob + nb],
                                                 onesqf[0:1, 0:nb],
                                                 mvs[0:1, 0:1]))
            vbc = p.vv.n
            expm = [0] * 64
            pvm = [0] * 64
            for kt in range(nkt):
                par = kt % 2
                ps_sl = psim[:, par * QH:(par + 1) * QH]
                pp_sl = p_sb[:, par * QH:(par + 1) * QH]

                @p.blk.tensor
                def _(t, h=h, fc=fc, pb=pb, q0=q0, kt=kt, par=par, ps_sl=ps_sl,
                      vbc=vbc):
                    t.wait_ge(p.vv.sem, vbc)
                    if kt >= 2:
                        t.wait_ge(p.ss.sem, expm[kt - 2])
                    for (o2, n2) in f_chunks(QH, 512):
                        t.matmul(ps_sl[:, o2:o2 + n2],
                                 k_b[pb:pb + 64, kt * 128:(kt + 1) * 128],
                                 q_b[pb:pb + 64, fc * nt + q0 + o2: fc * nt + q0 + o2 + n2],
                                 start=True, stop=False)
                        m = t.matmul(ps_sl[:, o2:o2 + n2],
                                     onesr2[0:1, :],
                                     mbuf[0:1, o2:o2 + n2],
                                     start=False, stop=True)
                    p.mm.inc(m)
                mmsim = p.mm.n

                @p.blk.scalar
                def _(s, kt=kt, ps_sl=ps_sl, pp_sl=pp_sl, mmsim=mmsim):
                    s.wait_ge(p.mm.sem, mmsim)
                    if kt >= 2:
                        s.wait_ge(p.mm.sem, pvm[kt - 2])
                    expm[kt] = p.ss.inc(s.activation(pp_sl, ps_sl, AF.Exp))

                @p.blk.tensor
                def _(t, kt=kt, pp_sl=pp_sl):
                    t.wait_ge(p.ss.sem, expm[kt])
                    for (o2, n2) in f_chunks(QH, 512):
                        m = t.matmul(pout[:, o2:o2 + n2],
                                     v_b[:, kt * 65:(kt + 1) * 65],
                                     pp_sl[:, o2:o2 + n2],
                                     start=(kt == 0), stop=(kt == nkt - 1))
                    pvm[kt] = p.mm.inc(m)

            @p.blk.vector
            def _(v):
                v.wait_ge(p.mm.sem, p.mm.n)
                p.vv.inc(v.reciprocal(rcro[0:1, :], pout[64:65, :]))
            vrc = p.vv.n

            @p.blk.tensor
            def _(t):
                t.wait_ge(p.vv.sem, vrc)
                for (o2, n2) in f_chunks(QH, 512):
                    m = t.matmul(p.ps_main[0:64, 0:n2], ones64f[0:1, 0:64],
                                 rcro[0:1, o2:o2 + n2], start=True, stop=True)
                    p.mm.inc(m)
                    mk = p.mm.n

                    @p.blk.vector
                    def _(v2, o2=o2, n2=n2, mk=mk):
                        v2.wait_ge(p.mm.sem, mk)
                        p.vv.inc(v2.tensor_copy(rbro[:, o2:o2 + n2],
                                                p.ps_main[0:64, 0:n2]))
                    t.wait_ge(p.vv.sem, p.vv.n)
            vb2 = p.vv.n

            @p.blk.vector
            def _(v, h=h, fc=fc, pb=pb, q0=q0):
                v.wait_ge(p.vv.sem, vb2)
                p.vv.inc(v.tensor_tensor(
                    att[pb:pb + 64, fc * nt + q0: fc * nt + q0 + QH],
                    pout[0:64, :], rbro[:, :], ALU.mult))
            vatt = p.vv.n

            @p.blk.tensor
            def _(t):
                t.wait_ge(p.vv.sem, vatt)
    p.bar()

    if upto == "att":
        @p.blk.sync
        def _(sy):
            p.mm.wait(sy)
            p.vv.wait(sy)
            p.st.inc(sy.dma_start(out=xout, in_=att[:]), 16)
        p.bar()
        s1.close()
        attsc.close()
        return _finish()

    s1.close()
    s2 = ExitStack()
    wrk = s2.enter_context(p.nc.sbuf_tensor("wrk", [128, DC * nt], F32))
    wo1 = s2.enter_context(p.nc.sbuf_tensor("wo1", [128, DC * 128], BF16))
    for fc in range(DC):
        @p.blk.gpsimd
        def _(g, fc=fc):
            g.wait_ge(p.mm.sem, p.mm.n)
            p.dg.inc(g.dma_start(
                out=wo1[:].rearrange("p (c f) -> p c f", c=DC),
                in_=wo_i[:, fc * 128:(fc + 1) * 128]
                .rearrange("(c p) f -> p c f", p=128)), 16)
        dgw = p.dg.n

        for (o2, n2) in f_chunks(nt, 512):
            @p.blk.tensor
            def _(t, fc=fc, o2=o2, n2=n2):
                t.wait_ge(p.dg.sem, dgw)
                t.wait_ge(p.vv.sem, p.vv.n)
                for ic in range(DC):
                    m = t.matmul(p.ps_main[:, 0:n2],
                                 wo1[:, ic * 128:(ic + 1) * 128],
                                 att[:, ic * nt + o2: ic * nt + o2 + n2],
                                 start=(ic == 0), stop=(ic == DC - 1))
                p.mm.inc(m)
            mk = p.mm.n

            @p.blk.vector
            def _(v, fc=fc, o2=o2, n2=n2):
                v.wait_ge(p.mm.sem, mk)
                p.vv.inc(v.tensor_copy(wrk[:, fc * nt + o2: fc * nt + o2 + n2],
                                       p.ps_main[:, 0:n2]))
            vk = p.vv.n

            @p.blk.tensor
            def _(t):
                t.wait_ge(p.vv.sem, vk)
    p.bar()

    for ch in range(nt // TC):
        t0 = ch * TC
        ln_fm(p, lambda c: c * nt + t0, TC, lambda c: c * nt + t0, None,
              lnb_bufs, g_col=lambda c: g2c[:, c:c + 1],
              b_col=lambda c: b2c[:, c:c + 1], src_sb=wrk, dst_sb=wrk)
        p.bar()
        ln_fm(p, lambda c: c * nt + t0, TC, lambda c: c * nt + t0,
              lambda c: c * nt + t0, lnb_bufs, src_sb=wrk, dst_sb=x_f, x_sb=x_f)
        p.bar()
    if upto == "x3":
        @p.blk.sync
        def _(sy):
            p.vv.wait(sy)
            p.ss.wait(sy)
            for c in range(DC):
                p.st.inc(sy.dma_start(out=xout[c * 128:(c + 1) * 128, :],
                                      in_=x_f[:, c * nt:(c + 1) * nt]), 16)
        p.bar()
        s2.close()
        attsc.close()
        return _finish()
    s2.close()
    attsc.close()

    s3 = ExitStack()

    def s3b(name, shape, dtype):
        return s3.enter_context(p.nc.sbuf_tensor(name, shape, dtype))

    lg = s3b("lg", [1, 8], F32)
    uu = s3b("uu_s", [1, 8], F32)
    selmax = s3b("selmax", [1, 1], F32)
    oh8 = s3b("oh8", [1, 8], F32)
    iota8 = s3b("iota8", [1, 8], F32)
    idxf = s3b("idxf", [1, 1], F32)
    idxi = s3b("idxi", [1, 1], I32)
    wsel = s3b("wsel", [128, DC * 128], BF16)
    bselr = s3b("bselr", [1, DM], F32)
    bselc = s3b("bselc", [128, DC], F32)
    xn3 = s3b("xn3", [128, DC * nt], BF16)

    @p.blk.sync
    def _(sy):
        p.dh.inc(sy.dma_start(out=lg[:], in_=logits_i), 16)
        p.dh.inc(sy.dma_start(out=uu[:], in_=uu_i), 16)

    @p.blk.gpsimd
    def _(g):
        for k8 in range(8):
            p.pg.inc(g.memset(iota8[0:1, k8:k8 + 1], float(k8)))
    p.bar()

    for ch in range(nt // TC):
        t0 = ch * TC
        ln_fm(p, lambda c: c * nt + t0, TC, lambda c: c * nt + t0, None,
              lnb_bufs, src_sb=x_f, dst_sb=xn3)
        p.bar()

    @p.blk.scalar
    def _(s):
        p.ss.inc(s.activation(uu[:], uu[:], AF.Ln))
        s.wait_ge(p.ss.sem, p.ss.n)
        p.ss.inc(s.activation(uu[:], uu[:], AF.Ln, scale=-1.0))

    @p.blk.vector
    def _(v):
        v.wait_ge(p.ss.sem, p.ss.n)
        p.vv.inc(v.tensor_tensor(oh8[:], lg[:], uu[:], ALU.subtract))
        v.wait_ge(p.vv.sem, p.vv.n)
        p.vv.inc(v.tensor_reduce(selmax[:], oh8[:], AX.X, ALU.max))
        v.wait_ge(p.vv.sem, p.vv.n)
        p.vv.inc(v.tensor_scalar(oh8[:], oh8[:], selmax[0:1, 0:1], None, ALU.is_ge))
        v.wait_ge(p.vv.sem, p.vv.n)
        p.vv.inc(v.tensor_mul(oh8[:], oh8[:], iota8[:]))
        v.wait_ge(p.vv.sem, p.vv.n)
        p.vv.inc(v.tensor_reduce(idxf[:], oh8[:], AX.X, ALU.add))
        v.wait_ge(p.vv.sem, p.vv.n)
        p.vv.inc(v.tensor_copy(idxi[:], idxf[:]))
    p.bar()
    if upto == "idx":
        dbgrow = s3b("dbgrow", [1, 256], F32)

        @p.blk.gpsimd
        def _(g):
            p.pg.inc(g.memset(dbgrow[:], 0.0))
        p.bar()

        @p.blk.vector
        def _(v):
            p.vv.inc(v.tensor_copy(dbgrow[0:1, 0:1], idxf[:]))
            v.wait_ge(p.vv.sem, p.vv.n)
            p.vv.inc(v.tensor_copy(dbgrow[0:1, 8:16], oh8[:]))
            v.wait_ge(p.vv.sem, p.vv.n)
            p.vv.inc(v.tensor_copy(dbgrow[0:1, 16:24], lg[:]))
            v.wait_ge(p.vv.sem, p.vv.n)
            p.vv.inc(v.tensor_copy(dbgrow[0:1, 24:32], uu[:]))
        p.bar()

        @p.blk.sync
        def _(sy):
            p.vv.wait(sy)
            p.st.inc(sy.dma_start(out=xout, in_=dbgrow[:]), 16)
        p.bar()
        s3.close()
        return _finish()

    @p.blk.gpsimd
    def _(g):
        with g.register("ixb") as ix:
            g.reg_load(ix, idxi[0:1, 0:1])
            for o in range(8):
                with g.If_cmp(ix, o, "IS_EQ"):
                    p.sg.inc(g.dma_start(out=bselr[:], in_=bmix_i[o:o + 1, :]), 16)
            p.sg.n -= 7 * 16
    p.bar()
    row_to_cols(p, bselr, DC, bselc, 0, p.ps_main)
    p.bar()

    mix_f = s3b("mix_f", [128, DC * nt], F32)
    for fc in range(DC):
        @p.blk.gpsimd
        def _(g, fc=fc):
            g.wait_ge(p.mm.sem, p.mm.n)
            with g.register(f"ixw{fc}") as ix:
                g.reg_load(ix, idxi[0:1, 0:1])
                for o in range(8):
                    with g.If_cmp(ix, o, "IS_EQ"):
                        p.sg.inc(g.dma_start(
                            out=wsel[:].rearrange("p (c f) -> p c f", c=DC),
                            in_=wmix_i[o][:, fc * 128:(fc + 1) * 128]
                            .rearrange("(c p) f -> p c f", p=128)), 16)
                p.sg.n -= 7 * 16
        sgw = p.sg.n

        for (o2, n2) in f_chunks(nt, 512):
            @p.blk.tensor
            def _(t, fc=fc, o2=o2, n2=n2):
                t.wait_ge(p.sg.sem, sgw)
                t.wait_ge(p.ss.sem, p.ss.n)
                for c in range(DC):
                    m = t.matmul(p.ps_main[:, 0:n2],
                                 wsel[:, c * 128:(c + 1) * 128],
                                 xn3[:, c * nt + o2: c * nt + o2 + n2],
                                 start=(c == 0), stop=(c == DC - 1))
                p.mm.inc(m)
            mk = p.mm.n

            @p.blk.scalar
            def _(s, fc=fc, o2=o2, n2=n2):
                s.wait_ge(p.mm.sem, mk)
                p.ss.inc(s.activation(mix_f[:, fc * nt + o2: fc * nt + o2 + n2],
                                      p.ps_main[:, 0:n2], AF.Identity,
                                      bias=bselc[:, fc:fc + 1]))
    p.bar()
    if upto == "mix":
        @p.blk.sync
        def _(sy):
            p.ss.wait(sy)
            p.vv.wait(sy)
            for c in range(DC):
                p.st.inc(sy.dma_start(out=xout[c * 128:(c + 1) * 128, :],
                                      in_=mix_f[:, c * nt:(c + 1) * nt]), 16)
        p.bar()
        s3.close()
        return _finish()

    for ch in range(nt // TC):
        t0 = ch * TC
        ln_fm(p, lambda c: c * nt + t0, TC, lambda c: c * nt + t0,
              lambda c: c * nt + t0, lnb_bufs, src_sb=mix_f, dst_sb=x_f, x_sb=x_f)
        p.bar()

    if upto == "fin":
        @p.blk.sync
        def _(sy):
            p.vv.wait(sy)
            p.ss.wait(sy)
            for c in range(DC):
                p.st.inc(sy.dma_start(out=xout[c * 128:(c + 1) * 128, :],
                                      in_=x_f[:, c * nt:(c + 1) * nt]), 16)
        p.bar()
        s3.close()
        return _finish()
    s3.close()

    s4 = ExitStack()
    ttm = s4.enter_context(p.nc.sbuf_tensor("ttm", [128, DM], F32))
    for t_ in range(nt // 128):
        pairs = [(x_f[:, c * nt + t_ * 128: c * nt + t_ * 128 + 128],
                  ttm[:, c * 128:(c + 1) * 128]) for c in range(DC)]
        transpose_tiles(p, pairs, identf, psim[:, 0:128], psim[:, 512:640])
        vdone = p.vv.n

        @p.blk.sync
        def _(sy, t_=t_):
            sy.wait_ge(p.vv.sem, vdone)
            p.st.inc(sy.dma_start(out=xout[t_ * 128:(t_ + 1) * 128, :],
                                  in_=ttm[:, 0:DM]), 16)

        @p.blk.tensor
        def _(t):
            t.wait_ge(p.st.sem, p.st.n)
    p.bar()

    @p.blk.sync
    def _(sy):
        p.dh.wait(sy)
        p.st.wait(sy)

    @p.blk.gpsimd
    def _(g):
        p.dg.wait(g)
        p.sg.wait(g)
    s4.close()
    return p.done()



_PROGS = {}


def _prog(key, builder, *args):
    if key not in _PROGS:
        _PROGS[key] = builder(*args)
    return _PROGS[key]


def kernel(**inputs):
    f32 = np.float32
    x = np.asarray(inputs["L0_attn_in_seq"], f32).reshape(16384, DM)
    identf = np.eye(128, dtype=f32)
    identb = np.eye(128, dtype=BF)
    sel2c = np.zeros((2, 128), f32)
    sel2c[0, 0:64] = 1.0
    sel2c[1, 64:128] = 1.0

    def dup64(a):
        a = np.asarray(a, f32).reshape(-1)
        return np.concatenate([a, a])[:, None].astype(f32)

    ncA = _prog("A", build_launch_A, NT)
    base = dict(tk=np.asarray(inputs["L0_attn_t_k"], f32),
                tv=np.asarray(inputs["L0_attn_t_v"], f32),
                tq=np.asarray(inputs["L0_attn_t_q"], f32),
                w1=np.asarray(inputs["L0_attn_W1"], f32),
                w2=np.asarray(inputs["L0_attn_W2"], f32),
                b1=np.asarray(inputs["L0_attn_b1"], f32).reshape(1, 256),
                b2=np.asarray(inputs["L0_attn_b2"], f32).reshape(1, DM),
                identf=identf, identb=identb)
    imsA = [dict(x_tm=np.ascontiguousarray(x[c * NT:(c + 1) * NT]), **base)
            for c in range(NCORES)]
    rA = run_bass_kernel_spmd(ncA, imsA, core_ids=list(range(NCORES))).results

    ncB = _prog("B", build_launch_BC, "B", NT)

    def gargs(pref, rs, cores):
        out = {}
        for k in range(4):
            r = rs[cores[k]]
            out[f"{pref}w1t_{k}"] = r["gw1t"]
            out[f"{pref}w2_{k}"] = r["gw2"]
            out[f"{pref}b1_{k}"] = r["gb1"]
            out[f"{pref}b2_{k}"] = r["gb2"]
        return out

    g0 = gargs("g0", rA, [0, 1, 2, 3])
    imsB = [dict(xfm=rA[c]["xfm"], qfm=rA[c]["qfm"], zbf=rA[c]["zbf"],
                 ztm=rA[c]["ztm"], zmyo=rA[c]["zmyo"],
                 w1=base["w1"], w2=base["w2"], b1=base["b1"], b2=base["b2"],
                 identf=identf, identb=identb, **g0)
            for c in range(NCORES)]
    rB = run_bass_kernel_spmd(ncB, imsB, core_ids=list(range(NCORES))).results

    ncC = _prog("C", build_launch_BC, "C", NT)
    g1 = gargs("g1", rB, [4, 5, 6, 7])
    imsC = [dict(xfm=rA[c]["xfm"], qfm=rA[c]["qfm"], zbf=rA[c]["zbf"],
                 ztm=rA[c]["ztm"], zmyo=rA[c]["zmyo"],
                 w1=base["w1"], w2=base["w2"], b1=base["b1"], b2=base["b2"],
                 identf=identf, identb=identb, **g0, **g1)
            for c in range(NCORES)]
    rC = run_bass_kernel_spmd(ncC, imsC, core_ids=list(range(NCORES))).results

    xs = [rB[c]["xnew"] if c < 4 else rC[c]["xnew"] for c in range(NCORES)]
    xouter = xs

    ncG = _prog("G", build_launch_G, False, NT)
    imsG = [dict(yfm=xs[c]) for c in range(NCORES)]
    rG = run_bass_kernel_spmd(ncG, imsG, core_ids=list(range(NCORES))).results
    xs = [rG[c]["ofm"] for c in range(NCORES)]

    ncD = _prog("D", build_launch_D, NT)
    KR = 512 + NT
    NQW = (256 + NT) // WIN
    m2 = host_band_masks()
    wqkv_all = np.asarray(inputs["L0_ffn_Wqkv"], f32)
    wo_all = np.asarray(inputs["L0_ffn_Wo"], f32)
    w1_all = np.asarray(inputs["L0_ffn_W1"], f32)
    w2_all = np.asarray(inputs["L0_ffn_W2"], f32)
    b1_all = np.asarray(inputs["L0_ffn_b1"], f32)
    b2_all = np.asarray(inputs["L0_ffn_b2"], f32)
    lng_all = np.asarray(inputs["L0_ffn_ln_g"], f32)
    lnb_all = np.asarray(inputs["L0_ffn_ln_b"], f32)
    qs_all = np.asarray(inputs["L0_ffn_q_scale"], f32)
    ks_all = np.asarray(inputs["L0_ffn_k_scale"], f32)

    core_rot = []
    core_masks = []
    core_hval = []
    for c in range(NCORES):
        sb_ = (c % 4) * NT
        pos = np.arange(sb_ - 512, sb_ + NT)
        core_rot.append(host_rot_tables_for(pos))
        mk = np.broadcast_to(m2[0][None], (NQW, 4, 128, WIN)).copy()
        if c % 4 == 0:
            mk[1] = m2[1]
        core_masks.append(np.ascontiguousarray(mk))
        core_hval.append(np.full((128, 1), 0.0 if c % 4 == 0 else 1.0, f32))

    for li in range(6):
        xg0 = np.concatenate(xs[0:4], axis=1)
        xg1 = np.concatenate(xs[4:8], axis=1)
        imsD = []
        for c in range(NCORES):
            xg = xg0 if c < 4 else xg1
            sb_ = (c % 4) * NT
            if sb_ == 0:
                xin = np.concatenate(
                    [np.zeros((DM, HALO), f32), xg[:, 0:NT]], axis=1)
            else:
                xin = np.ascontiguousarray(xg[:, sb_ - HALO: sb_ + NT])
            rqc, rqs_, rkc, rks_ = core_rot[c]
            imsD.append(dict(
                xin=xin, wqkv=wqkv_all[li], wo=wo_all[li], w1=w1_all[li],
                w2=w2_all[li], b1=b1_all[li:li + 1], b2=b2_all[li:li + 1],
                lng=lng_all[li:li + 1], lnb=lnb_all[li:li + 1],
                qscol=dup64(qs_all[li]), kscol=dup64(ks_all[li]),
                rotqc=rqc, rotqs=rqs_, rotkc=rkc, rotks=rks_,
                masks=core_masks[c], hvalid=core_hval[c], sel2c=sel2c,
                identb=identb))
        rD = run_bass_kernel_spmd(ncD, imsD, core_ids=list(range(NCORES))).results
        xs = [rD[c]["xout"] for c in range(NCORES)]

    ncH = _prog("H", build_launch_G, True, NT)
    imsH = [dict(yfm=xs[c], xfm=xouter[c]) for c in range(NCORES)]
    rH = run_bass_kernel_spmd(ncH, imsH, core_ids=list(range(NCORES))).results
    xs = [rH[c]["ofm"] for c in range(NCORES)]

    ncE = _prog("E", build_launch_E, NT)
    imsE = [dict(xfm=xs[c], wq=np.asarray(inputs["L1_attn_Wq"], f32),
                 wkv=np.asarray(inputs["L1_attn_Wkv"], f32),
                 ln1g=np.asarray(inputs["L1_attn_ln1_g"], f32).reshape(1, DM),
                 ln1b=np.asarray(inputs["L1_attn_ln1_b"], f32).reshape(1, DM))
            for c in range(NCORES)]
    rE = run_bass_kernel_spmd(ncE, imsE, core_ids=list(range(NCORES))).results

    ncF = _prog("F", build_launch_F, NT, 8192)
    kf0 = np.concatenate([rE[c]["kfm"] for c in range(4)], axis=1)
    kf1 = np.concatenate([rE[c]["kfm"] for c in range(4, 8)], axis=1)
    vf0 = np.concatenate([rE[c]["vaug"] for c in range(4)], axis=1)
    vf1 = np.concatenate([rE[c]["vaug"] for c in range(4, 8)], axis=1)
    imsF = []
    for c in range(NCORES):
        imsF.append(dict(
            xfm=xs[c], qfm=rE[c]["qfm"],
            kfull=kf0 if c < 4 else kf1, vfull=vf0 if c < 4 else vf1,
            wo=np.asarray(inputs["L1_attn_Wo"], f32),
            ln2g=np.asarray(inputs["L1_attn_ln2_g"], f32).reshape(1, DM),
            ln2b=np.asarray(inputs["L1_attn_ln2_b"], f32).reshape(1, DM),
            wmix=np.asarray(inputs["L1_ffn_W"], f32),
            bmix=np.asarray(inputs["L1_ffn_b"], f32),
            logits=np.asarray(inputs["L1_ffn_logits"], f32).reshape(1, 8),
            uu=np.asarray(inputs["L1_ffn_u"], f32).reshape(1, 8),
            sel2c=sel2c, identf=identf))
    rF = run_bass_kernel_spmd(ncF, imsF, core_ids=list(range(NCORES))).results

    out = np.concatenate([rF[c]["xout"] for c in range(NCORES)], axis=0)
    return out.reshape(2, 8192, DM).astype(np.float32)


# version 03
# speedup vs baseline: 1.4247x; 1.4247x over previous
import numpy as np
import ml_dtypes
from contextlib import ExitStack

import concourse.bass as bass
import concourse.mybir as mybir
from concourse.bass_utils import run_bass_kernel_spmd

F32 = mybir.dt.float32
BF16 = mybir.dt.bfloat16
I32 = mybir.dt.int32
AF = mybir.ActivationFunctionType
ALU = mybir.AluOpType
AX = mybir.AxisListType

NCORES = 8
DM = 1024
DC = 8
NT = 2048
LR = 0.001
CGRAD = 2.0 / (8192.0 * 1024.0)
EPS = 1e-5
NEG = -(2.0 ** 100)
HEADS = 16
DH = 64
WIN = 256
DIN = 2730
QKSC = 8.0
HALO = 513
TC = 512
BF = ml_dtypes.bfloat16

_ENGS = ("tensor", "vector", "scalar", "gpsimd", "sync")


def f_chunks(total, step=128):
    out = []
    o = 0
    while o < total:
        out.append((o, min(step, total - o)))
        o += step
    return out


class S:
    def __init__(self, nc, ctx, name):
        self.sem = ctx.enter_context(nc.semaphore(name))
        self.n = 0

    def inc(self, instr, k=1):
        instr.then_inc(self.sem, k)
        self.n += k
        return self.n

    def wait(self, eng, n=None):
        eng.wait_ge(self.sem, self.n if n is None else n)


class P:
    def __init__(self):
        self.nc = bass.Bass("TRN2", target_bir_lowering=False, debug=False)
        self.ctx = ExitStack()
        self.blk = None
        for nm in ("dg", "dh", "da", "mm", "vv", "ss", "pg", "st", "sg"):
            setattr(self, nm, S(self.nc, self.ctx, nm))

    def sb(self, name, shape, dtype):
        return self.ctx.enter_context(self.nc.sbuf_tensor(name, shape, dtype))

    def pt(self, name, shape, dtype=F32):
        return self.ctx.enter_context(self.nc.psum_tensor(name, shape, dtype))

    def inp(self, name, shape, dtype=F32):
        return self.nc.dram_tensor(name, shape, dtype, kind="ExternalInput").ap()

    def outp(self, name, shape, dtype=F32):
        return self.nc.dram_tensor(name, shape, dtype, kind="ExternalOutput").ap()

    def begin(self, pool="full"):
        self.pool = pool
        self.blk = self.ctx.enter_context(self.nc.Block())
        self.c_onesf = self.sb("c_onesf", [128, 1], F32)
        self.c_onesb = self.sb("c_onesb", [128, 1], BF16)
        self.c_onesr_f = self.sb("c_onesr_f", [1, 128], F32)
        self.c_onesr_b = self.sb("c_onesr_b", [1, 128], BF16)

        if self.pool == "full":
            self.psA_b = self.pt("pool_psA_b", [128, 128], BF16)
            self.psB_b = self.pt("pool_psB_b", [128, 128], BF16)
        if self.pool != "tiny":
            self.psA_f = self.pt("pool_psA_f", [128, 128], F32)
            self.psB_f = self.pt("pool_psB_f", [128, 128], F32)
        self.ps_main = self.pt("pool_ps_main", [128, 512], F32)

        @self.blk.gpsimd
        def _(g):
            self.pg.inc(g.memset(self.c_onesf[:], 1.0))
            self.pg.inc(g.memset(self.c_onesb[:], 1.0))
            self.pg.inc(g.memset(self.c_onesr_f[:], 1.0))
            self.pg.inc(g.memset(self.c_onesr_b[:], 1.0))
        return self.blk

    def bar(self):
        marks = [(s, s.n) for s in
                 (self.dg, self.dh, self.da, self.mm, self.vv, self.ss,
                  self.pg, self.sg)]

        def mkbody(eng_name):
            def body(e):
                for sem, val in marks:
                    if val:
                        e.wait_ge(sem.sem, val)
            return body
        self.blk.tensor(mkbody("tensor"))
        self.blk.vector(mkbody("vector"))
        self.blk.scalar(mkbody("scalar"))
        self.blk.gpsimd(mkbody("gpsimd"))
        self.blk.sync(mkbody("sync"))

    def done(self):
        self.ctx.close()
        return self.nc



def transpose_tiles(p, pairs, ident, psA, psB):
    mb = p.mm.n
    vb = p.vv.n
    ps = (psA, psB)

    @p.blk.tensor
    def _(t):
        t.wait_ge(p.vv.sem, vb)
        for i, (s, d) in enumerate(pairs):
            if i >= 2:
                t.wait_ge(p.vv.sem, vb + i - 1)
            p.mm.inc(t.transpose(ps[i % 2][:, :], s, ident[:]))

    @p.blk.vector
    def _(v):
        for i, (s, d) in enumerate(pairs):
            v.wait_ge(p.mm.sem, mb + i + 1)
            p.vv.inc(v.tensor_copy(d, ps[i % 2][:, 0:d.shape[-1]]))


def row_to_cols(p, row_ap, ncol, out_sb, out_off, psum):
    mb = p.mm.n
    vb = p.vv.n

    @p.blk.tensor
    def _(t):
        t.wait_ge(p.vv.sem, vb)
        for c in range(ncol):
            m = t.matmul(psum[:, c:c + 1],
                         row_ap[0:1, c * 128:(c + 1) * 128],
                         p.c_onesf[0:1, 0:1], start=True, stop=True)
        p.mm.inc(m)

    @p.blk.vector
    def _(v):
        v.wait_ge(p.mm.sem, p.mm.n)
        p.vv.inc(v.tensor_copy(out_sb[:, out_off:out_off + ncol], psum[:, 0:ncol]))


def ln_tm_tile(p, x_ap, xn_ap, sq_ap, scr, d=DM, eps=EPS):
    @p.blk.vector
    def _(v):
        p.vv.inc(v.tensor_reduce(scr["s"][:], x_ap, AX.X, ALU.add))

    @p.blk.scalar
    def _(s):
        p.ss.inc(s.activation(sq_ap, x_ap, AF.Square, accum_out=scr["q"][:]))

    vb, sb_ = p.vv.n, p.ss.n

    @p.blk.vector
    def _(v):
        v.wait_ge(p.vv.sem, vb)
        v.wait_ge(p.ss.sem, sb_)
        p.vv.inc(v.tensor_scalar_mul(scr["mu"][:], scr["s"][:], 1.0 / d))
        p.vv.inc(v.tensor_scalar_mul(scr["rstd"][:], scr["q"][:], 1.0 / d))
        v.wait_ge(p.vv.sem, p.vv.n)
        p.vv.inc(v.tensor_mul(scr["s"][:], scr["mu"][:], scr["mu"][:]))
        v.wait_ge(p.vv.sem, p.vv.n)
        p.vv.inc(v.tensor_tensor(scr["rstd"][:], scr["rstd"][:], scr["s"][:], ALU.subtract))
        v.wait_ge(p.vv.sem, p.vv.n)
        p.vv.inc(v.tensor_scalar_add(scr["rstd"][:], scr["rstd"][:], eps))

    sb2 = p.ss.n

    @p.blk.scalar
    def _(s):
        s.wait_ge(p.vv.sem, p.vv.n)
        p.ss.inc(s.activation(scr["rstd"][:], scr["rstd"][:], AF.Sqrt))

    @p.blk.vector
    def _(v):
        v.wait_ge(p.ss.sem, p.ss.n)
        p.vv.inc(v.reciprocal(scr["rstd"][:], scr["rstd"][:]))
        v.wait_ge(p.vv.sem, p.vv.n)
        p.vv.inc(v.tensor_scalar(xn_ap, x_ap, scr["mu"][:], scr["rstd"][:],
                                 ALU.subtract, ALU.mult))


def ln_fm(p, src_off_fn, nfree, dst_off_fn, x_add_off_fn, bufs, g_col=None,
          b_col=None, src_sb=None, dst_sb=None, x_sb=None):
    assert nfree <= 512
    sq = bufs["sq"]; tmp = bufs["tmp"]
    mu = bufs["mu"]; rstd = bufs["rstd"]
    mub = bufs["mub"]; rsb = bufs["rsb"]
    ps = p.ps_main

    mm_b0 = p.mm.n
    ss_b0 = p.ss.n

    @p.blk.scalar
    def _(s):
        for c in range(DC):
            if c >= 2:
                s.wait_ge(p.mm.sem, mm_b0 + 1 + c - 1)
            p.ss.inc(s.activation(sq[:, (c % 2) * nfree:(c % 2 + 1) * nfree],
                                  src_sb[:, src_off_fn(c):src_off_fn(c) + nfree],
                                  AF.Square))

    @p.blk.tensor
    def _(t):
        t.wait_ge(p.vv.sem, p.vv.n)
        for c in range(DC):
            m = t.matmul(ps[0:1, 0:nfree], p.c_onesf[:, 0:1],
                         src_sb[:, src_off_fn(c):src_off_fn(c) + nfree],
                         start=(c == 0), stop=(c == DC - 1))
        p.mm.inc(m)
    mm1 = p.mm.n

    @p.blk.vector
    def _(v):
        v.wait_ge(p.mm.sem, mm1)
        p.vv.inc(v.tensor_scalar_mul(mu[:], ps[0:1, 0:nfree], 1.0 / DM))
    vv1 = p.vv.n

    @p.blk.tensor
    def _(t):
        t.wait_ge(p.vv.sem, vv1)
        for c in range(DC):
            t.wait_ge(p.ss.sem, ss_b0 + c + 1)
            p.mm.inc(t.matmul(ps[0:1, 0:nfree], p.c_onesf[:, 0:1],
                              sq[:, (c % 2) * nfree:(c % 2 + 1) * nfree],
                              start=(c == 0), stop=(c == DC - 1)))
    mm2 = p.mm.n

    @p.blk.vector
    def _(v):
        v.wait_ge(p.mm.sem, mm2)
        p.vv.inc(v.tensor_scalar_mul(rstd[:], ps[0:1, 0:nfree], 1.0 / DM))
        v.wait_ge(p.vv.sem, p.vv.n)
        p.vv.inc(v.tensor_mul(tmp[0:1, 0:nfree], mu[:], mu[:]))
        v.wait_ge(p.vv.sem, p.vv.n)
        p.vv.inc(v.tensor_tensor(rstd[:], rstd[:], tmp[0:1, 0:nfree], ALU.subtract))
        v.wait_ge(p.vv.sem, p.vv.n)
        p.vv.inc(v.tensor_scalar_add(rstd[:], rstd[:], EPS))

    @p.blk.scalar
    def _(s):
        s.wait_ge(p.vv.sem, p.vv.n)
        p.ss.inc(s.activation(rstd[:], rstd[:], AF.Sqrt))

    @p.blk.vector
    def _(v):
        v.wait_ge(p.ss.sem, p.ss.n)
        p.vv.inc(v.reciprocal(rstd[:], rstd[:]))
    vrec = p.vv.n

    @p.blk.tensor
    def _(t):
        t.wait_ge(p.vv.sem, vrec)
        p.mm.inc(t.matmul(ps[:, 0:nfree], p.c_onesr_f[0:1, :], mu[0:1, 0:nfree],
                          start=True, stop=True))
    mmb1 = p.mm.n

    @p.blk.vector
    def _(v):
        v.wait_ge(p.mm.sem, mmb1)
        p.vv.inc(v.tensor_copy(mub[:, 0:nfree], ps[:, 0:nfree]))
    vvb1 = p.vv.n

    @p.blk.tensor
    def _(t):
        t.wait_ge(p.vv.sem, vvb1)
        p.mm.inc(t.matmul(ps[:, 0:nfree], p.c_onesr_f[0:1, :], rstd[0:1, 0:nfree],
                          start=True, stop=True))
    mmb2 = p.mm.n

    @p.blk.vector
    def _(v):
        v.wait_ge(p.mm.sem, mmb2)
        p.vv.inc(v.tensor_copy(rsb[:, 0:nfree], ps[:, 0:nfree]))
        v.wait_ge(p.vv.sem, p.vv.n)
        for c in range(DC):
            src = src_sb[:, src_off_fn(c):src_off_fn(c) + nfree]
            p.vv.inc(v.tensor_tensor(tmp[:, 0:nfree], src, mub[:, 0:nfree],
                                     ALU.subtract))
            v.wait_ge(p.vv.sem, p.vv.n)
            p.vv.inc(v.tensor_tensor(tmp[:, 0:nfree], tmp[:, 0:nfree],
                                     rsb[:, 0:nfree], ALU.mult))
            v.wait_ge(p.vv.sem, p.vv.n)
            if g_col is not None:
                p.vv.inc(v.tensor_scalar(tmp[:, 0:nfree], tmp[:, 0:nfree],
                                         g_col(c), b_col(c), ALU.mult, ALU.add))
                v.wait_ge(p.vv.sem, p.vv.n)
            dst = dst_sb[:, dst_off_fn(c):dst_off_fn(c) + nfree]
            if x_add_off_fn is not None:
                xa = x_sb[:, x_add_off_fn(c):x_add_off_fn(c) + nfree]
                p.vv.inc(v.tensor_tensor(dst, tmp[:, 0:nfree], xa, ALU.add))
            else:
                p.vv.inc(v.tensor_copy(dst, tmp[:, 0:nfree]))
            v.wait_ge(p.vv.sem, p.vv.n)



def ttt_prep_params(p, w1_in, w2_in, b1_in, b2_in, identf, masters=False):
    r = {}
    r["w1t_b"] = p.sb("w1t_b", [128, DC * 256], BF16)
    r["w2_b"] = p.sb("w2_b", [128, DC * 256], BF16)
    r["w2t_b"] = p.sb("w2t_b", [128, 2 * 1024], BF16)
    r["b1c"] = p.sb("b1c", [128, 2], F32)
    r["b2c"] = p.sb("b2c", [128, DC], F32)
    r["b1row"] = p.sb("b1row", [1, 256], F32)
    r["b2row"] = p.sb("b2row", [1, 1024], F32)
    if masters:
        r["w1t_f"] = p.sb("w1t_f", [128, DC * 256], F32)
        r["w2_f"] = p.sb("w2_f", [128, DC * 256], F32)
    _w1w2 = ExitStack()
    w1r = _w1w2.enter_context(p.nc.sbuf_tensor("w1r_tmp", [128, 2 * 1024], F32))
    w2r = _w1w2.enter_context(p.nc.sbuf_tensor("w2r_tmp", [128, DC * 256], F32))
    psA = p.psA_f
    psB = p.psB_f
    pcol = p.psA_f

    @p.blk.sync
    def _(sy):
        for c in range(2):
            p.dh.inc(sy.dma_start(out=w1r[:, c * 1024:(c + 1) * 1024],
                                  in_=w1_in[c * 128:(c + 1) * 128, :]), 16)
        for c in range(DC):
            p.dh.inc(sy.dma_start(out=w2r[:, c * 256:(c + 1) * 256],
                                  in_=w2_in[c * 128:(c + 1) * 128, :]), 16)
        p.dh.inc(sy.dma_start(out=r["b1row"][:], in_=b1_in), 16)
        p.dh.inc(sy.dma_start(out=r["b2row"][:], in_=b2_in), 16)
    p.bar()

    dst_w1t = r["w1t_f"] if masters else r["w1t_b"]
    pairs = []
    for hc in range(DC):
        for mc in range(2):
            pairs.append((w1r[:, mc * 1024 + hc * 128: mc * 1024 + hc * 128 + 128],
                          dst_w1t[:, hc * 256 + mc * 128: hc * 256 + mc * 128 + 128]))
    for mc in range(2):
        for hc in range(DC):
            pairs.append((w2r[:, hc * 256 + mc * 128: hc * 256 + mc * 128 + 128],
                          r["w2t_b"][:, mc * 1024 + hc * 128: mc * 1024 + hc * 128 + 128]))
    transpose_tiles(p, pairs, identf, psA, psB)

    @p.blk.vector
    def _(v):
        v.wait_ge(p.dh.sem, p.dh.n)
        if masters:
            for c in range(DC):
                p.vv.inc(v.tensor_copy(r["w2_f"][:, c * 256:(c + 1) * 256],
                                       w2r[:, c * 256:(c + 1) * 256]))
        for c in range(DC):
            p.vv.inc(v.tensor_copy(r["w2_b"][:, c * 256:(c + 1) * 256],
                                   w2r[:, c * 256:(c + 1) * 256]))
    row_to_cols(p, r["b1row"], 2, r["b1c"], 0, pcol)
    row_to_cols(p, r["b2row"], DC, r["b2c"], 0, pcol)
    p.bar()
    _w1w2.close()
    if masters:
        @p.blk.vector
        def _(v):
            for c in range(DC):
                p.vv.inc(v.tensor_copy(r["w1t_b"][:, c * 256:(c + 1) * 256],
                                       r["w1t_f"][:, c * 256:(c + 1) * 256]))
        p.bar()
    return r


def ttt_update_params(p, pr, g, lr=LR):
    @p.blk.vector
    def _(v):
        for c in range(DC):
            sl = slice(c * 256, (c + 1) * 256)
            p.vv.inc(v.scalar_tensor_tensor(pr["w1t_f"][:, sl], g["gw1t"][:, sl],
                                            -lr, pr["w1t_f"][:, sl], ALU.mult, ALU.add))
            p.vv.inc(v.scalar_tensor_tensor(pr["w2_f"][:, sl], g["gw2"][:, sl],
                                            -lr, pr["w2_f"][:, sl], ALU.mult, ALU.add))
        p.vv.inc(v.scalar_tensor_tensor(pr["b1row"][:], g["gb1"][:], -lr,
                                        pr["b1row"][:], ALU.mult, ALU.add))
        p.vv.inc(v.scalar_tensor_tensor(pr["b2row"][:], g["gb2"][:], -lr,
                                        pr["b2row"][:], ALU.mult, ALU.add))
    p.bar()

    @p.blk.vector
    def _(v):
        for c in range(DC):
            sl = slice(c * 256, (c + 1) * 256)
            p.vv.inc(v.tensor_copy(pr["w1t_b"][:, sl], pr["w1t_f"][:, sl]))
            p.vv.inc(v.tensor_copy(pr["w2_b"][:, sl], pr["w2_f"][:, sl]))
    pairs = []
    for mc in range(2):
        for hc in range(DC):
            pairs.append((pr["w2_f"][:, hc * 256 + mc * 128: hc * 256 + mc * 128 + 128],
                          pr["w2t_b"][:, mc * 1024 + hc * 128: mc * 1024 + hc * 128 + 128]))
    transpose_tiles(p, pairs, pr["identf"], pr["psA"], pr["psB"])
    row_to_cols(p, pr["b1row"], 2, pr["b1c"], 0, pr["pcol"])
    row_to_cols(p, pr["b2row"], DC, pr["b2c"], 0, pr["pcol"])
    p.bar()


def ttt_recon_fwd(p, pr, zin_bf_sb, zoff_fn, n, apb, h_b, psum):
    for mc in range(2):
        mmn0 = p.mm.n

        @p.blk.tensor
        def _(t):
            for hc in range(DC):
                m = t.matmul(psum[:, 0:n],
                             pr["w1t_b"][:, hc * 256 + mc * 128: hc * 256 + mc * 128 + 128],
                             zin_bf_sb[:, zoff_fn(hc):zoff_fn(hc) + n],
                             start=(hc == 0), stop=(hc == DC - 1))
            p.mm.inc(m)

        @p.blk.scalar
        def _(s):
            s.wait_ge(p.mm.sem, p.mm.n)
            p.ss.inc(s.activation(apb[:, mc * n:(mc + 1) * n], psum[:, 0:n],
                                  AF.Identity, bias=pr["b1c"][:, mc:mc + 1]))
            s.wait_ge(p.ss.sem, p.ss.n)
            p.ss.inc(s.activation(h_b[:, mc * n:(mc + 1) * n],
                                  apb[:, mc * n:(mc + 1) * n], AF.Gelu))

        @p.blk.tensor
        def _(t):
            t.wait_ge(p.ss.sem, p.ss.n)


def ttt_recon_out(p, pr, h_b, n, zadd_sb, zadd_off_fn, out_sb, out_off_fn, psum,
                  out_dtype_note="f32"):
    for hc in range(DC):
        @p.blk.tensor
        def _(t):
            for mc in range(2):
                m = t.matmul(psum[:, 0:n],
                             pr["w2t_b"][:, mc * 1024 + hc * 128: mc * 1024 + hc * 128 + 128],
                             h_b[:, mc * n:(mc + 1) * n],
                             start=(mc == 0), stop=(mc == 1))
            p.mm.inc(m)

        @p.blk.scalar
        def _(s):
            s.wait_ge(p.mm.sem, p.mm.n)
            p.ss.inc(s.activation(psum[:, 0:n], psum[:, 0:n], AF.Identity,
                                  bias=pr["b2c"][:, hc:hc + 1]))

        @p.blk.vector
        def _(v):
            v.wait_ge(p.ss.sem, p.ss.n)
            p.vv.inc(v.tensor_tensor(out_sb[:, out_off_fn(hc):out_off_fn(hc) + n],
                                     psum[:, 0:n],
                                     zadd_sb[:, zadd_off_fn(hc):zadd_off_fn(hc) + n],
                                     ALU.add))

        @p.blk.tensor
        def _(t):
            t.wait_ge(p.vv.sem, p.vv.n)


def ttt_grad_chunk(p, pr, bufs, n=TC):
    nt_t = n // 128
    z_bf = bufs["z_bf"]; z_tm = bufs["z_tm"]; zmy = bufs["zmy"]
    apb = bufs["apb"]; h_b = bufs["h_b"]; gp = bufs["gp"]
    rb = bufs["rb"]; r_tm = bufs["r_tm"]; h_tm = bufs["h_tm"]
    da_b = bufs["da_b"]; da_tm = bufs["da_tm"]
    psum = bufs["psum"]; identb = bufs["identb"]
    psA = bufs["psA"]; psB = bufs["psB"]

    ttt_recon_fwd(p, pr, z_bf, lambda hc: hc * n, n, apb, h_b, psum)
    p.bar()

    @p.blk.scalar
    def _(s):
        for mc in range(2):
            sl = slice(mc * n, (mc + 1) * n)
            p.ss.inc(s.activation(gp[:, sl], apb[:, sl], AF.Erf,
                                  scale=0.7071067811865476))
            p.ss.inc(s.activation(bufs["gp2"][:, sl], apb[:, sl], AF.Square))
        s.wait_ge(p.ss.sem, p.ss.n)
        for mc in range(2):
            sl = slice(mc * n, (mc + 1) * n)
            p.ss.inc(s.activation(bufs["gp2"][:, sl], bufs["gp2"][:, sl],
                                  AF.Exp, scale=-0.5))

    @p.blk.vector
    def _(v):
        v.wait_ge(p.ss.sem, p.ss.n)
        for mc in range(2):
            sl = slice(mc * n, (mc + 1) * n)
            p.vv.inc(v.tensor_scalar(gp[:, sl], gp[:, sl], 0.5, 0.5,
                                     ALU.mult, ALU.add))
            p.vv.inc(v.tensor_mul(bufs["gp2"][:, sl], bufs["gp2"][:, sl],
                                  apb[:, sl]))
        v.wait_ge(p.vv.sem, p.vv.n)
        for mc in range(2):
            sl = slice(mc * n, (mc + 1) * n)
            p.vv.inc(v.scalar_tensor_tensor(gp[:, sl], bufs["gp2"][:, sl],
                                            0.3989422804014327, gp[:, sl],
                                            ALU.mult, ALU.add))
    p.bar()

    pairs = []
    for mc in range(2):
        for t_ in range(nt_t):
            pairs.append((h_b[:, mc * n + t_ * 128: mc * n + t_ * 128 + 128],
                          h_tm[:, t_ * 256 + mc * 128: t_ * 256 + mc * 128 + 128]))
    transpose_tiles(p, pairs, identb, psA, psB)
    p.bar()

    for hc in range(DC):
        @p.blk.tensor
        def _(t):
            for mc in range(2):
                m = t.matmul(psum[:, 0:n],
                             pr["w2t_b"][:, mc * 1024 + hc * 128: mc * 1024 + hc * 128 + 128],
                             h_b[:, mc * n:(mc + 1) * n],
                             start=(mc == 0), stop=(mc == 1))
            p.mm.inc(m)

        @p.blk.scalar
        def _(s):
            s.wait_ge(p.mm.sem, p.mm.n)
            p.ss.inc(s.activation(psum[:, 0:n], psum[:, 0:n], AF.Identity,
                                  bias=pr["b2c"][:, hc:hc + 1]))

        @p.blk.vector
        def _(v):
            v.wait_ge(p.ss.sem, p.ss.n)
            p.vv.inc(v.tensor_tensor(rb[:, hc * n:(hc + 1) * n], psum[:, 0:n],
                                     zmy[:, hc * n:(hc + 1) * n], ALU.add))

        @p.blk.tensor
        def _(t):
            t.wait_ge(p.vv.sem, p.vv.n)
    p.bar()

    pairs = []
    for hc in range(DC):
        for t_ in range(nt_t):
            pairs.append((rb[:, hc * n + t_ * 128: hc * n + t_ * 128 + 128],
                          r_tm[:, t_ * 1024 + hc * 128: t_ * 1024 + hc * 128 + 128]))
    transpose_tiles(p, pairs, identb, psA, psB)
    p.bar()

    for mc in range(2):
        @p.blk.tensor
        def _(t):
            for hc in range(DC):
                m = t.matmul(psum[:, 0:n],
                             pr["w2_b"][:, hc * 256 + mc * 128: hc * 256 + mc * 128 + 128],
                             rb[:, hc * n:(hc + 1) * n],
                             start=(hc == 0), stop=(hc == DC - 1))
            p.mm.inc(m)

        @p.blk.vector
        def _(v):
            v.wait_ge(p.mm.sem, p.mm.n)
            p.vv.inc(v.tensor_tensor(da_b[:, mc * n:(mc + 1) * n], psum[:, 0:n],
                                     gp[:, mc * n:(mc + 1) * n], ALU.mult))

        @p.blk.tensor
        def _(t):
            t.wait_ge(p.vv.sem, p.vv.n)
    p.bar()

    pairs = []
    for mc in range(2):
        for t_ in range(nt_t):
            pairs.append((da_b[:, mc * n + t_ * 128: mc * n + t_ * 128 + 128],
                          da_tm[:, t_ * 256 + mc * 128: t_ * 256 + mc * 128 + 128]))
    transpose_tiles(p, pairs, identb, psA, psB)
    p.bar()

    for hc in range(DC):
        @p.blk.tensor
        def _(t):
            for t_ in range(nt_t):
                m = t.matmul(psum[:, 0:256],
                             z_tm[:, t_ * 1024 + hc * 128: t_ * 1024 + hc * 128 + 128],
                             da_tm[:, t_ * 256:(t_ + 1) * 256],
                             start=(t_ == 0), stop=(t_ == nt_t - 1))
            p.mm.inc(m)

        @p.blk.vector
        def _(v):
            v.wait_ge(p.mm.sem, p.mm.n)
            sl = slice(hc * 256, (hc + 1) * 256)
            p.vv.inc(v.tensor_tensor(bufs["acc_w1t"][:, sl], bufs["acc_w1t"][:, sl],
                                     psum[:, 0:256], ALU.add))

        @p.blk.tensor
        def _(t):
            t.wait_ge(p.vv.sem, p.vv.n)
            for t_ in range(nt_t):
                m = t.matmul(psum[:, 0:256],
                             r_tm[:, t_ * 1024 + hc * 128: t_ * 1024 + hc * 128 + 128],
                             h_tm[:, t_ * 256:(t_ + 1) * 256],
                             start=(t_ == 0), stop=(t_ == nt_t - 1))
            p.mm.inc(m)

        @p.blk.vector
        def _(v):
            v.wait_ge(p.mm.sem, p.mm.n)
            sl = slice(hc * 256, (hc + 1) * 256)
            p.vv.inc(v.tensor_tensor(bufs["acc_w2"][:, sl], bufs["acc_w2"][:, sl],
                                     psum[:, 0:256], ALU.add))

        @p.blk.tensor
        def _(t):
            t.wait_ge(p.vv.sem, p.vv.n)

    first = bufs["db_first"]

    @p.blk.tensor
    def _(t):
        for t_ in range(nt_t):
            st0 = first and t_ == 0
            sp = bool(bufs.get("db_stop")) and t_ == nt_t - 1
            t.matmul(bufs["psum_b1"][0:1, 0:256], p.c_onesb[:, 0:1],
                     da_tm[:, t_ * 256:(t_ + 1) * 256],
                     start=st0, stop=sp)
            t.matmul(bufs["psum_b2"][0:1, 0:512], p.c_onesb[:, 0:1],
                     r_tm[:, t_ * 1024:t_ * 1024 + 512],
                     start=st0, stop=sp)
            m = t.matmul(bufs["psum_b2"][0:1, 512:1024], p.c_onesb[:, 0:1],
                         r_tm[:, t_ * 1024 + 512:t_ * 1024 + 1024],
                         start=st0, stop=sp)
        p.mm.inc(m)
    bufs["db_first"] = False
    p.bar()



def build_launch_A(nt=NT):
    p = P()
    nch = nt // TC
    x_tm = p.inp("x_tm", [nt, DM])
    tk_in = p.inp("tk", [DM, DM]); tv_in = p.inp("tv", [DM, DM])
    tq_in = p.inp("tq", [DM, DM])
    w1_in = p.inp("w1", [256, DM]); w2_in = p.inp("w2", [DM, 256])
    b1_in = p.inp("b1", [1, 256]); b2_in = p.inp("b2", [1, DM])
    identf_in = p.inp("identf", [128, 128])
    identb_in = p.inp("identb", [128, 128], BF16)
    xfm_o = p.outp("xfm", [DM, nt])
    qfm_o = p.outp("qfm", [DM, nt])
    zbf_o = p.outp("zbf", [DM, nt], BF16)
    ztm_o = p.outp("ztm", [nt, DM], BF16)
    zmy_o = p.outp("zmyo", [DM, nt])
    gw1t_o = p.outp("gw1t", [128, DC * 256])
    gw2_o = p.outp("gw2", [128, DC * 256])
    gb1_o = p.outp("gb1", [1, 256])
    gb2_o = p.outp("gb2", [1, DM])

    p.begin()
    identf = p.sb("sb_identf", [128, 128], F32)
    identb = p.sb("sb_identb", [128, 128], BF16)

    @p.blk.sync
    def _(sy):
        p.dh.inc(sy.dma_start(out=identf[:], in_=identf_in), 16)
        p.dh.inc(sy.dma_start(out=identb[:], in_=identb_in), 16)

    pr = ttt_prep_params(p, w1_in, w2_in, b1_in, b2_in, identf, masters=False)

    tkT = p.sb("tkT", [128, DC * DM], BF16)
    tvT = p.sb("tvT", [128, DC * DM], BF16)
    tqT = p.sb("tqT", [128, DC * DM], BF16)
    psA = p.psA_b
    psB = p.psB_b
    psAf = p.psA_f
    psBf = p.psB_f
    with p.nc.sbuf_tensor("traw", [128, DC * DM], BF16) as traw:
        for t_in, tT in ((tk_in, tkT), (tv_in, tvT), (tq_in, tqT)):
            @p.blk.gpsimd
            def _(g):
                for c in range(DC):
                    p.dg.inc(g.dma_start(out=traw[:, c * DM:(c + 1) * DM],
                                         in_=t_in[c * 128:(c + 1) * 128, :]), 16)
            p.bar()
            pairs = []
            for hc in range(DC):
                for dc in range(DC):
                    pairs.append(
                        (traw[:, hc * DM + dc * 128: hc * DM + dc * 128 + 128],
                         tT[:, dc * DM + hc * 128: dc * DM + hc * 128 + 128]))
            transpose_tiles(p, pairs, identb, psA, psB)
            p.bar()

    acc_w1t = p.sb("acc_w1t", [128, DC * 256], F32)
    acc_w2 = p.sb("acc_w2", [128, DC * 256], F32)
    psum_b1 = p.pt("psum_b1", [1, 256])
    psum_b2 = p.pt("psum_b2", [1, 1024])

    @p.blk.vector
    def _(v):
        p.vv.inc(v.memset(acc_w1t[:], 0.0))
        p.vv.inc(v.memset(acc_w2[:], 0.0))
    p.bar()

    n = TC
    nt_t = n // 128
    xtmc = p.sb("xtmc", [128, nt_t * DM], F32)
    sqs = p.sb("sqs", [128, DM], F32)
    scr = {k: p.sb("lns_" + k, [128, 1], F32) for k in ("s", "q", "mu", "rstd")}
    xntm = p.sb("xntm", [128, nt_t * DM], BF16)
    xnfm = p.sb("xnfm", [128, DC * n], BF16)
    z_bf = p.sb("z_bf", [128, DC * n], BF16)
    z_tm = p.sb("z_tm", [128, nt_t * DM], BF16)
    zmy = p.sb("zmy", [128, DC * n], F32)
    zfr = p.sb("zfr", [128, n], F32)
    qfr = p.sb("qfr", [128, n], F32)
    apb = p.sb("apb", [128, 2 * n], F32)
    h_b = p.sb("h_b", [128, 2 * n], BF16)
    gp = p.sb("gp", [128, 2 * n], F32)
    gp2 = p.sb("gp2", [128, 2 * n], F32)
    rb = p.sb("rb", [128, DC * n], BF16)
    r_tm = p.sb("r_tm", [128, nt_t * DM], BF16)
    h_tm = p.sb("h_tm", [128, nt_t * 256], BF16)
    da_b = p.sb("da_b", [128, 2 * n], BF16)
    da_tm = p.sb("da_tm", [128, nt_t * 256], BF16)
    psum = p.ps_main
    psum2 = p.ps_main

    bufs = dict(z_bf=z_bf, z_tm=z_tm, zmy=zmy, apb=apb, h_b=h_b, gp=gp, gp2=gp2,
                rb=rb, r_tm=r_tm, h_tm=h_tm, da_b=da_b, da_tm=da_tm,
                psum=psum, psA=psA, psB=psB, identb=identb,
                acc_w1t=acc_w1t, acc_w2=acc_w2, psum_b1=psum_b1,
                psum_b2=psum_b2, db_first=True, db_stop=False)

    for ch in range(nch):
        if ch == nch - 1:
            bufs["db_stop"] = True
        t0 = ch * n

        @p.blk.gpsimd
        def _(g):
            for t_ in range(nt_t):
                p.dg.inc(g.dma_start(out=xtmc[:, t_ * DM:(t_ + 1) * DM],
                                     in_=x_tm[t0 + t_ * 128: t0 + (t_ + 1) * 128, :]), 16)
        p.bar()
        for t_ in range(nt_t):
            ln_tm_tile(p, xtmc[:, t_ * DM:(t_ + 1) * DM],
                       xntm[:, t_ * DM:(t_ + 1) * DM], sqs[:, :], scr)
            p.bar()
        pairs = []
        for t_ in range(nt_t):
            for dc in range(DC):
                pairs.append((xntm[:, t_ * DM + dc * 128: t_ * DM + dc * 128 + 128],
                              xnfm[:, dc * n + t_ * 128: dc * n + t_ * 128 + 128]))
        transpose_tiles(p, pairs, identb, psA, psB)
        xf_stage = zmy
        pairs = []
        for t_ in range(nt_t):
            for dc in range(DC):
                pairs.append((xtmc[:, t_ * DM + dc * 128: t_ * DM + dc * 128 + 128],
                              xf_stage[:, dc * n + t_ * 128: dc * n + t_ * 128 + 128]))
        transpose_tiles(p, pairs, identf, psAf, psBf)
        p.bar()

        @p.blk.sync
        def _(sy):
            for dc in range(DC):
                p.dh.inc(sy.dma_start(out=xfm_o[dc * 128:(dc + 1) * 128, t0:t0 + n],
                                      in_=xf_stage[:, dc * n:(dc + 1) * n]), 16)
        p.bar()

        for (tT, mode) in ((tkT, "z"), (tvT, "y"), (tqT, "q")):
            for hc in range(DC):
                @p.blk.tensor
                def _(t):
                    for dc in range(DC):
                        m = t.matmul(psum[:, 0:n],
                                     tT[:, dc * DM + hc * 128: dc * DM + hc * 128 + 128],
                                     xnfm[:, dc * n:(dc + 1) * n],
                                     start=(dc == 0), stop=(dc == DC - 1))
                    p.mm.inc(m)

                if mode == "z":
                    @p.blk.vector
                    def _(v):
                        v.wait_ge(p.mm.sem, p.mm.n)
                        p.vv.inc(v.tensor_copy(z_bf[:, hc * n:(hc + 1) * n],
                                               psum[:, 0:n]))

                    @p.blk.scalar
                    def _(s):
                        s.wait_ge(p.mm.sem, p.mm.n)
                        p.ss.inc(s.activation(zfr[:, 0:n], psum[:, 0:n], AF.Identity))

                    @p.blk.tensor
                    def _(t):
                        t.wait_ge(p.vv.sem, p.vv.n)
                        t.wait_ge(p.ss.sem, p.ss.n)
                        for dc in range(DC):
                            m = t.matmul(psum2[:, 0:n],
                                         tvT[:, dc * DM + hc * 128: dc * DM + hc * 128 + 128],
                                         xnfm[:, dc * n:(dc + 1) * n],
                                         start=(dc == 0), stop=(dc == DC - 1))
                        p.mm.inc(m)

                    @p.blk.vector
                    def _(v):
                        v.wait_ge(p.mm.sem, p.mm.n)
                        p.vv.inc(v.tensor_tensor(zmy[:, hc * n:(hc + 1) * n],
                                                 zfr[:, 0:n], psum2[:, 0:n],
                                                 ALU.subtract))

                    @p.blk.tensor
                    def _(t):
                        t.wait_ge(p.vv.sem, p.vv.n)
                elif mode == "q":
                    @p.blk.vector
                    def _(v):
                        v.wait_ge(p.mm.sem, p.mm.n)
                        p.vv.inc(v.tensor_copy(qfr[:, 0:n], psum[:, 0:n]))

                    @p.blk.sync
                    def _(sy):
                        sy.wait_ge(p.vv.sem, p.vv.n)
                        p.dh.inc(sy.dma_start(
                            out=qfm_o[hc * 128:(hc + 1) * 128, t0:t0 + n],
                            in_=qfr[:, 0:n]), 16)

                    @p.blk.tensor
                    def _(t):
                        t.wait_ge(p.dh.sem, p.dh.n)
        p.bar()

        for t_ in range(nt_t):
            for half in range(2):
                @p.blk.tensor
                def _(t):
                    for dc in range(DC):
                        m = t.matmul(psum[:, 0:512],
                                     xnfm[:, dc * n + t_ * 128: dc * n + t_ * 128 + 128],
                                     tkT[:, dc * DM + half * 512: dc * DM + (half + 1) * 512],
                                     start=(dc == 0), stop=(dc == DC - 1))
                    p.mm.inc(m)

                @p.blk.vector
                def _(v):
                    v.wait_ge(p.mm.sem, p.mm.n)
                    p.vv.inc(v.tensor_copy(
                        z_tm[:, t_ * DM + half * 512: t_ * DM + (half + 1) * 512],
                        psum[:, 0:512]))

                @p.blk.tensor
                def _(t):
                    t.wait_ge(p.vv.sem, p.vv.n)
        p.bar()

        @p.blk.gpsimd
        def _(g):
            for hc in range(DC):
                p.dg.inc(g.dma_start(out=zbf_o[hc * 128:(hc + 1) * 128, t0:t0 + n],
                                     in_=z_bf[:, hc * n:(hc + 1) * n]), 16)
                p.dg.inc(g.dma_start(out=zmy_o[hc * 128:(hc + 1) * 128, t0:t0 + n],
                                     in_=zmy[:, hc * n:(hc + 1) * n]), 16)
            for t_ in range(nt_t):
                p.dg.inc(g.dma_start(out=ztm_o[t0 + t_ * 128: t0 + (t_ + 1) * 128, :],
                                     in_=z_tm[:, t_ * DM:(t_ + 1) * DM]), 16)

        ttt_grad_chunk(p, pr, bufs, n)

    grow = p.sb("grow", [1, 1280], F32)

    @p.blk.vector
    def _(v):
        p.vv.inc(v.tensor_scalar_mul(acc_w1t[:], acc_w1t[:], CGRAD))
        p.vv.inc(v.tensor_scalar_mul(acc_w2[:], acc_w2[:], CGRAD))
        p.vv.inc(v.tensor_scalar_mul(grow[0:1, 1024:1280], psum_b1[0:1, :], CGRAD))
        p.vv.inc(v.tensor_scalar_mul(grow[0:1, 0:1024], psum_b2[0:1, :], CGRAD))
    p.bar()

    @p.blk.sync
    def _(sy):
        p.st.inc(sy.dma_start(out=gw1t_o, in_=acc_w1t[:]), 16)
        p.st.inc(sy.dma_start(out=gw2_o, in_=acc_w2[:]), 16)
        p.st.inc(sy.dma_start(out=gb1_o, in_=grow[0:1, 1024:1280]), 16)
        p.st.inc(sy.dma_start(out=gb2_o, in_=grow[0:1, 0:1024]), 16)
        p.st.wait(sy)

    @p.blk.gpsimd
    def _(g):
        p.dg.wait(g)

    @p.blk.sync
    def _(sy):
        p.dh.wait(sy)
    return p.done()



def build_launch_BC(which, nt=NT):
    p = P()
    nch = nt // TC
    xfm_i = p.inp("xfm", [DM, nt])
    qfm_i = p.inp("qfm", [DM, nt])
    zbf_i = p.inp("zbf", [DM, nt], BF16)
    ztm_i = p.inp("ztm", [nt, DM], BF16)
    zmy_i = p.inp("zmyo", [DM, nt])
    w1_in = p.inp("w1", [256, DM]); w2_in = p.inp("w2", [DM, 256])
    b1_in = p.inp("b1", [1, 256]); b2_in = p.inp("b2", [1, DM])
    identf_in = p.inp("identf", [128, 128])
    identb_in = p.inp("identb", [128, 128], BF16)
    gsets = []
    rounds = 1 if which == "B" else 2
    for r_ in range(rounds):
        for k in range(4):
            gsets.append({
                "w1t": p.inp(f"g{r_}w1t_{k}", [128, DC * 256]),
                "w2": p.inp(f"g{r_}w2_{k}", [128, DC * 256]),
                "b1": p.inp(f"g{r_}b1_{k}", [1, 256]),
                "b2": p.inp(f"g{r_}b2_{k}", [1, DM]),
            })
    xnew_o = p.outp("xnew", [DM, nt])
    if which == "B":
        gw1t_o = p.outp("gw1t", [128, DC * 256])
        gw2_o = p.outp("gw2", [128, DC * 256])
        gb1_o = p.outp("gb1", [1, 256])
        gb2_o = p.outp("gb2", [1, DM])

    p.begin()
    identf = p.sb("sb_identf", [128, 128], F32)
    identb = p.sb("sb_identb", [128, 128], BF16)

    @p.blk.sync
    def _(sy):
        p.dh.inc(sy.dma_start(out=identf[:], in_=identf_in), 16)
        p.dh.inc(sy.dma_start(out=identb[:], in_=identb_in), 16)

    pr = ttt_prep_params(p, w1_in, w2_in, b1_in, b2_in, identf, masters=True)
    pr["identf"] = identf
    pr["psA"] = p.psA_f
    pr["psB"] = p.psB_f
    pr["pcol"] = p.psA_f

    _gldsc = ExitStack()
    gsum = {
        "gw1t": _gldsc.enter_context(p.nc.sbuf_tensor("gsum_w1t", [128, DC * 256], F32)),
        "gw2": _gldsc.enter_context(p.nc.sbuf_tensor("gsum_w2", [128, DC * 256], F32)),
        "gb1": _gldsc.enter_context(p.nc.sbuf_tensor("gsum_b1", [1, 256], F32)),
        "gb2": _gldsc.enter_context(p.nc.sbuf_tensor("gsum_b2", [1, DM], F32)),
    }
    gld = {
        "w1t": _gldsc.enter_context(p.nc.sbuf_tensor("gld_w1t", [128, DC * 256], F32)),
        "w2": _gldsc.enter_context(p.nc.sbuf_tensor("gld_w2", [128, DC * 256], F32)),
        "b1": _gldsc.enter_context(p.nc.sbuf_tensor("gld_b1", [1, 256], F32)),
        "b2": _gldsc.enter_context(p.nc.sbuf_tensor("gld_b2", [1, DM], F32)),
    }
    for r_ in range(rounds):
        for k in range(4):
            gs = gsets[r_ * 4 + k]

            @p.blk.sync
            def _(sy):
                p.dh.inc(sy.dma_start(out=gld["w1t"][:], in_=gs["w1t"]), 16)
                p.dh.inc(sy.dma_start(out=gld["w2"][:], in_=gs["w2"]), 16)
                p.dh.inc(sy.dma_start(out=gld["b1"][:], in_=gs["b1"]), 16)
                p.dh.inc(sy.dma_start(out=gld["b2"][:], in_=gs["b2"]), 16)
            p.bar()

            @p.blk.vector
            def _(v):
                for key, big in (("gw1t", "w1t"), ("gw2", "w2"),
                                 ("gb1", "b1"), ("gb2", "b2")):
                    rows = gsum[key].shape[0]
                    w = gsum[key].shape[1]
                    if k == 0:
                        p.vv.inc(v.tensor_copy(gsum[key][:], gld[big][0:rows, 0:w]))
                    else:
                        p.vv.inc(v.tensor_tensor(gsum[key][:], gsum[key][:],
                                                 gld[big][0:rows, 0:w], ALU.add))
                    v.wait_ge(p.vv.sem, p.vv.n)
            p.bar()
        ttt_update_params(p, pr, gsum)
    _gldsc.close()

    n = TC
    nt_t = n // 128
    xfmc = p.sb("xfmc", [128, DC * n], F32)
    qfc = p.sb("qfc", [128, DC * n], F32)
    qbc = p.sb("qbc", [128, DC * n], BF16)
    outr = p.sb("outr", [128, DC * n], F32)
    apb = p.sb("apb", [128, 2 * n], F32)
    h_b = p.sb("h_b", [128, 2 * n], BF16)
    psum = p.ps_main
    lnb_bufs = {
        "sq": p.sb("lnf_sq", [128, 2 * n], F32),
        "tmp": p.sb("lnf_tmp", [128, n], F32),
        "mu": p.sb("lnf_mu", [1, n], F32),
        "rstd": p.sb("lnf_rstd", [1, n], F32),
        "mub": p.sb("lnf_mub", [128, n], F32),
        "rsb": p.sb("lnf_rsb", [128, n], F32),
    }
    if which == "B":
        z_bf = p.sb("z_bf", [128, DC * n], BF16)
        z_tm = p.sb("z_tm", [128, nt_t * DM], BF16)
        zmy = p.sb("zmy", [128, DC * n], F32)
        gp = p.sb("gp", [128, 2 * n], F32)
        gp2 = p.sb("gp2", [128, 2 * n], F32)
        rb = p.sb("rb", [128, DC * n], BF16)
        r_tm = p.sb("r_tm", [128, nt_t * DM], BF16)
        h_tm = p.sb("h_tm", [128, nt_t * 256], BF16)
        da_b = p.sb("da_b", [128, 2 * n], BF16)
        da_tm = p.sb("da_tm", [128, nt_t * 256], BF16)
        acc_w1t = p.sb("acc_w1t", [128, DC * 256], F32)
        acc_w2 = p.sb("acc_w2", [128, DC * 256], F32)
        psum_b1 = p.pt("psum_b1", [1, 256])
        psum_b2 = p.pt("psum_b2", [1, 1024])

        @p.blk.vector
        def _(v):
            p.vv.inc(v.memset(acc_w1t[:], 0.0))
            p.vv.inc(v.memset(acc_w2[:], 0.0))
        p.bar()
        bufs = dict(z_bf=z_bf, z_tm=z_tm, zmy=zmy, apb=apb, h_b=h_b, gp=gp,
                    gp2=gp2, rb=rb, r_tm=r_tm, h_tm=h_tm, da_b=da_b,
                    da_tm=da_tm, psum=psum, psA=p.psA_b, psB=p.psB_b,
                    identb=identb, acc_w1t=acc_w1t, acc_w2=acc_w2,
                    psum_b1=psum_b1, psum_b2=psum_b2,
                    db_first=True, db_stop=False)

    for ch in range(nch):
        t0 = ch * n

        @p.blk.gpsimd
        def _(g):
            for hc in range(DC):
                p.dg.inc(g.dma_start(out=xfmc[:, hc * n:(hc + 1) * n],
                                     in_=xfm_i[hc * 128:(hc + 1) * 128, t0:t0 + n]), 16)
                p.dg.inc(g.dma_start(out=qfc[:, hc * n:(hc + 1) * n],
                                     in_=qfm_i[hc * 128:(hc + 1) * 128, t0:t0 + n]), 16)
        p.bar()

        @p.blk.vector
        def _(v):
            for hc in range(DC):
                p.vv.inc(v.tensor_copy(qbc[:, hc * n:(hc + 1) * n],
                                       qfc[:, hc * n:(hc + 1) * n]))
        p.bar()

        ttt_recon_fwd(p, pr, qbc, lambda hc: hc * n, n, apb, h_b, psum)
        p.bar()
        ttt_recon_out(p, pr, h_b, n, qfc, lambda hc: hc * n,
                      outr, lambda hc: hc * n, psum)
        p.bar()

        ln_fm(p, lambda c: c * n, n, lambda c: c * n, lambda c: c * n,
              lnb_bufs, src_sb=outr, dst_sb=outr, x_sb=xfmc)
        p.bar()

        @p.blk.sync
        def _(sy):
            for hc in range(DC):
                p.dh.inc(sy.dma_start(out=xnew_o[hc * 128:(hc + 1) * 128, t0:t0 + n],
                                      in_=outr[:, hc * n:(hc + 1) * n]), 16)

        if which == "B":
            if ch == nch - 1:
                bufs["db_stop"] = True

            @p.blk.gpsimd
            def _(g):
                for hc in range(DC):
                    p.dg.inc(g.dma_start(out=z_bf[:, hc * n:(hc + 1) * n],
                                         in_=zbf_i[hc * 128:(hc + 1) * 128, t0:t0 + n]), 16)
                    p.dg.inc(g.dma_start(out=zmy[:, hc * n:(hc + 1) * n],
                                         in_=zmy_i[hc * 128:(hc + 1) * 128, t0:t0 + n]), 16)
                for t_ in range(nt_t):
                    p.dg.inc(g.dma_start(out=z_tm[:, t_ * DM:(t_ + 1) * DM],
                                         in_=ztm_i[t0 + t_ * 128:t0 + (t_ + 1) * 128, :]), 16)
            p.bar()
            ttt_grad_chunk(p, pr, bufs, n)
        else:
            p.bar()

    if which == "B":
        grow = p.sb("grow", [1, 1280], F32)

        @p.blk.vector
        def _(v):
            p.vv.inc(v.tensor_scalar_mul(acc_w1t[:], acc_w1t[:], CGRAD))
            p.vv.inc(v.tensor_scalar_mul(acc_w2[:], acc_w2[:], CGRAD))
            p.vv.inc(v.tensor_scalar_mul(grow[0:1, 1024:1280], psum_b1[0:1, :], CGRAD))
            p.vv.inc(v.tensor_scalar_mul(grow[0:1, 0:1024], psum_b2[0:1, :], CGRAD))
        p.bar()

        @p.blk.sync
        def _(sy):
            p.st.inc(sy.dma_start(out=gw1t_o, in_=acc_w1t[:]), 16)
            p.st.inc(sy.dma_start(out=gw2_o, in_=acc_w2[:]), 16)
            p.st.inc(sy.dma_start(out=gb1_o, in_=grow[0:1, 1024:1280]), 16)
            p.st.inc(sy.dma_start(out=gb2_o, in_=grow[0:1, 0:1024]), 16)
            p.st.wait(sy)

    @p.blk.gpsimd
    def _(g):
        p.dg.wait(g)

    @p.blk.sync
    def _(sy):
        p.dh.wait(sy)
    return p.done()



def build_launch_D(nt=NT):
    KR = 512 + nt
    QR = 256 + nt
    NQW = QR // WIN
    XN = HALO + nt
    MR = 1 + nt
    ach = f_chunks(DIN, 128)
    NA = len(ach)
    nkt = KR // 128

    p = P()
    xin = p.inp("xin", [DM, XN])
    wqkv_i = p.inp("wqkv", [DM, 3 * 1024])
    wo_i = p.inp("wo", [1024, DM])
    w1_i = p.inp("w1", [DM, 2 * DIN])
    w2_i = p.inp("w2", [DIN, DM])
    b1_i = p.inp("b1", [1, 2 * DIN])
    b2_i = p.inp("b2", [1, DM])
    lng_i = p.inp("lng", [1, DM])
    lnb_i = p.inp("lnb", [1, DM])
    qsc_i = p.inp("qscol", [128, 1])
    ksc_i = p.inp("kscol", [128, 1])
    rq_c = p.inp("rotqc", [128, KR]); rq_s = p.inp("rotqs", [128, KR])
    rk_c = p.inp("rotkc", [128, KR]); rk_s = p.inp("rotks", [128, KR])
    masks_i = p.inp("masks", [NQW, 4, 128, WIN], BF16)
    hval_i = p.inp("hvalid", [128, 1])
    sel2_i = p.inp("sel2c", [2, 128])
    identb_in = p.inp("identb", [128, 128], BF16)
    xout = p.outp("xout", [DM, nt])
    xmid_hbm = p.nc.dram_tensor("xmid_hbm", [DM, MR], F32).ap()

    p.begin(pool="tiny")
    pnorm = p.pt("pnorm", [2, 512])
    psim = p.pt("psim", [128, 2048])
    poutA = p.pt("poutA", [65, 512])
    poutB = p.pt("poutB", [65, 512])

    identb = p.sb("sb_identb", [128, 128], BF16)
    qsc = p.sb("qsc", [128, 1], F32)
    ksc = p.sb("ksc", [128, 1], F32)
    hval = p.sb("hval", [128, 1], F32)
    lngc = p.sb("lngc", [128, DC], F32)
    lnbc = p.sb("lnbc", [128, DC], F32)
    b2c = p.sb("b2c", [128, DC], F32)
    b1ca = p.sb("b1ca", [128, NA], F32)
    b1cg = p.sb("b1cg", [128, NA], F32)
    hselb = p.sb("hselb", [128, 2], BF16)
    sel2 = p.sb("sel2", [2, 128], F32)
    onesq = p.sb("onesq", [1, 512], F32)
    ones256b = p.sb("ones256b", [1, 256], BF16)
    p0s = ExitStack()
    b1r = p0s.enter_context(p.nc.sbuf_tensor("b1r", [1, 2 * DIN], F32))

    @p.blk.gpsimd
    def _(g):
        p.dg.inc(g.dma_start(out=qsc[:], in_=qsc_i), 16)
        p.dg.inc(g.dma_start(out=ksc[:], in_=ksc_i), 16)

        p.dg.inc(g.dma_start(out=hval[:], in_=hval_i), 16)
        p.dg.inc(g.dma_start(out=b1r[:], in_=b1_i), 16)
        p.dg.inc(g.dma_start(out=identb[:], in_=identb_in), 16)
        p.dg.inc(g.dma_start(out=sel2[:], in_=sel2_i), 16)
        p.pg.inc(g.memset(hselb[:], 0.0))
        p.pg.inc(g.memset(onesq[:], 1.0))
        p.pg.inc(g.memset(ones256b[:], 1.0))
        g.wait_ge(p.pg.sem, p.pg.n)
        p.pg.inc(g.memset(hselb[0:64, 0:1], 1.0))
        p.pg.inc(g.memset(hselb[64:128, 1:2], 1.0))
    p.bar()

    @p.blk.vector
    def _(v):
        p.vv.inc(v.tensor_scalar_mul(qsc[:], qsc[:], QKSC))
    p.bar()

    rowbuf = p0s.enter_context(p.nc.sbuf_tensor("rowbuf", [1, 3 * DM], F32))

    @p.blk.sync
    def _(sy):
        p.dh.inc(sy.dma_start(out=rowbuf[0:1, 0:DM], in_=lng_i), 16)
        p.dh.inc(sy.dma_start(out=rowbuf[0:1, DM:2 * DM], in_=lnb_i), 16)
        p.dh.inc(sy.dma_start(out=rowbuf[0:1, 2 * DM:3 * DM], in_=b2_i), 16)
    p.bar()
    row_to_cols(p, rowbuf[0:1, 0:DM], DC, lngc, 0, p.ps_main)
    row_to_cols(p, rowbuf[0:1, DM:2 * DM], DC, lnbc, 0, p.ps_main)
    row_to_cols(p, rowbuf[0:1, 2 * DM:3 * DM], DC, b2c, 0, p.ps_main)
    p.bar()
    for j, (off, size) in enumerate(ach):
        for (roff, dst) in ((off, b1ca), (DIN + off, b1cg)):
            @p.blk.tensor
            def _(t, roff=roff, size=size):
                t.wait_ge(p.vv.sem, p.vv.n)
                p.mm.inc(t.matmul(p.ps_main[0:size, 0:1], b1r[0:1, roff:roff + size],
                                  p.c_onesf[0:1, 0:1], start=True, stop=True))

            @p.blk.vector
            def _(v, dst=dst, size=size, j=j):
                v.wait_ge(p.mm.sem, p.mm.n)
                p.vv.inc(v.tensor_copy(dst[0:size, j:j + 1], p.ps_main[0:size, 0:1]))
    p.bar()
    p0s.close()

    _s0 = p.nc.enter_named_scope("P1", False)[0]
    pqk = ExitStack()

    def qsb(name, shape, dtype):
        return pqk.enter_context(p.nc.sbuf_tensor(name, shape, dtype))

    q2 = qsb("q2", [128, DC * QR], BF16)
    k2 = qsb("k2", [128, DC * KR], BF16)
    v_km = qsb("v_km", [128, nkt * 1040], BF16)
    qnr = qsb("qnr", [2, DC * NQW], F32)
    knr = qsb("knr", [2, DC * (NQW + 1)], F32)
    mnegb = qsb("mnegb", [1, HEADS * WIN], BF16)
    mvals1 = qsb("mvals1", [1, HEADS * NQW], F32)
    mrow = qsb("mrow", [2, NQW], F32)

    p1s = ExitStack()
    xb = p1s.enter_context(p.nc.sbuf_tensor("xb", [128, DC * XN], BF16))

    @p.blk.gpsimd
    def _(g):
        for c in range(DC):
            p.dg.inc(g.dma_start(out=xb[:, c * XN:(c + 1) * XN],
                                 in_=xin[c * 128:(c + 1) * 128, :]), 16)
    p.bar()

    def xs_ap(c, i0, ncols):
        off = (1 if c < 4 else 0) + i0
        return xb[:, c * XN + off: c * XN + off + ncols]

    p1b = ExitStack()

    def usb(name, shape, dtype):
        return p1b.enter_context(p.nc.sbuf_tensor(name, shape, dtype))

    rt_c = usb("rt_c", [128, KR], F32)
    rt_s = usb("rt_s", [128, KR], F32)
    permb = usb("permb", [128, 128], BF16)
    qraw = usb("qraw", [128, 512], F32)
    sqs = usb("sqs", [128, 512], BF16)
    u_f = usb("u_f", [128, 512], F32)
    swp = usb("swp", [128, 512], BF16)
    nrm = usb("nrm", [2, 512], F32)
    wq1 = usb("wq1", [128, DC * 128], BF16)

    @p.blk.vector
    def _(v):
        p.pg.wait(v)
        p.dg.wait(v)
        p.vv.inc(v.tensor_copy(permb[0:32, :], identb[32:64, :]))
        p.vv.inc(v.tensor_copy(permb[32:64, :], identb[0:32, :]))
        p.vv.inc(v.tensor_copy(permb[64:96, :], identb[96:128, :]))
        p.vv.inc(v.tensor_copy(permb[96:128, :], identb[64:96, :]))
    p.bar()

    for side in ("q", "k"):
        NS = QR if side == "q" else KR
        qoff = 256 if side == "q" else 0
        wcol0 = 0 if side == "q" else 1024
        dst = q2 if side == "q" else k2
        nr_dst = qnr if side == "q" else knr
        nwin = NQW if side == "q" else NQW + 1
        scol = qsc if side == "q" else ksc
        rc_i = rq_c if side == "q" else rk_c
        rs_i = rq_s if side == "q" else rk_s

        @p.blk.sync
        def _(sy, rc_i=rc_i, rs_i=rs_i):
            sy.wait_ge(p.vv.sem, p.vv.n)
            sy.wait_ge(p.mm.sem, p.mm.n)
            p.dh.inc(sy.dma_start(out=rt_c[:], in_=rc_i), 16)
            p.dh.inc(sy.dma_start(out=rt_s[:], in_=rs_i), 16)
        p.bar()
        for (o, nn) in f_chunks(KR, 512):
            @p.blk.vector
            def _(v, o=o, nn=nn):
                v.wait_ge(p.vv.sem, p.vv.n)
                v.wait_ge(p.mm.sem, p.mm.n)
                p.vv.inc(v.tensor_copy(p.ps_main[0:128, 0:nn], rt_s[:, o:o + nn]))
                v.wait_ge(p.vv.sem, p.vv.n)
                for base in (0, 64):
                    p.vv.inc(v.tensor_scalar_mul(
                        rt_s[base:base + 32, o:o + nn],
                        p.ps_main[base + 32:base + 64, 0:nn], 1.0))
                    p.vv.inc(v.tensor_scalar_mul(
                        rt_s[base + 32:base + 64, o:o + nn],
                        p.ps_main[base:base + 32, 0:nn], -1.0))
                v.wait_ge(p.vv.sem, p.vv.n)
        p.bar()

        for fc in range(DC):
            @p.blk.gpsimd
            def _(g, fc=fc):
                g.wait_ge(p.mm.sem, p.mm.n)
                p.dg.inc(g.dma_start(
                    out=wq1[:].rearrange("p (c f) -> p c f", c=DC),
                    in_=wqkv_i[:, wcol0 + fc * 128: wcol0 + (fc + 1) * 128]
                    .rearrange("(c p) f -> p c f", p=128)), 16)
            dgw = p.dg.n

            for ci, (o, nn) in enumerate(f_chunks(NS, 512)):
                @p.blk.tensor
                def _(t, o=o, nn=nn):
                    t.wait_ge(p.dg.sem, dgw)
                    t.wait_ge(p.vv.sem, p.vv.n)
                    t.wait_ge(p.ss.sem, p.ss.n)
                    for c in range(DC):
                        m = t.matmul(p.ps_main[:, 0:nn],
                                     wq1[:, c * 128:(c + 1) * 128],
                                     xs_ap(c, qoff + o, nn),
                                     start=(c == 0), stop=(c == DC - 1))
                    p.mm.inc(m)

                @p.blk.vector
                def _(v, nn=nn):
                    v.wait_ge(p.mm.sem, p.mm.n)
                    p.vv.inc(v.tensor_copy(qraw[:, 0:nn], p.ps_main[:, 0:nn]))

                @p.blk.scalar
                def _(s, nn=nn):
                    s.wait_ge(p.vv.sem, p.vv.n)
                    p.ss.inc(s.activation(sqs[:, 0:nn], qraw[:, 0:nn], AF.Square))

                @p.blk.tensor
                def _(t, nn=nn):
                    t.wait_ge(p.ss.sem, p.ss.n)
                    p.mm.inc(t.matmul(pnorm[0:2, 0:nn], hselb[:, 0:2], sqs[:, 0:nn],
                                      start=True, stop=True))

                @p.blk.vector
                def _(v, nn=nn):
                    v.wait_ge(p.mm.sem, p.mm.n)
                    p.vv.inc(v.tensor_scalar_max(nrm[:, 0:nn], pnorm[0:2, 0:nn],
                                                 1e-24))

                @p.blk.scalar
                def _(s, nn=nn):
                    s.wait_ge(p.vv.sem, p.vv.n)
                    p.ss.inc(s.activation(nrm[:, 0:nn], nrm[:, 0:nn], AF.Sqrt))

                @p.blk.vector
                def _(v, nn=nn):
                    v.wait_ge(p.ss.sem, p.ss.n)
                    p.vv.inc(v.tensor_scalar_max(nrm[:, 0:nn], nrm[:, 0:nn], 1e-12))
                    v.wait_ge(p.vv.sem, p.vv.n)
                    p.vv.inc(v.reciprocal(nrm[:, 0:nn], nrm[:, 0:nn]))
                vrec = p.vv.n

                @p.blk.tensor
                def _(t, nn=nn):
                    t.wait_ge(p.vv.sem, vrec)
                    p.mm.inc(t.matmul(p.ps_main[:, 0:nn], sel2[0:2, :],
                                      nrm[0:2, 0:nn], start=True, stop=True))
                mmrb = p.mm.n

                @p.blk.vector
                def _(v, nn=nn):
                    v.wait_ge(p.mm.sem, mmrb)
                    p.vv.inc(v.scalar_tensor_tensor(u_f[:, 0:nn], qraw[:, 0:nn],
                                                    scol[:, 0:1], p.ps_main[:, 0:nn],
                                                    ALU.mult, ALU.mult))
                @p.blk.scalar
                def _(s, nn=nn):
                    s.wait_ge(p.vv.sem, p.vv.n)
                    p.ss.inc(s.activation(sqs[:, 0:nn], u_f[:, 0:nn], AF.Square))

                @p.blk.tensor
                def _(t, nn=nn):
                    t.wait_ge(p.ss.sem, p.ss.n)
                    p.mm.inc(t.matmul(pnorm[0:2, 0:nn], hselb[:, 0:2], sqs[:, 0:nn],
                                      start=True, stop=True))
                mmn2 = p.mm.n

                @p.blk.vector
                def _(v, o=o, nn=nn):
                    v.wait_ge(p.mm.sem, mmn2)
                    w0 = o // WIN
                    for wl in range(nn // WIN):
                        p.vv.inc(v.tensor_reduce(
                            nr_dst[:, (fc * nwin + w0 + wl):(fc * nwin + w0 + wl) + 1],
                            pnorm[0:2, wl * WIN:(wl + 1) * WIN], AX.X, ALU.max))

                @p.blk.vector
                def _(v, o=o, nn=nn):
                    v.wait_ge(p.vv.sem, p.vv.n)
                    p.vv.inc(v.tensor_mul(swp[:, 0:nn], u_f[:, 0:nn],
                                          rt_s[:, qoff + o:qoff + o + nn]))
                    p.vv.inc(v.tensor_mul(qraw[:, 0:nn], u_f[:, 0:nn],
                                          rt_c[:, qoff + o:qoff + o + nn]))
                vrot = p.vv.n

                @p.blk.tensor
                def _(t, nn=nn):
                    t.wait_ge(p.vv.sem, vrot)
                    p.mm.inc(t.matmul(p.ps_main[:, 0:nn], permb[:, :],
                                      swp[:, 0:nn], start=True, stop=True))
                mmsw = p.mm.n

                @p.blk.vector
                def _(v, o=o, nn=nn, dst=dst):
                    v.wait_ge(p.mm.sem, mmsw)
                    p.vv.inc(v.tensor_tensor(dst[:, fc * NS + o: fc * NS + o + nn],
                                             p.ps_main[:, 0:nn], qraw[:, 0:nn],
                                             ALU.add))
                    v.wait_ge(p.vv.sem, p.vv.n)
        p.bar()

    @p.blk.gpsimd
    def _(g):
        g.wait_ge(p.vv.sem, p.vv.n)
        p.pg.inc(g.memset(
            v_km[:].rearrange("p (t u) -> p t u", u=65)[:, :, 64:65], 1.0))
    p.bar()
    for vc in range(8):
        @p.blk.gpsimd
        def _(g, vc=vc):
            g.wait_ge(p.mm.sem, p.mm.n)
            p.dg.inc(g.dma_start(
                out=wq1[:].rearrange("p (c f) -> p c f", c=DC),
                in_=wqkv_i[:, 2048 + vc * 128: 2048 + (vc + 1) * 128]
                .rearrange("(c p) f -> p c f", p=128)), 16)
        dgv = p.dg.n

        for kt in range(nkt):
            @p.blk.tensor
            def _(t, vc=vc, kt=kt):
                t.wait_ge(p.dg.sem, dgv)
                t.wait_ge(p.vv.sem, p.vv.n)
                for c in range(DC):
                    m = t.matmul(p.ps_main[:, 0:128],
                                 xs_ap(c, kt * 128, 128),
                                 wq1[:, c * 128:(c + 1) * 128],
                                 start=(c == 0), stop=(c == DC - 1))
                p.mm.inc(m)

            @p.blk.vector
            def _(v, vc=vc, kt=kt):
                v.wait_ge(p.mm.sem, p.mm.n)
                dst = v_km[:].rearrange("p (t u) -> p t u", u=65)[
                    :, kt * 16 + 2 * vc: kt * 16 + 2 * vc + 2, 0:64]
                p.vv.inc(v.tensor_copy(
                    dst, p.ps_main[:, 0:128].rearrange("p (a b) -> p a b", a=2)))
    p.bar()
    p1b.close()
    p1s.close()
    att = pqk.enter_context(p.nc.sbuf_tensor("att", [128, DC * QR], BF16))

    p.nc.leave_named_scope("P1", _s0, False)
    _s1 = p.nc.enter_named_scope("P2", False)[0]
    p2s = ExitStack()
    masks = p2s.enter_context(p.nc.sbuf_tensor("masks_sb", [128, NQW * 4 * WIN], BF16))
    p_sb = p2s.enter_context(p.nc.sbuf_tensor("p_sb", [128, 2048], BF16))
    rcro = p2s.enter_context(p.nc.sbuf_tensor("rcro", [1, 256], F32))
    rbro = p2s.enter_context(p.nc.sbuf_tensor("rbro", [64, 256], F32))

    @p.blk.sync
    def _(sy):
        p.dh.inc(sy.dma_start(
            out=masks[:].rearrange("p (w k f) -> p w k f", w=NQW, k=4),
            in_=masks_i.rearrange("w k p f -> p w k f")), 16)

    @p.blk.scalar
    def _(s):
        s.wait_ge(p.vv.sem, p.vv.n)
        s.wait_ge(p.mm.sem, p.mm.n)
        p.ss.inc(s.activation(qnr[:], qnr[:], AF.Sqrt))
        p.ss.inc(s.activation(knr[:], knr[:], AF.Sqrt))
    p.bar()
    for fc in range(DC):
        @p.blk.vector
        def _(v, fc=fc):
            v.wait_ge(p.vv.sem, p.vv.n)
            p.vv.inc(v.tensor_tensor(mrow[:, 0:NQW],
                                     knr[:, fc * (NQW + 1): fc * (NQW + 1) + NQW],
                                     knr[:, fc * (NQW + 1) + 1: fc * (NQW + 1) + NQW + 1],
                                     ALU.max))
            v.wait_ge(p.vv.sem, p.vv.n)
            p.vv.inc(v.tensor_tensor(mrow[:, 0:NQW], mrow[:, 0:NQW],
                                     qnr[:, fc * NQW:(fc + 1) * NQW], ALU.mult))
            v.wait_ge(p.vv.sem, p.vv.n)
            p.vv.inc(v.tensor_scalar(mrow[:, 0:NQW], mrow[:, 0:NQW],
                                     -1.0, 30.0, ALU.mult, ALU.add))
        vmr = p.vv.n

        for hh in range(2):
            h = fc * 2 + hh
            @p.blk.tensor
            def _(t, hh=hh):
                t.wait_ge(p.vv.sem, p.vv.n)
                p.mm.inc(t.matmul(pnorm[0:1, 0:NQW], sel2[0:2, hh * 64:hh * 64 + 1],
                                  mrow[0:2, 0:NQW], start=True, stop=True))
            mmx = p.mm.n

            @p.blk.vector
            def _(v, h=h):
                v.wait_ge(p.mm.sem, mmx)
                p.vv.inc(v.tensor_copy(mvals1[0:1, h * NQW:(h + 1) * NQW],
                                       pnorm[0:1, 0:NQW]))
    p.bar()

    mcolall = p2s.enter_context(p.nc.sbuf_tensor("mcolall", [128, HEADS * NQW], F32))

    @p.blk.tensor
    def _(t):
        t.wait_ge(p.vv.sem, p.vv.n)
        p.mm.inc(t.matmul(p.ps_main[:, 0:HEADS * NQW], p.c_onesr_f[0:1, :],
                          mvals1[0:1, :], start=True, stop=True))

    @p.blk.vector
    def _(v):
        v.wait_ge(p.mm.sem, p.mm.n)
        p.vv.inc(v.tensor_copy(mcolall[:], p.ps_main[:, 0:HEADS * NQW]))
    p.bar()

    expm = {}
    pvm = {}
    mskm = {}
    evacm = {}
    pouts = (poutA, poutB)
    NIT = NQW * HEADS

    def emit_pv(i):
        w_, h_ = i // HEADS, i % HEADS
        po = pouts[i % 2]
        pp_sl = p_sb[:, (i % 2) * 1024:(i % 2 + 1) * 1024]

        @p.blk.tensor
        def _(t, i=i, w_=w_, h_=h_, po=po, pp_sl=pp_sl):
            t.wait_ge(p.ss.sem, expm[i])
            if i - 2 in evacm:
                t.wait_ge(p.vv.sem, evacm[i - 2])
            for kc in range(4):
                kt = 2 * w_ + kc
                m = t.matmul(po[:, 0:256],
                             v_km[:, kt * 1040 + h_ * 65: kt * 1040 + h_ * 65 + 65],
                             pp_sl[:, kc * 256:(kc + 1) * 256],
                             start=(kc == 0), stop=(kc == 3))
            pvm[i] = p.mm.inc(m)

    def emit_evac(i):
        w_, h_ = i // HEADS, i % HEADS
        fc_, pb_ = h_ // 2, (h_ % 2) * 64
        po = pouts[i % 2]

        @p.blk.vector
        def _(v, i=i, po=po):
            v.wait_ge(p.mm.sem, pvm[i])
            v.wait_ge(p.vv.sem, p.vv.n)
            p.vv.inc(v.reciprocal(rcro[0:1, :], po[64:65, 0:256]))
        vrc = p.vv.n

        @p.blk.tensor
        def _(t, vrc=vrc):
            t.wait_ge(p.vv.sem, p.vv.n)
            p.mm.inc(t.matmul(p.ps_main[0:64, 0:256], onesq[0:1, 0:64],
                              rcro[0:1, :], start=True, stop=True))
        mmbc = p.mm.n

        @p.blk.vector
        def _(v, i=i, w_=w_, fc_=fc_, pb_=pb_, po=po, mmbc=mmbc):
            v.wait_ge(p.mm.sem, mmbc)
            p.vv.inc(v.tensor_copy(rbro[:, :], p.ps_main[0:64, 0:256]))
            v.wait_ge(p.vv.sem, p.vv.n)
            p.vv.inc(v.tensor_tensor(
                att[pb_:pb_ + 64, fc_ * QR + w_ * WIN: fc_ * QR + (w_ + 1) * WIN],
                po[0:64, 0:256], rbro[:, :], ALU.mult))
        evacm[i] = p.vv.n

    for w in range(NQW):
        for h in range(HEADS):
            i = w * HEADS + h
            fc, pb = h // 2, (h % 2) * 64
            par = i % 2
            ps_sl = psim[:, par * 1024:(par + 1) * 1024]
            pp_sl = p_sb[:, par * 1024:(par + 1) * 1024]

            @p.blk.tensor
            def _(t, w=w, h=h, fc=fc, pb=pb, ps_sl=ps_sl, i=i):
                if i >= 2:
                    t.wait_ge(p.ss.sem, expm[i - 2])
                else:
                    t.wait_ge(p.vv.sem, p.vv.n)
                    t.wait_ge(p.ss.sem, p.ss.n)
                for kc in range(4):
                    m = t.matmul(ps_sl[:, kc * 256:(kc + 1) * 256],
                                 k2[pb:pb + 64,
                                    fc * KR + w * WIN + kc * 128: fc * KR + w * WIN + kc * 128 + 128],
                                 q2[pb:pb + 64, fc * QR + w * WIN: fc * QR + (w + 1) * WIN],
                                 start=True, stop=True)
                p.mm.inc(m)
            mmsim = p.mm.n

            @p.blk.vector
            def _(v, w=w, ps_sl=ps_sl, mmsim=mmsim):
                v.wait_ge(p.mm.sem, mmsim)
                p.vv.inc(v.tensor_tensor(ps_sl, ps_sl,
                                         masks[:, w * 1024:(w + 1) * 1024],
                                         ALU.add))
            mskm[i] = p.vv.n

            @p.blk.scalar
            def _(s, i=i, w=w, h=h, ps_sl=ps_sl, pp_sl=pp_sl):
                s.wait_ge(p.vv.sem, mskm[i])
                if i >= 2:
                    s.wait_ge(p.mm.sem, pvm[i - 2])
                expm[i] = p.ss.inc(s.activation(
                    pp_sl, ps_sl, AF.Exp,
                    bias=mcolall[:, h * NQW + w: h * NQW + w + 1]))

            if i >= 1:
                emit_pv(i - 1)
                emit_evac(i - 1)
    emit_pv(NIT - 1)
    emit_evac(NIT - 1)
    p.bar()
    p2s.close()

    p.nc.leave_named_scope("P2", _s1, False)
    _s2 = p.nc.enter_named_scope("P3", False)[0]
    p3s = ExitStack()
    wo1 = p3s.enter_context(p.nc.sbuf_tensor("wo1", [128, DC * 128], BF16))
    xr3 = p3s.enter_context(p.nc.sbuf_tensor("xr3", [128, 512], F32))
    xm3 = p3s.enter_context(p.nc.sbuf_tensor("xm3", [128, 512], F32))
    mchunks = f_chunks(MR, 512)
    for fc in range(DC):
        @p.blk.gpsimd
        def _(g, fc=fc):
            g.wait_ge(p.mm.sem, p.mm.n)
            p.dg.inc(g.dma_start(
                out=wo1[:].rearrange("p (c f) -> p c f", c=DC),
                in_=wo_i[:, fc * 128:(fc + 1) * 128]
                .rearrange("(c p) f -> p c f", p=128)), 16)
        dgw = p.dg.n

        for (o, nn) in mchunks:
            @p.blk.gpsimd
            def _(g, fc=fc, o=o, nn=nn):
                g.wait_ge(p.vv.sem, p.vv.n)
                p.da.inc(g.dma_start(
                    out=xr3[:, 0:nn],
                    in_=xin[fc * 128:(fc + 1) * 128, XN - MR + o: XN - MR + o + nn],
                    allow_slow_non_contiguous=(nn == 1)), 16)
            dgx = p.da.n

            @p.blk.tensor
            def _(t, fc=fc, o=o, nn=nn):
                t.wait_ge(p.dg.sem, dgw)
                t.wait_ge(p.vv.sem, p.vv.n)
                for ic in range(DC):
                    m = t.matmul(p.ps_main[:, 0:nn],
                                 wo1[:, ic * 128:(ic + 1) * 128],
                                 att[:, ic * QR + (QR - MR) + o: ic * QR + (QR - MR) + o + nn],
                                 start=(ic == 0), stop=(ic == DC - 1))
                p.mm.inc(m)
            mmwo = p.mm.n

            @p.blk.vector
            def _(v, fc=fc, o=o, nn=nn):
                v.wait_ge(p.mm.sem, mmwo)
                v.wait_ge(p.da.sem, dgx)
                p.vv.inc(v.tensor_tensor(xm3[:, 0:nn], p.ps_main[:, 0:nn],
                                         xr3[:, 0:nn], ALU.add))
                v.wait_ge(p.vv.sem, p.vv.n)
                if o == 0:
                    p.vv.inc(v.tensor_scalar_mul(xm3[:, 0:1], xm3[:, 0:1],
                                                 hval[:, 0:1]))
                    v.wait_ge(p.vv.sem, p.vv.n)

            @p.blk.sync
            def _(sy, fc=fc, o=o, nn=nn):
                sy.wait_ge(p.vv.sem, p.vv.n)
                p.dh.inc(sy.dma_start(
                    out=xmid_hbm[fc * 128:(fc + 1) * 128, o:o + nn],
                    in_=xm3[:, 0:nn],
                    allow_slow_non_contiguous=(nn == 1)), 16)

            @p.blk.vector
            def _(v):
                v.wait_ge(p.dh.sem, p.dh.n)
    p.bar()
    p3s.close()
    pqk.close()

    p.nc.leave_named_scope("P3", _s2, False)
    _s3 = p.nc.enter_named_scope("P4", False)[0]
    TF = min(1024, nt)
    p4s = ExitStack()
    w2r = p4s.enter_context(p.nc.sbuf_tensor("w2r", [128, NA * 1024], BF16))
    xmc = p4s.enter_context(p.nc.sbuf_tensor("xmc", [128, DC * (TF + 1)], F32))
    xn2b = p4s.enter_context(p.nc.sbuf_tensor("xn2b", [128, DC * TF], BF16))
    gbuf = p4s.enter_context(p.nc.sbuf_tensor("gbuf", [128, NA * TF], BF16))
    w1t1 = [p4s.enter_context(p.nc.sbuf_tensor(f"w1t1{b_}", [128, DC * 128], BF16))
            for b_ in range(2)]
    w1t2 = [p4s.enter_context(p.nc.sbuf_tensor(f"w1t2{b_}", [128, DC * 128], BF16))
            for b_ in range(2)]
    w1t2f = [p4s.enter_context(p.nc.sbuf_tensor(f"w1t2f{b_}", [128, DC * 128], F32))
             for b_ in range(2)]
    ggs = p4s.enter_context(p.nc.sbuf_tensor("ggs", [128, TF], F32))
    avs = p4s.enter_context(p.nc.sbuf_tensor("avs", [128, TF], F32))
    xob = p4s.enter_context(p.nc.sbuf_tensor("xob", [128, 512], F32))
    lnb_bufs = {
        "sq": p4s.enter_context(p.nc.sbuf_tensor("lnf_sq", [128, 2 * 512], F32)),
        "tmp": p4s.enter_context(p.nc.sbuf_tensor("lnf_tmp", [128, 512], F32)),
        "mu": p4s.enter_context(p.nc.sbuf_tensor("lnf_mu", [1, 512], F32)),
        "rstd": p4s.enter_context(p.nc.sbuf_tensor("lnf_rstd", [1, 512], F32)),
        "mub": p4s.enter_context(p.nc.sbuf_tensor("lnf_mub", [128, 512], F32)),
        "rsb": p4s.enter_context(p.nc.sbuf_tensor("lnf_rsb", [128, 512], F32)),
    }

    @p.blk.gpsimd
    def _(g):
        for j, (off, size) in enumerate(ach):
            p.dg.inc(g.dma_start(out=w2r[0:size, j * 1024:(j + 1) * 1024],
                                 in_=w2_i[off:off + size, :]), 16)
    p.bar()

    for tch in range(nt // TF):
        t0 = tch * TF

        @p.blk.gpsimd
        def _(g, t0=t0):
            g.wait_ge(p.vv.sem, p.vv.n)
            g.wait_ge(p.mm.sem, p.mm.n)
            for c in range(DC):
                p.dg.inc(g.dma_start(
                    out=xmc[:, c * (TF + 1):(c + 1) * (TF + 1)],
                    in_=xmid_hbm[c * 128:(c + 1) * 128, t0:t0 + TF + 1]), 16)
        p.bar()

        for hf in range(TF // 512):
            ln_fm(p, lambda c, hf=hf: c * (TF + 1) + (1 if c < 4 else 0) + hf * 512,
                  512, lambda c, hf=hf: c * TF + hf * 512, None, lnb_bufs,
                  g_col=lambda c: lngc[:, c:c + 1], b_col=lambda c: lnbc[:, c:c + 1],
                  src_sb=xmc, dst_sb=xn2b)
            p.bar()

        w1m = {}
        mmj = {}

        def load_w1(j):
            off, size = ach[j]
            b_ = j % 2
            war = mmj.get(j - 2, 0)
            sA = p.dg if b_ == 0 else p.da
            sB = p.dh if b_ == 0 else p.sg

            @p.blk.gpsimd
            def _(g, off=off, size=size, b_=b_, war=war, sA=sA):
                g.wait_ge(p.mm.sem, war)
                sA.inc(g.dma_start(
                    out=w1t1[b_][:].rearrange("p (c f) -> p c f", c=DC)[:, :, 0:size],
                    in_=w1_i[:, off:off + size].rearrange("(c p) f -> p c f", p=128)), 16)

            @p.blk.sync
            def _(sy, off=off, size=size, b_=b_, war=war, sB=sB):
                sy.wait_ge(p.mm.sem, war)
                sy.wait_ge(p.vv.sem, p.vv.n)
                sB.inc(sy.dma_start(
                    out=w1t2f[b_][:].rearrange("p (c f) -> p c f", c=DC)[:, :, 0:size],
                    in_=w1_i[:, DIN + off:DIN + off + size]
                    .rearrange("(c p) f -> p c f", p=128)), 16)
            sBm = sB.n

            @p.blk.vector
            def _(v, b_=b_, war=war, sB=sB, sBm=sBm):
                v.wait_ge(sB.sem, sBm)
                v.wait_ge(p.mm.sem, war)
                p.vv.inc(v.tensor_copy(w1t2[b_][:], w1t2f[b_][:]))
            w1m[j] = (sA, sA.n, p.vv.n)

        load_w1(0)
        for j, (off, size) in enumerate(ach):
            if j + 1 < NA:
                load_w1(j + 1)
            b_ = j % 2

            @p.blk.tensor
            def _(t, j=j, size=size, b_=b_):
                t.wait_ge(w1m[j][0].sem, w1m[j][1])
                t.wait_ge(p.vv.sem, w1m[j][2])
                t.wait_ge(p.ss.sem, p.ss.n)
                for hf in range(TF // 512):
                    for c in range(DC):
                        t.matmul(psim[0:size, hf * 512:hf * 512 + 512],
                                 w1t1[b_][:, c * 128:c * 128 + size],
                                 xn2b[:, c * TF + hf * 512: c * TF + hf * 512 + 512],
                                 start=(c == 0), stop=(c == DC - 1))
                    for c in range(DC):
                        m = t.matmul(psim[0:size, 1024 + hf * 512:1024 + hf * 512 + 512],
                                     w1t2[b_][:, c * 128:c * 128 + size],
                                     xn2b[:, c * TF + hf * 512: c * TF + hf * 512 + 512],
                                     start=(c == 0), stop=(c == DC - 1))
                p.mm.inc(m)
            mmj[j] = p.mm.n

            @p.blk.scalar
            def _(s, j=j, size=size):
                s.wait_ge(p.mm.sem, mmj[j])
                s.wait_ge(p.vv.sem, p.vv.n)
                p.ss.inc(s.activation(avs[0:size, :], psim[0:size, 0:TF],
                                      AF.Identity, bias=b1ca[0:size, j:j + 1]))
                p.ss.inc(s.activation(ggs[0:size, :], psim[0:size, 1024:1024 + TF],
                                      AF.Gelu, bias=b1cg[0:size, j:j + 1]))
            ssgg = p.ss.n

            @p.blk.vector
            def _(v, j=j, size=size):
                v.wait_ge(p.ss.sem, ssgg)
                p.vv.inc(v.tensor_tensor(gbuf[0:size, j * TF:(j + 1) * TF],
                                         avs[0:size, :], ggs[0:size, :], ALU.mult))
        p.bar()

        for oc in range(DC):
            for hf in range(TF // 512):
                @p.blk.tensor
                def _(t, oc=oc, hf=hf):
                    t.wait_ge(p.ss.sem, p.ss.n)
                    t.wait_ge(p.vv.sem, p.vv.n)
                    for j, (off, size) in enumerate(ach):
                        m = t.matmul(p.ps_main[:, 0:512],
                                     w2r[0:size, j * 1024 + oc * 128: j * 1024 + oc * 128 + 128],
                                     gbuf[0:size, j * TF + hf * 512: j * TF + hf * 512 + 512],
                                     start=(j == 0), stop=(j == NA - 1))
                    p.mm.inc(m)
                mmw2 = p.mm.n

                @p.blk.scalar
                def _(s, oc=oc):
                    s.wait_ge(p.mm.sem, mmw2)
                    s.wait_ge(p.st.sem, p.st.n)
                    p.ss.inc(s.activation(xob[:, :], p.ps_main[:, 0:512], AF.Identity,
                                          bias=b2c[:, oc:oc + 1]))
                ssb2 = p.ss.n

                @p.blk.vector
                def _(v, oc=oc, hf=hf):
                    v.wait_ge(p.ss.sem, ssb2)
                    p.vv.inc(v.tensor_tensor(
                        xob[:, :], xob[:, :],
                        xmc[:, oc * (TF + 1) + 1 + hf * 512: oc * (TF + 1) + 1 + hf * 512 + 512],
                        ALU.add))
                vvxo = p.vv.n

                @p.blk.sync
                def _(sy, oc=oc, t0=t0, hf=hf):
                    sy.wait_ge(p.vv.sem, vvxo)
                    p.st.inc(sy.dma_start(
                        out=xout[oc * 128:(oc + 1) * 128, t0 + hf * 512: t0 + hf * 512 + 512],
                        in_=xob[:, :]), 16)

                @p.blk.vector
                def _(v):
                    v.wait_ge(p.st.sem, p.st.n)
    p.bar()
    p4s.close()

    @p.blk.gpsimd
    def _(g):
        p.dg.wait(g)

    @p.blk.sync
    def _(sy):
        p.dh.wait(sy)
        p.st.wait(sy)
    return p.done()



def host_rotary_tables(n=8192):
    half = np.arange(0, DH, 2, dtype=np.float64)
    inv_freq = 1.0 / 10000.0 ** (half / DH)
    t = np.arange(n, dtype=np.float64)
    freqs = t[:, None] * inv_freq[None, :]
    freqs = np.concatenate([freqs, freqs], -1)
    base_scale = (half + 0.4 * DH) / (1.4 * DH)
    power = (t - n // 2) / (WIN // 2)
    scale = base_scale[None, :] ** power[:, None]
    scale = np.concatenate([scale, scale], -1)
    return freqs.astype(np.float32), scale.astype(np.float32)


def host_rot_tables_for(pos):
    pos = np.asarray(pos, np.float64)
    half = np.arange(0, DH, 2, dtype=np.float64)
    inv_freq = 1.0 / 10000.0 ** (half / DH)
    f0 = pos[:, None] * inv_freq[None, :]
    f = np.concatenate([f0, f0], -1)
    base_scale = (half + 0.4 * DH) / (1.4 * DH)
    power = (pos - 8192 // 2) / (WIN // 2)
    s0 = base_scale[None, :] ** power[:, None]
    s_ = np.concatenate([s0, s0], -1)
    cos, sin = np.cos(f), np.sin(f)

    def dup(a):
        return np.ascontiguousarray(np.concatenate([a, a], axis=0)).astype(np.float32)

    return (dup((cos * s_).T), dup((sin * s_).T),
            dup((cos / s_).T), dup((sin / s_).T))


def host_band_masks():
    q = np.arange(WIN)[None, :]
    out = np.zeros((2, 4, 128, WIN), np.float32)
    for kc in range(4):
        k = (kc * 128 + np.arange(128))[:, None]
        valid0 = (k >= q) & (k <= q + WIN)
        valid1 = valid0 & (k >= WIN)
        out[0, kc] = np.where(valid0, 0.0, NEG)
        out[1, kc] = np.where(valid1, 0.0, NEG)
    return out.astype(BF)



def build_launch_E(nt=NT):
    p = P()
    xfm_i = p.inp("xfm", [DM, nt])
    wq_i = p.inp("wq", [DM, DM])
    wkv_i = p.inp("wkv", [DM, 128])
    g_i = p.inp("ln1g", [1, DM])
    b_i = p.inp("ln1b", [1, DM])
    qfm_o = p.outp("qfm", [DM, nt], BF16)
    kfm_o = p.outp("kfm", [64, nt], BF16)
    vaug_o = p.outp("vaug", [128, (nt // 128) * 65], BF16)

    p.begin(pool="min")
    gc = p.sb("gc", [128, DC], F32)
    bc = p.sb("bc", [128, DC], F32)
    rowb = p.sb("rowb", [1, 2 * DM], F32)
    xfc = p.sb("xfc", [128, DC * TC], F32)
    xnb = p.sb("xnb", [128, DC * TC], BF16)
    wq1 = p.sb("wq1", [128, DC * 128], BF16)
    wkv1 = p.sb("wkv1", [128, DC * 128], BF16)
    ob = p.sb("ob", [128, TC], BF16)
    lnb_bufs = {
        "sq": p.sb("lnf_sq", [128, 2 * TC], F32),
        "tmp": p.sb("lnf_tmp", [128, TC], F32),
        "mu": p.sb("lnf_mu", [1, TC], F32),
        "rstd": p.sb("lnf_rstd", [1, TC], F32),
        "mub": p.sb("lnf_mub", [128, TC], F32),
        "rsb": p.sb("lnf_rsb", [128, TC], F32),
    }

    @p.blk.sync
    def _(sy):
        p.dh.inc(sy.dma_start(out=rowb[0:1, 0:DM], in_=g_i), 16)
        p.dh.inc(sy.dma_start(out=rowb[0:1, DM:2 * DM], in_=b_i), 16)
    p.bar()
    row_to_cols(p, rowb[0:1, 0:DM], DC, gc, 0, p.psA_f)
    row_to_cols(p, rowb[0:1, DM:2 * DM], DC, bc, 0, p.psA_f)

    @p.blk.gpsimd
    def _(g):
        p.dg.inc(g.dma_start(out=wkv1[:].rearrange("p (c f) -> p c f", c=DC),
                             in_=wkv_i.rearrange("(c p) f -> p c f", p=128)), 16)
    p.bar()

    for ch in range(nt // TC):
        t0 = ch * TC

        @p.blk.gpsimd
        def _(g, t0=t0):
            g.wait_ge(p.vv.sem, p.vv.n)
            g.wait_ge(p.mm.sem, p.mm.n)
            for c in range(DC):
                p.dg.inc(g.dma_start(out=xfc[:, c * TC:(c + 1) * TC],
                                     in_=xfm_i[c * 128:(c + 1) * 128, t0:t0 + TC]), 16)
        p.bar()
        ln_fm(p, lambda c: c * TC, TC, lambda c: c * TC, None, lnb_bufs,
              src_sb=xfc, dst_sb=xfc)
        p.bar()
        ln_fm(p, lambda c: c * TC, TC, lambda c: c * TC, None, lnb_bufs,
              g_col=lambda c: gc[:, c:c + 1], b_col=lambda c: bc[:, c:c + 1],
              src_sb=xfc, dst_sb=xnb)
        p.bar()

        for fc in range(DC):
            @p.blk.gpsimd
            def _(g, fc=fc):
                g.wait_ge(p.mm.sem, p.mm.n)
                p.dg.inc(g.dma_start(
                    out=wq1[:].rearrange("p (c f) -> p c f", c=DC),
                    in_=wq_i[:, fc * 128:(fc + 1) * 128]
                    .rearrange("(c p) f -> p c f", p=128)), 16)
            dgw = p.dg.n

            @p.blk.tensor
            def _(t, fc=fc):
                t.wait_ge(p.dg.sem, dgw)
                t.wait_ge(p.dh.sem, p.dh.n)
                for c in range(DC):
                    m = t.matmul(p.ps_main[:, 0:TC],
                                 wq1[:, c * 128:(c + 1) * 128],
                                 xnb[:, c * TC:(c + 1) * TC],
                                 start=(c == 0), stop=(c == DC - 1))
                p.mm.inc(m)
            mmq = p.mm.n

            @p.blk.vector
            def _(v):
                v.wait_ge(p.mm.sem, mmq)
                p.vv.inc(v.tensor_scalar_mul(ob[:, :], p.ps_main[:, 0:TC], 0.125))
            vvq = p.vv.n

            @p.blk.sync
            def _(sy, fc=fc, t0=t0):
                sy.wait_ge(p.vv.sem, vvq)
                p.dh.inc(sy.dma_start(out=qfm_o[fc * 128:(fc + 1) * 128, t0:t0 + TC],
                                      in_=ob[:, :]), 16)

            @p.blk.tensor
            def _(t):
                t.wait_ge(p.dh.sem, p.dh.n)

        @p.blk.tensor
        def _(t):
            t.wait_ge(p.dh.sem, p.dh.n)
            t.wait_ge(p.vv.sem, p.vv.n)
            for c in range(DC):
                m = t.matmul(p.ps_main[:, 0:TC],
                             wkv1[:, c * 128:(c + 1) * 128],
                             xnb[:, c * TC:(c + 1) * TC],
                             start=(c == 0), stop=(c == DC - 1))
            p.mm.inc(m)
        mmkv = p.mm.n

        @p.blk.vector
        def _(v):
            v.wait_ge(p.mm.sem, mmkv)
            p.vv.inc(v.tensor_copy(ob[0:64, :], p.ps_main[0:64, 0:TC]))
        vvk = p.vv.n

        @p.blk.sync
        def _(sy, t0=t0):
            sy.wait_ge(p.vv.sem, vvk)
            p.dh.inc(sy.dma_start(out=kfm_o[:, t0:t0 + TC], in_=ob[0:64, :]), 16)

        for t_ in range(TC // 128):
            @p.blk.tensor
            def _(t, t_=t_):
                t.wait_ge(p.vv.sem, p.vv.n)
                t.wait_ge(p.dh.sem, p.dh.n)
                for c in range(DC):
                    m = t.matmul(p.ps_main[:, 0:64],
                                 xnb[:, c * TC + t_ * 128: c * TC + t_ * 128 + 128],
                                 wkv1[:, c * 128 + 64:(c + 1) * 128],
                                 start=(c == 0), stop=(c == DC - 1))
                p.mm.inc(m)
            mmv = p.mm.n

            @p.blk.vector
            def _(v, t_=t_, ch=ch):
                v.wait_ge(p.mm.sem, mmv)
                kt = ch * (TC // 128) + t_
                p.vv.inc(v.tensor_copy(ob[:, 0:64], p.ps_main[:, 0:64]))
                v.wait_ge(p.vv.sem, p.vv.n)
                p.vv.inc(v.memset(ob[:, 64:65], 1.0))
            vvv = p.vv.n

            @p.blk.sync
            def _(sy, t_=t_, ch=ch):
                sy.wait_ge(p.vv.sem, vvv)
                kt = ch * (TC // 128) + t_
                p.dh.inc(sy.dma_start(out=vaug_o[:, kt * 65:(kt + 1) * 65],
                                      in_=ob[:, 0:65]), 16)

            @p.blk.tensor
            def _(t):
                t.wait_ge(p.dh.sem, p.dh.n)
    p.bar()

    @p.blk.sync
    def _(sy):
        p.dh.wait(sy)

    @p.blk.gpsimd
    def _(g):
        p.dg.wait(g)
    return p.done()



def build_launch_G(with_x, nt=NT):
    p = P()
    yfm_i = p.inp("yfm", [DM, nt])
    xfm_i = p.inp("xfm", [DM, nt]) if with_x else None
    ofm_o = p.outp("ofm", [DM, nt])

    p.begin(pool="min")
    yc = p.sb("yc", [128, DC * TC], F32)
    oc = p.sb("oc", [128, DC * TC], F32)
    xc = p.sb("xc", [128, DC * TC], F32) if with_x else None
    lnb_bufs = {
        "sq": p.sb("lnf_sq", [128, 2 * TC], F32),
        "tmp": p.sb("lnf_tmp", [128, TC], F32),
        "mu": p.sb("lnf_mu", [1, TC], F32),
        "rstd": p.sb("lnf_rstd", [1, TC], F32),
        "mub": p.sb("lnf_mub", [128, TC], F32),
        "rsb": p.sb("lnf_rsb", [128, TC], F32),
    }

    for ch in range(nt // TC):
        t0 = ch * TC

        @p.blk.gpsimd
        def _(g, t0=t0):
            g.wait_ge(p.vv.sem, p.vv.n)
            g.wait_ge(p.ss.sem, p.ss.n)
            g.wait_ge(p.dh.sem, p.dh.n)
            for c in range(DC):
                p.dg.inc(g.dma_start(out=yc[:, c * TC:(c + 1) * TC],
                                     in_=yfm_i[c * 128:(c + 1) * 128, t0:t0 + TC]), 16)
                if with_x:
                    p.dg.inc(g.dma_start(out=xc[:, c * TC:(c + 1) * TC],
                                         in_=xfm_i[c * 128:(c + 1) * 128, t0:t0 + TC]), 16)
        p.bar()
        ln_fm(p, lambda c: c * TC, TC, lambda c: c * TC,
              (lambda c: c * TC) if with_x else None, lnb_bufs,
              src_sb=yc, dst_sb=oc, x_sb=xc)
        p.bar()

        @p.blk.sync
        def _(sy, t0=t0):
            p.vv.wait(sy)
            p.ss.wait(sy)
            for c in range(DC):
                p.dh.inc(sy.dma_start(out=ofm_o[c * 128:(c + 1) * 128, t0:t0 + TC],
                                      in_=oc[:, c * TC:(c + 1) * TC]), 16)
        p.bar()

    @p.blk.sync
    def _(sy):
        p.dh.wait(sy)

    @p.blk.gpsimd
    def _(g):
        p.dg.wait(g)
    return p.done()



def build_launch_F(nt=NT, nk=8192, qh_ov=None, dbg=False, upto=None):
    nkt = nk // 128
    QH = qh_ov if qh_ov else min(1024, nt)
    p = P()
    xfm_i = p.inp("xfm", [DM, nt])
    qfm_i = p.inp("qfm", [DM, nt], BF16)
    kfm_i = p.inp("kfull", [64, nk], BF16)
    vaug_i = p.inp("vfull", [128, nkt * 65], BF16)
    wo_i = p.inp("wo", [DM, DM])
    g2_i = p.inp("ln2g", [1, DM]); b2_i = p.inp("ln2b", [1, DM])
    wmix_i = p.inp("wmix", [8, DM, DM])
    bmix_i = p.inp("bmix", [8, DM])
    logits_i = p.inp("logits", [1, 8])
    uu_i = p.inp("uu", [1, 8])
    sel2_i = p.inp("sel2c", [2, 128])
    identf_in = p.inp("identf", [128, 128])
    if upto == "att":
        xout = p.outp("xout", [128, DC * nt], BF16)
    elif upto in ("x3", "mix", "fin"):
        xout = p.outp("xout", [DM, nt])
    elif upto == "idx":
        xout = p.outp("xout", [1, 256])
    else:
        xout = p.outp("xout", [nt, DM])

    def _finish():
        @p.blk.sync
        def _(sy):
            p.dh.wait(sy)
            p.st.wait(sy)

        @p.blk.gpsimd
        def _(g):
            p.dg.wait(g)
            p.sg.wait(g)
        return p.done()

    p.begin(pool="tiny")
    psim = p.pt("psim", [128, 2 * QH])
    pout = p.pt("pout", [65, QH])
    pnorm = p.pt("pnorm", [2, 512])

    nqh = nt // QH
    identf = p.sb("sb_identf", [128, 128], F32)
    g2c = p.sb("g2c", [128, DC], F32)
    b2c = p.sb("b2c", [128, DC], F32)
    sel2 = p.sb("sel2", [2, 128], F32)
    hselb = p.sb("hselb", [128, 2], BF16)
    ones64f = p.sb("ones64f", [1, 64], F32)
    onesr2 = p.sb("onesr2", [1, 128], BF16)
    x_f = p.sb("x_f", [128, DC * nt], F32)
    kmaxc = p.sb("kmaxc", [2, 1], F32)
    mvF = p.sb("mvF", [2, DC * nqh], F32)
    mvs = p.sb("mvs", [1, 1], F32)
    lnb_bufs = {
        "sq": p.sb("lnf_sq", [128, 2 * TC], F32),
        "tmp": p.sb("lnf_tmp", [128, TC], F32),
        "mu": p.sb("lnf_mu", [1, TC], F32),
        "rstd": p.sb("lnf_rstd", [1, TC], F32),
        "mub": p.sb("lnf_mub", [128, TC], F32),
        "rsb": p.sb("lnf_rsb", [128, TC], F32),
    }
    attsc = ExitStack()
    att = attsc.enter_context(p.nc.sbuf_tensor("att", [128, DC * nt], BF16))
    s1 = ExitStack()

    def s1b(name, shape, dtype):
        return s1.enter_context(p.nc.sbuf_tensor(name, shape, dtype))

    rowb = s1b("rowb", [1, 2 * DM], F32)
    q_b = s1b("q_b", [128, DC * nt], BF16)
    k_b = s1b("k_b", [128, nk], BF16)
    v_b = s1b("v_b", [128, nkt * 65], BF16)
    p_sb = s1b("p_sb", [128, 2 * QH], BF16)
    qn2 = s1b("qn2", [2, nt], F32)
    mcol = s1b("mcol", [128, 1], F32)
    rcro = s1b("rcro", [1, QH], F32)
    rbro = s1b("rbro", [64, QH], F32)
    krow = s1b("krow", [1, QH], F32)
    nkc = len(f_chunks(nk, QH))
    km8 = s1b("km8", [1, nkc], F32)
    onesqf = s1b("onesqf", [1, 512], F32)

    @p.blk.sync
    def _(sy):
        p.dh.inc(sy.dma_start(out=identf[:], in_=identf_in), 16)
        p.dh.inc(sy.dma_start(out=rowb[0:1, 0:DM], in_=g2_i), 16)
        p.dh.inc(sy.dma_start(out=rowb[0:1, DM:2 * DM], in_=b2_i), 16)
        p.dh.inc(sy.dma_start(out=sel2[:], in_=sel2_i), 16)
        p.dh.inc(sy.dma_start(out=k_b[0:64, :], in_=kfm_i), 16)
        p.dh.inc(sy.dma_start(out=v_b[:], in_=vaug_i), 16)

    @p.blk.gpsimd
    def _(g):
        for c in range(DC):
            p.dg.inc(g.dma_start(out=x_f[:, c * nt:(c + 1) * nt],
                                 in_=xfm_i[c * 128:(c + 1) * 128, :]), 16)
            p.dg.inc(g.dma_start(out=q_b[:, c * nt:(c + 1) * nt],
                                 in_=qfm_i[c * 128:(c + 1) * 128, :]), 16)
        p.pg.inc(g.memset(hselb[:], 0.0))
        p.pg.inc(g.memset(ones64f[:], 1.0))
        p.pg.inc(g.memset(onesr2[:], 1.0))
        p.pg.inc(g.memset(onesqf[:], 1.0))
        g.wait_ge(p.pg.sem, p.pg.n)
        p.pg.inc(g.memset(hselb[0:64, 0:1], 1.0))
        p.pg.inc(g.memset(hselb[64:128, 1:2], 1.0))
    p.bar()

    @p.blk.vector
    def _(v):
        p.vv.inc(v.tensor_copy(k_b[64:128, :], k_b[0:64, :]))
    p.bar()
    row_to_cols(p, rowb[0:1, 0:DM], DC, g2c, 0, p.ps_main)
    row_to_cols(p, rowb[0:1, DM:2 * DM], DC, b2c, 0, p.ps_main)
    p.bar()

    _f0 = p.nc.enter_named_scope("Fprep", False)[0]
    for ci, (o, nn) in enumerate(f_chunks(nk, QH)):
        @p.blk.scalar
        def _(s, o=o, nn=nn):
            s.wait_ge(p.mm.sem, p.mm.n)
            s.wait_ge(p.vv.sem, p.vv.n)
            p.ss.inc(s.activation(p_sb[0:64, 0:nn], k_b[0:64, o:o + nn], AF.Square))
        sk = p.ss.n

        @p.blk.tensor
        def _(t, o=o, nn=nn):
            t.wait_ge(p.ss.sem, sk)
            t.wait_ge(p.vv.sem, p.vv.n)
            for (o2, n2) in f_chunks(nn, 512):
                p.mm.inc(t.matmul(p.ps_main[0:1, 0:n2], p.c_onesb[0:64, 0:1],
                                  p_sb[0:64, o2:o2 + n2], start=True, stop=True))
                mk = p.mm.n

                @p.blk.vector
                def _(v, o2=o2, n2=n2, mk=mk):
                    v.wait_ge(p.mm.sem, mk)
                    v.wait_ge(p.vv.sem, p.vv.n)
                    p.vv.inc(v.tensor_copy(krow[0:1, o2:o2 + n2],
                                           p.ps_main[0:1, 0:n2]))
                t.wait_ge(p.vv.sem, p.vv.n)

        @p.blk.vector
        def _(v, ci=ci, nn=nn):
            v.wait_ge(p.vv.sem, p.vv.n)
            p.vv.inc(v.tensor_reduce(km8[0:1, ci:ci + 1], krow[0:1, 0:nn],
                                     AX.X, ALU.max))
    p.bar()

    @p.blk.vector
    def _(v):
        p.vv.inc(v.tensor_reduce(kmaxc[0:1, 0:1], km8[0:1, :], AX.X, ALU.max))

    @p.blk.scalar
    def _(s):
        s.wait_ge(p.vv.sem, p.vv.n)
        p.ss.inc(s.activation(kmaxc[0:1, 0:1], kmaxc[0:1, 0:1], AF.Sqrt))

    @p.blk.tensor
    def _(t):
        t.wait_ge(p.ss.sem, p.ss.n)
        p.mm.inc(t.matmul(pnorm[0:2, 0:1], ones64f[0:1, 0:2], kmaxc[0:1, 0:1],
                          start=True, stop=True))

    @p.blk.vector
    def _(v):
        v.wait_ge(p.mm.sem, p.mm.n)
        p.vv.inc(v.tensor_copy(kmaxc[0:2, 0:1], pnorm[0:2, 0:1]))
    p.bar()

    for fc in range(DC):
        @p.blk.scalar
        def _(s, fc=fc):
            s.wait_ge(p.mm.sem, p.mm.n)
            s.wait_ge(p.vv.sem, p.vv.n)
            p.ss.inc(s.activation(p_sb[:, 0:nt], q_b[:, fc * nt:(fc + 1) * nt],
                                  AF.Square))
        sk = p.ss.n

        for (o2, n2) in f_chunks(nt, 512):
            @p.blk.tensor
            def _(t, o2=o2, n2=n2):
                t.wait_ge(p.ss.sem, sk)
                t.wait_ge(p.vv.sem, p.vv.n)
                p.mm.inc(t.matmul(pnorm[0:2, 0:n2], hselb[:, 0:2],
                                  p_sb[:, o2:o2 + n2], start=True, stop=True))
            mk = p.mm.n

            @p.blk.vector
            def _(v, o2=o2, n2=n2):
                v.wait_ge(p.mm.sem, mk)
                p.vv.inc(v.tensor_copy(qn2[:, o2:o2 + n2], pnorm[0:2, 0:n2]))
        p.bar()

        @p.blk.scalar
        def _(s):
            p.ss.inc(s.activation(qn2[:], qn2[:], AF.Sqrt))
        sk2 = p.ss.n

        @p.blk.vector
        def _(v):
            v.wait_ge(p.ss.sem, sk2)
            p.vv.inc(v.tensor_scalar(qn2[:], qn2[:], kmaxc[0:2, 0:1], -1.0,
                                     ALU.mult, ALU.mult))
            v.wait_ge(p.vv.sem, p.vv.n)
            p.vv.inc(v.tensor_scalar_add(qn2[:], qn2[:], 30.0))
        p.bar()

        for qhb in range(nqh):
            @p.blk.vector
            def _(v, fc=fc, qhb=qhb):
                v.wait_ge(p.vv.sem, p.vv.n)
                p.vv.inc(v.tensor_reduce(
                    mvF[0:2, fc * nqh + qhb: fc * nqh + qhb + 1],
                    qn2[:, qhb * QH:(qhb + 1) * QH], AX.X, ALU.min))
        p.bar()

    p.nc.leave_named_scope("Fprep", _f0, False)
    _f1 = p.nc.enter_named_scope("Fattn", False)[0]
    for h in range(HEADS):
        fc, pb = h // 2, (h % 2) * 64
        for qh in range(nqh):
            q0 = qh * QH
            @p.blk.tensor
            def _(t, fc=fc, hh=h % 2, qh=qh):
                t.wait_ge(p.vv.sem, p.vv.n)
                p.mm.inc(t.matmul(pnorm[0:1, 0:1], sel2[0:2, hh * 64:hh * 64 + 1],
                                  mvF[0:2, fc * nqh + qh: fc * nqh + qh + 1],
                                  start=True, stop=True))
            mex = p.mm.n

            @p.blk.vector
            def _(v, mex=mex):
                v.wait_ge(p.mm.sem, mex)
                p.vv.inc(v.tensor_copy(mvs[0:1, 0:1], pnorm[0:1, 0:1]))
            vmv = p.vv.n

            @p.blk.tensor
            def _(t, vmv=vmv):
                t.wait_ge(p.vv.sem, vmv)
                p.mm.inc(t.matmul(p.ps_main[:, 0:1], p.c_onesr_f[0:1, :],
                                  mvs[0:1, 0:1], start=True, stop=True))
            mbx = p.mm.n

            @p.blk.vector
            def _(v, mbx=mbx):
                v.wait_ge(p.mm.sem, mbx)
                v.wait_ge(p.ss.sem, p.ss.n)
                p.vv.inc(v.tensor_copy(mcol[:, 0:1], p.ps_main[:, 0:1]))
            vbc = p.vv.n
            expm = [0] * 64
            pvm = [0] * 64

            def emit_pv(kt):
                pp = p_sb[:, (kt % 2) * QH:(kt % 2 + 1) * QH]

                @p.blk.tensor
                def _(t, kt=kt, pp=pp):
                    t.wait_ge(p.ss.sem, expm[kt])
                    for (o2, n2) in f_chunks(QH, 512):
                        m = t.matmul(pout[:, o2:o2 + n2],
                                     v_b[:, kt * 65:(kt + 1) * 65],
                                     pp[:, o2:o2 + n2],
                                     start=(kt == 0), stop=(kt == nkt - 1))
                    pvm[kt] = p.mm.inc(m)

            for kt in range(nkt):
                par = kt % 2
                ps_sl = psim[:, par * QH:(par + 1) * QH]
                pp_sl = p_sb[:, par * QH:(par + 1) * QH]

                @p.blk.tensor
                def _(t, h=h, fc=fc, pb=pb, q0=q0, kt=kt, par=par, ps_sl=ps_sl,
                      vbc=vbc):
                    if kt == 0:
                        t.wait_ge(p.vv.sem, vbc)
                    if kt >= 2:
                        t.wait_ge(p.ss.sem, expm[kt - 2])
                    for (o2, n2) in f_chunks(QH, 512):
                        m = t.matmul(ps_sl[:, o2:o2 + n2],
                                     k_b[pb:pb + 64, kt * 128:(kt + 1) * 128],
                                     q_b[pb:pb + 64, fc * nt + q0 + o2: fc * nt + q0 + o2 + n2],
                                     start=True, stop=True)
                    p.mm.inc(m)
                mmsim = p.mm.n

                @p.blk.scalar
                def _(s, kt=kt, ps_sl=ps_sl, pp_sl=pp_sl, mmsim=mmsim, vbc=vbc):
                    s.wait_ge(p.mm.sem, mmsim)
                    if kt == 0:
                        s.wait_ge(p.vv.sem, vbc)
                    if kt >= 2:
                        s.wait_ge(p.mm.sem, pvm[kt - 2])
                    expm[kt] = p.ss.inc(s.activation(pp_sl, ps_sl, AF.Exp,
                                                     bias=mcol[:, 0:1]))

                if kt >= 1:
                    emit_pv(kt - 1)
            emit_pv(nkt - 1)

            @p.blk.vector
            def _(v):
                v.wait_ge(p.mm.sem, p.mm.n)
                p.vv.inc(v.reciprocal(rcro[0:1, :], pout[64:65, :]))
            vrc = p.vv.n

            @p.blk.tensor
            def _(t):
                t.wait_ge(p.vv.sem, vrc)
                for (o2, n2) in f_chunks(QH, 512):
                    m = t.matmul(p.ps_main[0:64, 0:n2], ones64f[0:1, 0:64],
                                 rcro[0:1, o2:o2 + n2], start=True, stop=True)
                    p.mm.inc(m)
                    mk = p.mm.n

                    @p.blk.vector
                    def _(v2, o2=o2, n2=n2, mk=mk):
                        v2.wait_ge(p.mm.sem, mk)
                        p.vv.inc(v2.tensor_copy(rbro[:, o2:o2 + n2],
                                                p.ps_main[0:64, 0:n2]))
                    t.wait_ge(p.vv.sem, p.vv.n)
            vb2 = p.vv.n

            @p.blk.vector
            def _(v, h=h, fc=fc, pb=pb, q0=q0):
                v.wait_ge(p.vv.sem, vb2)
                p.vv.inc(v.tensor_tensor(
                    att[pb:pb + 64, fc * nt + q0: fc * nt + q0 + QH],
                    pout[0:64, :], rbro[:, :], ALU.mult))
            vatt = p.vv.n

            @p.blk.tensor
            def _(t):
                t.wait_ge(p.vv.sem, vatt)
    p.bar()

    p.nc.leave_named_scope("Fattn", _f1, False)
    _f2 = p.nc.enter_named_scope("Fwo", False)[0]
    if upto == "att":
        @p.blk.sync
        def _(sy):
            p.mm.wait(sy)
            p.vv.wait(sy)
            p.st.inc(sy.dma_start(out=xout, in_=att[:]), 16)
        p.bar()
        s1.close()
        attsc.close()
        return _finish()

    s1.close()
    s2 = ExitStack()
    wrk = s2.enter_context(p.nc.sbuf_tensor("wrk", [128, DC * nt], F32))
    wo1 = s2.enter_context(p.nc.sbuf_tensor("wo1", [128, DC * 128], BF16))
    for fc in range(DC):
        @p.blk.gpsimd
        def _(g, fc=fc):
            g.wait_ge(p.mm.sem, p.mm.n)
            p.dg.inc(g.dma_start(
                out=wo1[:].rearrange("p (c f) -> p c f", c=DC),
                in_=wo_i[:, fc * 128:(fc + 1) * 128]
                .rearrange("(c p) f -> p c f", p=128)), 16)
        dgw = p.dg.n

        for (o2, n2) in f_chunks(nt, 512):
            @p.blk.tensor
            def _(t, fc=fc, o2=o2, n2=n2):
                t.wait_ge(p.dg.sem, dgw)
                t.wait_ge(p.vv.sem, p.vv.n)
                for ic in range(DC):
                    m = t.matmul(p.ps_main[:, 0:n2],
                                 wo1[:, ic * 128:(ic + 1) * 128],
                                 att[:, ic * nt + o2: ic * nt + o2 + n2],
                                 start=(ic == 0), stop=(ic == DC - 1))
                p.mm.inc(m)
            mk = p.mm.n

            @p.blk.vector
            def _(v, fc=fc, o2=o2, n2=n2):
                v.wait_ge(p.mm.sem, mk)
                p.vv.inc(v.tensor_copy(wrk[:, fc * nt + o2: fc * nt + o2 + n2],
                                       p.ps_main[:, 0:n2]))
            vk = p.vv.n

            @p.blk.tensor
            def _(t):
                t.wait_ge(p.vv.sem, vk)
    p.bar()

    for ch in range(nt // TC):
        t0 = ch * TC
        ln_fm(p, lambda c: c * nt + t0, TC, lambda c: c * nt + t0, None,
              lnb_bufs, g_col=lambda c: g2c[:, c:c + 1],
              b_col=lambda c: b2c[:, c:c + 1], src_sb=wrk, dst_sb=wrk)
        p.bar()
        ln_fm(p, lambda c: c * nt + t0, TC, lambda c: c * nt + t0,
              lambda c: c * nt + t0, lnb_bufs, src_sb=wrk, dst_sb=x_f, x_sb=x_f)
        p.bar()
    p.nc.leave_named_scope("Fwo", _f2, False)
    _f3 = p.nc.enter_named_scope("Fmix", False)[0]
    if upto == "x3":
        @p.blk.sync
        def _(sy):
            p.vv.wait(sy)
            p.ss.wait(sy)
            for c in range(DC):
                p.st.inc(sy.dma_start(out=xout[c * 128:(c + 1) * 128, :],
                                      in_=x_f[:, c * nt:(c + 1) * nt]), 16)
        p.bar()
        s2.close()
        attsc.close()
        return _finish()
    s2.close()
    attsc.close()

    s3 = ExitStack()

    def s3b(name, shape, dtype):
        return s3.enter_context(p.nc.sbuf_tensor(name, shape, dtype))

    lg = s3b("lg", [1, 8], F32)
    uu = s3b("uu_s", [1, 8], F32)
    selmax = s3b("selmax", [1, 1], F32)
    oh8 = s3b("oh8", [1, 8], F32)
    iota8 = s3b("iota8", [1, 8], F32)
    idxf = s3b("idxf", [1, 1], F32)
    idxi = s3b("idxi", [1, 1], I32)
    wsel = s3b("wsel", [128, DC * 128], BF16)
    bselr = s3b("bselr", [1, DM], F32)
    bselc = s3b("bselc", [128, DC], F32)
    xn3 = s3b("xn3", [128, DC * nt], BF16)

    @p.blk.sync
    def _(sy):
        p.dh.inc(sy.dma_start(out=lg[:], in_=logits_i), 16)
        p.dh.inc(sy.dma_start(out=uu[:], in_=uu_i), 16)

    @p.blk.gpsimd
    def _(g):
        for k8 in range(8):
            p.pg.inc(g.memset(iota8[0:1, k8:k8 + 1], float(k8)))
    p.bar()

    for ch in range(nt // TC):
        t0 = ch * TC
        ln_fm(p, lambda c: c * nt + t0, TC, lambda c: c * nt + t0, None,
              lnb_bufs, src_sb=x_f, dst_sb=xn3)
        p.bar()

    @p.blk.scalar
    def _(s):
        p.ss.inc(s.activation(uu[:], uu[:], AF.Ln))
        s.wait_ge(p.ss.sem, p.ss.n)
        p.ss.inc(s.activation(uu[:], uu[:], AF.Ln, scale=-1.0))

    @p.blk.vector
    def _(v):
        v.wait_ge(p.ss.sem, p.ss.n)
        p.vv.inc(v.tensor_tensor(oh8[:], lg[:], uu[:], ALU.subtract))
        v.wait_ge(p.vv.sem, p.vv.n)
        p.vv.inc(v.tensor_reduce(selmax[:], oh8[:], AX.X, ALU.max))
        v.wait_ge(p.vv.sem, p.vv.n)
        p.vv.inc(v.tensor_scalar(oh8[:], oh8[:], selmax[0:1, 0:1], None, ALU.is_ge))
        v.wait_ge(p.vv.sem, p.vv.n)
        p.vv.inc(v.tensor_mul(oh8[:], oh8[:], iota8[:]))
        v.wait_ge(p.vv.sem, p.vv.n)
        p.vv.inc(v.tensor_reduce(idxf[:], oh8[:], AX.X, ALU.add))
        v.wait_ge(p.vv.sem, p.vv.n)
        p.vv.inc(v.tensor_copy(idxi[:], idxf[:]))
    p.bar()
    if upto == "idx":
        dbgrow = s3b("dbgrow", [1, 256], F32)

        @p.blk.gpsimd
        def _(g):
            p.pg.inc(g.memset(dbgrow[:], 0.0))
        p.bar()

        @p.blk.vector
        def _(v):
            p.vv.inc(v.tensor_copy(dbgrow[0:1, 0:1], idxf[:]))
            v.wait_ge(p.vv.sem, p.vv.n)
            p.vv.inc(v.tensor_copy(dbgrow[0:1, 8:16], oh8[:]))
            v.wait_ge(p.vv.sem, p.vv.n)
            p.vv.inc(v.tensor_copy(dbgrow[0:1, 16:24], lg[:]))
            v.wait_ge(p.vv.sem, p.vv.n)
            p.vv.inc(v.tensor_copy(dbgrow[0:1, 24:32], uu[:]))
        p.bar()

        @p.blk.sync
        def _(sy):
            p.vv.wait(sy)
            p.st.inc(sy.dma_start(out=xout, in_=dbgrow[:]), 16)
        p.bar()
        s3.close()
        return _finish()

    @p.blk.gpsimd
    def _(g):
        with g.register("ixb") as ix:
            g.reg_load(ix, idxi[0:1, 0:1])
            for o in range(8):
                with g.If_cmp(ix, o, "IS_EQ"):
                    p.sg.inc(g.dma_start(out=bselr[:], in_=bmix_i[o:o + 1, :]), 16)
            p.sg.n -= 7 * 16
    p.bar()
    row_to_cols(p, bselr, DC, bselc, 0, p.ps_main)
    p.bar()

    mix_f = s3b("mix_f", [128, DC * nt], F32)
    for fc in range(DC):
        @p.blk.gpsimd
        def _(g, fc=fc):
            g.wait_ge(p.mm.sem, p.mm.n)
            with g.register(f"ixw{fc}") as ix:
                g.reg_load(ix, idxi[0:1, 0:1])
                for o in range(8):
                    with g.If_cmp(ix, o, "IS_EQ"):
                        p.sg.inc(g.dma_start(
                            out=wsel[:].rearrange("p (c f) -> p c f", c=DC),
                            in_=wmix_i[o][:, fc * 128:(fc + 1) * 128]
                            .rearrange("(c p) f -> p c f", p=128)), 16)
                p.sg.n -= 7 * 16
        sgw = p.sg.n

        for (o2, n2) in f_chunks(nt, 512):
            @p.blk.tensor
            def _(t, fc=fc, o2=o2, n2=n2):
                t.wait_ge(p.sg.sem, sgw)
                t.wait_ge(p.ss.sem, p.ss.n)
                for c in range(DC):
                    m = t.matmul(p.ps_main[:, 0:n2],
                                 wsel[:, c * 128:(c + 1) * 128],
                                 xn3[:, c * nt + o2: c * nt + o2 + n2],
                                 start=(c == 0), stop=(c == DC - 1))
                p.mm.inc(m)
            mk = p.mm.n

            @p.blk.scalar
            def _(s, fc=fc, o2=o2, n2=n2):
                s.wait_ge(p.mm.sem, mk)
                p.ss.inc(s.activation(mix_f[:, fc * nt + o2: fc * nt + o2 + n2],
                                      p.ps_main[:, 0:n2], AF.Identity,
                                      bias=bselc[:, fc:fc + 1]))
    p.bar()
    if upto == "mix":
        @p.blk.sync
        def _(sy):
            p.ss.wait(sy)
            p.vv.wait(sy)
            for c in range(DC):
                p.st.inc(sy.dma_start(out=xout[c * 128:(c + 1) * 128, :],
                                      in_=mix_f[:, c * nt:(c + 1) * nt]), 16)
        p.bar()
        s3.close()
        return _finish()

    for ch in range(nt // TC):
        t0 = ch * TC
        ln_fm(p, lambda c: c * nt + t0, TC, lambda c: c * nt + t0,
              lambda c: c * nt + t0, lnb_bufs, src_sb=mix_f, dst_sb=x_f, x_sb=x_f)
        p.bar()

    p.nc.leave_named_scope("Fmix", _f3, False)
    _f4 = p.nc.enter_named_scope("Ftr", False)[0]
    if upto == "fin":
        @p.blk.sync
        def _(sy):
            p.vv.wait(sy)
            p.ss.wait(sy)
            for c in range(DC):
                p.st.inc(sy.dma_start(out=xout[c * 128:(c + 1) * 128, :],
                                      in_=x_f[:, c * nt:(c + 1) * nt]), 16)
        p.bar()
        s3.close()
        return _finish()
    s3.close()

    s4 = ExitStack()
    ttm = s4.enter_context(p.nc.sbuf_tensor("ttm", [128, DM], F32))
    for t_ in range(nt // 128):
        pairs = [(x_f[:, c * nt + t_ * 128: c * nt + t_ * 128 + 128],
                  ttm[:, c * 128:(c + 1) * 128]) for c in range(DC)]
        transpose_tiles(p, pairs, identf, psim[:, 0:128], psim[:, QH:QH + 128])
        vdone = p.vv.n

        @p.blk.sync
        def _(sy, t_=t_):
            sy.wait_ge(p.vv.sem, vdone)
            p.st.inc(sy.dma_start(out=xout[t_ * 128:(t_ + 1) * 128, :],
                                  in_=ttm[:, 0:DM]), 16)

        @p.blk.tensor
        def _(t):
            t.wait_ge(p.st.sem, p.st.n)
    p.bar()

    @p.blk.sync
    def _(sy):
        p.dh.wait(sy)
        p.st.wait(sy)

    @p.blk.gpsimd
    def _(g):
        p.dg.wait(g)
        p.sg.wait(g)
    p.nc.leave_named_scope("Ftr", _f4, False)
    s4.close()
    return p.done()



_PROGS = {}


def _prog(key, builder, *args):
    if key not in _PROGS:
        _PROGS[key] = builder(*args)
    return _PROGS[key]


def kernel(**inputs):
    f32 = np.float32
    x = np.asarray(inputs["L0_attn_in_seq"], f32).reshape(16384, DM)
    identf = np.eye(128, dtype=f32)
    identb = np.eye(128, dtype=BF)
    sel2c = np.zeros((2, 128), f32)
    sel2c[0, 0:64] = 1.0
    sel2c[1, 64:128] = 1.0

    def dup64(a):
        a = np.asarray(a, f32).reshape(-1)
        return np.concatenate([a, a])[:, None].astype(f32)

    ncA = _prog("A", build_launch_A, NT)
    base = dict(tk=np.asarray(inputs["L0_attn_t_k"], f32),
                tv=np.asarray(inputs["L0_attn_t_v"], f32),
                tq=np.asarray(inputs["L0_attn_t_q"], f32),
                w1=np.asarray(inputs["L0_attn_W1"], f32),
                w2=np.asarray(inputs["L0_attn_W2"], f32),
                b1=np.asarray(inputs["L0_attn_b1"], f32).reshape(1, 256),
                b2=np.asarray(inputs["L0_attn_b2"], f32).reshape(1, DM),
                identf=identf, identb=identb)
    imsA = [dict(x_tm=np.ascontiguousarray(x[c * NT:(c + 1) * NT]), **base)
            for c in range(NCORES)]
    rA = run_bass_kernel_spmd(ncA, imsA, core_ids=list(range(NCORES))).results

    ncB = _prog("B", build_launch_BC, "B", NT)

    def gargs(pref, rs, cores):
        out = {}
        for k in range(4):
            r = rs[cores[k]]
            out[f"{pref}w1t_{k}"] = r["gw1t"]
            out[f"{pref}w2_{k}"] = r["gw2"]
            out[f"{pref}b1_{k}"] = r["gb1"]
            out[f"{pref}b2_{k}"] = r["gb2"]
        return out

    g0 = gargs("g0", rA, [0, 1, 2, 3])
    imsB = [dict(xfm=rA[c]["xfm"], qfm=rA[c]["qfm"], zbf=rA[c]["zbf"],
                 ztm=rA[c]["ztm"], zmyo=rA[c]["zmyo"],
                 w1=base["w1"], w2=base["w2"], b1=base["b1"], b2=base["b2"],
                 identf=identf, identb=identb, **g0)
            for c in range(NCORES)]
    rB = run_bass_kernel_spmd(ncB, imsB, core_ids=list(range(NCORES))).results

    ncC = _prog("C", build_launch_BC, "C", NT)
    g1 = gargs("g1", rB, [4, 5, 6, 7])
    imsC = [dict(xfm=rA[c]["xfm"], qfm=rA[c]["qfm"], zbf=rA[c]["zbf"],
                 ztm=rA[c]["ztm"], zmyo=rA[c]["zmyo"],
                 w1=base["w1"], w2=base["w2"], b1=base["b1"], b2=base["b2"],
                 identf=identf, identb=identb, **g0, **g1)
            for c in range(NCORES)]
    rC = run_bass_kernel_spmd(ncC, imsC, core_ids=list(range(NCORES))).results

    xs = [rB[c]["xnew"] if c < 4 else rC[c]["xnew"] for c in range(NCORES)]
    xouter = xs

    ncG = _prog("G", build_launch_G, False, NT)
    imsG = [dict(yfm=xs[c]) for c in range(NCORES)]
    rG = run_bass_kernel_spmd(ncG, imsG, core_ids=list(range(NCORES))).results
    xs = [rG[c]["ofm"] for c in range(NCORES)]

    ncD = _prog("D", build_launch_D, NT)
    KR = 512 + NT
    NQW = (256 + NT) // WIN
    m2 = host_band_masks()
    wqkv_all = np.asarray(inputs["L0_ffn_Wqkv"], f32)
    wo_all = np.asarray(inputs["L0_ffn_Wo"], f32)
    w1_all = np.asarray(inputs["L0_ffn_W1"], f32)
    w2_all = np.asarray(inputs["L0_ffn_W2"], f32)
    b1_all = np.asarray(inputs["L0_ffn_b1"], f32)
    b2_all = np.asarray(inputs["L0_ffn_b2"], f32)
    lng_all = np.asarray(inputs["L0_ffn_ln_g"], f32)
    lnb_all = np.asarray(inputs["L0_ffn_ln_b"], f32)
    qs_all = np.asarray(inputs["L0_ffn_q_scale"], f32)
    ks_all = np.asarray(inputs["L0_ffn_k_scale"], f32)

    core_rot = []
    core_masks = []
    core_hval = []
    for c in range(NCORES):
        sb_ = (c % 4) * NT
        pos = np.arange(sb_ - 512, sb_ + NT)
        core_rot.append(host_rot_tables_for(pos))
        mk = np.broadcast_to(m2[0][None], (NQW, 4, 128, WIN)).copy()
        if c % 4 == 0:
            mk[1] = m2[1]
        core_masks.append(np.ascontiguousarray(mk))
        core_hval.append(np.full((128, 1), 0.0 if c % 4 == 0 else 1.0, f32))

    for li in range(6):
        xg0 = np.concatenate(xs[0:4], axis=1)
        xg1 = np.concatenate(xs[4:8], axis=1)
        imsD = []
        for c in range(NCORES):
            xg = xg0 if c < 4 else xg1
            sb_ = (c % 4) * NT
            if sb_ == 0:
                xin = np.concatenate(
                    [np.zeros((DM, HALO), f32), xg[:, 0:NT]], axis=1)
            else:
                xin = np.ascontiguousarray(xg[:, sb_ - HALO: sb_ + NT])
            rqc, rqs_, rkc, rks_ = core_rot[c]
            imsD.append(dict(
                xin=xin, wqkv=wqkv_all[li], wo=wo_all[li], w1=w1_all[li],
                w2=w2_all[li], b1=b1_all[li:li + 1], b2=b2_all[li:li + 1],
                lng=lng_all[li:li + 1], lnb=lnb_all[li:li + 1],
                qscol=dup64(qs_all[li]), kscol=dup64(ks_all[li]),
                rotqc=rqc, rotqs=rqs_, rotkc=rkc, rotks=rks_,
                masks=core_masks[c], hvalid=core_hval[c], sel2c=sel2c,
                identb=identb))
        rD = run_bass_kernel_spmd(ncD, imsD, core_ids=list(range(NCORES))).results
        xs = [rD[c]["xout"] for c in range(NCORES)]

    ncH = _prog("H", build_launch_G, True, NT)
    imsH = [dict(yfm=xs[c], xfm=xouter[c]) for c in range(NCORES)]
    rH = run_bass_kernel_spmd(ncH, imsH, core_ids=list(range(NCORES))).results
    xs = [rH[c]["ofm"] for c in range(NCORES)]

    ncE = _prog("E", build_launch_E, NT)
    imsE = [dict(xfm=xs[c], wq=np.asarray(inputs["L1_attn_Wq"], f32),
                 wkv=np.asarray(inputs["L1_attn_Wkv"], f32),
                 ln1g=np.asarray(inputs["L1_attn_ln1_g"], f32).reshape(1, DM),
                 ln1b=np.asarray(inputs["L1_attn_ln1_b"], f32).reshape(1, DM))
            for c in range(NCORES)]
    rE = run_bass_kernel_spmd(ncE, imsE, core_ids=list(range(NCORES))).results

    ncF = _prog("F", build_launch_F, NT, 8192)
    kf0 = np.concatenate([rE[c]["kfm"] for c in range(4)], axis=1)
    kf1 = np.concatenate([rE[c]["kfm"] for c in range(4, 8)], axis=1)
    vf0 = np.concatenate([rE[c]["vaug"] for c in range(4)], axis=1)
    vf1 = np.concatenate([rE[c]["vaug"] for c in range(4, 8)], axis=1)
    imsF = []
    for c in range(NCORES):
        imsF.append(dict(
            xfm=xs[c], qfm=rE[c]["qfm"],
            kfull=kf0 if c < 4 else kf1, vfull=vf0 if c < 4 else vf1,
            wo=np.asarray(inputs["L1_attn_Wo"], f32),
            ln2g=np.asarray(inputs["L1_attn_ln2_g"], f32).reshape(1, DM),
            ln2b=np.asarray(inputs["L1_attn_ln2_b"], f32).reshape(1, DM),
            wmix=np.asarray(inputs["L1_ffn_W"], f32),
            bmix=np.asarray(inputs["L1_ffn_b"], f32),
            logits=np.asarray(inputs["L1_ffn_logits"], f32).reshape(1, 8),
            uu=np.asarray(inputs["L1_ffn_u"], f32).reshape(1, 8),
            sel2c=sel2c, identf=identf))
    rF = run_bass_kernel_spmd(ncF, imsF, core_ids=list(range(NCORES))).results

    out = np.concatenate([rF[c]["xout"] for c in range(NCORES)], axis=0)
    return out.reshape(2, 8192, DM).astype(np.float32)


# version 04 (final)
# speedup vs baseline: 1.4283x; 1.0025x over previous
import numpy as np
import ml_dtypes
from contextlib import ExitStack

import concourse.bass as bass
import concourse.mybir as mybir
from concourse.bass_utils import run_bass_kernel_spmd

F32 = mybir.dt.float32
BF16 = mybir.dt.bfloat16
I32 = mybir.dt.int32
AF = mybir.ActivationFunctionType
ALU = mybir.AluOpType
AX = mybir.AxisListType

NCORES = 8
DM = 1024
DC = 8
NT = 2048
LR = 0.001
CGRAD = 2.0 / (8192.0 * 1024.0)
EPS = 1e-5
NEG = -(2.0 ** 100)
HEADS = 16
DH = 64
WIN = 256
DIN = 2730
QKSC = 8.0
HALO = 513
TC = 512
BF = ml_dtypes.bfloat16

_ENGS = ("tensor", "vector", "scalar", "gpsimd", "sync")


def f_chunks(total, step=128):
    out = []
    o = 0
    while o < total:
        out.append((o, min(step, total - o)))
        o += step
    return out


class S:
    def __init__(self, nc, ctx, name):
        self.sem = ctx.enter_context(nc.semaphore(name))
        self.n = 0

    def inc(self, instr, k=1):
        instr.then_inc(self.sem, k)
        self.n += k
        return self.n

    def wait(self, eng, n=None):
        eng.wait_ge(self.sem, self.n if n is None else n)


class P:
    def __init__(self):
        self.nc = bass.Bass("TRN2", target_bir_lowering=False, debug=False)
        self.ctx = ExitStack()
        self.blk = None
        for nm in ("dg", "dh", "da", "mm", "vv", "ss", "pg", "st", "sg"):
            setattr(self, nm, S(self.nc, self.ctx, nm))

    def sb(self, name, shape, dtype):
        return self.ctx.enter_context(self.nc.sbuf_tensor(name, shape, dtype))

    def pt(self, name, shape, dtype=F32):
        return self.ctx.enter_context(self.nc.psum_tensor(name, shape, dtype))

    def inp(self, name, shape, dtype=F32):
        return self.nc.dram_tensor(name, shape, dtype, kind="ExternalInput").ap()

    def outp(self, name, shape, dtype=F32):
        return self.nc.dram_tensor(name, shape, dtype, kind="ExternalOutput").ap()

    def begin(self, pool="full"):
        self.pool = pool
        self.blk = self.ctx.enter_context(self.nc.Block())
        self.c_onesf = self.sb("c_onesf", [128, 1], F32)
        self.c_onesb = self.sb("c_onesb", [128, 1], BF16)
        self.c_onesr_f = self.sb("c_onesr_f", [1, 128], F32)
        self.c_onesr_b = self.sb("c_onesr_b", [1, 128], BF16)

        if self.pool == "full":
            self.psA_b = self.pt("pool_psA_b", [128, 128], BF16)
            self.psB_b = self.pt("pool_psB_b", [128, 128], BF16)
        if self.pool != "tiny":
            self.psA_f = self.pt("pool_psA_f", [128, 128], F32)
            self.psB_f = self.pt("pool_psB_f", [128, 128], F32)
        self.ps_main = self.pt("pool_ps_main", [128, 512], F32)

        @self.blk.gpsimd
        def _(g):
            self.pg.inc(g.memset(self.c_onesf[:], 1.0))
            self.pg.inc(g.memset(self.c_onesb[:], 1.0))
            self.pg.inc(g.memset(self.c_onesr_f[:], 1.0))
            self.pg.inc(g.memset(self.c_onesr_b[:], 1.0))
        return self.blk

    def bar(self):
        marks = [(s, s.n) for s in
                 (self.dg, self.dh, self.da, self.mm, self.vv, self.ss,
                  self.pg, self.sg)]

        def mkbody(eng_name):
            def body(e):
                for sem, val in marks:
                    if val:
                        e.wait_ge(sem.sem, val)
            return body
        self.blk.tensor(mkbody("tensor"))
        self.blk.vector(mkbody("vector"))
        self.blk.scalar(mkbody("scalar"))
        self.blk.gpsimd(mkbody("gpsimd"))
        self.blk.sync(mkbody("sync"))

    def done(self):
        self.ctx.close()
        return self.nc



def transpose_tiles(p, pairs, ident, psA, psB):
    mb = p.mm.n
    vb = p.vv.n
    ps = (psA, psB)

    @p.blk.tensor
    def _(t):
        t.wait_ge(p.vv.sem, vb)
        for i, (s, d) in enumerate(pairs):
            if i >= 2:
                t.wait_ge(p.vv.sem, vb + i - 1)
            p.mm.inc(t.transpose(ps[i % 2][:, :], s, ident[:]))

    @p.blk.vector
    def _(v):
        for i, (s, d) in enumerate(pairs):
            v.wait_ge(p.mm.sem, mb + i + 1)
            p.vv.inc(v.tensor_copy(d, ps[i % 2][:, 0:d.shape[-1]]))


def row_to_cols(p, row_ap, ncol, out_sb, out_off, psum):
    mb = p.mm.n
    vb = p.vv.n

    @p.blk.tensor
    def _(t):
        t.wait_ge(p.vv.sem, vb)
        for c in range(ncol):
            m = t.matmul(psum[:, c:c + 1],
                         row_ap[0:1, c * 128:(c + 1) * 128],
                         p.c_onesf[0:1, 0:1], start=True, stop=True)
        p.mm.inc(m)

    @p.blk.vector
    def _(v):
        v.wait_ge(p.mm.sem, p.mm.n)
        p.vv.inc(v.tensor_copy(out_sb[:, out_off:out_off + ncol], psum[:, 0:ncol]))


def ln_tm_tile(p, x_ap, xn_ap, sq_ap, scr, d=DM, eps=EPS):
    @p.blk.vector
    def _(v):
        p.vv.inc(v.tensor_reduce(scr["s"][:], x_ap, AX.X, ALU.add))

    @p.blk.scalar
    def _(s):
        p.ss.inc(s.activation(sq_ap, x_ap, AF.Square, accum_out=scr["q"][:]))

    vb, sb_ = p.vv.n, p.ss.n

    @p.blk.vector
    def _(v):
        v.wait_ge(p.vv.sem, vb)
        v.wait_ge(p.ss.sem, sb_)
        p.vv.inc(v.tensor_scalar_mul(scr["mu"][:], scr["s"][:], 1.0 / d))
        p.vv.inc(v.tensor_scalar_mul(scr["rstd"][:], scr["q"][:], 1.0 / d))
        v.wait_ge(p.vv.sem, p.vv.n)
        p.vv.inc(v.tensor_mul(scr["s"][:], scr["mu"][:], scr["mu"][:]))
        v.wait_ge(p.vv.sem, p.vv.n)
        p.vv.inc(v.tensor_tensor(scr["rstd"][:], scr["rstd"][:], scr["s"][:], ALU.subtract))
        v.wait_ge(p.vv.sem, p.vv.n)
        p.vv.inc(v.tensor_scalar_add(scr["rstd"][:], scr["rstd"][:], eps))

    sb2 = p.ss.n

    @p.blk.scalar
    def _(s):
        s.wait_ge(p.vv.sem, p.vv.n)
        p.ss.inc(s.activation(scr["rstd"][:], scr["rstd"][:], AF.Sqrt))

    @p.blk.vector
    def _(v):
        v.wait_ge(p.ss.sem, p.ss.n)
        p.vv.inc(v.reciprocal(scr["rstd"][:], scr["rstd"][:]))
        v.wait_ge(p.vv.sem, p.vv.n)
        p.vv.inc(v.tensor_scalar(xn_ap, x_ap, scr["mu"][:], scr["rstd"][:],
                                 ALU.subtract, ALU.mult))


def ln_fm(p, src_off_fn, nfree, dst_off_fn, x_add_off_fn, bufs, g_col=None,
          b_col=None, src_sb=None, dst_sb=None, x_sb=None):
    assert nfree <= 512
    sq = bufs["sq"]; tmp = bufs["tmp"]
    mu = bufs["mu"]; rstd = bufs["rstd"]
    mub = bufs["mub"]; rsb = bufs["rsb"]
    ps = p.ps_main

    mm_b0 = p.mm.n
    ss_b0 = p.ss.n

    @p.blk.scalar
    def _(s):
        for c in range(DC):
            if c >= 2:
                s.wait_ge(p.mm.sem, mm_b0 + 1 + c - 1)
            p.ss.inc(s.activation(sq[:, (c % 2) * nfree:(c % 2 + 1) * nfree],
                                  src_sb[:, src_off_fn(c):src_off_fn(c) + nfree],
                                  AF.Square))

    @p.blk.tensor
    def _(t):
        t.wait_ge(p.vv.sem, p.vv.n)
        for c in range(DC):
            m = t.matmul(ps[0:1, 0:nfree], p.c_onesf[:, 0:1],
                         src_sb[:, src_off_fn(c):src_off_fn(c) + nfree],
                         start=(c == 0), stop=(c == DC - 1))
        p.mm.inc(m)
    mm1 = p.mm.n

    @p.blk.vector
    def _(v):
        v.wait_ge(p.mm.sem, mm1)
        p.vv.inc(v.tensor_scalar_mul(mu[:], ps[0:1, 0:nfree], 1.0 / DM))
    vv1 = p.vv.n

    @p.blk.tensor
    def _(t):
        t.wait_ge(p.vv.sem, vv1)
        for c in range(DC):
            t.wait_ge(p.ss.sem, ss_b0 + c + 1)
            p.mm.inc(t.matmul(ps[0:1, 0:nfree], p.c_onesf[:, 0:1],
                              sq[:, (c % 2) * nfree:(c % 2 + 1) * nfree],
                              start=(c == 0), stop=(c == DC - 1)))
    mm2 = p.mm.n

    @p.blk.vector
    def _(v):
        v.wait_ge(p.mm.sem, mm2)
        p.vv.inc(v.tensor_scalar_mul(rstd[:], ps[0:1, 0:nfree], 1.0 / DM))
        v.wait_ge(p.vv.sem, p.vv.n)
        p.vv.inc(v.tensor_mul(tmp[0:1, 0:nfree], mu[:], mu[:]))
        v.wait_ge(p.vv.sem, p.vv.n)
        p.vv.inc(v.tensor_tensor(rstd[:], rstd[:], tmp[0:1, 0:nfree], ALU.subtract))
        v.wait_ge(p.vv.sem, p.vv.n)
        p.vv.inc(v.tensor_scalar_add(rstd[:], rstd[:], EPS))

    @p.blk.scalar
    def _(s):
        s.wait_ge(p.vv.sem, p.vv.n)
        p.ss.inc(s.activation(rstd[:], rstd[:], AF.Sqrt))

    @p.blk.vector
    def _(v):
        v.wait_ge(p.ss.sem, p.ss.n)
        p.vv.inc(v.reciprocal(rstd[:], rstd[:]))
    vrec = p.vv.n

    @p.blk.tensor
    def _(t):
        t.wait_ge(p.vv.sem, vrec)
        p.mm.inc(t.matmul(ps[:, 0:nfree], p.c_onesr_f[0:1, :], mu[0:1, 0:nfree],
                          start=True, stop=True))
    mmb1 = p.mm.n

    @p.blk.vector
    def _(v):
        v.wait_ge(p.mm.sem, mmb1)
        p.vv.inc(v.tensor_copy(mub[:, 0:nfree], ps[:, 0:nfree]))
    vvb1 = p.vv.n

    @p.blk.tensor
    def _(t):
        t.wait_ge(p.vv.sem, vvb1)
        p.mm.inc(t.matmul(ps[:, 0:nfree], p.c_onesr_f[0:1, :], rstd[0:1, 0:nfree],
                          start=True, stop=True))
    mmb2 = p.mm.n

    @p.blk.vector
    def _(v):
        v.wait_ge(p.mm.sem, mmb2)
        p.vv.inc(v.tensor_copy(rsb[:, 0:nfree], ps[:, 0:nfree]))
        v.wait_ge(p.vv.sem, p.vv.n)
        for c in range(DC):
            src = src_sb[:, src_off_fn(c):src_off_fn(c) + nfree]
            p.vv.inc(v.tensor_tensor(tmp[:, 0:nfree], src, mub[:, 0:nfree],
                                     ALU.subtract))
            v.wait_ge(p.vv.sem, p.vv.n)
            p.vv.inc(v.tensor_tensor(tmp[:, 0:nfree], tmp[:, 0:nfree],
                                     rsb[:, 0:nfree], ALU.mult))
            v.wait_ge(p.vv.sem, p.vv.n)
            if g_col is not None:
                p.vv.inc(v.tensor_scalar(tmp[:, 0:nfree], tmp[:, 0:nfree],
                                         g_col(c), b_col(c), ALU.mult, ALU.add))
                v.wait_ge(p.vv.sem, p.vv.n)
            dst = dst_sb[:, dst_off_fn(c):dst_off_fn(c) + nfree]
            if x_add_off_fn is not None:
                xa = x_sb[:, x_add_off_fn(c):x_add_off_fn(c) + nfree]
                p.vv.inc(v.tensor_tensor(dst, tmp[:, 0:nfree], xa, ALU.add))
            else:
                p.vv.inc(v.tensor_copy(dst, tmp[:, 0:nfree]))
            v.wait_ge(p.vv.sem, p.vv.n)



def ttt_prep_params(p, w1_in, w2_in, b1_in, b2_in, identf, masters=False):
    r = {}
    r["w1t_b"] = p.sb("w1t_b", [128, DC * 256], BF16)
    r["w2_b"] = p.sb("w2_b", [128, DC * 256], BF16)
    r["w2t_b"] = p.sb("w2t_b", [128, 2 * 1024], BF16)
    r["b1c"] = p.sb("b1c", [128, 2], F32)
    r["b2c"] = p.sb("b2c", [128, DC], F32)
    r["b1row"] = p.sb("b1row", [1, 256], F32)
    r["b2row"] = p.sb("b2row", [1, 1024], F32)
    if masters:
        r["w1t_f"] = p.sb("w1t_f", [128, DC * 256], F32)
        r["w2_f"] = p.sb("w2_f", [128, DC * 256], F32)
    _w1w2 = ExitStack()
    w1r = _w1w2.enter_context(p.nc.sbuf_tensor("w1r_tmp", [128, 2 * 1024], F32))
    w2r = _w1w2.enter_context(p.nc.sbuf_tensor("w2r_tmp", [128, DC * 256], F32))
    psA = p.psA_f
    psB = p.psB_f
    pcol = p.psA_f

    @p.blk.sync
    def _(sy):
        for c in range(2):
            p.dh.inc(sy.dma_start(out=w1r[:, c * 1024:(c + 1) * 1024],
                                  in_=w1_in[c * 128:(c + 1) * 128, :]), 16)
        for c in range(DC):
            p.dh.inc(sy.dma_start(out=w2r[:, c * 256:(c + 1) * 256],
                                  in_=w2_in[c * 128:(c + 1) * 128, :]), 16)
        p.dh.inc(sy.dma_start(out=r["b1row"][:], in_=b1_in), 16)
        p.dh.inc(sy.dma_start(out=r["b2row"][:], in_=b2_in), 16)
    p.bar()

    dst_w1t = r["w1t_f"] if masters else r["w1t_b"]
    pairs = []
    for hc in range(DC):
        for mc in range(2):
            pairs.append((w1r[:, mc * 1024 + hc * 128: mc * 1024 + hc * 128 + 128],
                          dst_w1t[:, hc * 256 + mc * 128: hc * 256 + mc * 128 + 128]))
    for mc in range(2):
        for hc in range(DC):
            pairs.append((w2r[:, hc * 256 + mc * 128: hc * 256 + mc * 128 + 128],
                          r["w2t_b"][:, mc * 1024 + hc * 128: mc * 1024 + hc * 128 + 128]))
    transpose_tiles(p, pairs, identf, psA, psB)

    @p.blk.vector
    def _(v):
        v.wait_ge(p.dh.sem, p.dh.n)
        if masters:
            for c in range(DC):
                p.vv.inc(v.tensor_copy(r["w2_f"][:, c * 256:(c + 1) * 256],
                                       w2r[:, c * 256:(c + 1) * 256]))
        for c in range(DC):
            p.vv.inc(v.tensor_copy(r["w2_b"][:, c * 256:(c + 1) * 256],
                                   w2r[:, c * 256:(c + 1) * 256]))
    row_to_cols(p, r["b1row"], 2, r["b1c"], 0, pcol)
    row_to_cols(p, r["b2row"], DC, r["b2c"], 0, pcol)
    p.bar()
    _w1w2.close()
    if masters:
        @p.blk.vector
        def _(v):
            for c in range(DC):
                p.vv.inc(v.tensor_copy(r["w1t_b"][:, c * 256:(c + 1) * 256],
                                       r["w1t_f"][:, c * 256:(c + 1) * 256]))
        p.bar()
    return r


def ttt_update_params(p, pr, g, lr=LR):
    @p.blk.vector
    def _(v):
        for c in range(DC):
            sl = slice(c * 256, (c + 1) * 256)
            p.vv.inc(v.scalar_tensor_tensor(pr["w1t_f"][:, sl], g["gw1t"][:, sl],
                                            -lr, pr["w1t_f"][:, sl], ALU.mult, ALU.add))
            p.vv.inc(v.scalar_tensor_tensor(pr["w2_f"][:, sl], g["gw2"][:, sl],
                                            -lr, pr["w2_f"][:, sl], ALU.mult, ALU.add))
        p.vv.inc(v.scalar_tensor_tensor(pr["b1row"][:], g["gb1"][:], -lr,
                                        pr["b1row"][:], ALU.mult, ALU.add))
        p.vv.inc(v.scalar_tensor_tensor(pr["b2row"][:], g["gb2"][:], -lr,
                                        pr["b2row"][:], ALU.mult, ALU.add))
    p.bar()

    @p.blk.vector
    def _(v):
        for c in range(DC):
            sl = slice(c * 256, (c + 1) * 256)
            p.vv.inc(v.tensor_copy(pr["w1t_b"][:, sl], pr["w1t_f"][:, sl]))
            p.vv.inc(v.tensor_copy(pr["w2_b"][:, sl], pr["w2_f"][:, sl]))
    pairs = []
    for mc in range(2):
        for hc in range(DC):
            pairs.append((pr["w2_f"][:, hc * 256 + mc * 128: hc * 256 + mc * 128 + 128],
                          pr["w2t_b"][:, mc * 1024 + hc * 128: mc * 1024 + hc * 128 + 128]))
    transpose_tiles(p, pairs, pr["identf"], pr["psA"], pr["psB"])
    row_to_cols(p, pr["b1row"], 2, pr["b1c"], 0, pr["pcol"])
    row_to_cols(p, pr["b2row"], DC, pr["b2c"], 0, pr["pcol"])
    p.bar()


def ttt_recon_fwd(p, pr, zin_bf_sb, zoff_fn, n, apb, h_b, psum):
    for mc in range(2):
        mmn0 = p.mm.n

        @p.blk.tensor
        def _(t):
            for hc in range(DC):
                m = t.matmul(psum[:, 0:n],
                             pr["w1t_b"][:, hc * 256 + mc * 128: hc * 256 + mc * 128 + 128],
                             zin_bf_sb[:, zoff_fn(hc):zoff_fn(hc) + n],
                             start=(hc == 0), stop=(hc == DC - 1))
            p.mm.inc(m)

        @p.blk.scalar
        def _(s):
            s.wait_ge(p.mm.sem, p.mm.n)
            p.ss.inc(s.activation(apb[:, mc * n:(mc + 1) * n], psum[:, 0:n],
                                  AF.Identity, bias=pr["b1c"][:, mc:mc + 1]))
            s.wait_ge(p.ss.sem, p.ss.n)
            p.ss.inc(s.activation(h_b[:, mc * n:(mc + 1) * n],
                                  apb[:, mc * n:(mc + 1) * n], AF.Gelu))

        @p.blk.tensor
        def _(t):
            t.wait_ge(p.ss.sem, p.ss.n)


def ttt_recon_out(p, pr, h_b, n, zadd_sb, zadd_off_fn, out_sb, out_off_fn, psum,
                  out_dtype_note="f32"):
    for hc in range(DC):
        @p.blk.tensor
        def _(t):
            for mc in range(2):
                m = t.matmul(psum[:, 0:n],
                             pr["w2t_b"][:, mc * 1024 + hc * 128: mc * 1024 + hc * 128 + 128],
                             h_b[:, mc * n:(mc + 1) * n],
                             start=(mc == 0), stop=(mc == 1))
            p.mm.inc(m)

        @p.blk.scalar
        def _(s):
            s.wait_ge(p.mm.sem, p.mm.n)
            p.ss.inc(s.activation(psum[:, 0:n], psum[:, 0:n], AF.Identity,
                                  bias=pr["b2c"][:, hc:hc + 1]))

        @p.blk.vector
        def _(v):
            v.wait_ge(p.ss.sem, p.ss.n)
            p.vv.inc(v.tensor_tensor(out_sb[:, out_off_fn(hc):out_off_fn(hc) + n],
                                     psum[:, 0:n],
                                     zadd_sb[:, zadd_off_fn(hc):zadd_off_fn(hc) + n],
                                     ALU.add))

        @p.blk.tensor
        def _(t):
            t.wait_ge(p.vv.sem, p.vv.n)


def ttt_grad_chunk(p, pr, bufs, n=TC):
    nt_t = n // 128
    z_bf = bufs["z_bf"]; z_tm = bufs["z_tm"]; zmy = bufs["zmy"]
    apb = bufs["apb"]; h_b = bufs["h_b"]; gp = bufs["gp"]
    rb = bufs["rb"]; r_tm = bufs["r_tm"]; h_tm = bufs["h_tm"]
    da_b = bufs["da_b"]; da_tm = bufs["da_tm"]
    psum = bufs["psum"]; identb = bufs["identb"]
    psA = bufs["psA"]; psB = bufs["psB"]

    ttt_recon_fwd(p, pr, z_bf, lambda hc: hc * n, n, apb, h_b, psum)
    p.bar()

    @p.blk.scalar
    def _(s):
        for mc in range(2):
            sl = slice(mc * n, (mc + 1) * n)
            p.ss.inc(s.activation(gp[:, sl], apb[:, sl], AF.Erf,
                                  scale=0.7071067811865476))
            p.ss.inc(s.activation(bufs["gp2"][:, sl], apb[:, sl], AF.Square))
        s.wait_ge(p.ss.sem, p.ss.n)
        for mc in range(2):
            sl = slice(mc * n, (mc + 1) * n)
            p.ss.inc(s.activation(bufs["gp2"][:, sl], bufs["gp2"][:, sl],
                                  AF.Exp, scale=-0.5))

    @p.blk.vector
    def _(v):
        v.wait_ge(p.ss.sem, p.ss.n)
        for mc in range(2):
            sl = slice(mc * n, (mc + 1) * n)
            p.vv.inc(v.tensor_scalar(gp[:, sl], gp[:, sl], 0.5, 0.5,
                                     ALU.mult, ALU.add))
            p.vv.inc(v.tensor_mul(bufs["gp2"][:, sl], bufs["gp2"][:, sl],
                                  apb[:, sl]))
        v.wait_ge(p.vv.sem, p.vv.n)
        for mc in range(2):
            sl = slice(mc * n, (mc + 1) * n)
            p.vv.inc(v.scalar_tensor_tensor(gp[:, sl], bufs["gp2"][:, sl],
                                            0.3989422804014327, gp[:, sl],
                                            ALU.mult, ALU.add))
    p.bar()

    pairs = []
    for mc in range(2):
        for t_ in range(nt_t):
            pairs.append((h_b[:, mc * n + t_ * 128: mc * n + t_ * 128 + 128],
                          h_tm[:, t_ * 256 + mc * 128: t_ * 256 + mc * 128 + 128]))
    transpose_tiles(p, pairs, identb, psA, psB)
    p.bar()

    for hc in range(DC):
        @p.blk.tensor
        def _(t):
            for mc in range(2):
                m = t.matmul(psum[:, 0:n],
                             pr["w2t_b"][:, mc * 1024 + hc * 128: mc * 1024 + hc * 128 + 128],
                             h_b[:, mc * n:(mc + 1) * n],
                             start=(mc == 0), stop=(mc == 1))
            p.mm.inc(m)

        @p.blk.scalar
        def _(s):
            s.wait_ge(p.mm.sem, p.mm.n)
            p.ss.inc(s.activation(psum[:, 0:n], psum[:, 0:n], AF.Identity,
                                  bias=pr["b2c"][:, hc:hc + 1]))

        @p.blk.vector
        def _(v):
            v.wait_ge(p.ss.sem, p.ss.n)
            p.vv.inc(v.tensor_tensor(rb[:, hc * n:(hc + 1) * n], psum[:, 0:n],
                                     zmy[:, hc * n:(hc + 1) * n], ALU.add))

        @p.blk.tensor
        def _(t):
            t.wait_ge(p.vv.sem, p.vv.n)
    p.bar()

    pairs = []
    for hc in range(DC):
        for t_ in range(nt_t):
            pairs.append((rb[:, hc * n + t_ * 128: hc * n + t_ * 128 + 128],
                          r_tm[:, t_ * 1024 + hc * 128: t_ * 1024 + hc * 128 + 128]))
    transpose_tiles(p, pairs, identb, psA, psB)
    p.bar()

    for mc in range(2):
        @p.blk.tensor
        def _(t):
            for hc in range(DC):
                m = t.matmul(psum[:, 0:n],
                             pr["w2_b"][:, hc * 256 + mc * 128: hc * 256 + mc * 128 + 128],
                             rb[:, hc * n:(hc + 1) * n],
                             start=(hc == 0), stop=(hc == DC - 1))
            p.mm.inc(m)

        @p.blk.vector
        def _(v):
            v.wait_ge(p.mm.sem, p.mm.n)
            p.vv.inc(v.tensor_tensor(da_b[:, mc * n:(mc + 1) * n], psum[:, 0:n],
                                     gp[:, mc * n:(mc + 1) * n], ALU.mult))

        @p.blk.tensor
        def _(t):
            t.wait_ge(p.vv.sem, p.vv.n)
    p.bar()

    pairs = []
    for mc in range(2):
        for t_ in range(nt_t):
            pairs.append((da_b[:, mc * n + t_ * 128: mc * n + t_ * 128 + 128],
                          da_tm[:, t_ * 256 + mc * 128: t_ * 256 + mc * 128 + 128]))
    transpose_tiles(p, pairs, identb, psA, psB)
    p.bar()

    for hc in range(DC):
        @p.blk.tensor
        def _(t):
            for t_ in range(nt_t):
                m = t.matmul(psum[:, 0:256],
                             z_tm[:, t_ * 1024 + hc * 128: t_ * 1024 + hc * 128 + 128],
                             da_tm[:, t_ * 256:(t_ + 1) * 256],
                             start=(t_ == 0), stop=(t_ == nt_t - 1))
            p.mm.inc(m)

        @p.blk.vector
        def _(v):
            v.wait_ge(p.mm.sem, p.mm.n)
            sl = slice(hc * 256, (hc + 1) * 256)
            p.vv.inc(v.tensor_tensor(bufs["acc_w1t"][:, sl], bufs["acc_w1t"][:, sl],
                                     psum[:, 0:256], ALU.add))

        @p.blk.tensor
        def _(t):
            t.wait_ge(p.vv.sem, p.vv.n)
            for t_ in range(nt_t):
                m = t.matmul(psum[:, 0:256],
                             r_tm[:, t_ * 1024 + hc * 128: t_ * 1024 + hc * 128 + 128],
                             h_tm[:, t_ * 256:(t_ + 1) * 256],
                             start=(t_ == 0), stop=(t_ == nt_t - 1))
            p.mm.inc(m)

        @p.blk.vector
        def _(v):
            v.wait_ge(p.mm.sem, p.mm.n)
            sl = slice(hc * 256, (hc + 1) * 256)
            p.vv.inc(v.tensor_tensor(bufs["acc_w2"][:, sl], bufs["acc_w2"][:, sl],
                                     psum[:, 0:256], ALU.add))

        @p.blk.tensor
        def _(t):
            t.wait_ge(p.vv.sem, p.vv.n)

    first = bufs["db_first"]

    @p.blk.tensor
    def _(t):
        for t_ in range(nt_t):
            st0 = first and t_ == 0
            sp = bool(bufs.get("db_stop")) and t_ == nt_t - 1
            t.matmul(bufs["psum_b1"][0:1, 0:256], p.c_onesb[:, 0:1],
                     da_tm[:, t_ * 256:(t_ + 1) * 256],
                     start=st0, stop=sp)
            t.matmul(bufs["psum_b2"][0:1, 0:512], p.c_onesb[:, 0:1],
                     r_tm[:, t_ * 1024:t_ * 1024 + 512],
                     start=st0, stop=sp)
            m = t.matmul(bufs["psum_b2"][0:1, 512:1024], p.c_onesb[:, 0:1],
                         r_tm[:, t_ * 1024 + 512:t_ * 1024 + 1024],
                         start=st0, stop=sp)
        p.mm.inc(m)
    bufs["db_first"] = False
    p.bar()



def build_launch_A(nt=NT):
    p = P()
    nch = nt // TC
    x_tm = p.inp("x_tm", [nt, DM])
    tk_in = p.inp("tk", [DM, DM]); tv_in = p.inp("tv", [DM, DM])
    tq_in = p.inp("tq", [DM, DM])
    w1_in = p.inp("w1", [256, DM]); w2_in = p.inp("w2", [DM, 256])
    b1_in = p.inp("b1", [1, 256]); b2_in = p.inp("b2", [1, DM])
    identf_in = p.inp("identf", [128, 128])
    identb_in = p.inp("identb", [128, 128], BF16)
    xfm_o = p.outp("xfm", [DM, nt])
    qfm_o = p.outp("qfm", [DM, nt])
    zbf_o = p.outp("zbf", [DM, nt], BF16)
    ztm_o = p.outp("ztm", [nt, DM], BF16)
    zmy_o = p.outp("zmyo", [DM, nt])
    gw1t_o = p.outp("gw1t", [128, DC * 256])
    gw2_o = p.outp("gw2", [128, DC * 256])
    gb1_o = p.outp("gb1", [1, 256])
    gb2_o = p.outp("gb2", [1, DM])

    p.begin()
    identf = p.sb("sb_identf", [128, 128], F32)
    identb = p.sb("sb_identb", [128, 128], BF16)

    @p.blk.sync
    def _(sy):
        p.dh.inc(sy.dma_start(out=identf[:], in_=identf_in), 16)
        p.dh.inc(sy.dma_start(out=identb[:], in_=identb_in), 16)

    pr = ttt_prep_params(p, w1_in, w2_in, b1_in, b2_in, identf, masters=False)

    tkT = p.sb("tkT", [128, DC * DM], BF16)
    tvT = p.sb("tvT", [128, DC * DM], BF16)
    tqT = p.sb("tqT", [128, DC * DM], BF16)
    psA = p.psA_b
    psB = p.psB_b
    psAf = p.psA_f
    psBf = p.psB_f
    with p.nc.sbuf_tensor("traw", [128, DC * DM], BF16) as traw:
        for t_in, tT in ((tk_in, tkT), (tv_in, tvT), (tq_in, tqT)):
            @p.blk.gpsimd
            def _(g):
                for c in range(DC):
                    p.dg.inc(g.dma_start(out=traw[:, c * DM:(c + 1) * DM],
                                         in_=t_in[c * 128:(c + 1) * 128, :]), 16)
            p.bar()
            pairs = []
            for hc in range(DC):
                for dc in range(DC):
                    pairs.append(
                        (traw[:, hc * DM + dc * 128: hc * DM + dc * 128 + 128],
                         tT[:, dc * DM + hc * 128: dc * DM + hc * 128 + 128]))
            transpose_tiles(p, pairs, identb, psA, psB)
            p.bar()

    acc_w1t = p.sb("acc_w1t", [128, DC * 256], F32)
    acc_w2 = p.sb("acc_w2", [128, DC * 256], F32)
    psum_b1 = p.pt("psum_b1", [1, 256])
    psum_b2 = p.pt("psum_b2", [1, 1024])

    @p.blk.vector
    def _(v):
        p.vv.inc(v.memset(acc_w1t[:], 0.0))
        p.vv.inc(v.memset(acc_w2[:], 0.0))
    p.bar()

    n = TC
    nt_t = n // 128
    xtmc = p.sb("xtmc", [128, nt_t * DM], F32)
    sqs = p.sb("sqs", [128, DM], F32)
    scr = {k: p.sb("lns_" + k, [128, 1], F32) for k in ("s", "q", "mu", "rstd")}
    xntm = p.sb("xntm", [128, nt_t * DM], BF16)
    xnfm = p.sb("xnfm", [128, DC * n], BF16)
    z_bf = p.sb("z_bf", [128, DC * n], BF16)
    z_tm = p.sb("z_tm", [128, nt_t * DM], BF16)
    zmy = p.sb("zmy", [128, DC * n], F32)
    zfr = p.sb("zfr", [128, n], F32)
    qfr = p.sb("qfr", [128, n], F32)
    apb = p.sb("apb", [128, 2 * n], F32)
    h_b = p.sb("h_b", [128, 2 * n], BF16)
    gp = p.sb("gp", [128, 2 * n], F32)
    gp2 = p.sb("gp2", [128, 2 * n], F32)
    rb = p.sb("rb", [128, DC * n], BF16)
    r_tm = p.sb("r_tm", [128, nt_t * DM], BF16)
    h_tm = p.sb("h_tm", [128, nt_t * 256], BF16)
    da_b = p.sb("da_b", [128, 2 * n], BF16)
    da_tm = p.sb("da_tm", [128, nt_t * 256], BF16)
    psum = p.ps_main
    psum2 = p.ps_main

    bufs = dict(z_bf=z_bf, z_tm=z_tm, zmy=zmy, apb=apb, h_b=h_b, gp=gp, gp2=gp2,
                rb=rb, r_tm=r_tm, h_tm=h_tm, da_b=da_b, da_tm=da_tm,
                psum=psum, psA=psA, psB=psB, identb=identb,
                acc_w1t=acc_w1t, acc_w2=acc_w2, psum_b1=psum_b1,
                psum_b2=psum_b2, db_first=True, db_stop=False)

    for ch in range(nch):
        if ch == nch - 1:
            bufs["db_stop"] = True
        t0 = ch * n

        @p.blk.gpsimd
        def _(g):
            for t_ in range(nt_t):
                p.dg.inc(g.dma_start(out=xtmc[:, t_ * DM:(t_ + 1) * DM],
                                     in_=x_tm[t0 + t_ * 128: t0 + (t_ + 1) * 128, :]), 16)
        p.bar()
        for t_ in range(nt_t):
            ln_tm_tile(p, xtmc[:, t_ * DM:(t_ + 1) * DM],
                       xntm[:, t_ * DM:(t_ + 1) * DM], sqs[:, :], scr)
            p.bar()
        pairs = []
        for t_ in range(nt_t):
            for dc in range(DC):
                pairs.append((xntm[:, t_ * DM + dc * 128: t_ * DM + dc * 128 + 128],
                              xnfm[:, dc * n + t_ * 128: dc * n + t_ * 128 + 128]))
        transpose_tiles(p, pairs, identb, psA, psB)
        xf_stage = zmy
        pairs = []
        for t_ in range(nt_t):
            for dc in range(DC):
                pairs.append((xtmc[:, t_ * DM + dc * 128: t_ * DM + dc * 128 + 128],
                              xf_stage[:, dc * n + t_ * 128: dc * n + t_ * 128 + 128]))
        transpose_tiles(p, pairs, identf, psAf, psBf)
        p.bar()

        @p.blk.sync
        def _(sy):
            for dc in range(DC):
                p.dh.inc(sy.dma_start(out=xfm_o[dc * 128:(dc + 1) * 128, t0:t0 + n],
                                      in_=xf_stage[:, dc * n:(dc + 1) * n]), 16)
        p.bar()

        for (tT, mode) in ((tkT, "z"), (tvT, "y"), (tqT, "q")):
            for hc in range(DC):
                @p.blk.tensor
                def _(t):
                    for dc in range(DC):
                        m = t.matmul(psum[:, 0:n],
                                     tT[:, dc * DM + hc * 128: dc * DM + hc * 128 + 128],
                                     xnfm[:, dc * n:(dc + 1) * n],
                                     start=(dc == 0), stop=(dc == DC - 1))
                    p.mm.inc(m)

                if mode == "z":
                    @p.blk.vector
                    def _(v):
                        v.wait_ge(p.mm.sem, p.mm.n)
                        p.vv.inc(v.tensor_copy(z_bf[:, hc * n:(hc + 1) * n],
                                               psum[:, 0:n]))

                    @p.blk.scalar
                    def _(s):
                        s.wait_ge(p.mm.sem, p.mm.n)
                        p.ss.inc(s.activation(zfr[:, 0:n], psum[:, 0:n], AF.Identity))

                    @p.blk.tensor
                    def _(t):
                        t.wait_ge(p.vv.sem, p.vv.n)
                        t.wait_ge(p.ss.sem, p.ss.n)
                        for dc in range(DC):
                            m = t.matmul(psum2[:, 0:n],
                                         tvT[:, dc * DM + hc * 128: dc * DM + hc * 128 + 128],
                                         xnfm[:, dc * n:(dc + 1) * n],
                                         start=(dc == 0), stop=(dc == DC - 1))
                        p.mm.inc(m)

                    @p.blk.vector
                    def _(v):
                        v.wait_ge(p.mm.sem, p.mm.n)
                        p.vv.inc(v.tensor_tensor(zmy[:, hc * n:(hc + 1) * n],
                                                 zfr[:, 0:n], psum2[:, 0:n],
                                                 ALU.subtract))

                    @p.blk.tensor
                    def _(t):
                        t.wait_ge(p.vv.sem, p.vv.n)
                elif mode == "q":
                    @p.blk.vector
                    def _(v):
                        v.wait_ge(p.mm.sem, p.mm.n)
                        p.vv.inc(v.tensor_copy(qfr[:, 0:n], psum[:, 0:n]))

                    @p.blk.sync
                    def _(sy):
                        sy.wait_ge(p.vv.sem, p.vv.n)
                        p.dh.inc(sy.dma_start(
                            out=qfm_o[hc * 128:(hc + 1) * 128, t0:t0 + n],
                            in_=qfr[:, 0:n]), 16)

                    @p.blk.tensor
                    def _(t):
                        t.wait_ge(p.dh.sem, p.dh.n)
        p.bar()

        for t_ in range(nt_t):
            for half in range(2):
                @p.blk.tensor
                def _(t):
                    for dc in range(DC):
                        m = t.matmul(psum[:, 0:512],
                                     xnfm[:, dc * n + t_ * 128: dc * n + t_ * 128 + 128],
                                     tkT[:, dc * DM + half * 512: dc * DM + (half + 1) * 512],
                                     start=(dc == 0), stop=(dc == DC - 1))
                    p.mm.inc(m)

                @p.blk.vector
                def _(v):
                    v.wait_ge(p.mm.sem, p.mm.n)
                    p.vv.inc(v.tensor_copy(
                        z_tm[:, t_ * DM + half * 512: t_ * DM + (half + 1) * 512],
                        psum[:, 0:512]))

                @p.blk.tensor
                def _(t):
                    t.wait_ge(p.vv.sem, p.vv.n)
        p.bar()

        @p.blk.gpsimd
        def _(g):
            for hc in range(DC):
                p.dg.inc(g.dma_start(out=zbf_o[hc * 128:(hc + 1) * 128, t0:t0 + n],
                                     in_=z_bf[:, hc * n:(hc + 1) * n]), 16)
                p.dg.inc(g.dma_start(out=zmy_o[hc * 128:(hc + 1) * 128, t0:t0 + n],
                                     in_=zmy[:, hc * n:(hc + 1) * n]), 16)
            for t_ in range(nt_t):
                p.dg.inc(g.dma_start(out=ztm_o[t0 + t_ * 128: t0 + (t_ + 1) * 128, :],
                                     in_=z_tm[:, t_ * DM:(t_ + 1) * DM]), 16)

        ttt_grad_chunk(p, pr, bufs, n)

    grow = p.sb("grow", [1, 1280], F32)

    @p.blk.vector
    def _(v):
        p.vv.inc(v.tensor_scalar_mul(acc_w1t[:], acc_w1t[:], CGRAD))
        p.vv.inc(v.tensor_scalar_mul(acc_w2[:], acc_w2[:], CGRAD))
        p.vv.inc(v.tensor_scalar_mul(grow[0:1, 1024:1280], psum_b1[0:1, :], CGRAD))
        p.vv.inc(v.tensor_scalar_mul(grow[0:1, 0:1024], psum_b2[0:1, :], CGRAD))
    p.bar()

    @p.blk.sync
    def _(sy):
        p.st.inc(sy.dma_start(out=gw1t_o, in_=acc_w1t[:]), 16)
        p.st.inc(sy.dma_start(out=gw2_o, in_=acc_w2[:]), 16)
        p.st.inc(sy.dma_start(out=gb1_o, in_=grow[0:1, 1024:1280]), 16)
        p.st.inc(sy.dma_start(out=gb2_o, in_=grow[0:1, 0:1024]), 16)
        p.st.wait(sy)

    @p.blk.gpsimd
    def _(g):
        p.dg.wait(g)

    @p.blk.sync
    def _(sy):
        p.dh.wait(sy)
    return p.done()



def build_launch_BC(which, nt=NT):
    p = P()
    nch = nt // TC
    xfm_i = p.inp("xfm", [DM, nt])
    qfm_i = p.inp("qfm", [DM, nt])
    zbf_i = p.inp("zbf", [DM, nt], BF16)
    ztm_i = p.inp("ztm", [nt, DM], BF16)
    zmy_i = p.inp("zmyo", [DM, nt])
    w1_in = p.inp("w1", [256, DM]); w2_in = p.inp("w2", [DM, 256])
    b1_in = p.inp("b1", [1, 256]); b2_in = p.inp("b2", [1, DM])
    identf_in = p.inp("identf", [128, 128])
    identb_in = p.inp("identb", [128, 128], BF16)
    gsets = []
    rounds = 1 if which == "B" else 2
    for r_ in range(rounds):
        for k in range(4):
            gsets.append({
                "w1t": p.inp(f"g{r_}w1t_{k}", [128, DC * 256]),
                "w2": p.inp(f"g{r_}w2_{k}", [128, DC * 256]),
                "b1": p.inp(f"g{r_}b1_{k}", [1, 256]),
                "b2": p.inp(f"g{r_}b2_{k}", [1, DM]),
            })
    xnew_o = p.outp("xnew", [DM, nt])
    if which == "B":
        gw1t_o = p.outp("gw1t", [128, DC * 256])
        gw2_o = p.outp("gw2", [128, DC * 256])
        gb1_o = p.outp("gb1", [1, 256])
        gb2_o = p.outp("gb2", [1, DM])

    p.begin()
    identf = p.sb("sb_identf", [128, 128], F32)
    identb = p.sb("sb_identb", [128, 128], BF16)

    @p.blk.sync
    def _(sy):
        p.dh.inc(sy.dma_start(out=identf[:], in_=identf_in), 16)
        p.dh.inc(sy.dma_start(out=identb[:], in_=identb_in), 16)

    pr = ttt_prep_params(p, w1_in, w2_in, b1_in, b2_in, identf, masters=True)
    pr["identf"] = identf
    pr["psA"] = p.psA_f
    pr["psB"] = p.psB_f
    pr["pcol"] = p.psA_f

    _gldsc = ExitStack()
    gsum = {
        "gw1t": _gldsc.enter_context(p.nc.sbuf_tensor("gsum_w1t", [128, DC * 256], F32)),
        "gw2": _gldsc.enter_context(p.nc.sbuf_tensor("gsum_w2", [128, DC * 256], F32)),
        "gb1": _gldsc.enter_context(p.nc.sbuf_tensor("gsum_b1", [1, 256], F32)),
        "gb2": _gldsc.enter_context(p.nc.sbuf_tensor("gsum_b2", [1, DM], F32)),
    }
    gld = {
        "w1t": _gldsc.enter_context(p.nc.sbuf_tensor("gld_w1t", [128, DC * 256], F32)),
        "w2": _gldsc.enter_context(p.nc.sbuf_tensor("gld_w2", [128, DC * 256], F32)),
        "b1": _gldsc.enter_context(p.nc.sbuf_tensor("gld_b1", [1, 256], F32)),
        "b2": _gldsc.enter_context(p.nc.sbuf_tensor("gld_b2", [1, DM], F32)),
    }
    for r_ in range(rounds):
        for k in range(4):
            gs = gsets[r_ * 4 + k]

            @p.blk.sync
            def _(sy):
                p.dh.inc(sy.dma_start(out=gld["w1t"][:], in_=gs["w1t"]), 16)
                p.dh.inc(sy.dma_start(out=gld["w2"][:], in_=gs["w2"]), 16)
                p.dh.inc(sy.dma_start(out=gld["b1"][:], in_=gs["b1"]), 16)
                p.dh.inc(sy.dma_start(out=gld["b2"][:], in_=gs["b2"]), 16)
            p.bar()

            @p.blk.vector
            def _(v):
                for key, big in (("gw1t", "w1t"), ("gw2", "w2"),
                                 ("gb1", "b1"), ("gb2", "b2")):
                    rows = gsum[key].shape[0]
                    w = gsum[key].shape[1]
                    if k == 0:
                        p.vv.inc(v.tensor_copy(gsum[key][:], gld[big][0:rows, 0:w]))
                    else:
                        p.vv.inc(v.tensor_tensor(gsum[key][:], gsum[key][:],
                                                 gld[big][0:rows, 0:w], ALU.add))
                    v.wait_ge(p.vv.sem, p.vv.n)
            p.bar()
        ttt_update_params(p, pr, gsum)
    _gldsc.close()

    n = TC
    nt_t = n // 128
    xfmc = p.sb("xfmc", [128, DC * n], F32)
    qfc = p.sb("qfc", [128, DC * n], F32)
    qbc = p.sb("qbc", [128, DC * n], BF16)
    outr = p.sb("outr", [128, DC * n], F32)
    apb = p.sb("apb", [128, 2 * n], F32)
    h_b = p.sb("h_b", [128, 2 * n], BF16)
    psum = p.ps_main
    lnb_bufs = {
        "sq": p.sb("lnf_sq", [128, 2 * n], F32),
        "tmp": p.sb("lnf_tmp", [128, n], F32),
        "mu": p.sb("lnf_mu", [1, n], F32),
        "rstd": p.sb("lnf_rstd", [1, n], F32),
        "mub": p.sb("lnf_mub", [128, n], F32),
        "rsb": p.sb("lnf_rsb", [128, n], F32),
    }
    if which == "B":
        z_bf = p.sb("z_bf", [128, DC * n], BF16)
        z_tm = p.sb("z_tm", [128, nt_t * DM], BF16)
        zmy = p.sb("zmy", [128, DC * n], F32)
        gp = p.sb("gp", [128, 2 * n], F32)
        gp2 = p.sb("gp2", [128, 2 * n], F32)
        rb = p.sb("rb", [128, DC * n], BF16)
        r_tm = p.sb("r_tm", [128, nt_t * DM], BF16)
        h_tm = p.sb("h_tm", [128, nt_t * 256], BF16)
        da_b = p.sb("da_b", [128, 2 * n], BF16)
        da_tm = p.sb("da_tm", [128, nt_t * 256], BF16)
        acc_w1t = p.sb("acc_w1t", [128, DC * 256], F32)
        acc_w2 = p.sb("acc_w2", [128, DC * 256], F32)
        psum_b1 = p.pt("psum_b1", [1, 256])
        psum_b2 = p.pt("psum_b2", [1, 1024])

        @p.blk.vector
        def _(v):
            p.vv.inc(v.memset(acc_w1t[:], 0.0))
            p.vv.inc(v.memset(acc_w2[:], 0.0))
        p.bar()
        bufs = dict(z_bf=z_bf, z_tm=z_tm, zmy=zmy, apb=apb, h_b=h_b, gp=gp,
                    gp2=gp2, rb=rb, r_tm=r_tm, h_tm=h_tm, da_b=da_b,
                    da_tm=da_tm, psum=psum, psA=p.psA_b, psB=p.psB_b,
                    identb=identb, acc_w1t=acc_w1t, acc_w2=acc_w2,
                    psum_b1=psum_b1, psum_b2=psum_b2,
                    db_first=True, db_stop=False)

    for ch in range(nch):
        t0 = ch * n

        @p.blk.gpsimd
        def _(g):
            for hc in range(DC):
                p.dg.inc(g.dma_start(out=xfmc[:, hc * n:(hc + 1) * n],
                                     in_=xfm_i[hc * 128:(hc + 1) * 128, t0:t0 + n]), 16)
                p.dg.inc(g.dma_start(out=qfc[:, hc * n:(hc + 1) * n],
                                     in_=qfm_i[hc * 128:(hc + 1) * 128, t0:t0 + n]), 16)
        p.bar()

        @p.blk.vector
        def _(v):
            for hc in range(DC):
                p.vv.inc(v.tensor_copy(qbc[:, hc * n:(hc + 1) * n],
                                       qfc[:, hc * n:(hc + 1) * n]))
        p.bar()

        ttt_recon_fwd(p, pr, qbc, lambda hc: hc * n, n, apb, h_b, psum)
        p.bar()
        ttt_recon_out(p, pr, h_b, n, qfc, lambda hc: hc * n,
                      outr, lambda hc: hc * n, psum)
        p.bar()

        ln_fm(p, lambda c: c * n, n, lambda c: c * n, lambda c: c * n,
              lnb_bufs, src_sb=outr, dst_sb=outr, x_sb=xfmc)
        p.bar()

        @p.blk.sync
        def _(sy):
            for hc in range(DC):
                p.dh.inc(sy.dma_start(out=xnew_o[hc * 128:(hc + 1) * 128, t0:t0 + n],
                                      in_=outr[:, hc * n:(hc + 1) * n]), 16)

        if which == "B":
            if ch == nch - 1:
                bufs["db_stop"] = True

            @p.blk.gpsimd
            def _(g):
                for hc in range(DC):
                    p.dg.inc(g.dma_start(out=z_bf[:, hc * n:(hc + 1) * n],
                                         in_=zbf_i[hc * 128:(hc + 1) * 128, t0:t0 + n]), 16)
                    p.dg.inc(g.dma_start(out=zmy[:, hc * n:(hc + 1) * n],
                                         in_=zmy_i[hc * 128:(hc + 1) * 128, t0:t0 + n]), 16)
                for t_ in range(nt_t):
                    p.dg.inc(g.dma_start(out=z_tm[:, t_ * DM:(t_ + 1) * DM],
                                         in_=ztm_i[t0 + t_ * 128:t0 + (t_ + 1) * 128, :]), 16)
            p.bar()
            ttt_grad_chunk(p, pr, bufs, n)
        else:
            p.bar()

    if which == "B":
        grow = p.sb("grow", [1, 1280], F32)

        @p.blk.vector
        def _(v):
            p.vv.inc(v.tensor_scalar_mul(acc_w1t[:], acc_w1t[:], CGRAD))
            p.vv.inc(v.tensor_scalar_mul(acc_w2[:], acc_w2[:], CGRAD))
            p.vv.inc(v.tensor_scalar_mul(grow[0:1, 1024:1280], psum_b1[0:1, :], CGRAD))
            p.vv.inc(v.tensor_scalar_mul(grow[0:1, 0:1024], psum_b2[0:1, :], CGRAD))
        p.bar()

        @p.blk.sync
        def _(sy):
            p.st.inc(sy.dma_start(out=gw1t_o, in_=acc_w1t[:]), 16)
            p.st.inc(sy.dma_start(out=gw2_o, in_=acc_w2[:]), 16)
            p.st.inc(sy.dma_start(out=gb1_o, in_=grow[0:1, 1024:1280]), 16)
            p.st.inc(sy.dma_start(out=gb2_o, in_=grow[0:1, 0:1024]), 16)
            p.st.wait(sy)

    @p.blk.gpsimd
    def _(g):
        p.dg.wait(g)

    @p.blk.sync
    def _(sy):
        p.dh.wait(sy)
    return p.done()



def build_launch_D(nt=NT):
    KR = 512 + nt
    QR = 256 + nt
    NQW = QR // WIN
    XN = HALO + nt
    MR = 1 + nt
    ach = f_chunks(DIN, 128)
    NA = len(ach)
    nkt = KR // 128

    p = P()
    xin = p.inp("xin", [DM, XN])
    wqkv_i = p.inp("wqkv", [DM, 3 * 1024])
    wo_i = p.inp("wo", [1024, DM])
    w1_i = p.inp("w1", [DM, 2 * DIN])
    w2_i = p.inp("w2", [DIN, DM])
    b1_i = p.inp("b1", [1, 2 * DIN])
    b2_i = p.inp("b2", [1, DM])
    lng_i = p.inp("lng", [1, DM])
    lnb_i = p.inp("lnb", [1, DM])
    qsc_i = p.inp("qscol", [128, 1])
    ksc_i = p.inp("kscol", [128, 1])
    rq_c = p.inp("rotqc", [128, KR]); rq_s = p.inp("rotqs", [128, KR])
    rk_c = p.inp("rotkc", [128, KR]); rk_s = p.inp("rotks", [128, KR])
    masks_i = p.inp("masks", [NQW, 4, 128, WIN], BF16)
    hval_i = p.inp("hvalid", [128, 1])
    sel2_i = p.inp("sel2c", [2, 128])
    identb_in = p.inp("identb", [128, 128], BF16)
    xout = p.outp("xout", [DM, nt])
    xmid_hbm = p.nc.dram_tensor("xmid_hbm", [DM, MR], F32).ap()

    p.begin(pool="tiny")
    pnorm = p.pt("pnorm", [2, 512])
    psim = p.pt("psim", [128, 2048])
    poutA = p.pt("poutA", [65, 512])
    poutB = p.pt("poutB", [65, 512])

    identb = p.sb("sb_identb", [128, 128], BF16)
    qsc = p.sb("qsc", [128, 1], F32)
    ksc = p.sb("ksc", [128, 1], F32)
    hval = p.sb("hval", [128, 1], F32)
    lngc = p.sb("lngc", [128, DC], F32)
    lnbc = p.sb("lnbc", [128, DC], F32)
    b2c = p.sb("b2c", [128, DC], F32)
    b1ca = p.sb("b1ca", [128, NA], F32)
    b1cg = p.sb("b1cg", [128, NA], F32)
    hselb = p.sb("hselb", [128, 2], BF16)
    sel2 = p.sb("sel2", [2, 128], F32)
    onesq = p.sb("onesq", [1, 512], F32)
    ones256b = p.sb("ones256b", [1, 256], BF16)
    p0s = ExitStack()
    b1r = p0s.enter_context(p.nc.sbuf_tensor("b1r", [1, 2 * DIN], F32))

    @p.blk.gpsimd
    def _(g):
        p.dg.inc(g.dma_start(out=qsc[:], in_=qsc_i), 16)
        p.dg.inc(g.dma_start(out=ksc[:], in_=ksc_i), 16)

        p.dg.inc(g.dma_start(out=hval[:], in_=hval_i), 16)
        p.dg.inc(g.dma_start(out=b1r[:], in_=b1_i), 16)
        p.dg.inc(g.dma_start(out=identb[:], in_=identb_in), 16)
        p.dg.inc(g.dma_start(out=sel2[:], in_=sel2_i), 16)
        p.pg.inc(g.memset(hselb[:], 0.0))
        p.pg.inc(g.memset(onesq[:], 1.0))
        p.pg.inc(g.memset(ones256b[:], 1.0))
        g.wait_ge(p.pg.sem, p.pg.n)
        p.pg.inc(g.memset(hselb[0:64, 0:1], 1.0))
        p.pg.inc(g.memset(hselb[64:128, 1:2], 1.0))
    p.bar()

    @p.blk.vector
    def _(v):
        p.vv.inc(v.tensor_scalar_mul(qsc[:], qsc[:], QKSC))
    p.bar()

    rowbuf = p0s.enter_context(p.nc.sbuf_tensor("rowbuf", [1, 3 * DM], F32))

    @p.blk.sync
    def _(sy):
        p.dh.inc(sy.dma_start(out=rowbuf[0:1, 0:DM], in_=lng_i), 16)
        p.dh.inc(sy.dma_start(out=rowbuf[0:1, DM:2 * DM], in_=lnb_i), 16)
        p.dh.inc(sy.dma_start(out=rowbuf[0:1, 2 * DM:3 * DM], in_=b2_i), 16)
    p.bar()
    row_to_cols(p, rowbuf[0:1, 0:DM], DC, lngc, 0, p.ps_main)
    row_to_cols(p, rowbuf[0:1, DM:2 * DM], DC, lnbc, 0, p.ps_main)
    row_to_cols(p, rowbuf[0:1, 2 * DM:3 * DM], DC, b2c, 0, p.ps_main)
    p.bar()
    for j, (off, size) in enumerate(ach):
        for (roff, dst) in ((off, b1ca), (DIN + off, b1cg)):
            @p.blk.tensor
            def _(t, roff=roff, size=size):
                t.wait_ge(p.vv.sem, p.vv.n)
                p.mm.inc(t.matmul(p.ps_main[0:size, 0:1], b1r[0:1, roff:roff + size],
                                  p.c_onesf[0:1, 0:1], start=True, stop=True))

            @p.blk.vector
            def _(v, dst=dst, size=size, j=j):
                v.wait_ge(p.mm.sem, p.mm.n)
                p.vv.inc(v.tensor_copy(dst[0:size, j:j + 1], p.ps_main[0:size, 0:1]))
    p.bar()
    p0s.close()

    _s0 = p.nc.enter_named_scope("P1", False)[0]
    pqk = ExitStack()

    def qsb(name, shape, dtype):
        return pqk.enter_context(p.nc.sbuf_tensor(name, shape, dtype))

    q2 = qsb("q2", [128, DC * QR], BF16)
    k2 = qsb("k2", [128, DC * KR], BF16)
    v_km = qsb("v_km", [128, nkt * 1040], BF16)
    qnr = qsb("qnr", [2, DC * NQW], F32)
    knr = qsb("knr", [2, DC * (NQW + 1)], F32)
    mnegb = qsb("mnegb", [1, HEADS * WIN], BF16)
    mvals1 = qsb("mvals1", [1, HEADS * NQW], F32)
    mrow = qsb("mrow", [2, NQW], F32)

    p1s = ExitStack()
    xb = p1s.enter_context(p.nc.sbuf_tensor("xb", [128, DC * XN], BF16))

    @p.blk.gpsimd
    def _(g):
        for c in range(DC):
            p.dg.inc(g.dma_start(out=xb[:, c * XN:(c + 1) * XN],
                                 in_=xin[c * 128:(c + 1) * 128, :]), 16)
    p.bar()

    def xs_ap(c, i0, ncols):
        off = (1 if c < 4 else 0) + i0
        return xb[:, c * XN + off: c * XN + off + ncols]

    p1b = ExitStack()

    def usb(name, shape, dtype):
        return p1b.enter_context(p.nc.sbuf_tensor(name, shape, dtype))

    rt_c = usb("rt_c", [128, KR], F32)
    rt_s = usb("rt_s", [128, KR], F32)
    permb = usb("permb", [128, 128], BF16)
    qraw = usb("qraw", [128, 512], F32)
    sqs = usb("sqs", [128, 512], BF16)
    u_f = usb("u_f", [128, 512], F32)
    swp = usb("swp", [128, 512], BF16)
    nrm = usb("nrm", [2, 512], F32)
    wq1 = usb("wq1", [128, DC * 128], BF16)

    @p.blk.vector
    def _(v):
        p.pg.wait(v)
        p.dg.wait(v)
        p.vv.inc(v.tensor_copy(permb[0:32, :], identb[32:64, :]))
        p.vv.inc(v.tensor_copy(permb[32:64, :], identb[0:32, :]))
        p.vv.inc(v.tensor_copy(permb[64:96, :], identb[96:128, :]))
        p.vv.inc(v.tensor_copy(permb[96:128, :], identb[64:96, :]))
    p.bar()

    for side in ("q", "k"):
        NS = QR if side == "q" else KR
        qoff = 256 if side == "q" else 0
        wcol0 = 0 if side == "q" else 1024
        dst = q2 if side == "q" else k2
        nr_dst = qnr if side == "q" else knr
        nwin = NQW if side == "q" else NQW + 1
        scol = qsc if side == "q" else ksc
        rc_i = rq_c if side == "q" else rk_c
        rs_i = rq_s if side == "q" else rk_s

        @p.blk.sync
        def _(sy, rc_i=rc_i, rs_i=rs_i):
            sy.wait_ge(p.vv.sem, p.vv.n)
            sy.wait_ge(p.mm.sem, p.mm.n)
            p.dh.inc(sy.dma_start(out=rt_c[:], in_=rc_i), 16)
            p.dh.inc(sy.dma_start(out=rt_s[:], in_=rs_i), 16)
        p.bar()
        for (o, nn) in f_chunks(KR, 512):
            @p.blk.vector
            def _(v, o=o, nn=nn):
                v.wait_ge(p.vv.sem, p.vv.n)
                v.wait_ge(p.mm.sem, p.mm.n)
                p.vv.inc(v.tensor_copy(p.ps_main[0:128, 0:nn], rt_s[:, o:o + nn]))
                v.wait_ge(p.vv.sem, p.vv.n)
                for base in (0, 64):
                    p.vv.inc(v.tensor_scalar_mul(
                        rt_s[base:base + 32, o:o + nn],
                        p.ps_main[base + 32:base + 64, 0:nn], 1.0))
                    p.vv.inc(v.tensor_scalar_mul(
                        rt_s[base + 32:base + 64, o:o + nn],
                        p.ps_main[base:base + 32, 0:nn], -1.0))
                v.wait_ge(p.vv.sem, p.vv.n)
        p.bar()

        for fc in range(DC):
            @p.blk.gpsimd
            def _(g, fc=fc):
                g.wait_ge(p.mm.sem, p.mm.n)
                p.dg.inc(g.dma_start(
                    out=wq1[:].rearrange("p (c f) -> p c f", c=DC),
                    in_=wqkv_i[:, wcol0 + fc * 128: wcol0 + (fc + 1) * 128]
                    .rearrange("(c p) f -> p c f", p=128)), 16)
            dgw = p.dg.n

            for ci, (o, nn) in enumerate(f_chunks(NS, 512)):
                @p.blk.tensor
                def _(t, o=o, nn=nn):
                    t.wait_ge(p.dg.sem, dgw)
                    t.wait_ge(p.vv.sem, p.vv.n)
                    t.wait_ge(p.ss.sem, p.ss.n)
                    for c in range(DC):
                        m = t.matmul(p.ps_main[:, 0:nn],
                                     wq1[:, c * 128:(c + 1) * 128],
                                     xs_ap(c, qoff + o, nn),
                                     start=(c == 0), stop=(c == DC - 1))
                    p.mm.inc(m)

                @p.blk.vector
                def _(v, nn=nn):
                    v.wait_ge(p.mm.sem, p.mm.n)
                    p.vv.inc(v.tensor_copy(qraw[:, 0:nn], p.ps_main[:, 0:nn]))

                @p.blk.scalar
                def _(s, nn=nn):
                    s.wait_ge(p.vv.sem, p.vv.n)
                    p.ss.inc(s.activation(sqs[:, 0:nn], qraw[:, 0:nn], AF.Square))

                @p.blk.tensor
                def _(t, nn=nn):
                    t.wait_ge(p.ss.sem, p.ss.n)
                    p.mm.inc(t.matmul(pnorm[0:2, 0:nn], hselb[:, 0:2], sqs[:, 0:nn],
                                      start=True, stop=True))

                @p.blk.vector
                def _(v, nn=nn):
                    v.wait_ge(p.mm.sem, p.mm.n)
                    p.vv.inc(v.tensor_scalar_max(nrm[:, 0:nn], pnorm[0:2, 0:nn],
                                                 1e-24))

                @p.blk.scalar
                def _(s, nn=nn):
                    s.wait_ge(p.vv.sem, p.vv.n)
                    p.ss.inc(s.activation(nrm[:, 0:nn], nrm[:, 0:nn], AF.Sqrt))

                @p.blk.vector
                def _(v, nn=nn):
                    v.wait_ge(p.ss.sem, p.ss.n)
                    p.vv.inc(v.tensor_scalar_max(nrm[:, 0:nn], nrm[:, 0:nn], 1e-12))
                    v.wait_ge(p.vv.sem, p.vv.n)
                    p.vv.inc(v.reciprocal(nrm[:, 0:nn], nrm[:, 0:nn]))
                vrec = p.vv.n

                @p.blk.tensor
                def _(t, nn=nn):
                    t.wait_ge(p.vv.sem, vrec)
                    p.mm.inc(t.matmul(p.ps_main[:, 0:nn], sel2[0:2, :],
                                      nrm[0:2, 0:nn], start=True, stop=True))
                mmrb = p.mm.n

                @p.blk.vector
                def _(v, nn=nn):
                    v.wait_ge(p.mm.sem, mmrb)
                    p.vv.inc(v.scalar_tensor_tensor(u_f[:, 0:nn], qraw[:, 0:nn],
                                                    scol[:, 0:1], p.ps_main[:, 0:nn],
                                                    ALU.mult, ALU.mult))
                @p.blk.scalar
                def _(s, nn=nn):
                    s.wait_ge(p.vv.sem, p.vv.n)
                    p.ss.inc(s.activation(sqs[:, 0:nn], u_f[:, 0:nn], AF.Square))

                @p.blk.tensor
                def _(t, nn=nn):
                    t.wait_ge(p.ss.sem, p.ss.n)
                    p.mm.inc(t.matmul(pnorm[0:2, 0:nn], hselb[:, 0:2], sqs[:, 0:nn],
                                      start=True, stop=True))
                mmn2 = p.mm.n

                @p.blk.vector
                def _(v, o=o, nn=nn):
                    v.wait_ge(p.mm.sem, mmn2)
                    w0 = o // WIN
                    for wl in range(nn // WIN):
                        p.vv.inc(v.tensor_reduce(
                            nr_dst[:, (fc * nwin + w0 + wl):(fc * nwin + w0 + wl) + 1],
                            pnorm[0:2, wl * WIN:(wl + 1) * WIN], AX.X, ALU.max))

                @p.blk.vector
                def _(v, o=o, nn=nn):
                    v.wait_ge(p.vv.sem, p.vv.n)
                    p.vv.inc(v.tensor_mul(swp[:, 0:nn], u_f[:, 0:nn],
                                          rt_s[:, qoff + o:qoff + o + nn]))
                    p.vv.inc(v.tensor_mul(qraw[:, 0:nn], u_f[:, 0:nn],
                                          rt_c[:, qoff + o:qoff + o + nn]))
                vrot = p.vv.n

                @p.blk.tensor
                def _(t, nn=nn):
                    t.wait_ge(p.vv.sem, vrot)
                    p.mm.inc(t.matmul(p.ps_main[:, 0:nn], permb[:, :],
                                      swp[:, 0:nn], start=True, stop=True))
                mmsw = p.mm.n

                @p.blk.vector
                def _(v, o=o, nn=nn, dst=dst):
                    v.wait_ge(p.mm.sem, mmsw)
                    p.vv.inc(v.tensor_tensor(dst[:, fc * NS + o: fc * NS + o + nn],
                                             p.ps_main[:, 0:nn], qraw[:, 0:nn],
                                             ALU.add))
                    v.wait_ge(p.vv.sem, p.vv.n)
        p.bar()

    @p.blk.gpsimd
    def _(g):
        g.wait_ge(p.vv.sem, p.vv.n)
        p.pg.inc(g.memset(
            v_km[:].rearrange("p (t u) -> p t u", u=65)[:, :, 64:65], 1.0))
    p.bar()
    for vc in range(8):
        @p.blk.gpsimd
        def _(g, vc=vc):
            g.wait_ge(p.mm.sem, p.mm.n)
            p.dg.inc(g.dma_start(
                out=wq1[:].rearrange("p (c f) -> p c f", c=DC),
                in_=wqkv_i[:, 2048 + vc * 128: 2048 + (vc + 1) * 128]
                .rearrange("(c p) f -> p c f", p=128)), 16)
        dgv = p.dg.n

        for kt in range(nkt):
            @p.blk.tensor
            def _(t, vc=vc, kt=kt):
                t.wait_ge(p.dg.sem, dgv)
                t.wait_ge(p.vv.sem, p.vv.n)
                for c in range(DC):
                    m = t.matmul(p.ps_main[:, 0:128],
                                 xs_ap(c, kt * 128, 128),
                                 wq1[:, c * 128:(c + 1) * 128],
                                 start=(c == 0), stop=(c == DC - 1))
                p.mm.inc(m)

            @p.blk.vector
            def _(v, vc=vc, kt=kt):
                v.wait_ge(p.mm.sem, p.mm.n)
                dst = v_km[:].rearrange("p (t u) -> p t u", u=65)[
                    :, kt * 16 + 2 * vc: kt * 16 + 2 * vc + 2, 0:64]
                p.vv.inc(v.tensor_copy(
                    dst, p.ps_main[:, 0:128].rearrange("p (a b) -> p a b", a=2)))
    p.bar()
    p1b.close()
    p1s.close()
    att = pqk.enter_context(p.nc.sbuf_tensor("att", [128, DC * QR], BF16))

    p.nc.leave_named_scope("P1", _s0, False)
    _s1 = p.nc.enter_named_scope("P2", False)[0]
    p2s = ExitStack()
    masks = p2s.enter_context(p.nc.sbuf_tensor("masks_sb", [128, NQW * 4 * WIN], BF16))
    p_sb = p2s.enter_context(p.nc.sbuf_tensor("p_sb", [128, 2048], BF16))
    rcro = p2s.enter_context(p.nc.sbuf_tensor("rcro", [1, 256], F32))
    rbro = p2s.enter_context(p.nc.sbuf_tensor("rbro", [64, 256], F32))

    @p.blk.sync
    def _(sy):
        p.dh.inc(sy.dma_start(
            out=masks[:].rearrange("p (w k f) -> p w k f", w=NQW, k=4),
            in_=masks_i.rearrange("w k p f -> p w k f")), 16)

    @p.blk.scalar
    def _(s):
        s.wait_ge(p.vv.sem, p.vv.n)
        s.wait_ge(p.mm.sem, p.mm.n)
        p.ss.inc(s.activation(qnr[:], qnr[:], AF.Sqrt))
        p.ss.inc(s.activation(knr[:], knr[:], AF.Sqrt))
    p.bar()
    for fc in range(DC):
        @p.blk.vector
        def _(v, fc=fc):
            v.wait_ge(p.vv.sem, p.vv.n)
            p.vv.inc(v.tensor_tensor(mrow[:, 0:NQW],
                                     knr[:, fc * (NQW + 1): fc * (NQW + 1) + NQW],
                                     knr[:, fc * (NQW + 1) + 1: fc * (NQW + 1) + NQW + 1],
                                     ALU.max))
            v.wait_ge(p.vv.sem, p.vv.n)
            p.vv.inc(v.tensor_tensor(mrow[:, 0:NQW], mrow[:, 0:NQW],
                                     qnr[:, fc * NQW:(fc + 1) * NQW], ALU.mult))
            v.wait_ge(p.vv.sem, p.vv.n)
            p.vv.inc(v.tensor_scalar(mrow[:, 0:NQW], mrow[:, 0:NQW],
                                     -1.0, 30.0, ALU.mult, ALU.add))
        vmr = p.vv.n

        for hh in range(2):
            h = fc * 2 + hh
            @p.blk.tensor
            def _(t, hh=hh):
                t.wait_ge(p.vv.sem, p.vv.n)
                p.mm.inc(t.matmul(pnorm[0:1, 0:NQW], sel2[0:2, hh * 64:hh * 64 + 1],
                                  mrow[0:2, 0:NQW], start=True, stop=True))
            mmx = p.mm.n

            @p.blk.vector
            def _(v, h=h):
                v.wait_ge(p.mm.sem, mmx)
                p.vv.inc(v.tensor_copy(mvals1[0:1, h * NQW:(h + 1) * NQW],
                                       pnorm[0:1, 0:NQW]))
    p.bar()

    mcolall = p2s.enter_context(p.nc.sbuf_tensor("mcolall", [128, HEADS * NQW], F32))

    @p.blk.tensor
    def _(t):
        t.wait_ge(p.vv.sem, p.vv.n)
        p.mm.inc(t.matmul(p.ps_main[:, 0:HEADS * NQW], p.c_onesr_f[0:1, :],
                          mvals1[0:1, :], start=True, stop=True))

    @p.blk.vector
    def _(v):
        v.wait_ge(p.mm.sem, p.mm.n)
        p.vv.inc(v.tensor_copy(mcolall[:], p.ps_main[:, 0:HEADS * NQW]))
    p.bar()

    expm = {}
    pvm = {}
    mskm = {}
    evacm = {}
    pouts = (poutA, poutB)
    NIT = NQW * HEADS

    def emit_pv(i):
        w_, h_ = i // HEADS, i % HEADS
        po = pouts[i % 2]
        pp_sl = p_sb[:, (i % 2) * 1024:(i % 2 + 1) * 1024]

        @p.blk.tensor
        def _(t, i=i, w_=w_, h_=h_, po=po, pp_sl=pp_sl):
            t.wait_ge(p.ss.sem, expm[i])
            if i - 2 in evacm:
                t.wait_ge(p.vv.sem, evacm[i - 2])
            for kc in range(4):
                kt = 2 * w_ + kc
                m = t.matmul(po[:, 0:256],
                             v_km[:, kt * 1040 + h_ * 65: kt * 1040 + h_ * 65 + 65],
                             pp_sl[:, kc * 256:(kc + 1) * 256],
                             start=(kc == 0), stop=(kc == 3))
            pvm[i] = p.mm.inc(m)

    def emit_evac(i):
        w_, h_ = i // HEADS, i % HEADS
        fc_, pb_ = h_ // 2, (h_ % 2) * 64
        po = pouts[i % 2]

        @p.blk.vector
        def _(v, i=i, po=po):
            v.wait_ge(p.mm.sem, pvm[i])
            v.wait_ge(p.vv.sem, p.vv.n)
            p.vv.inc(v.reciprocal(rcro[0:1, :], po[64:65, 0:256]))
        vrc = p.vv.n

        @p.blk.tensor
        def _(t, vrc=vrc):
            t.wait_ge(p.vv.sem, p.vv.n)
            p.mm.inc(t.matmul(p.ps_main[0:64, 0:256], onesq[0:1, 0:64],
                              rcro[0:1, :], start=True, stop=True))
        mmbc = p.mm.n

        @p.blk.vector
        def _(v, i=i, w_=w_, fc_=fc_, pb_=pb_, po=po, mmbc=mmbc):
            v.wait_ge(p.mm.sem, mmbc)
            p.vv.inc(v.tensor_copy(rbro[:, :], p.ps_main[0:64, 0:256]))
            v.wait_ge(p.vv.sem, p.vv.n)
            p.vv.inc(v.tensor_tensor(
                att[pb_:pb_ + 64, fc_ * QR + w_ * WIN: fc_ * QR + (w_ + 1) * WIN],
                po[0:64, 0:256], rbro[:, :], ALU.mult))
        evacm[i] = p.vv.n

    for w in range(NQW):
        for h in range(HEADS):
            i = w * HEADS + h
            fc, pb = h // 2, (h % 2) * 64
            par = i % 2
            ps_sl = psim[:, par * 1024:(par + 1) * 1024]
            pp_sl = p_sb[:, par * 1024:(par + 1) * 1024]

            @p.blk.tensor
            def _(t, w=w, h=h, fc=fc, pb=pb, ps_sl=ps_sl, i=i):
                if i >= 2:
                    t.wait_ge(p.ss.sem, expm[i - 2])
                else:
                    t.wait_ge(p.vv.sem, p.vv.n)
                    t.wait_ge(p.ss.sem, p.ss.n)
                for kc in range(4):
                    m = t.matmul(ps_sl[:, kc * 256:(kc + 1) * 256],
                                 k2[pb:pb + 64,
                                    fc * KR + w * WIN + kc * 128: fc * KR + w * WIN + kc * 128 + 128],
                                 q2[pb:pb + 64, fc * QR + w * WIN: fc * QR + (w + 1) * WIN],
                                 start=True, stop=True)
                p.mm.inc(m)
            mmsim = p.mm.n

            @p.blk.vector
            def _(v, w=w, ps_sl=ps_sl, mmsim=mmsim):
                v.wait_ge(p.mm.sem, mmsim)
                p.vv.inc(v.tensor_tensor(ps_sl, ps_sl,
                                         masks[:, w * 1024:(w + 1) * 1024],
                                         ALU.add))
            mskm[i] = p.vv.n

            @p.blk.scalar
            def _(s, i=i, w=w, h=h, ps_sl=ps_sl, pp_sl=pp_sl):
                s.wait_ge(p.vv.sem, mskm[i])
                if i >= 2:
                    s.wait_ge(p.mm.sem, pvm[i - 2])
                expm[i] = p.ss.inc(s.activation(
                    pp_sl, ps_sl, AF.Exp,
                    bias=mcolall[:, h * NQW + w: h * NQW + w + 1]))

            if i >= 1:
                emit_pv(i - 1)
                emit_evac(i - 1)
    emit_pv(NIT - 1)
    emit_evac(NIT - 1)
    p.bar()
    p2s.close()

    p.nc.leave_named_scope("P2", _s1, False)
    _s2 = p.nc.enter_named_scope("P3", False)[0]
    p3s = ExitStack()
    wo1 = p3s.enter_context(p.nc.sbuf_tensor("wo1", [128, DC * 128], BF16))
    xr3 = p3s.enter_context(p.nc.sbuf_tensor("xr3", [128, 512], F32))
    xm3 = p3s.enter_context(p.nc.sbuf_tensor("xm3", [128, 512], F32))
    mchunks = f_chunks(MR, 512)
    for fc in range(DC):
        @p.blk.gpsimd
        def _(g, fc=fc):
            g.wait_ge(p.mm.sem, p.mm.n)
            p.dg.inc(g.dma_start(
                out=wo1[:].rearrange("p (c f) -> p c f", c=DC),
                in_=wo_i[:, fc * 128:(fc + 1) * 128]
                .rearrange("(c p) f -> p c f", p=128)), 16)
        dgw = p.dg.n

        for (o, nn) in mchunks:
            @p.blk.gpsimd
            def _(g, fc=fc, o=o, nn=nn):
                g.wait_ge(p.vv.sem, p.vv.n)
                p.da.inc(g.dma_start(
                    out=xr3[:, 0:nn],
                    in_=xin[fc * 128:(fc + 1) * 128, XN - MR + o: XN - MR + o + nn],
                    allow_slow_non_contiguous=(nn == 1)), 16)
            dgx = p.da.n

            @p.blk.tensor
            def _(t, fc=fc, o=o, nn=nn):
                t.wait_ge(p.dg.sem, dgw)
                t.wait_ge(p.vv.sem, p.vv.n)
                for ic in range(DC):
                    m = t.matmul(p.ps_main[:, 0:nn],
                                 wo1[:, ic * 128:(ic + 1) * 128],
                                 att[:, ic * QR + (QR - MR) + o: ic * QR + (QR - MR) + o + nn],
                                 start=(ic == 0), stop=(ic == DC - 1))
                p.mm.inc(m)
            mmwo = p.mm.n

            @p.blk.vector
            def _(v, fc=fc, o=o, nn=nn):
                v.wait_ge(p.mm.sem, mmwo)
                v.wait_ge(p.da.sem, dgx)
                p.vv.inc(v.tensor_tensor(xm3[:, 0:nn], p.ps_main[:, 0:nn],
                                         xr3[:, 0:nn], ALU.add))
                v.wait_ge(p.vv.sem, p.vv.n)
                if o == 0:
                    p.vv.inc(v.tensor_scalar_mul(xm3[:, 0:1], xm3[:, 0:1],
                                                 hval[:, 0:1]))
                    v.wait_ge(p.vv.sem, p.vv.n)

            @p.blk.sync
            def _(sy, fc=fc, o=o, nn=nn):
                sy.wait_ge(p.vv.sem, p.vv.n)
                p.dh.inc(sy.dma_start(
                    out=xmid_hbm[fc * 128:(fc + 1) * 128, o:o + nn],
                    in_=xm3[:, 0:nn],
                    allow_slow_non_contiguous=(nn == 1)), 16)

            @p.blk.vector
            def _(v):
                v.wait_ge(p.dh.sem, p.dh.n)
    p.bar()
    p3s.close()
    pqk.close()

    p.nc.leave_named_scope("P3", _s2, False)
    _s3 = p.nc.enter_named_scope("P4", False)[0]
    TF = min(1024, nt)
    p4s = ExitStack()
    w2r = p4s.enter_context(p.nc.sbuf_tensor("w2r", [128, NA * 1024], BF16))
    xmc = p4s.enter_context(p.nc.sbuf_tensor("xmc", [128, DC * (TF + 1)], F32))
    xn2b = p4s.enter_context(p.nc.sbuf_tensor("xn2b", [128, DC * TF], BF16))
    gbuf = p4s.enter_context(p.nc.sbuf_tensor("gbuf", [128, NA * TF], BF16))
    w1t1 = [p4s.enter_context(p.nc.sbuf_tensor(f"w1t1{b_}", [128, DC * 128], BF16))
            for b_ in range(2)]
    w1t2 = [p4s.enter_context(p.nc.sbuf_tensor(f"w1t2{b_}", [128, DC * 128], BF16))
            for b_ in range(2)]
    w1t2f = [p4s.enter_context(p.nc.sbuf_tensor(f"w1t2f{b_}", [128, DC * 128], F32))
             for b_ in range(2)]
    ggs = p4s.enter_context(p.nc.sbuf_tensor("ggs", [128, TF], F32))
    avs = p4s.enter_context(p.nc.sbuf_tensor("avs", [128, TF], F32))
    xob = p4s.enter_context(p.nc.sbuf_tensor("xob", [128, 512], F32))
    lnb_bufs = {
        "sq": p4s.enter_context(p.nc.sbuf_tensor("lnf_sq", [128, 2 * 512], F32)),
        "tmp": p4s.enter_context(p.nc.sbuf_tensor("lnf_tmp", [128, 512], F32)),
        "mu": p4s.enter_context(p.nc.sbuf_tensor("lnf_mu", [1, 512], F32)),
        "rstd": p4s.enter_context(p.nc.sbuf_tensor("lnf_rstd", [1, 512], F32)),
        "mub": p4s.enter_context(p.nc.sbuf_tensor("lnf_mub", [128, 512], F32)),
        "rsb": p4s.enter_context(p.nc.sbuf_tensor("lnf_rsb", [128, 512], F32)),
    }

    @p.blk.gpsimd
    def _(g):
        for j, (off, size) in enumerate(ach):
            p.dg.inc(g.dma_start(out=w2r[0:size, j * 1024:(j + 1) * 1024],
                                 in_=w2_i[off:off + size, :]), 16)
    p.bar()

    for tch in range(nt // TF):
        t0 = tch * TF

        @p.blk.gpsimd
        def _(g, t0=t0):
            g.wait_ge(p.vv.sem, p.vv.n)
            g.wait_ge(p.mm.sem, p.mm.n)
            for c in range(DC):
                p.dg.inc(g.dma_start(
                    out=xmc[:, c * (TF + 1):(c + 1) * (TF + 1)],
                    in_=xmid_hbm[c * 128:(c + 1) * 128, t0:t0 + TF + 1]), 16)
        p.bar()

        for hf in range(TF // 512):
            ln_fm(p, lambda c, hf=hf: c * (TF + 1) + (1 if c < 4 else 0) + hf * 512,
                  512, lambda c, hf=hf: c * TF + hf * 512, None, lnb_bufs,
                  g_col=lambda c: lngc[:, c:c + 1], b_col=lambda c: lnbc[:, c:c + 1],
                  src_sb=xmc, dst_sb=xn2b)
            p.bar()

        w1m = {}
        mmj = {}

        def load_w1(j):
            off, size = ach[j]
            b_ = j % 2
            war = mmj.get(j - 2, 0)
            sA = p.dg if b_ == 0 else p.da
            sB = p.dh if b_ == 0 else p.sg

            @p.blk.gpsimd
            def _(g, off=off, size=size, b_=b_, war=war, sA=sA):
                g.wait_ge(p.mm.sem, war)
                sA.inc(g.dma_start(
                    out=w1t1[b_][:].rearrange("p (c f) -> p c f", c=DC)[:, :, 0:size],
                    in_=w1_i[:, off:off + size].rearrange("(c p) f -> p c f", p=128)), 16)

            @p.blk.sync
            def _(sy, off=off, size=size, b_=b_, war=war, sB=sB):
                sy.wait_ge(p.mm.sem, war)
                sy.wait_ge(p.vv.sem, p.vv.n)
                sB.inc(sy.dma_start(
                    out=w1t2f[b_][:].rearrange("p (c f) -> p c f", c=DC)[:, :, 0:size],
                    in_=w1_i[:, DIN + off:DIN + off + size]
                    .rearrange("(c p) f -> p c f", p=128)), 16)
            sBm = sB.n

            @p.blk.vector
            def _(v, b_=b_, war=war, sB=sB, sBm=sBm):
                v.wait_ge(sB.sem, sBm)
                v.wait_ge(p.mm.sem, war)
                p.vv.inc(v.tensor_copy(w1t2[b_][:], w1t2f[b_][:]))
            w1m[j] = (sA, sA.n, p.vv.n)

        load_w1(0)
        for j, (off, size) in enumerate(ach):
            if j + 1 < NA:
                load_w1(j + 1)
            b_ = j % 2

            @p.blk.tensor
            def _(t, j=j, size=size, b_=b_):
                t.wait_ge(w1m[j][0].sem, w1m[j][1])
                t.wait_ge(p.vv.sem, w1m[j][2])
                t.wait_ge(p.ss.sem, p.ss.n)
                for hf in range(TF // 512):
                    for c in range(DC):
                        t.matmul(psim[0:size, hf * 512:hf * 512 + 512],
                                 w1t1[b_][:, c * 128:c * 128 + size],
                                 xn2b[:, c * TF + hf * 512: c * TF + hf * 512 + 512],
                                 start=(c == 0), stop=(c == DC - 1))
                    for c in range(DC):
                        m = t.matmul(psim[0:size, 1024 + hf * 512:1024 + hf * 512 + 512],
                                     w1t2[b_][:, c * 128:c * 128 + size],
                                     xn2b[:, c * TF + hf * 512: c * TF + hf * 512 + 512],
                                     start=(c == 0), stop=(c == DC - 1))
                p.mm.inc(m)
            mmj[j] = p.mm.n

            @p.blk.scalar
            def _(s, j=j, size=size):
                s.wait_ge(p.mm.sem, mmj[j])
                s.wait_ge(p.vv.sem, p.vv.n)
                p.ss.inc(s.activation(avs[0:size, :], psim[0:size, 0:TF],
                                      AF.Identity, bias=b1ca[0:size, j:j + 1]))
                p.ss.inc(s.activation(ggs[0:size, :], psim[0:size, 1024:1024 + TF],
                                      AF.Gelu, bias=b1cg[0:size, j:j + 1]))
            ssgg = p.ss.n

            @p.blk.vector
            def _(v, j=j, size=size):
                v.wait_ge(p.ss.sem, ssgg)
                p.vv.inc(v.tensor_tensor(gbuf[0:size, j * TF:(j + 1) * TF],
                                         avs[0:size, :], ggs[0:size, :], ALU.mult))
        p.bar()

        for oc in range(DC):
            for hf in range(TF // 512):
                @p.blk.tensor
                def _(t, oc=oc, hf=hf):
                    t.wait_ge(p.ss.sem, p.ss.n)
                    t.wait_ge(p.vv.sem, p.vv.n)
                    for j, (off, size) in enumerate(ach):
                        m = t.matmul(p.ps_main[:, 0:512],
                                     w2r[0:size, j * 1024 + oc * 128: j * 1024 + oc * 128 + 128],
                                     gbuf[0:size, j * TF + hf * 512: j * TF + hf * 512 + 512],
                                     start=(j == 0), stop=(j == NA - 1))
                    p.mm.inc(m)
                mmw2 = p.mm.n

                @p.blk.scalar
                def _(s, oc=oc):
                    s.wait_ge(p.mm.sem, mmw2)
                    s.wait_ge(p.st.sem, p.st.n)
                    p.ss.inc(s.activation(xob[:, :], p.ps_main[:, 0:512], AF.Identity,
                                          bias=b2c[:, oc:oc + 1]))
                ssb2 = p.ss.n

                @p.blk.vector
                def _(v, oc=oc, hf=hf):
                    v.wait_ge(p.ss.sem, ssb2)
                    p.vv.inc(v.tensor_tensor(
                        xob[:, :], xob[:, :],
                        xmc[:, oc * (TF + 1) + 1 + hf * 512: oc * (TF + 1) + 1 + hf * 512 + 512],
                        ALU.add))
                vvxo = p.vv.n

                @p.blk.sync
                def _(sy, oc=oc, t0=t0, hf=hf):
                    sy.wait_ge(p.vv.sem, vvxo)
                    p.st.inc(sy.dma_start(
                        out=xout[oc * 128:(oc + 1) * 128, t0 + hf * 512: t0 + hf * 512 + 512],
                        in_=xob[:, :]), 16)

                @p.blk.vector
                def _(v):
                    v.wait_ge(p.st.sem, p.st.n)
    p.bar()
    p4s.close()

    @p.blk.gpsimd
    def _(g):
        p.dg.wait(g)

    @p.blk.sync
    def _(sy):
        p.dh.wait(sy)
        p.st.wait(sy)
    return p.done()



def host_rotary_tables(n=8192):
    half = np.arange(0, DH, 2, dtype=np.float64)
    inv_freq = 1.0 / 10000.0 ** (half / DH)
    t = np.arange(n, dtype=np.float64)
    freqs = t[:, None] * inv_freq[None, :]
    freqs = np.concatenate([freqs, freqs], -1)
    base_scale = (half + 0.4 * DH) / (1.4 * DH)
    power = (t - n // 2) / (WIN // 2)
    scale = base_scale[None, :] ** power[:, None]
    scale = np.concatenate([scale, scale], -1)
    return freqs.astype(np.float32), scale.astype(np.float32)


def host_rot_tables_for(pos):
    pos = np.asarray(pos, np.float64)
    half = np.arange(0, DH, 2, dtype=np.float64)
    inv_freq = 1.0 / 10000.0 ** (half / DH)
    f0 = pos[:, None] * inv_freq[None, :]
    f = np.concatenate([f0, f0], -1)
    base_scale = (half + 0.4 * DH) / (1.4 * DH)
    power = (pos - 8192 // 2) / (WIN // 2)
    s0 = base_scale[None, :] ** power[:, None]
    s_ = np.concatenate([s0, s0], -1)
    cos, sin = np.cos(f), np.sin(f)

    def dup(a):
        return np.ascontiguousarray(np.concatenate([a, a], axis=0)).astype(np.float32)

    return (dup((cos * s_).T), dup((sin * s_).T),
            dup((cos / s_).T), dup((sin / s_).T))


def host_band_masks():
    q = np.arange(WIN)[None, :]
    out = np.zeros((2, 4, 128, WIN), np.float32)
    for kc in range(4):
        k = (kc * 128 + np.arange(128))[:, None]
        valid0 = (k >= q) & (k <= q + WIN)
        valid1 = valid0 & (k >= WIN)
        out[0, kc] = np.where(valid0, 0.0, NEG)
        out[1, kc] = np.where(valid1, 0.0, NEG)
    return out.astype(BF)



def build_launch_E(nt=NT):
    p = P()
    yfm_i = p.inp("yfm", [DM, nt])
    xo_i = p.inp("xfm", [DM, nt])
    x2_o = p.outp("x2fm", [DM, nt])
    wq_i = p.inp("wq", [DM, DM])
    wkv_i = p.inp("wkv", [DM, 128])
    g_i = p.inp("ln1g", [1, DM])
    b_i = p.inp("ln1b", [1, DM])
    qfm_o = p.outp("qfm", [DM, nt], BF16)
    kfm_o = p.outp("kfm", [64, nt], BF16)
    vaug_o = p.outp("vaug", [128, (nt // 128) * 65], BF16)

    p.begin(pool="min")
    gc = p.sb("gc", [128, DC], F32)
    bc = p.sb("bc", [128, DC], F32)
    rowb = p.sb("rowb", [1, 2 * DM], F32)
    y6c = p.sb("y6c", [128, DC * TC], F32)
    xfc = p.sb("xfc", [128, DC * TC], F32)
    xnb = p.sb("xnb", [128, DC * TC], BF16)
    wq1 = p.sb("wq1", [128, DC * 128], BF16)
    wkv1 = p.sb("wkv1", [128, DC * 128], BF16)
    ob = p.sb("ob", [128, TC], BF16)
    lnb_bufs = {
        "sq": p.sb("lnf_sq", [128, 2 * TC], F32),
        "tmp": p.sb("lnf_tmp", [128, TC], F32),
        "mu": p.sb("lnf_mu", [1, TC], F32),
        "rstd": p.sb("lnf_rstd", [1, TC], F32),
        "mub": p.sb("lnf_mub", [128, TC], F32),
        "rsb": p.sb("lnf_rsb", [128, TC], F32),
    }

    @p.blk.sync
    def _(sy):
        p.dh.inc(sy.dma_start(out=rowb[0:1, 0:DM], in_=g_i), 16)
        p.dh.inc(sy.dma_start(out=rowb[0:1, DM:2 * DM], in_=b_i), 16)
    p.bar()
    row_to_cols(p, rowb[0:1, 0:DM], DC, gc, 0, p.psA_f)
    row_to_cols(p, rowb[0:1, DM:2 * DM], DC, bc, 0, p.psA_f)

    @p.blk.gpsimd
    def _(g):
        p.dg.inc(g.dma_start(out=wkv1[:].rearrange("p (c f) -> p c f", c=DC),
                             in_=wkv_i.rearrange("(c p) f -> p c f", p=128)), 16)
    p.bar()

    for ch in range(nt // TC):
        t0 = ch * TC

        @p.blk.gpsimd
        def _(g, t0=t0):
            g.wait_ge(p.vv.sem, p.vv.n)
            g.wait_ge(p.mm.sem, p.mm.n)
            g.wait_ge(p.st.sem, p.st.n)
            for c in range(DC):
                p.dg.inc(g.dma_start(out=y6c[:, c * TC:(c + 1) * TC],
                                     in_=yfm_i[c * 128:(c + 1) * 128, t0:t0 + TC]), 16)
                p.dg.inc(g.dma_start(out=xfc[:, c * TC:(c + 1) * TC],
                                     in_=xo_i[c * 128:(c + 1) * 128, t0:t0 + TC]), 16)
        p.bar()
        ln_fm(p, lambda c: c * TC, TC, lambda c: c * TC, lambda c: c * TC,
              lnb_bufs, src_sb=y6c, dst_sb=xfc, x_sb=xfc)
        p.bar()

        @p.blk.sync
        def _(sy, t0=t0):
            p.vv.wait(sy)
            p.ss.wait(sy)
            for c in range(DC):
                p.st.inc(sy.dma_start(out=x2_o[c * 128:(c + 1) * 128, t0:t0 + TC],
                                      in_=xfc[:, c * TC:(c + 1) * TC]), 16)

        @p.blk.vector
        def _(v):
            v.wait_ge(p.st.sem, p.st.n)

        @p.blk.scalar
        def _(s):
            s.wait_ge(p.st.sem, p.st.n)

        ln_fm(p, lambda c: c * TC, TC, lambda c: c * TC, None, lnb_bufs,
              src_sb=xfc, dst_sb=xfc)
        p.bar()
        ln_fm(p, lambda c: c * TC, TC, lambda c: c * TC, None, lnb_bufs,
              g_col=lambda c: gc[:, c:c + 1], b_col=lambda c: bc[:, c:c + 1],
              src_sb=xfc, dst_sb=xnb)
        p.bar()

        for fc in range(DC):
            @p.blk.gpsimd
            def _(g, fc=fc):
                g.wait_ge(p.mm.sem, p.mm.n)
                p.dg.inc(g.dma_start(
                    out=wq1[:].rearrange("p (c f) -> p c f", c=DC),
                    in_=wq_i[:, fc * 128:(fc + 1) * 128]
                    .rearrange("(c p) f -> p c f", p=128)), 16)
            dgw = p.dg.n

            @p.blk.tensor
            def _(t, fc=fc):
                t.wait_ge(p.dg.sem, dgw)
                t.wait_ge(p.dh.sem, p.dh.n)
                for c in range(DC):
                    m = t.matmul(p.ps_main[:, 0:TC],
                                 wq1[:, c * 128:(c + 1) * 128],
                                 xnb[:, c * TC:(c + 1) * TC],
                                 start=(c == 0), stop=(c == DC - 1))
                p.mm.inc(m)
            mmq = p.mm.n

            @p.blk.vector
            def _(v):
                v.wait_ge(p.mm.sem, mmq)
                p.vv.inc(v.tensor_scalar_mul(ob[:, :], p.ps_main[:, 0:TC], 0.125))
            vvq = p.vv.n

            @p.blk.sync
            def _(sy, fc=fc, t0=t0):
                sy.wait_ge(p.vv.sem, vvq)
                p.dh.inc(sy.dma_start(out=qfm_o[fc * 128:(fc + 1) * 128, t0:t0 + TC],
                                      in_=ob[:, :]), 16)

            @p.blk.tensor
            def _(t):
                t.wait_ge(p.dh.sem, p.dh.n)

        @p.blk.tensor
        def _(t):
            t.wait_ge(p.dh.sem, p.dh.n)
            t.wait_ge(p.vv.sem, p.vv.n)
            for c in range(DC):
                m = t.matmul(p.ps_main[:, 0:TC],
                             wkv1[:, c * 128:(c + 1) * 128],
                             xnb[:, c * TC:(c + 1) * TC],
                             start=(c == 0), stop=(c == DC - 1))
            p.mm.inc(m)
        mmkv = p.mm.n

        @p.blk.vector
        def _(v):
            v.wait_ge(p.mm.sem, mmkv)
            p.vv.inc(v.tensor_copy(ob[0:64, :], p.ps_main[0:64, 0:TC]))
        vvk = p.vv.n

        @p.blk.sync
        def _(sy, t0=t0):
            sy.wait_ge(p.vv.sem, vvk)
            p.dh.inc(sy.dma_start(out=kfm_o[:, t0:t0 + TC], in_=ob[0:64, :]), 16)

        for t_ in range(TC // 128):
            @p.blk.tensor
            def _(t, t_=t_):
                t.wait_ge(p.vv.sem, p.vv.n)
                t.wait_ge(p.dh.sem, p.dh.n)
                for c in range(DC):
                    m = t.matmul(p.ps_main[:, 0:64],
                                 xnb[:, c * TC + t_ * 128: c * TC + t_ * 128 + 128],
                                 wkv1[:, c * 128 + 64:(c + 1) * 128],
                                 start=(c == 0), stop=(c == DC - 1))
                p.mm.inc(m)
            mmv = p.mm.n

            @p.blk.vector
            def _(v, t_=t_, ch=ch):
                v.wait_ge(p.mm.sem, mmv)
                kt = ch * (TC // 128) + t_
                p.vv.inc(v.tensor_copy(ob[:, 0:64], p.ps_main[:, 0:64]))
                v.wait_ge(p.vv.sem, p.vv.n)
                p.vv.inc(v.memset(ob[:, 64:65], 1.0))
            vvv = p.vv.n

            @p.blk.sync
            def _(sy, t_=t_, ch=ch):
                sy.wait_ge(p.vv.sem, vvv)
                kt = ch * (TC // 128) + t_
                p.dh.inc(sy.dma_start(out=vaug_o[:, kt * 65:(kt + 1) * 65],
                                      in_=ob[:, 0:65]), 16)

            @p.blk.tensor
            def _(t):
                t.wait_ge(p.dh.sem, p.dh.n)
    p.bar()

    @p.blk.sync
    def _(sy):
        p.dh.wait(sy)
        p.st.wait(sy)

    @p.blk.gpsimd
    def _(g):
        p.dg.wait(g)
    return p.done()



def build_launch_G(with_x, nt=NT):
    p = P()
    yfm_i = p.inp("yfm", [DM, nt])
    xfm_i = p.inp("xfm", [DM, nt]) if with_x else None
    ofm_o = p.outp("ofm", [DM, nt])

    p.begin(pool="min")
    yc = p.sb("yc", [128, DC * TC], F32)
    oc = p.sb("oc", [128, DC * TC], F32)
    xc = p.sb("xc", [128, DC * TC], F32) if with_x else None
    lnb_bufs = {
        "sq": p.sb("lnf_sq", [128, 2 * TC], F32),
        "tmp": p.sb("lnf_tmp", [128, TC], F32),
        "mu": p.sb("lnf_mu", [1, TC], F32),
        "rstd": p.sb("lnf_rstd", [1, TC], F32),
        "mub": p.sb("lnf_mub", [128, TC], F32),
        "rsb": p.sb("lnf_rsb", [128, TC], F32),
    }

    for ch in range(nt // TC):
        t0 = ch * TC

        @p.blk.gpsimd
        def _(g, t0=t0):
            g.wait_ge(p.vv.sem, p.vv.n)
            g.wait_ge(p.ss.sem, p.ss.n)
            g.wait_ge(p.dh.sem, p.dh.n)
            for c in range(DC):
                p.dg.inc(g.dma_start(out=yc[:, c * TC:(c + 1) * TC],
                                     in_=yfm_i[c * 128:(c + 1) * 128, t0:t0 + TC]), 16)
                if with_x:
                    p.dg.inc(g.dma_start(out=xc[:, c * TC:(c + 1) * TC],
                                         in_=xfm_i[c * 128:(c + 1) * 128, t0:t0 + TC]), 16)
        p.bar()
        ln_fm(p, lambda c: c * TC, TC, lambda c: c * TC,
              (lambda c: c * TC) if with_x else None, lnb_bufs,
              src_sb=yc, dst_sb=oc, x_sb=xc)
        p.bar()

        @p.blk.sync
        def _(sy, t0=t0):
            p.vv.wait(sy)
            p.ss.wait(sy)
            for c in range(DC):
                p.dh.inc(sy.dma_start(out=ofm_o[c * 128:(c + 1) * 128, t0:t0 + TC],
                                      in_=oc[:, c * TC:(c + 1) * TC]), 16)
        p.bar()

    @p.blk.sync
    def _(sy):
        p.dh.wait(sy)

    @p.blk.gpsimd
    def _(g):
        p.dg.wait(g)
    return p.done()



def build_launch_F(nt=NT, nk=8192, qh_ov=None, dbg=False, upto=None):
    nkt = nk // 128
    QH = qh_ov if qh_ov else min(1024, nt)
    p = P()
    xfm_i = p.inp("xfm", [DM, nt])
    qfm_i = p.inp("qfm", [DM, nt], BF16)
    kfm_i = p.inp("kfull", [64, nk], BF16)
    vaug_i = p.inp("vfull", [128, nkt * 65], BF16)
    wo_i = p.inp("wo", [DM, DM])
    g2_i = p.inp("ln2g", [1, DM]); b2_i = p.inp("ln2b", [1, DM])
    wmix_i = p.inp("wmix", [8, DM, DM])
    bmix_i = p.inp("bmix", [8, DM])
    logits_i = p.inp("logits", [1, 8])
    uu_i = p.inp("uu", [1, 8])
    sel2_i = p.inp("sel2c", [2, 128])
    identf_in = p.inp("identf", [128, 128])
    if upto == "att":
        xout = p.outp("xout", [128, DC * nt], BF16)
    elif upto in ("x3", "mix", "fin"):
        xout = p.outp("xout", [DM, nt])
    elif upto == "idx":
        xout = p.outp("xout", [1, 256])
    else:
        xout = p.outp("xout", [nt, DM])

    def _finish():
        @p.blk.sync
        def _(sy):
            p.dh.wait(sy)
            p.st.wait(sy)

        @p.blk.gpsimd
        def _(g):
            p.dg.wait(g)
            p.sg.wait(g)
        return p.done()

    p.begin(pool="tiny")
    psim = p.pt("psim", [128, 2 * QH])
    pout = p.pt("pout", [65, QH])
    pnorm = p.pt("pnorm", [2, 512])

    nqh = nt // QH
    identf = p.sb("sb_identf", [128, 128], F32)
    g2c = p.sb("g2c", [128, DC], F32)
    b2c = p.sb("b2c", [128, DC], F32)
    sel2 = p.sb("sel2", [2, 128], F32)
    hselb = p.sb("hselb", [128, 2], BF16)
    ones64f = p.sb("ones64f", [1, 64], F32)
    onesr2 = p.sb("onesr2", [1, 128], BF16)
    x_f = p.sb("x_f", [128, DC * nt], F32)
    kmaxc = p.sb("kmaxc", [2, 1], F32)
    mvF = p.sb("mvF", [2, DC * nqh], F32)
    mvs = p.sb("mvs", [1, 1], F32)
    lnb_bufs = {
        "sq": p.sb("lnf_sq", [128, 2 * TC], F32),
        "tmp": p.sb("lnf_tmp", [128, TC], F32),
        "mu": p.sb("lnf_mu", [1, TC], F32),
        "rstd": p.sb("lnf_rstd", [1, TC], F32),
        "mub": p.sb("lnf_mub", [128, TC], F32),
        "rsb": p.sb("lnf_rsb", [128, TC], F32),
    }
    attsc = ExitStack()
    att = attsc.enter_context(p.nc.sbuf_tensor("att", [128, DC * nt], BF16))
    s1 = ExitStack()

    def s1b(name, shape, dtype):
        return s1.enter_context(p.nc.sbuf_tensor(name, shape, dtype))

    rowb = s1b("rowb", [1, 2 * DM], F32)
    q_b = s1b("q_b", [128, DC * nt], BF16)
    k_b = s1b("k_b", [128, nk], BF16)
    v_b = s1b("v_b", [128, nkt * 65], BF16)
    p_sb = s1b("p_sb", [128, 2 * QH], BF16)
    qn2 = s1b("qn2", [2, nt], F32)
    mcol = s1b("mcol", [128, 1], F32)
    rcro = s1b("rcro", [1, QH], F32)
    rbro = s1b("rbro", [64, QH], F32)
    krow = s1b("krow", [1, QH], F32)
    nkc = len(f_chunks(nk, QH))
    km8 = s1b("km8", [1, nkc], F32)
    onesqf = s1b("onesqf", [1, 512], F32)

    @p.blk.sync
    def _(sy):
        p.dh.inc(sy.dma_start(out=identf[:], in_=identf_in), 16)
        p.dh.inc(sy.dma_start(out=rowb[0:1, 0:DM], in_=g2_i), 16)
        p.dh.inc(sy.dma_start(out=rowb[0:1, DM:2 * DM], in_=b2_i), 16)
        p.dh.inc(sy.dma_start(out=sel2[:], in_=sel2_i), 16)
        p.dh.inc(sy.dma_start(out=k_b[0:64, :], in_=kfm_i), 16)
        p.dh.inc(sy.dma_start(out=v_b[:], in_=vaug_i), 16)

    @p.blk.gpsimd
    def _(g):
        for c in range(DC):
            p.dg.inc(g.dma_start(out=x_f[:, c * nt:(c + 1) * nt],
                                 in_=xfm_i[c * 128:(c + 1) * 128, :]), 16)
            p.dg.inc(g.dma_start(out=q_b[:, c * nt:(c + 1) * nt],
                                 in_=qfm_i[c * 128:(c + 1) * 128, :]), 16)
        p.pg.inc(g.memset(hselb[:], 0.0))
        p.pg.inc(g.memset(ones64f[:], 1.0))
        p.pg.inc(g.memset(onesr2[:], 1.0))
        p.pg.inc(g.memset(onesqf[:], 1.0))
        g.wait_ge(p.pg.sem, p.pg.n)
        p.pg.inc(g.memset(hselb[0:64, 0:1], 1.0))
        p.pg.inc(g.memset(hselb[64:128, 1:2], 1.0))
    p.bar()

    @p.blk.vector
    def _(v):
        p.vv.inc(v.tensor_copy(k_b[64:128, :], k_b[0:64, :]))
    p.bar()
    row_to_cols(p, rowb[0:1, 0:DM], DC, g2c, 0, p.ps_main)
    row_to_cols(p, rowb[0:1, DM:2 * DM], DC, b2c, 0, p.ps_main)
    p.bar()

    _f0 = p.nc.enter_named_scope("Fprep", False)[0]
    for ci, (o, nn) in enumerate(f_chunks(nk, QH)):
        @p.blk.scalar
        def _(s, o=o, nn=nn):
            s.wait_ge(p.mm.sem, p.mm.n)
            s.wait_ge(p.vv.sem, p.vv.n)
            p.ss.inc(s.activation(p_sb[0:64, 0:nn], k_b[0:64, o:o + nn], AF.Square))
        sk = p.ss.n

        @p.blk.tensor
        def _(t, o=o, nn=nn):
            t.wait_ge(p.ss.sem, sk)
            t.wait_ge(p.vv.sem, p.vv.n)
            for (o2, n2) in f_chunks(nn, 512):
                p.mm.inc(t.matmul(p.ps_main[0:1, 0:n2], p.c_onesb[0:64, 0:1],
                                  p_sb[0:64, o2:o2 + n2], start=True, stop=True))
                mk = p.mm.n

                @p.blk.vector
                def _(v, o2=o2, n2=n2, mk=mk):
                    v.wait_ge(p.mm.sem, mk)
                    v.wait_ge(p.vv.sem, p.vv.n)
                    p.vv.inc(v.tensor_copy(krow[0:1, o2:o2 + n2],
                                           p.ps_main[0:1, 0:n2]))
                t.wait_ge(p.vv.sem, p.vv.n)

        @p.blk.vector
        def _(v, ci=ci, nn=nn):
            v.wait_ge(p.vv.sem, p.vv.n)
            p.vv.inc(v.tensor_reduce(km8[0:1, ci:ci + 1], krow[0:1, 0:nn],
                                     AX.X, ALU.max))
    p.bar()

    @p.blk.vector
    def _(v):
        p.vv.inc(v.tensor_reduce(kmaxc[0:1, 0:1], km8[0:1, :], AX.X, ALU.max))

    @p.blk.scalar
    def _(s):
        s.wait_ge(p.vv.sem, p.vv.n)
        p.ss.inc(s.activation(kmaxc[0:1, 0:1], kmaxc[0:1, 0:1], AF.Sqrt))

    @p.blk.tensor
    def _(t):
        t.wait_ge(p.ss.sem, p.ss.n)
        p.mm.inc(t.matmul(pnorm[0:2, 0:1], ones64f[0:1, 0:2], kmaxc[0:1, 0:1],
                          start=True, stop=True))

    @p.blk.vector
    def _(v):
        v.wait_ge(p.mm.sem, p.mm.n)
        p.vv.inc(v.tensor_copy(kmaxc[0:2, 0:1], pnorm[0:2, 0:1]))
    p.bar()

    for fc in range(DC):
        @p.blk.scalar
        def _(s, fc=fc):
            s.wait_ge(p.mm.sem, p.mm.n)
            s.wait_ge(p.vv.sem, p.vv.n)
            p.ss.inc(s.activation(p_sb[:, 0:nt], q_b[:, fc * nt:(fc + 1) * nt],
                                  AF.Square))
        sk = p.ss.n

        for (o2, n2) in f_chunks(nt, 512):
            @p.blk.tensor
            def _(t, o2=o2, n2=n2):
                t.wait_ge(p.ss.sem, sk)
                t.wait_ge(p.vv.sem, p.vv.n)
                p.mm.inc(t.matmul(pnorm[0:2, 0:n2], hselb[:, 0:2],
                                  p_sb[:, o2:o2 + n2], start=True, stop=True))
            mk = p.mm.n

            @p.blk.vector
            def _(v, o2=o2, n2=n2):
                v.wait_ge(p.mm.sem, mk)
                p.vv.inc(v.tensor_copy(qn2[:, o2:o2 + n2], pnorm[0:2, 0:n2]))
        p.bar()

        @p.blk.scalar
        def _(s):
            p.ss.inc(s.activation(qn2[:], qn2[:], AF.Sqrt))
        sk2 = p.ss.n

        @p.blk.vector
        def _(v):
            v.wait_ge(p.ss.sem, sk2)
            p.vv.inc(v.tensor_scalar(qn2[:], qn2[:], kmaxc[0:2, 0:1], -1.0,
                                     ALU.mult, ALU.mult))
            v.wait_ge(p.vv.sem, p.vv.n)
            p.vv.inc(v.tensor_scalar_add(qn2[:], qn2[:], 30.0))
        p.bar()

        for qhb in range(nqh):
            @p.blk.vector
            def _(v, fc=fc, qhb=qhb):
                v.wait_ge(p.vv.sem, p.vv.n)
                p.vv.inc(v.tensor_reduce(
                    mvF[0:2, fc * nqh + qhb: fc * nqh + qhb + 1],
                    qn2[:, qhb * QH:(qhb + 1) * QH], AX.X, ALU.min))
        p.bar()

    p.nc.leave_named_scope("Fprep", _f0, False)
    _f1 = p.nc.enter_named_scope("Fattn", False)[0]
    for h in range(HEADS):
        fc, pb = h // 2, (h % 2) * 64
        for qh in range(nqh):
            q0 = qh * QH
            @p.blk.tensor
            def _(t, fc=fc, hh=h % 2, qh=qh):
                t.wait_ge(p.vv.sem, p.vv.n)
                p.mm.inc(t.matmul(pnorm[0:1, 0:1], sel2[0:2, hh * 64:hh * 64 + 1],
                                  mvF[0:2, fc * nqh + qh: fc * nqh + qh + 1],
                                  start=True, stop=True))
            mex = p.mm.n

            @p.blk.vector
            def _(v, mex=mex):
                v.wait_ge(p.mm.sem, mex)
                p.vv.inc(v.tensor_copy(mvs[0:1, 0:1], pnorm[0:1, 0:1]))
            vmv = p.vv.n

            @p.blk.tensor
            def _(t, vmv=vmv):
                t.wait_ge(p.vv.sem, vmv)
                p.mm.inc(t.matmul(p.ps_main[:, 0:1], p.c_onesr_f[0:1, :],
                                  mvs[0:1, 0:1], start=True, stop=True))
            mbx = p.mm.n

            @p.blk.vector
            def _(v, mbx=mbx):
                v.wait_ge(p.mm.sem, mbx)
                v.wait_ge(p.ss.sem, p.ss.n)
                p.vv.inc(v.tensor_copy(mcol[:, 0:1], p.ps_main[:, 0:1]))
            vbc = p.vv.n
            expm = [0] * 64
            pvm = [0] * 64

            def emit_pv(kt):
                pp = p_sb[:, (kt % 2) * QH:(kt % 2 + 1) * QH]

                @p.blk.tensor
                def _(t, kt=kt, pp=pp):
                    t.wait_ge(p.ss.sem, expm[kt])
                    for (o2, n2) in f_chunks(QH, 512):
                        m = t.matmul(pout[:, o2:o2 + n2],
                                     v_b[:, kt * 65:(kt + 1) * 65],
                                     pp[:, o2:o2 + n2],
                                     start=(kt == 0), stop=(kt == nkt - 1))
                    pvm[kt] = p.mm.inc(m)

            for kt in range(nkt):
                par = kt % 2
                ps_sl = psim[:, par * QH:(par + 1) * QH]
                pp_sl = p_sb[:, par * QH:(par + 1) * QH]

                @p.blk.tensor
                def _(t, h=h, fc=fc, pb=pb, q0=q0, kt=kt, par=par, ps_sl=ps_sl,
                      vbc=vbc):
                    if kt == 0:
                        t.wait_ge(p.vv.sem, vbc)
                    if kt >= 2:
                        t.wait_ge(p.ss.sem, expm[kt - 2])
                    for (o2, n2) in f_chunks(QH, 512):
                        m = t.matmul(ps_sl[:, o2:o2 + n2],
                                     k_b[pb:pb + 64, kt * 128:(kt + 1) * 128],
                                     q_b[pb:pb + 64, fc * nt + q0 + o2: fc * nt + q0 + o2 + n2],
                                     start=True, stop=True)
                    p.mm.inc(m)
                mmsim = p.mm.n

                @p.blk.scalar
                def _(s, kt=kt, ps_sl=ps_sl, pp_sl=pp_sl, mmsim=mmsim, vbc=vbc):
                    s.wait_ge(p.mm.sem, mmsim)
                    if kt == 0:
                        s.wait_ge(p.vv.sem, vbc)
                    if kt >= 2:
                        s.wait_ge(p.mm.sem, pvm[kt - 2])
                    expm[kt] = p.ss.inc(s.activation(pp_sl, ps_sl, AF.Exp,
                                                     bias=mcol[:, 0:1]))

                if kt >= 1:
                    emit_pv(kt - 1)
            emit_pv(nkt - 1)

            @p.blk.vector
            def _(v):
                v.wait_ge(p.mm.sem, p.mm.n)
                p.vv.inc(v.reciprocal(rcro[0:1, :], pout[64:65, :]))
            vrc = p.vv.n

            @p.blk.tensor
            def _(t):
                t.wait_ge(p.vv.sem, vrc)
                for (o2, n2) in f_chunks(QH, 512):
                    m = t.matmul(p.ps_main[0:64, 0:n2], ones64f[0:1, 0:64],
                                 rcro[0:1, o2:o2 + n2], start=True, stop=True)
                    p.mm.inc(m)
                    mk = p.mm.n

                    @p.blk.vector
                    def _(v2, o2=o2, n2=n2, mk=mk):
                        v2.wait_ge(p.mm.sem, mk)
                        p.vv.inc(v2.tensor_copy(rbro[:, o2:o2 + n2],
                                                p.ps_main[0:64, 0:n2]))
                    t.wait_ge(p.vv.sem, p.vv.n)
            vb2 = p.vv.n

            @p.blk.vector
            def _(v, h=h, fc=fc, pb=pb, q0=q0):
                v.wait_ge(p.vv.sem, vb2)
                p.vv.inc(v.tensor_tensor(
                    att[pb:pb + 64, fc * nt + q0: fc * nt + q0 + QH],
                    pout[0:64, :], rbro[:, :], ALU.mult))
            vatt = p.vv.n

            @p.blk.tensor
            def _(t):
                t.wait_ge(p.vv.sem, vatt)
    p.bar()

    p.nc.leave_named_scope("Fattn", _f1, False)
    _f2 = p.nc.enter_named_scope("Fwo", False)[0]
    if upto == "att":
        @p.blk.sync
        def _(sy):
            p.mm.wait(sy)
            p.vv.wait(sy)
            p.st.inc(sy.dma_start(out=xout, in_=att[:]), 16)
        p.bar()
        s1.close()
        attsc.close()
        return _finish()

    s1.close()
    s2 = ExitStack()
    wrk = s2.enter_context(p.nc.sbuf_tensor("wrk", [128, DC * nt], F32))
    wo1 = s2.enter_context(p.nc.sbuf_tensor("wo1", [128, DC * 128], BF16))
    for fc in range(DC):
        @p.blk.gpsimd
        def _(g, fc=fc):
            g.wait_ge(p.mm.sem, p.mm.n)
            p.dg.inc(g.dma_start(
                out=wo1[:].rearrange("p (c f) -> p c f", c=DC),
                in_=wo_i[:, fc * 128:(fc + 1) * 128]
                .rearrange("(c p) f -> p c f", p=128)), 16)
        dgw = p.dg.n

        for (o2, n2) in f_chunks(nt, 512):
            @p.blk.tensor
            def _(t, fc=fc, o2=o2, n2=n2):
                t.wait_ge(p.dg.sem, dgw)
                t.wait_ge(p.vv.sem, p.vv.n)
                for ic in range(DC):
                    m = t.matmul(p.ps_main[:, 0:n2],
                                 wo1[:, ic * 128:(ic + 1) * 128],
                                 att[:, ic * nt + o2: ic * nt + o2 + n2],
                                 start=(ic == 0), stop=(ic == DC - 1))
                p.mm.inc(m)
            mk = p.mm.n

            @p.blk.vector
            def _(v, fc=fc, o2=o2, n2=n2):
                v.wait_ge(p.mm.sem, mk)
                p.vv.inc(v.tensor_copy(wrk[:, fc * nt + o2: fc * nt + o2 + n2],
                                       p.ps_main[:, 0:n2]))
            vk = p.vv.n

            @p.blk.tensor
            def _(t):
                t.wait_ge(p.vv.sem, vk)
    p.bar()

    for ch in range(nt // TC):
        t0 = ch * TC
        ln_fm(p, lambda c: c * nt + t0, TC, lambda c: c * nt + t0, None,
              lnb_bufs, g_col=lambda c: g2c[:, c:c + 1],
              b_col=lambda c: b2c[:, c:c + 1], src_sb=wrk, dst_sb=wrk)
        p.bar()
        ln_fm(p, lambda c: c * nt + t0, TC, lambda c: c * nt + t0,
              lambda c: c * nt + t0, lnb_bufs, src_sb=wrk, dst_sb=x_f, x_sb=x_f)
        p.bar()
    p.nc.leave_named_scope("Fwo", _f2, False)
    _f3 = p.nc.enter_named_scope("Fmix", False)[0]
    if upto == "x3":
        @p.blk.sync
        def _(sy):
            p.vv.wait(sy)
            p.ss.wait(sy)
            for c in range(DC):
                p.st.inc(sy.dma_start(out=xout[c * 128:(c + 1) * 128, :],
                                      in_=x_f[:, c * nt:(c + 1) * nt]), 16)
        p.bar()
        s2.close()
        attsc.close()
        return _finish()
    s2.close()
    attsc.close()

    s3 = ExitStack()

    def s3b(name, shape, dtype):
        return s3.enter_context(p.nc.sbuf_tensor(name, shape, dtype))

    lg = s3b("lg", [1, 8], F32)
    uu = s3b("uu_s", [1, 8], F32)
    selmax = s3b("selmax", [1, 1], F32)
    oh8 = s3b("oh8", [1, 8], F32)
    iota8 = s3b("iota8", [1, 8], F32)
    idxf = s3b("idxf", [1, 1], F32)
    idxi = s3b("idxi", [1, 1], I32)
    wsel = s3b("wsel", [128, DC * 128], BF16)
    bselr = s3b("bselr", [1, DM], F32)
    bselc = s3b("bselc", [128, DC], F32)
    xn3 = s3b("xn3", [128, DC * nt], BF16)

    @p.blk.sync
    def _(sy):
        p.dh.inc(sy.dma_start(out=lg[:], in_=logits_i), 16)
        p.dh.inc(sy.dma_start(out=uu[:], in_=uu_i), 16)

    @p.blk.gpsimd
    def _(g):
        for k8 in range(8):
            p.pg.inc(g.memset(iota8[0:1, k8:k8 + 1], float(k8)))
    p.bar()

    for ch in range(nt // TC):
        t0 = ch * TC
        ln_fm(p, lambda c: c * nt + t0, TC, lambda c: c * nt + t0, None,
              lnb_bufs, src_sb=x_f, dst_sb=xn3)
        p.bar()

    @p.blk.scalar
    def _(s):
        p.ss.inc(s.activation(uu[:], uu[:], AF.Ln))
        s.wait_ge(p.ss.sem, p.ss.n)
        p.ss.inc(s.activation(uu[:], uu[:], AF.Ln, scale=-1.0))

    @p.blk.vector
    def _(v):
        v.wait_ge(p.ss.sem, p.ss.n)
        p.vv.inc(v.tensor_tensor(oh8[:], lg[:], uu[:], ALU.subtract))
        v.wait_ge(p.vv.sem, p.vv.n)
        p.vv.inc(v.tensor_reduce(selmax[:], oh8[:], AX.X, ALU.max))
        v.wait_ge(p.vv.sem, p.vv.n)
        p.vv.inc(v.tensor_scalar(oh8[:], oh8[:], selmax[0:1, 0:1], None, ALU.is_ge))
        v.wait_ge(p.vv.sem, p.vv.n)
        p.vv.inc(v.tensor_mul(oh8[:], oh8[:], iota8[:]))
        v.wait_ge(p.vv.sem, p.vv.n)
        p.vv.inc(v.tensor_reduce(idxf[:], oh8[:], AX.X, ALU.add))
        v.wait_ge(p.vv.sem, p.vv.n)
        p.vv.inc(v.tensor_copy(idxi[:], idxf[:]))
    p.bar()
    if upto == "idx":
        dbgrow = s3b("dbgrow", [1, 256], F32)

        @p.blk.gpsimd
        def _(g):
            p.pg.inc(g.memset(dbgrow[:], 0.0))
        p.bar()

        @p.blk.vector
        def _(v):
            p.vv.inc(v.tensor_copy(dbgrow[0:1, 0:1], idxf[:]))
            v.wait_ge(p.vv.sem, p.vv.n)
            p.vv.inc(v.tensor_copy(dbgrow[0:1, 8:16], oh8[:]))
            v.wait_ge(p.vv.sem, p.vv.n)
            p.vv.inc(v.tensor_copy(dbgrow[0:1, 16:24], lg[:]))
            v.wait_ge(p.vv.sem, p.vv.n)
            p.vv.inc(v.tensor_copy(dbgrow[0:1, 24:32], uu[:]))
        p.bar()

        @p.blk.sync
        def _(sy):
            p.vv.wait(sy)
            p.st.inc(sy.dma_start(out=xout, in_=dbgrow[:]), 16)
        p.bar()
        s3.close()
        return _finish()

    @p.blk.gpsimd
    def _(g):
        with g.register("ixb") as ix:
            g.reg_load(ix, idxi[0:1, 0:1])
            for o in range(8):
                with g.If_cmp(ix, o, "IS_EQ"):
                    p.sg.inc(g.dma_start(out=bselr[:], in_=bmix_i[o:o + 1, :]), 16)
            p.sg.n -= 7 * 16
    p.bar()
    row_to_cols(p, bselr, DC, bselc, 0, p.ps_main)
    p.bar()

    mix_f = s3b("mix_f", [128, DC * nt], F32)
    for fc in range(DC):
        @p.blk.gpsimd
        def _(g, fc=fc):
            g.wait_ge(p.mm.sem, p.mm.n)
            with g.register(f"ixw{fc}") as ix:
                g.reg_load(ix, idxi[0:1, 0:1])
                for o in range(8):
                    with g.If_cmp(ix, o, "IS_EQ"):
                        p.sg.inc(g.dma_start(
                            out=wsel[:].rearrange("p (c f) -> p c f", c=DC),
                            in_=wmix_i[o][:, fc * 128:(fc + 1) * 128]
                            .rearrange("(c p) f -> p c f", p=128)), 16)
                p.sg.n -= 7 * 16
        sgw = p.sg.n

        for (o2, n2) in f_chunks(nt, 512):
            @p.blk.tensor
            def _(t, fc=fc, o2=o2, n2=n2):
                t.wait_ge(p.sg.sem, sgw)
                t.wait_ge(p.ss.sem, p.ss.n)
                for c in range(DC):
                    m = t.matmul(p.ps_main[:, 0:n2],
                                 wsel[:, c * 128:(c + 1) * 128],
                                 xn3[:, c * nt + o2: c * nt + o2 + n2],
                                 start=(c == 0), stop=(c == DC - 1))
                p.mm.inc(m)
            mk = p.mm.n

            @p.blk.scalar
            def _(s, fc=fc, o2=o2, n2=n2):
                s.wait_ge(p.mm.sem, mk)
                p.ss.inc(s.activation(mix_f[:, fc * nt + o2: fc * nt + o2 + n2],
                                      p.ps_main[:, 0:n2], AF.Identity,
                                      bias=bselc[:, fc:fc + 1]))
    p.bar()
    if upto == "mix":
        @p.blk.sync
        def _(sy):
            p.ss.wait(sy)
            p.vv.wait(sy)
            for c in range(DC):
                p.st.inc(sy.dma_start(out=xout[c * 128:(c + 1) * 128, :],
                                      in_=mix_f[:, c * nt:(c + 1) * nt]), 16)
        p.bar()
        s3.close()
        return _finish()

    for ch in range(nt // TC):
        t0 = ch * TC
        ln_fm(p, lambda c: c * nt + t0, TC, lambda c: c * nt + t0,
              lambda c: c * nt + t0, lnb_bufs, src_sb=mix_f, dst_sb=x_f, x_sb=x_f)
        p.bar()

    p.nc.leave_named_scope("Fmix", _f3, False)
    _f4 = p.nc.enter_named_scope("Ftr", False)[0]
    if upto == "fin":
        @p.blk.sync
        def _(sy):
            p.vv.wait(sy)
            p.ss.wait(sy)
            for c in range(DC):
                p.st.inc(sy.dma_start(out=xout[c * 128:(c + 1) * 128, :],
                                      in_=x_f[:, c * nt:(c + 1) * nt]), 16)
        p.bar()
        s3.close()
        return _finish()
    s3.close()

    s4 = ExitStack()
    ttm = s4.enter_context(p.nc.sbuf_tensor("ttm", [128, DM], F32))
    for t_ in range(nt // 128):
        pairs = [(x_f[:, c * nt + t_ * 128: c * nt + t_ * 128 + 128],
                  ttm[:, c * 128:(c + 1) * 128]) for c in range(DC)]
        transpose_tiles(p, pairs, identf, psim[:, 0:128], psim[:, QH:QH + 128])
        vdone = p.vv.n

        @p.blk.sync
        def _(sy, t_=t_):
            sy.wait_ge(p.vv.sem, vdone)
            p.st.inc(sy.dma_start(out=xout[t_ * 128:(t_ + 1) * 128, :],
                                  in_=ttm[:, 0:DM]), 16)

        @p.blk.tensor
        def _(t):
            t.wait_ge(p.st.sem, p.st.n)
    p.bar()

    @p.blk.sync
    def _(sy):
        p.dh.wait(sy)
        p.st.wait(sy)

    @p.blk.gpsimd
    def _(g):
        p.dg.wait(g)
        p.sg.wait(g)
    p.nc.leave_named_scope("Ftr", _f4, False)
    s4.close()
    return p.done()



_PROGS = {}


def _prog(key, builder, *args):
    if key not in _PROGS:
        _PROGS[key] = builder(*args)
    return _PROGS[key]


def kernel(**inputs):
    f32 = np.float32
    x = np.asarray(inputs["L0_attn_in_seq"], f32).reshape(16384, DM)
    identf = np.eye(128, dtype=f32)
    identb = np.eye(128, dtype=BF)
    sel2c = np.zeros((2, 128), f32)
    sel2c[0, 0:64] = 1.0
    sel2c[1, 64:128] = 1.0

    def dup64(a):
        a = np.asarray(a, f32).reshape(-1)
        return np.concatenate([a, a])[:, None].astype(f32)

    ncA = _prog("A", build_launch_A, NT)
    base = dict(tk=np.asarray(inputs["L0_attn_t_k"], f32),
                tv=np.asarray(inputs["L0_attn_t_v"], f32),
                tq=np.asarray(inputs["L0_attn_t_q"], f32),
                w1=np.asarray(inputs["L0_attn_W1"], f32),
                w2=np.asarray(inputs["L0_attn_W2"], f32),
                b1=np.asarray(inputs["L0_attn_b1"], f32).reshape(1, 256),
                b2=np.asarray(inputs["L0_attn_b2"], f32).reshape(1, DM),
                identf=identf, identb=identb)
    imsA = [dict(x_tm=np.ascontiguousarray(x[c * NT:(c + 1) * NT]), **base)
            for c in range(NCORES)]
    rA = run_bass_kernel_spmd(ncA, imsA, core_ids=list(range(NCORES))).results

    ncB = _prog("B", build_launch_BC, "B", NT)

    def gargs(pref, rs, cores):
        out = {}
        for k in range(4):
            r = rs[cores[k]]
            out[f"{pref}w1t_{k}"] = r["gw1t"]
            out[f"{pref}w2_{k}"] = r["gw2"]
            out[f"{pref}b1_{k}"] = r["gb1"]
            out[f"{pref}b2_{k}"] = r["gb2"]
        return out

    g0 = gargs("g0", rA, [0, 1, 2, 3])
    imsB = [dict(xfm=rA[c]["xfm"], qfm=rA[c]["qfm"], zbf=rA[c]["zbf"],
                 ztm=rA[c]["ztm"], zmyo=rA[c]["zmyo"],
                 w1=base["w1"], w2=base["w2"], b1=base["b1"], b2=base["b2"],
                 identf=identf, identb=identb, **g0)
            for c in range(NCORES)]
    rB = run_bass_kernel_spmd(ncB, imsB, core_ids=list(range(NCORES))).results

    ncC = _prog("C", build_launch_BC, "C", NT)
    g1 = gargs("g1", rB, [4, 5, 6, 7])
    imsC = [dict(xfm=rA[c]["xfm"], qfm=rA[c]["qfm"], zbf=rA[c]["zbf"],
                 ztm=rA[c]["ztm"], zmyo=rA[c]["zmyo"],
                 w1=base["w1"], w2=base["w2"], b1=base["b1"], b2=base["b2"],
                 identf=identf, identb=identb, **g0, **g1)
            for c in range(NCORES)]
    rC = run_bass_kernel_spmd(ncC, imsC, core_ids=list(range(NCORES))).results

    xs = [rB[c]["xnew"] if c < 4 else rC[c]["xnew"] for c in range(NCORES)]
    xouter = xs

    ncG = _prog("G", build_launch_G, False, NT)
    imsG = [dict(yfm=xs[c]) for c in range(NCORES)]
    rG = run_bass_kernel_spmd(ncG, imsG, core_ids=list(range(NCORES))).results
    xs = [rG[c]["ofm"] for c in range(NCORES)]

    ncD = _prog("D", build_launch_D, NT)
    KR = 512 + NT
    NQW = (256 + NT) // WIN
    m2 = host_band_masks()
    wqkv_all = np.asarray(inputs["L0_ffn_Wqkv"], f32)
    wo_all = np.asarray(inputs["L0_ffn_Wo"], f32)
    w1_all = np.asarray(inputs["L0_ffn_W1"], f32)
    w2_all = np.asarray(inputs["L0_ffn_W2"], f32)
    b1_all = np.asarray(inputs["L0_ffn_b1"], f32)
    b2_all = np.asarray(inputs["L0_ffn_b2"], f32)
    lng_all = np.asarray(inputs["L0_ffn_ln_g"], f32)
    lnb_all = np.asarray(inputs["L0_ffn_ln_b"], f32)
    qs_all = np.asarray(inputs["L0_ffn_q_scale"], f32)
    ks_all = np.asarray(inputs["L0_ffn_k_scale"], f32)

    core_rot = []
    core_masks = []
    core_hval = []
    for c in range(NCORES):
        sb_ = (c % 4) * NT
        pos = np.arange(sb_ - 512, sb_ + NT)
        core_rot.append(host_rot_tables_for(pos))
        mk = np.broadcast_to(m2[0][None], (NQW, 4, 128, WIN)).copy()
        if c % 4 == 0:
            mk[1] = m2[1]
        core_masks.append(np.ascontiguousarray(mk))
        core_hval.append(np.full((128, 1), 0.0 if c % 4 == 0 else 1.0, f32))

    for li in range(6):
        xg0 = np.concatenate(xs[0:4], axis=1)
        xg1 = np.concatenate(xs[4:8], axis=1)
        imsD = []
        for c in range(NCORES):
            xg = xg0 if c < 4 else xg1
            sb_ = (c % 4) * NT
            if sb_ == 0:
                xin = np.concatenate(
                    [np.zeros((DM, HALO), f32), xg[:, 0:NT]], axis=1)
            else:
                xin = np.ascontiguousarray(xg[:, sb_ - HALO: sb_ + NT])
            rqc, rqs_, rkc, rks_ = core_rot[c]
            imsD.append(dict(
                xin=xin, wqkv=wqkv_all[li], wo=wo_all[li], w1=w1_all[li],
                w2=w2_all[li], b1=b1_all[li:li + 1], b2=b2_all[li:li + 1],
                lng=lng_all[li:li + 1], lnb=lnb_all[li:li + 1],
                qscol=dup64(qs_all[li]), kscol=dup64(ks_all[li]),
                rotqc=rqc, rotqs=rqs_, rotkc=rkc, rotks=rks_,
                masks=core_masks[c], hvalid=core_hval[c], sel2c=sel2c,
                identb=identb))
        rD = run_bass_kernel_spmd(ncD, imsD, core_ids=list(range(NCORES))).results
        xs = [rD[c]["xout"] for c in range(NCORES)]

    ncE = _prog("E", build_launch_E, NT)
    imsE = [dict(yfm=xs[c], xfm=xouter[c], wq=np.asarray(inputs["L1_attn_Wq"], f32),
                 wkv=np.asarray(inputs["L1_attn_Wkv"], f32),
                 ln1g=np.asarray(inputs["L1_attn_ln1_g"], f32).reshape(1, DM),
                 ln1b=np.asarray(inputs["L1_attn_ln1_b"], f32).reshape(1, DM))
            for c in range(NCORES)]
    rE = run_bass_kernel_spmd(ncE, imsE, core_ids=list(range(NCORES))).results
    xs = [rE[c]["x2fm"] for c in range(NCORES)]

    ncF = _prog("F", build_launch_F, NT, 8192)
    kf0 = np.concatenate([rE[c]["kfm"] for c in range(4)], axis=1)
    kf1 = np.concatenate([rE[c]["kfm"] for c in range(4, 8)], axis=1)
    vf0 = np.concatenate([rE[c]["vaug"] for c in range(4)], axis=1)
    vf1 = np.concatenate([rE[c]["vaug"] for c in range(4, 8)], axis=1)
    imsF = []
    for c in range(NCORES):
        imsF.append(dict(
            xfm=xs[c], qfm=rE[c]["qfm"],
            kfull=kf0 if c < 4 else kf1, vfull=vf0 if c < 4 else vf1,
            wo=np.asarray(inputs["L1_attn_Wo"], f32),
            ln2g=np.asarray(inputs["L1_attn_ln2_g"], f32).reshape(1, DM),
            ln2b=np.asarray(inputs["L1_attn_ln2_b"], f32).reshape(1, DM),
            wmix=np.asarray(inputs["L1_ffn_W"], f32),
            bmix=np.asarray(inputs["L1_ffn_b"], f32),
            logits=np.asarray(inputs["L1_ffn_logits"], f32).reshape(1, 8),
            uu=np.asarray(inputs["L1_ffn_u"], f32).reshape(1, 8),
            sel2c=sel2c, identf=identf))
    rF = run_bass_kernel_spmd(ncF, imsF, core_ids=list(range(NCORES))).results

    out = np.concatenate([rF[c]["xout"] for c in range(NCORES)], axis=0)
    return out.reshape(2, 8192, DM).astype(np.float32)
